# version 17
# baseline (speedup 1.0000x reference)
import math
import numpy as np
import concourse.bass as bass
import concourse.mybir as mybir
from concourse.bass_utils import run_bass_kernel_spmd
from concourse.alu_op_type import AluOpType as OP
from bass_rust import ReduceOp
from contextlib import ExitStack

F32 = mybir.dt.float32
BF16 = mybir.dt.bfloat16
I32 = mybir.dt.int32
U32 = mybir.dt.uint32
AF = mybir.ActivationFunctionType

T = 2048
D = 1024
NT = 16
HA = 16
NEG = -1e30
MICH_ITERS = 10
LN_EPS = 1e-5
MAGIC = 3.0 * 2 ** 22

ENGS = ("v", "s", "g", "t", "q")
DEBUG_TAPS = []


class KB:
    def __init__(self, nc):
        self.nc = nc
        self.ops = {e: [] for e in ENGS}
        self.sem = {}
        self.cnt = {e: 0 for e in ENGS}
        self.dma_cnt = {}
        self.dma_sem_pool = ["d%d" % i for i in range(24)]
        self.dma_sem_map = {}
        self.win_store = [None, None, None, None]

    def on(self, e, fn):
        self.ops[e].append(fn)

    def bump(self, e, drain=True):
        self.cnt[e] += 1
        tgt = self.cnt[e]

        def fn(eng):
            if drain and e in ("v", "s", "g", "t"):
                eng.drain()
            eng.sem_inc(self.sem[e], 1)
        self.on(e, fn)
        return tgt

    def wait(self, e, producer_sem, count):
        if count <= 0:
            return
        self.on(e, lambda eng: eng.wait_ge(self.sem[producer_sem], count))

    def barrier(self, engines=("v", "s", "g", "t", "q")):
        marks = {e: self.bump(e) for e in engines}
        for e in engines:
            for p, c in marks.items():
                if p != e:
                    self.wait(e, p, c)

    def dma(self, queue_eng, out_ap, in_ap, key="st"):
        if key not in self.dma_sem_map:
            idx = len(self.dma_sem_map)
            assert idx < len(self.dma_sem_pool), "out of dma sems"
            self.dma_sem_map[key] = self.dma_sem_pool[idx]
            self.dma_cnt[self.dma_sem_map[key]] = 0
        sem = self.dma_sem_map[key]
        self.dma_cnt[sem] += 16
        tgt = self.dma_cnt[sem]
        self.on(queue_eng, lambda eng: eng.dma_start(out=out_ap, in_=in_ap).then_inc(self.sem[sem], 16))
        return sem, tgt


def build_core_program(debug_taps=()):
    nc = bass.Bass("TRN2", target_bir_lowering=False)
    kb = KB(nc)
    V, S, G, TE, Q = "v", "s", "g", "t", "q"

    dr = {}
    dr["x"] = nc.dram_tensor("x", [T, D], F32, kind="ExternalInput").ap()
    for nm in ("wq", "wk", "wv", "wo"):
        dr[nm] = nc.dram_tensor(nm, [D, D], F32, kind="ExternalInput").ap()
    dr["out"] = nc.dram_tensor("out", [T, D], F32, kind="ExternalOutput").ap()
    dr["strm"] = nc.dram_tensor("strm", [T, D], F32).ap()
    dr["w1"] = nc.dram_tensor("w1", [D, 4096], F32, kind="ExternalInput").ap()
    dr["w2"] = nc.dram_tensor("w2", [4096, D], F32, kind="ExternalInput").ap()
    dr["convw"] = nc.dram_tensor("convw", [128, 32, 3], F32, kind="ExternalInput").ap()
    dr["convb"] = nc.dram_tensor("convb", [128, 32], F32, kind="ExternalInput").ap()
    dr["w1q"] = nc.dram_tensor("w1q", [D, 4096], BF16).ap()
    dr["w2q"] = nc.dram_tensor("w2q", [4096, D], BF16).ap()
    for nm, shp in (("wdkv", [D, 256]), ("wuk", [256, 512]), ("wuv", [256, 512]),
                    ("wdq", [D, 512]), ("wuq", [512, 512]), ("wqr", [512, 256]),
                    ("wkr", [D, 32]), ("widxq", [D, 256]), ("widxk", [D, 64]),
                    ("wout", [512, D]), ("idxw", [1, 4]), ("rawd", [1, 16])):
        dr[nm] = nc.dram_tensor(nm, shp, F32, kind="ExternalInput").ap()
    for nm in ("tb_tq", "tb_t2", "tb_pp"):
        dr[nm] = nc.dram_tensor(nm, [T, 32], F32, kind="ExternalInput").ap()
    dr["tb_jj"] = nc.dram_tensor("tb_jj", [1, 1024], F32, kind="ExternalInput").ap()
    dr["mk_band"] = nc.dram_tensor("mk_band", [128, 256], F32, kind="ExternalInput").ap()
    dr["mk_b0"] = nc.dram_tensor("mk_b0", [128, 128], F32, kind="ExternalInput").ap()
    dr["mk_idiag"] = nc.dram_tensor("mk_idiag", [128, 128], F32, kind="ExternalInput").ap()
    dr["mk_iprev"] = nc.dram_tensor("mk_iprev", [128, 128], F32, kind="ExternalInput").ap()
    dr["gtab"] = nc.dram_tensor("gtab", [T, 1088], BF16).ap()
    taps = {}
    for nm, shape, dt in debug_taps:
        taps[nm] = nc.dram_tensor(nm, shape, dt, kind="ExternalOutput").ap()

    with ExitStack() as ctx:
        sbuf = lambda n_, s, d: ctx.enter_context(nc.sbuf_tensor(n_, s, d))
        xw = sbuf("xw", [128, 4, D], F32)
        xhT = sbuf("xhT", [128, 8, T], BF16)
        qT = sbuf("qT", [128, 8, T], BF16)
        bigA = sbuf("bigA", [128, 16384], F32)
        kT = bigA[:, 0:8192].bitcast(BF16).rearrange("p (a b) -> p a b", a=8)
        vtok = bigA[:, 8192:16384].bitcast(BF16).rearrange("p (a b) -> p a b", a=NT)
        ptk = sbuf("ptk", [128, 2, T], BF16)
        wAB = sbuf("wAB", [128, 16, 512], BF16)
        wA = wAB[:, 0:8, :]
        wB = wAB[:, 8:16, :]
        shr12 = sbuf("shr12", [128, 3072], F32)
        cands = sbuf("cands", [128, HA, 16], F32)
        taus = sbuf("taus", [128, HA], F32)
        tausq = sbuf("tausq", [128, NT, HA], F32)
        scA = sbuf("scA", [128, 512], F32)
        srow = scA[0:1, :]
        xnb = sbuf("xnb", [128, D], BF16)
        mv = sbuf("mv", [128, NT, 2], F32)
        msc = sbuf("msc", [128, 32], F32)
        msc2 = sbuf("msc2", [128, 32], F32)
        rst = sbuf("rst", [128, NT, 2], F32)
        stat6 = sbuf("stat6", [128, 12], F32)
        utri = sbuf("utri", [128, 128], F32)
        identf = sbuf("identf", [128, 128], F32)
        onesf = sbuf("onesf", [1, 128], F32)
        o2T = sbuf("o2T", [128, 8, 512], BF16)
        hpad = xhT.rearrange("p a b -> p (a b)")[:, 4096:8208].bitcast(F32)\
            .rearrange("p (a b) -> p a b", a=4)
        scD = shr12[:, 0:1024]
        sc4 = shr12[:, 1024:2048]
        sci = shr12[:, 2048:3072].bitcast(I32)
        amt = sbuf("amt", [128, 512], F32)
        amtn = shr12[:, 1024:1536]
        s2bc = shr12[:, 1536:2048]
        cvw = sbuf("cvw", [128, 32, 3], F32)
        cvb = sbuf("cvb", [128, 32], F32)
        ffs = sbuf("ffs", [128, NT, 4], F32)
        wst = sbuf("wst", [128, 176], F32)
        ltri01b = sbuf("ltri01b", [128, 128], BF16)
        identb = sbuf("identb", [128, 128], BF16)
        ones1 = sbuf("ones1", [1, 128], BF16)
        negonesf = sbuf("negonesf", [1, 128], F32)
        cst = sbuf("cst", [128, 16], F32)
        cst_slots = {}

        def const_f32(val):
            if val not in cst_slots:
                cst_slots[val] = len(cst_slots)
                assert len(cst_slots) <= 16
                nc.const_aps.register(F32, float(val), cst[:, cst_slots[val]:cst_slots[val] + 1])
            return cst[:, cst_slots[val]:cst_slots[val] + 1]

        psA = ctx.enter_context(nc.psum_tensor("psA", [128, 2048], F32))
        psB = ctx.enter_context(nc.psum_tensor("psB", [128, 2048], F32))
        aoT = xhT

        for e in ENGS:
            kb.sem[e] = ctx.enter_context(nc.semaphore("sem_" + e))
        for s_ in kb.dma_sem_pool:
            kb.sem[s_] = ctx.enter_context(nc.semaphore("sem_" + s_))

        const_f32(LN_EPS)

        def init_consts(g):
            for val, slot in cst_slots.items():
                g.memset(cst[:, slot:slot + 1], float(val))
            g.memset(utri[:], 0.0)
            g.memset(identb[:], 1.0)
            g.memset(ones1[:], 1.0)
            g.memset(negonesf[:], -1.0)
            g.memset(ltri01b[:], 1.0)
            g.memset(identf[:], 1.0)
            g.memset(onesf[:], 1.0)
            g.drain()
            g.affine_select(out=identf[:], in_=identf[:], compare_op=OP.is_equal,
                            fill=0.0, base=0, pattern=[[-1, 128]], channel_multiplier=1)
            g.affine_select(out=utri[:], in_=utri[:], compare_op=OP.is_ge,
                            fill=NEG, base=0, pattern=[[-1, 128]], channel_multiplier=1)
            g.affine_select(out=ltri01b[:], in_=ltri01b[:], compare_op=OP.is_ge,
                            fill=0.0, base=0, pattern=[[1, 128]], channel_multiplier=-1)
            g.affine_select(out=identb[:], in_=identb[:], compare_op=OP.is_equal,
                            fill=0.0, base=0, pattern=[[-1, 128]], channel_multiplier=1)
            g.drain()
            g.iota(wst[:, 40:48], [[1, 8]], base=1, channel_multiplier=0,
                   allow_small_or_imprecise_dtypes=True)
        kb.on(G, init_consts)
        gconst = kb.bump(G)
        kb.wait(TE, G, gconst)
        kb.wait(V, G, gconst)

        def f_rinv(v):
            v.reciprocal(out=wst[:, 40:48], in_=wst[:, 40:48])
            v.drain()
        kb.on(V, f_rinv)
        kb.bump(V)
        kb.wait(V, G, gconst)

        buf_free = {"wA": 0, "wB": 0}
        bufs = {"wA": wA, "wB": wB}

        def load_w_half(nm, bufname, half):
            src = dr[nm][:, half * 512:(half + 1) * 512].rearrange("(a p) b -> p a b", p=128)
            kb.wait(G, TE, buf_free[bufname])
            return kb.dma(G, bufs[bufname], src, key=bufname)

        ld_wq0 = load_w_half("wq", "wA", 0)
        ld_wq1 = load_w_half("wq", "wB", 1)

        def emit_ln_transpose(src_dram, dst):
            win_free = [0, 0, 0, 0]
            for i in range(NT):
                slot = i % 4
                kb.wait(Q, V, win_free[slot])
                if kb.win_store[slot] is not None:
                    kb.wait(Q, kb.win_store[slot][0], kb.win_store[slot][1])
                ld = kb.dma(Q, xw[:, slot, :], src_dram[i * 128:(i + 1) * 128, :], key="win%d" % slot)
                kb.wait(V, ld[0], ld[1])

                def f_stat(v, i=i, slot=slot):
                    v.bn_stats(out=stat6[:, 0:6], in_=xw[:, slot, 0:512])
                    v.bn_stats(out=stat6[:, 6:12], in_=xw[:, slot, 512:1024])
                    v.drain()
                    v.bn_aggr(out=mv[:, i, :], in_=stat6[:])
                    v.drain()
                kb.on(V, f_stat)
                c1 = kb.bump(V)
                kb.wait(S, V, c1)

                def f_sq(s, i=i):
                    s.activation(out=rst[:, i, 0:1], in_=mv[:, i, 1:2],
                                 func=AF.Sqrt, bias=LN_EPS, scale=1.0)
                kb.on(S, f_sq)
                c2 = kb.bump(S)
                kb.wait(V, S, c2)

                def f_norm(v, i=i, slot=slot):
                    v.reciprocal(out=rst[:, i, 1:2], in_=rst[:, i, 0:1])
                    v.drain()
                    v.tensor_scalar(out=xnb[:], in0=xw[:, slot, :], scalar1=mv[:, i, 0:1],
                                    scalar2=rst[:, i, 1:2], op0=OP.subtract, op1=OP.mult)
                kb.on(V, f_norm)
                cnorm = kb.bump(V)
                win_free[slot] = cnorm
                kb.wait(TE, V, cnorm)
                half = i % 2
                pview = psA[:, half * 512:(half + 1) * 512].bitcast(BF16)
                for dc in range(8):
                    def f_tp(t, dc=dc, pview=pview):
                        t.transpose(pview[:, dc * 128:(dc + 1) * 128],
                                    xnb[:, dc * 128:(dc + 1) * 128], identb[:])
                    kb.on(TE, f_tp)
                ctp = kb.bump(TE)
                kb.wait(V, TE, ctp)

                def f_evac(v, i=i, pview=pview, dst=dst):
                    v.tensor_copy(out=dst[:, :, i * 128:(i + 1) * 128],
                                  in_=pview.rearrange("p (a b) -> p a b", a=8))
                kb.on(V, f_evac)
                pe_ok = kb.bump(V)
                kb.wait(TE, V, pe_ok)

        emit_ln_transpose(dr["x"], xhT)
        kb.barrier()

        if "t_xhT" in taps:
            st = None
            for dc in range(8):
                st = kb.dma(Q, taps["t_xhT"][dc * 128:(dc + 1) * 128, :], xhT[:, dc, :], key="tap")
            kb.wait(Q, st[0], st[1])

        SCALE_A = 0.125

        def proj_featmajor(dst, wname, scale, preloaded=None):
            for half in range(2):
                bufname = "wA" if half == 0 else "wB"
                ld = preloaded[half] if preloaded else load_w_half(wname, bufname, half)
                kb.wait(TE, ld[0], ld[1])
                for ec in range(4):
                    e_glob = half * 4 + ec
                    for tc in range(4):
                        slot = (e_glob * 4 + tc) % 3
                        ps = psB[:, slot * 512:(slot + 1) * 512]
                        for dc in range(8):
                            def f_mm(t, ps=ps, dc=dc, ec=ec, tc=tc, wb=bufs[bufname]):
                                t.matmul(ps, wb[:, dc, ec * 128:(ec + 1) * 128],
                                         xhT[:, dc, tc * 512:(tc + 1) * 512],
                                         start=(dc == 0), stop=(dc == 7))
                            kb.on(TE, f_mm)
                        cm = kb.bump(TE)
                        kb.wait(S, TE, cm)

                        def f_ev(s, ps=ps, e_glob=e_glob, tc=tc, dst=dst, scale=scale):
                            s.activation(out=dst[:, e_glob, tc * 512:(tc + 1) * 512], in_=ps,
                                         func=AF.Copy, scale=scale)
                        kb.on(S, f_ev)
                        ce = kb.bump(S)
                        kb.wait(TE, S, ce)
                buf_free[bufname] = kb.cnt["t"]

        proj_featmajor(qT, "wq", SCALE_A, preloaded=(ld_wq0, ld_wq1))
        proj_featmajor(kT, "wk", 1.0)

        for half in range(2):
            bufname = "wA" if half == 0 else "wB"
            ld = load_w_half("wv", bufname, half)
            kb.wait(TE, ld[0], ld[1])
            for i in range(NT):
                slot = i % 3
                ps = psB[:, slot * 512:(slot + 1) * 512]
                for dc in range(8):
                    def f_mm(t, ps=ps, dc=dc, i=i, wb=bufs[bufname]):
                        t.matmul(ps, xhT[:, dc, i * 128:(i + 1) * 128], wb[:, dc, :],
                                 start=(dc == 0), stop=(dc == 7))
                    kb.on(TE, f_mm)
                cm = kb.bump(TE)
                kb.wait(S, TE, cm)

                def f_ev(s, ps=ps, i=i, half=half):
                    s.activation(out=vtok[:, i, half * 512:(half + 1) * 512], in_=ps, func=AF.Copy)
                kb.on(S, f_ev)
                ce = kb.bump(S)
                kb.wait(TE, S, ce)
            buf_free[bufname] = kb.cnt["t"]

        kb.barrier()
        ld_wo0 = load_w_half("wo", "wA", 0)
        ld_wo1 = load_w_half("wo", "wB", 1)

        slab_free = {0: 0, 1: 0}
        for qt in range(NT):
            kw = (qt + 1) * 128
            for h in range(HA):
                slab = psA if h % 2 == 0 else psB
                sid = h % 2
                kb.wait(TE, V, slab_free[sid])
                hp = (h % 2) * 64
                hc = h // 2
                for kc in range((kw + 511) // 512):
                    w = min(512, kw - kc * 512)

                    def f_mm(t, slab=slab, kc=kc, w=w, hp=hp, hc=hc, qt=qt):
                        t.matmul(slab[:, kc * 512:kc * 512 + w],
                                 qT[hp:hp + 64, hc, qt * 128:(qt + 1) * 128],
                                 kT[hp:hp + 64, hc, kc * 512:kc * 512 + w],
                                 start=True, stop=True)
                    kb.on(TE, f_mm)
                cm = kb.bump(TE)
                kb.wait(V, TE, cm)

                def f_ex(v, slab=slab, kw=kw, qt=qt, h=h):
                    v.tensor_tensor(out=slab[:, qt * 128:qt * 128 + 128],
                                    in0=slab[:, qt * 128:qt * 128 + 128], in1=utri[:], op=OP.add)
                    v.drain()
                    v.max(out=cands[:, h, 0:8], in_=slab[:, 0:kw])
                kb.on(V, f_ex)
                slab_free[sid] = kb.bump(V)

            def f_tau(v, qt=qt):
                cview = cands[:, :, 0:8]
                a = scA[:, 0:128].rearrange("p (a b) -> p a b", a=HA)
                bq = scA[:, 128:256].rearrange("p (a b) -> p a b", a=HA)
                v.drain()
                v.tensor_copy(out=a, in_=cview)
                v.drain()
                for d in (1, 2, 4):
                    v.tensor_tensor(out=bq[:, :, d:8], in0=a[:, :, d:8], in1=a[:, :, 0:8 - d], op=OP.add)
                    v.tensor_copy(out=bq[:, :, 0:d], in_=a[:, :, 0:d])
                    v.drain()
                    a, bq = bq, a
                v.tensor_scalar(out=a, in0=a, scalar1=1.0, scalar2=None, op0=OP.subtract)
                v.drain()
                v.tensor_tensor(out=a, in0=a,
                                in1=wst[:, 40:48].unsqueeze(1).broadcast_to((128, HA, 8)), op=OP.mult)
                v.drain()
                v.tensor_reduce(out=taus[:], in_=a, axis=mybir.AxisListType.X, op=OP.max)
                v.drain()
                v.tensor_copy(out=tausq[:, qt, :], in_=taus[:])
                v.drain()
            kb.on(V, f_tau)
            kb.bump(V)

        kb.barrier()
        if "t_tau" in taps:
            st = None
            for i in range(NT):
                st = kb.dma(Q, taps["t_tau"][i * 128:(i + 1) * 128, :], tausq[:, i, :], key="tap")
            kb.wait(Q, st[0], st[1])

        p1_pieces = []
        for n in range(8):
            for p in range(8):
                p1_pieces.append(("w1", n, p, 48 + n * 8 + p))
        for n in range(32):
            for p in range(2):
                p1_pieces.append(("w2", n, p, 112 + n * 2 + p))
        p1_land = [shr12[:, 2048:2560], shr12[:, 2560:3072]]
        p1_v = {}
        p1_ld = {}
        gp1_done = {}

        def p1_issue(k):
            if k >= len(p1_pieces):
                return
            wname, n, p, c = p1_pieces[k]
            if k >= 2:
                kb.wait(G, V, p1_v[k - 2])
            p1_ld[k] = kb.dma(G, p1_land[k % 2],
                              dr[wname][n * 128:(n + 1) * 128, p * 512:(p + 1) * 512], key="pqg%d" % (k % 2))

        def p1_reduce(k):
            wname, n, p, c = p1_pieces[k]
            kb.wait(V, p1_ld[k][0], p1_ld[k][1])

            def f_pr(v, c=c, b=k % 2):
                v.tensor_reduce(out=wst[:, c:c + 1], in_=p1_land[b], axis=mybir.AxisListType.X,
                                op=OP.add, apply_absolute_value=True)
                v.drain()
            kb.on(V, f_pr)
            p1_v[k] = kb.bump(V)
            p1_issue(k + 2)
        p1_issue(0)
        p1_issue(1)

        av_done = {0: 0, 1: 0}
        ev_prev = {0: 0, 1: 0, 2: 0, 3: 0}
        ntrf = shr12[0:1, 0:2048]
        taubc = shr12[:, 0:2048]
        for h in range(HA):
            for k in range(h * 8, h * 8 + 8):
                p1_reduce(k)
            if h == HA - 1:
                gp1_done["w1"] = gp1_done["w2"] = kb.cnt["v"]
            hp = (h % 2) * 64
            hc = h // 2
            for qt in range(NT):
                def f_tp(t, qt=qt, h=h):
                    t.transpose(psB[0:1, qt * 128:(qt + 1) * 128], tausq[:, qt, h:h + 1], identf[:])
                kb.on(TE, f_tp)
            ctp = kb.bump(TE)
            kb.wait(V, TE, ctp)

            def f_ntr(v):
                v.tensor_copy(out=ntrf, in_=psB[0:1, 0:T])
            kb.on(V, f_ntr)
            cn = kb.bump(V)
            kb.wait(TE, V, cn)

            def f_tbc(t):
                for c in range(4):
                    t.matmul(psB[:, c * 512:(c + 1) * 512], onesf[0:1, :],
                             ntrf[0:1, c * 512:(c + 1) * 512], start=True, stop=True)
            kb.on(TE, f_tbc)
            ctb = kb.bump(TE)
            kb.wait(V, TE, ctb)

            def f_tbev(v):
                for c in range(4):
                    v.tensor_copy(out=taubc[:, c * 512:(c + 1) * 512], in_=psB[:, c * 512:(c + 1) * 512])
                v.drain()
            kb.on(V, f_tbev)
            ctv = kb.bump(V)
            kb.wait(TE, V, ctv)

            def issue_av(pend):
                kt_, buf_, cevac_, cmask_ = pend
                kb.wait(TE, S, cevac_)
                kb.wait(TE, V, cmask_)
                for qc in range(4):
                    if (qc + 1) * 512 <= kt_ * 128:
                        continue
                    q0c = max(qc * 512, kt_ * 128)
                    w = (qc + 1) * 512 - q0c
                    last_kt = min(15, 4 * qc + 3)

                    def f_av(t, kt=kt_, q0c=q0c, w=w, h=h, buf=buf_, last_kt=last_kt):
                        t.matmul(psB[0:64, q0c:q0c + w],
                                 vtok[:, kt, h * 64:h * 64 + 64],
                                 ptk[:, buf, q0c - kt * 128:q0c - kt * 128 + w],
                                 start=(kt == 0), stop=(kt == last_kt))
                    kb.on(TE, f_av)
                av_done[buf_] = kb.bump(TE)

            pend = None
            chunkc = 0
            for kt in range(NT):
                buf = kt % 2
                qw_total = T - kt * 128
                for jc in range((qw_total + 511) // 512):
                    q0 = kt * 128 + jc * 512
                    w = min(512, T - q0)
                    slot = chunkc % 4
                    chunkc += 1
                    ps = psA[:, slot * 512:slot * 512 + w]
                    kb.wait(TE, S, ev_prev[slot])

                    def f_mm(t, ps=ps, kt=kt, q0=q0, w=w, hp=hp, hc=hc):
                        t.matmul(ps, kT[hp:hp + 64, hc, kt * 128:(kt + 1) * 128],
                                 qT[hp:hp + 64, hc, q0:q0 + w], start=True, stop=True)
                    kb.on(TE, f_mm)
                    cm = kb.bump(TE)
                    kb.wait(V, TE, cm)

                    def f_sub(v, ps=ps, q0=q0, w=w):
                        v.tensor_tensor(out=ps, in0=ps, in1=taubc[:, q0:q0 + w], op=OP.subtract)
                        v.drain()
                    kb.on(V, f_sub)
                    cs = kb.bump(V)
                    kb.wait(S, V, cs)
                    if jc == 0:
                        kb.wait(S, TE, av_done[buf])

                    def f_ev(s, ps=ps, kt=kt, q0=q0, w=w, buf=buf):
                        s.activation(out=ptk[:, buf, q0 - kt * 128:q0 - kt * 128 + w], in_=ps, func=AF.Relu)
                    kb.on(S, f_ev)
                    ev_prev[slot] = kb.bump(S)
                    if jc == 0:
                        kb.wait(V, S, ev_prev[slot])

                        def f_msk(v, buf=buf):
                            v.tensor_tensor(out=ptk[:, buf, 0:128], in0=ptk[:, buf, 0:128],
                                            in1=ltri01b[:], op=OP.mult)
                        kb.on(V, f_msk)
                        cmask = kb.bump(V)
                if pend is not None:
                    issue_av(pend)
                pend = (kt, buf, ev_prev[(chunkc - 1) % 4], cmask)
            issue_av(pend)
            kb.wait(V, TE, av_done[(NT - 1) % 2])

            def f_avev(v, hp=hp, hc=hc):
                v.tensor_copy(out=aoT[hp:hp + 64, hc, :], in_=psB[0:64, :])
            kb.on(V, f_avev)
            cae = kb.bump(V)
            kb.wait(TE, V, cae)

        kb.barrier()
        if "t_aoT" in taps:
            st = None
            for dc in range(8):
                st = kb.dma(Q, taps["t_aoT"][dc * 128:(dc + 1) * 128, :], aoT[:, dc, :], key="tap")
            kb.wait(Q, st[0], st[1])


        for ld in (ld_wo0, ld_wo1):
            kb.wait(TE, ld[0], ld[1])
        win_free = [0, 0, 0, 0]
        st = None
        st_all = []
        for i in range(NT):
            slot = i % 4
            kb.wait(Q, V, win_free[slot])
            if kb.win_store[slot] is not None:
                kb.wait(Q, kb.win_store[slot][0], kb.win_store[slot][1])
            ld = kb.dma(Q, xw[:, slot, :], dr["x"][i * 128:(i + 1) * 128, :], key="win%d" % slot)
            for half in range(2):
                wbuf = wA if half == 0 else wB
                pslot = (i * 2 + half) % 3
                ps = psB[:, pslot * 512:(pslot + 1) * 512]
                for dc in range(8):
                    def f_mm(t, ps=ps, dc=dc, i=i, wbuf=wbuf):
                        t.matmul(ps, aoT[:, dc, i * 128:(i + 1) * 128], wbuf[:, dc, :],
                                 start=(dc == 0), stop=(dc == 7))
                    kb.on(TE, f_mm)
                cm = kb.bump(TE)
                kb.wait(V, TE, cm)
                if half == 0:
                    kb.wait(V, ld[0], ld[1])

                def f_ev(v, ps=ps, slot=slot, half=half):
                    v.tensor_tensor(out=xw[:, slot, half * 512:(half + 1) * 512],
                                    in0=xw[:, slot, half * 512:(half + 1) * 512], in1=ps, op=OP.add)
                kb.on(V, f_ev)
                ce = kb.bump(V)
                kb.wait(TE, V, ce)
            cdone = kb.bump(V)
            kb.wait(Q, V, cdone)
            st = kb.dma(Q, dr["strm"][i * 128:(i + 1) * 128, :], xw[:, slot, :], key="win%d" % slot)
            win_free[slot] = cdone
            kb.win_store[slot] = st
            st_all.append(st)
        buf_free["wA"] = buf_free["wB"] = kb.cnt["t"]

        kb.barrier()
        for st in st_all[-4:]:
            kb.wait(Q, st[0], st[1])

        k2v = xhT[:, :, :].rearrange("p a b -> p (a b)").rearrange("p (a b) -> p a b", a=32)
        k1T = qT
        wf1 = [ptk[:, 0, 0:1024].rearrange("p (a b) -> p a b", a=8),
               ptk[:, 1, 0:1024].rearrange("p (a b) -> p a b", a=8),
               xhT[:, 0, 0:1024].rearrange("p (a b) -> p a b", a=8),
               xhT[:, 1, 0:1024].rearrange("p (a b) -> p a b", a=8)]
        wf2 = [wAB[:, :, :].rearrange("p a b -> p (a b)")[:, j * 4096:(j + 1) * 4096]
               .rearrange("p (a b) -> p a b", a=32) for j in range(2)]

        def emit_prequant(wname, wqname, nchunks, ccols, stat_col):
            land = [bigA[:, 0:4096], bigA[:, 4096:8192]]
            pbase = 48 if wname == "w1" else 112
            kb.wait(V, "v", gp1_done[wname]) if False else None

            def f_mean(v, pbase=pbase):
                v.tensor_reduce(out=wst[:, 32:33], in_=wst[:, pbase:pbase + 64], axis=mybir.AxisListType.X, op=OP.add)
                v.drain()
            kb.on(V, f_mean)
            cv = kb.bump(V)
            kb.wait(TE, V, cv)

            def f_tpm(t):
                t.transpose(psB[0:1, 0:128], wst[:, 32:33], identf[:])
            kb.on(TE, f_tpm)
            ct1 = kb.bump(TE)
            kb.wait(V, TE, ct1)

            def f_rtot(v):
                v.tensor_reduce(out=scA[0:1, 0:1], in_=psB[0:1, 0:128], axis=mybir.AxisListType.X, op=OP.add)
            kb.on(V, f_rtot)
            cv1 = kb.bump(V)
            kb.wait(TE, V, cv1)

            def f_bcm(t):
                t.matmul(psB[:, 256:257], onesf[0:1, :], scA[0:1, 0:1], start=True, stop=True)
            kb.on(TE, f_bcm)
            ct2 = kb.bump(TE)
            kb.wait(V, TE, ct2)

            def f_cptot(v):
                v.tensor_copy(out=wst[:, 33:34], in_=psB[:, 256:257])
                v.drain()
            kb.on(V, f_cptot)

            def f_scale(v, nchunks=nchunks, ccols=ccols, stat_col=stat_col):
                v.tensor_scalar(out=wst[:, 35:36], in0=wst[:, 33:34],
                                scalar1=1.0 / (nchunks * 128 * ccols), scalar2=1e-5,
                                op0=OP.mult, op1=OP.max)
                v.drain()
                v.reciprocal(out=wst[:, 34:35], in_=wst[:, 35:36])
                v.drain()
            kb.on(V, f_scale)
            cv2 = kb.bump(V)
            for b_ in (0, 1):
                kname = "pq%d" % b_
                if kname in kb.dma_sem_map:
                    sm = kb.dma_sem_map[kname]
                    kb.wait(Q, sm, kb.dma_cnt[sm])
            free_at = [cv2, cv2]
            st_done = [None, None]
            stq = None
            for n in range(nchunks):
                b = n % 2
                kb.wait(Q, V, free_at[b])
                if st_done[b] is not None:
                    kb.wait(Q, st_done[b][0], st_done[b][1])
                ld = kb.dma(Q, land[b][:, 0:ccols], dr[wname][n * 128:(n + 1) * 128, :], key="pq%d" % b)
                kb.wait(V, ld[0], ld[1])
                kb.wait(V, V if False else "v", cv2)

                def f_q(v, n=n, b=b, ccols=ccols):
                    fscr = bigA[:, 8192:8192 + ccols]
                    v.tensor_scalar(out=fscr, in0=land[b][:, 0:ccols], scalar1=wst[:, 34:35],
                                    scalar2=MAGIC, op0=OP.mult, op1=OP.add)
                    v.drain()
                    v.tensor_scalar(out=fscr, in0=fscr, scalar1=MAGIC, scalar2=-1.0,
                                    op0=OP.subtract, op1=OP.max)
                    v.drain()
                    v.tensor_scalar(out=land[b][:, 0:ccols].bitcast(BF16)[:, 0:ccols],
                                    in0=fscr, scalar1=1.0, scalar2=None, op0=OP.min)
                kb.on(V, f_q)
                cq = kb.bump(V)
                kb.wait(Q, V, cq)
                stq = kb.dma(Q, dr[wqname][n * 128:(n + 1) * 128, :],
                             land[b][:, 0:ccols].bitcast(BF16)[:, 0:ccols], key="pq%d" % b)
                st_done[b] = stq
                free_at[b] = cq
            kb.wait(Q, stq[0], stq[1])
            return wst[:, 35:36]

        kb.barrier()
        mc1 = emit_prequant("w1", "w1q", 8, 4096, 0)

        def f_savemc1(v):
            v.tensor_copy(out=wst[:, 36:37], in_=wst[:, 35:36])
        kb.on(V, f_savemc1)
        kb.bump(V)
        mc2 = emit_prequant("w2", "w2q", 32, 1024, 1)

        def f_savemc2(v):
            v.tensor_copy(out=wst[:, 37:38], in_=wst[:, 35:36])
        kb.on(V, f_savemc2)
        cmc = kb.bump(V)
        kb.barrier()
        for kname in ("pq0", "pq1"):
            if kname in kb.dma_sem_map:
                sm = kb.dma_sem_map[kname]
                for e_ in (V, S, G, TE):
                    kb.wait(e_, sm, kb.dma_cnt[sm])
        ldcv = kb.dma(Q, cvw[:], dr["convw"], key="cv")
        ldcb = kb.dma(Q, cvb[:], dr["convb"], key="cv")

        def emit_ffn(src_dram, dst_dram):
            win_free2 = [0, 0, 0, 0]
            for i in range(NT):
                slot = i % 4
                kb.wait(Q, V, win_free2[slot])
                if kb.win_store[slot] is not None:
                    kb.wait(Q, kb.win_store[slot][0], kb.win_store[slot][1])
                ld = kb.dma(Q, xw[:, slot, :], src_dram[i * 128:(i + 1) * 128, :], key="win%d" % slot)
                kb.wait(V, ld[0], ld[1])

                def f_stat(v, i=i, slot=slot):
                    v.bn_stats(out=stat6[:, 0:6], in_=xw[:, slot, 0:512])
                    v.bn_stats(out=stat6[:, 6:12], in_=xw[:, slot, 512:1024])
                    v.drain()
                    v.bn_aggr(out=mv[:, i, :], in_=stat6[:])
                    v.drain()
                kb.on(V, f_stat)
                c1 = kb.bump(V)
                kb.wait(S, V, c1)

                def f_sq(sx, i=i):
                    sx.activation(out=rst[:, i, 0:1], in_=mv[:, i, 1:2],
                                  func=AF.Sqrt, bias=LN_EPS, scale=1.0)
                kb.on(S, f_sq)
                c2 = kb.bump(S)
                kb.wait(V, S, c2)

                def f_norm(v, i=i, slot=slot):
                    v.reciprocal(out=rst[:, i, 1:2], in_=rst[:, i, 0:1])
                    v.drain()
                    v.tensor_scalar(out=sc4[:], in0=xw[:, slot, :], scalar1=mv[:, i, 0:1],
                                    scalar2=rst[:, i, 1:2], op0=OP.subtract, op1=OP.mult)
                    v.drain()
                    v.tensor_reduce(out=ffs[:, i, 0:1], in_=sc4[:], axis=mybir.AxisListType.X,
                                    op=OP.max, apply_absolute_value=True)
                    v.drain()
                    v.tensor_scalar(out=ffs[:, i, 1:2], in0=ffs[:, i, 0:1], scalar1=1e-5,
                                    scalar2=None, op0=OP.max)
                    v.drain()
                    v.reciprocal(out=ffs[:, i, 0:1], in_=ffs[:, i, 1:2])
                    v.drain()
                    v.tensor_scalar(out=ffs[:, i, 0:1], in0=ffs[:, i, 0:1], scalar1=127.0,
                                    scalar2=None, op0=OP.mult)
                    v.tensor_tensor(out=ffs[:, i, 1:2], in0=ffs[:, i, 1:2], in1=wst[:, 36:37], op=OP.mult)
                    v.drain()
                    v.tensor_scalar(out=ffs[:, i, 1:2], in0=ffs[:, i, 1:2], scalar1=1.0 / 127.0,
                                    scalar2=None, op0=OP.mult)
                kb.on(V, f_norm)
                c3 = kb.bump(V)

                def f_k1(v, i=i):
                    v.tensor_scalar(out=scD[:], in0=sc4[:], scalar1=ffs[:, i, 0:1],
                                    scalar2=MAGIC, op0=OP.mult, op1=OP.add)
                    v.drain()
                    v.tensor_scalar(out=xnb[:], in0=scD[:], scalar1=MAGIC,
                                    scalar2=ffs[:, i, 1:2], op0=OP.subtract, op1=OP.mult)
                kb.on(V, f_k1)
                cnorm = kb.bump(V)
                win_free2[slot] = cnorm
                kb.wait(TE, V, cnorm)
                half = i % 2
                pview = psA[:, half * 512:(half + 1) * 512].bitcast(BF16)
                for dc in range(8):
                    def f_tp(t, dc=dc, pview=pview):
                        t.transpose(pview[:, dc * 128:(dc + 1) * 128],
                                    xnb[:, dc * 128:(dc + 1) * 128], identb[:])
                    kb.on(TE, f_tp)
                ctp = kb.bump(TE)
                kb.wait(V, TE, ctp)

                def f_evac(v, i=i, pview=pview):
                    v.tensor_copy(out=k1T[:, :, i * 128:(i + 1) * 128],
                                  in_=pview.rearrange("p (a b) -> p a b", a=8))
                kb.on(V, f_evac)
                pe_ok = kb.bump(V)
                kb.wait(TE, V, pe_ok)
            kb.barrier()
            kb.wait(V, ldcb[0], ldcb[1])
            kb.wait(G, ldcb[0], ldcb[1])
            kb.wait(S, ldcb[0], ldcb[1])

            def f_hpinit(v):
                v.memset(hpad[:, :, :], 0.0)
                v.drain()
            kb.on(V, f_hpinit)
            chp = kb.bump(V)
            kb.wait(S, V, chp)
            st_res = None
            for tc in range(4):
                wf1_ld = {}
                wf1_free = [0, 0, 0, 0]

                def load_wf1(cc):
                    b = cc % 4
                    kb.wait(Q, TE, wf1_free[b])
                    wf1_ld[cc] = kb.dma(Q, wf1[b][:], dr["w1q"][:, cc * 128:(cc + 1) * 128]
                                        .rearrange("(a p) b -> p a b", p=128), key="wf1%d" % b)
                for c0 in range(4):
                    load_wf1(c0)
                amt_done = 0
                h2 = bigA.rearrange("p (a b) -> p a b", a=32)
                hevac_done = {0: 0, 1: 0, 2: 0, 3: 0}
                conv_v_done = {0: 0, 1: 0, 2: 0, 3: 0}
                scd_free = {0: 0, 1: 0}
                pend_silu = None

                def emit_silu_amt(pend):
                    cc_, par_, ccv_ = pend
                    kb.wait(S, V, ccv_)

                    def f_silu(sx, cc=cc_, par=par_):
                        sx.activation(out=h2[:, cc, :], in_=scD[:, par * 512:(par + 1) * 512],
                                      func=AF.Silu, bias=cvb[:, cc:cc + 1], scale=1.0)
                    kb.on(S, f_silu)
                    csl = kb.bump(S)
                    kb.wait(V, S, csl)

                    def f_amt(v, cc=cc_):
                        if cc == 0:
                            v.tensor_copy(out=amt[:], in_=h2[:, 0, :])
                            v.drain()
                            v.tensor_copy(out=amtn[:], in_=h2[:, 0, :])
                        else:
                            v.tensor_tensor(out=amt[:], in0=amt[:], in1=h2[:, cc, :], op=OP.max)
                            v.drain()
                            v.tensor_tensor(out=amtn[:], in0=amtn[:], in1=h2[:, cc, :], op=OP.min)
                        v.drain()
                    kb.on(V, f_amt)
                    return kb.bump(V)

                HALO = [1024, 1536, 1026, 1538]
                cm_hist = {}
                for it in range(34):
                    if it < 32:
                        cc = it
                        b = cc % 4
                        ld = wf1_ld[cc]
                        kb.wait(TE, ld[0], ld[1])
                        pslot = cc % 4
                        pmain = psA[:, pslot * 512:(pslot + 1) * 512]
                        phalo = psB[:, HALO[pslot]:HALO[pslot] + 2]
                        kb.wait(TE, S, hevac_done[pslot])
                        for dc in range(8):
                            def f_mm(t, dc=dc, cc=cc, tc=tc, b=b, pmain=pmain, phalo=phalo):
                                t.matmul(pmain, wf1[b][:, dc, :], k1T[:, dc, tc * 512:(tc + 1) * 512],
                                         start=(dc == 0), stop=(dc == 7))
                                if tc == 0:
                                    t.matmul(phalo[:, 1:2], wf1[b][:, dc, :],
                                             k1T[:, dc, 512:513], start=(dc == 0), stop=(dc == 7))
                                elif tc == 3:
                                    t.matmul(phalo[:, 0:1], wf1[b][:, dc, :],
                                             k1T[:, dc, tc * 512 - 1:tc * 512], start=(dc == 0), stop=(dc == 7))
                                else:
                                    t.matmul(phalo, wf1[b][:, dc, :],
                                             k1T[:, dc, tc * 512 - 1:tc * 512 + 513:513],
                                             start=(dc == 0), stop=(dc == 7))
                            kb.on(TE, f_mm)
                        wf1_free[b] = kb.cnt["t"] + 1
                        cm_hist[cc] = kb.bump(TE)
                        if cc + 4 < 32:
                            load_wf1(cc + 4)
                    if it >= 2:
                        cc = it - 2
                        b = cc % 4
                        par = cc % 2
                        pslot = cc % 4
                        pmain = psA[:, pslot * 512:(pslot + 1) * 512]
                        phalo = psB[:, HALO[pslot]:HALO[pslot] + 2]
                        kb.wait(S, TE, cm_hist[min(cc + 2, 31)])
                        kb.wait(S, V, conv_v_done[b])

                        def f_hevac(sx, cc=cc, tc=tc, pmain=pmain, phalo=phalo, b=b):
                            sx.activation(out=hpad[:, b, 1:513], in_=pmain, func=AF.Copy)
                            if tc == 0:
                                sx.activation(out=hpad[:, b, 513:514], in_=phalo[:, 1:2], func=AF.Copy)
                            elif tc == 3:
                                sx.activation(out=hpad[:, b, 0:1], in_=phalo[:, 0:1], func=AF.Copy)
                            else:
                                sx.activation(out=hpad[:, b, 0:514:513], in_=phalo, func=AF.Copy)
                        kb.on(S, f_hevac)
                        hevac_done[pslot] = kb.bump(S)
                        kb.wait(S, V, scd_free[par])

                        def f_tap1(sx, cc=cc, b=b, par=par):
                            sx.drain()
                            sx.activation(out=scD[:, par * 512:par * 512 + 512], in_=hpad[:, b, 0:512],
                                          func=AF.Copy, scale=cvw[:, cc, 0:1])
                            sx.drain()
                        kb.on(S, f_tap1)
                        che = kb.bump(S)
                        kb.wait(V, S, che)

                        def f_conv(v, cc=cc, b=b, tc=tc, par=par):
                            sd = scD[:, par * 512:par * 512 + 512]
                            if tc == 0:
                                v.memset(sd[:, 0:1], 0.0)
                                v.drain()
                            v.scalar_tensor_tensor(out=sd, in0=hpad[:, b, 1:513],
                                                   scalar=cvw[:, cc, 1:2], in1=sd,
                                                   op0=OP.mult, op1=OP.add)
                            v.drain()
                            if tc == 3:
                                v.memset(hpad[:, b, 513:514], 0.0)
                                v.drain()
                            v.scalar_tensor_tensor(out=sd, in0=hpad[:, b, 2:514],
                                                   scalar=cvw[:, cc, 2:3], in1=sd,
                                                   op0=OP.mult, op1=OP.add)
                        kb.on(V, f_conv)
                        ccv = kb.bump(V)
                        conv_v_done[b] = ccv
                        scd_free[par] = ccv
                        if pend_silu is not None:
                            amt_done = emit_silu_amt(pend_silu)
                        pend_silu = (cc, par, ccv)
                amt_done = emit_silu_amt(pend_silu)

                if tc == 0 and "t_h2" in taps:
                    kb.wait(Q, V, amt_done)
                    sth = kb.dma(Q, taps["t_h2"].rearrange("(a p) b -> p a b", p=128),
                                 bigA.rearrange("p (a b) -> p a b", a=32), key="tap")
                    kb.wait(Q, sth[0], sth[1])
                def f_amcomb(v):
                    v.scalar_tensor_tensor(out=amt[:], in0=amtn[:], scalar=-1.0, in1=amt[:],
                                           op0=OP.mult, op1=OP.max)
                kb.on(V, f_amcomb)
                amt_done = kb.bump(V)
                kb.wait(TE, V, amt_done)
                for bblk in range(4):
                    def f_tp(t, bblk=bblk):
                        t.transpose(psB[:, 1024 + bblk * 128:1024 + (bblk + 1) * 128],
                                    amt[:, bblk * 128:(bblk + 1) * 128], identf[:])
                    kb.on(TE, f_tp)
                ctp = kb.bump(TE)
                kb.wait(V, TE, ctp)

                def f_am2(v, tc=tc):
                    for bblk in range(4):
                        i = tc * 4 + bblk
                        v.tensor_reduce(out=ffs[:, i, 2:3],
                                        in_=psB[:, 1024 + bblk * 128:1024 + (bblk + 1) * 128],
                                        axis=mybir.AxisListType.X, op=OP.max)
                    v.drain()
                    for bblk in range(4):
                        i = tc * 4 + bblk
                        v.tensor_scalar(out=ffs[:, i, 2:3], in0=ffs[:, i, 2:3], scalar1=1e-5,
                                        scalar2=None, op0=OP.max)
                    v.drain()
                    for bblk in range(4):
                        i = tc * 4 + bblk
                        v.tensor_tensor(out=ffs[:, i, 3:4], in0=ffs[:, i, 2:3], in1=wst[:, 37:38], op=OP.mult)
                        v.reciprocal(out=ffs[:, i, 2:3], in_=ffs[:, i, 2:3])
                    v.drain()
                    for bblk in range(4):
                        i = tc * 4 + bblk
                        v.tensor_scalar(out=ffs[:, i, 3:4], in0=ffs[:, i, 3:4], scalar1=1.0 / 127.0,
                                        scalar2=None, op0=OP.mult)
                        v.tensor_scalar(out=ffs[:, i, 2:3], in0=ffs[:, i, 2:3], scalar1=127.0,
                                        scalar2=None, op0=OP.mult)
                    v.drain()
                kb.on(V, f_am2)
                cam = kb.bump(V)
                kb.wait(TE, V, cam)
                for bblk in range(4):
                    def f_tps(t, bblk=bblk, tc=tc):
                        t.transpose(psB[:, 1024 + bblk * 128:1024 + bblk * 128 + 128][0:1, :]
                                    if False else psB[0:1, 1024 + bblk * 128:1024 + (bblk + 1) * 128],
                                    ffs[:, tc * 4 + bblk, 2:3], identf[:])
                    kb.on(TE, f_tps)
                ctps = kb.bump(TE)
                kb.wait(V, TE, ctps)

                def f_srow(v):
                    v.tensor_copy(out=srow[0:1, :], in_=psB[0:1, 1024:1536])
                kb.on(V, f_srow)
                csr = kb.bump(V)
                kb.wait(TE, V, csr)

                def f_bc(t):
                    t.matmul(psB[:, 1536:2048], onesf[0:1, :], srow[0:1, :], start=True, stop=True)
                kb.on(TE, f_bc)
                cbc = kb.bump(TE)
                kb.wait(V, TE, cbc)

                def f_s2bc(v):
                    v.tensor_copy(out=s2bc[:], in_=psB[:, 1536:2048])
                kb.on(V, f_s2bc)
                cs2 = kb.bump(V)

                k2_done = 0
                for cc in range(32):
                    def f_k2(v, cc=cc):
                        if cc == 0:
                            v.drain()
                        v.tensor_tensor(out=sc4[:, 0:512], in0=h2[:, cc, :], in1=s2bc[:], op=OP.mult)
                        v.drain()
                        v.tensor_scalar(out=sc4[:, 0:512], in0=sc4[:, 0:512], scalar1=MAGIC,
                                        scalar2=MAGIC, op0=OP.add, op1=OP.subtract)
                        v.drain()
                        v.tensor_copy(out=k2v[:, cc, :], in_=sc4[:, 0:512])
                        v.drain()
                    kb.on(V, f_k2)
                k2_done = kb.bump(V)

                wf2_ld = {}
                wf2_free = [0, 0]

                def load_wf2(do):
                    b = do % 2
                    kb.wait(Q, TE, wf2_free[b])
                    wf2_ld[do] = kb.dma(Q, wf2[b][:], dr["w2q"][:, do * 128:(do + 1) * 128]
                                        .rearrange("(a p) b -> p a b", p=128), key="wf2%d" % b)
                load_wf2(0)
                load_wf2(1)
                kb.wait(TE, V, k2_done)
                o2_evac = {0: 0, 1: 0}
                for do in range(8):
                    b = do % 2
                    ld = wf2_ld[do]
                    kb.wait(TE, ld[0], ld[1])
                    pslot = do % 2
                    ps = psB[:, pslot * 512:(pslot + 1) * 512]
                    kb.wait(TE, V, o2_evac[pslot])
                    for cc in range(32):
                        def f_mm(t, ps=ps, cc=cc, do=do, b=b):
                            t.matmul(ps, wf2[b][:, cc, :], k2v[:, cc, :],
                                     start=(cc == 0), stop=(cc == 31))
                        kb.on(TE, f_mm)
                    wf2_free[b] = kb.cnt["t"] + 1
                    cm = kb.bump(TE)
                    if do + 2 < 8:
                        load_wf2(do + 2)
                    kb.wait(V, TE, cm)

                    def f_ev(v, ps=ps, do=do):
                        v.tensor_copy(out=o2T[:, do, :], in_=ps)
                    kb.on(V, f_ev)
                    o2_evac[pslot] = kb.bump(V)
                co2 = o2_evac[1]
                kb.wait(TE, V, co2)
                for bblk in range(4):
                    i = tc * 4 + bblk
                    slot = i % 4
                    kb.wait(Q, V, win_free2[slot])
                    if kb.win_store[slot] is not None:
                        kb.wait(Q, kb.win_store[slot][0], kb.win_store[slot][1])
                    ld = kb.dma(Q, xw[:, slot, :], src_dram[i * 128:(i + 1) * 128, :], key="win%d" % slot)
                    pview = psA[:, 0:512].bitcast(BF16)
                    for do in range(8):
                        def f_tp(t, do=do, bblk=bblk, pview=pview):
                            t.transpose(pview[:, do * 128:(do + 1) * 128],
                                        o2T[:, do, bblk * 128:(bblk + 1) * 128], identb[:])
                        kb.on(TE, f_tp)
                    ctp = kb.bump(TE)
                    kb.wait(V, TE, ctp)
                    kb.wait(V, ld[0], ld[1])

                    def f_res(v, i=i, slot=slot, pview=pview):
                        v.tensor_scalar(out=scD[:], in0=pview.rearrange("p (a b) -> p a b", a=8),
                                        scalar1=ffs[:, i, 3:4], scalar2=None, op0=OP.mult)
                        v.drain()
                        v.tensor_tensor(out=xw[:, slot, :], in0=xw[:, slot, :], in1=scD[:], op=OP.add)
                    kb.on(V, f_res)
                    cres = kb.bump(V)
                    win_free2[slot] = cres
                    kb.wait(TE, V, cres)
                    kb.wait(Q, V, cres)
                    st_res = kb.dma(Q, dst_dram[i * 128:(i + 1) * 128, :], xw[:, slot, :], key="win%d" % slot)
                    kb.win_store[slot] = st_res
            kb.barrier()
            kb.wait(Q, st_res[0], st_res[1])


        mkband = shr12[:, 1024:1280]
        mkb0 = shr12[:, 1280:1408]
        mkidiag = shr12[:, 1408:1536]
        mkiprev = shr12[:, 1536:1664]
        rawdrow = shr12[0:1, 2048:2112]
        jjrow = shr12[0:1, 0:2048]

        xT_b = qT
        xf = xhT[:, :, :].rearrange("p a b -> p (a b)")
        ckvT = xf[:, 0:4096].rearrange("p (a b) -> p a b", a=2)
        jslot = xf[:, 0:4096].rearrange("p (a b c) -> p a b c", a=NT, b=8)
        cqT = xf[:, 4096:12288].rearrange("p (a b) -> p a b", a=4)
        qitok = xf[:, 4096:8192].rearrange("p (a b) -> p a b", a=NT)
        jeff = xf[:, 8192:12288].rearrange("p (a b c) -> p a b c", a=NT, b=8)
        hefftok = xf[:, 12288:12800].rearrange("p (a b) -> p a b", a=NT)
        mutok = xf[:, 12800:13824].bitcast(F32).rearrange("p (a b) -> p a b", a=NT)
        kiT = xf[:, 13824:15872]

        bff = bigA[:, :]
        kTb = bff[:, 0:4096].bitcast(BF16).rearrange("p (a b) -> p a b", a=4)
        qallb = bff[:, 4096:8192].bitcast(BF16).rearrange("p (a b) -> p a b", a=NT)
        kstg = bff[:, 8192:12288].bitcast(BF16).rearrange("p (a b) -> p a b", a=NT)
        vstg = bff[:, 12288:16384].bitcast(BF16).rearrange("p (a b) -> p a b", a=NT)
        iacc = bff[:, 8192:10240]
        gout2 = [bff[:, 10240:12416].bitcast(BF16).rearrange("p (a b) -> p a b", a=4),
                 bff[:, 12416:14592].bitcast(BF16).rearrange("p (a b) -> p a b", a=4)]
        vwin = bff[:, 14592:15104].bitcast(BF16).rearrange("p (a b) -> p a b", a=2)
        prodv2 = None
        sbandv = psA[:, 0:2048].rearrange("p (a b) -> p a b", a=8)
        slacc = scA[:, 0:512]
        smv = bff[:, 16128:16384]

        wfl = wAB[:, :, :].rearrange("p a b -> p (a b)")
        jjbc = wfl[:, 4096:6144].bitcast(F32)
        qc_scr = wfl[:, 6144:6272]
        jf_scr = wfl[:, 6272:6400]
        hw_scr = wfl[:, 6400:6656].rearrange("p (a b) -> p a b", a=2)
        o_tok = wfl[:, 6656:7168]
        outTv = wfl[:, 7168:7680].rearrange("p (a b) -> p a b", a=4)
        h2t0 = wfl[:, 7680:7808]
        jf2_scr = wfl[:, 7808:7936]
        tview = tausq[:, :, :].rearrange("p a b -> p (a b)")
        idxq = cands[:, :, :].rearrange("p a b -> p (a b)").bitcast(U32)
        prodb = shr12[:, 2048:3072].bitcast(BF16)
        scE = shr12[:, 2048:3072]
        PI = math.pi
        SCB = float(96.0 ** -0.5)

        def emit_attn_b(src_dram, dst_dram):
            l1 = kb.dma(Q, rawdrow[0:1, 0:16], dr["rawd"], key="ab0")
            l1 = kb.dma(Q, jjrow[0:1, 0:1024], dr["tb_jj"], key="ab0")
            lidw = kb.dma(Q, jjrow[0:1, 1024:1028], dr["idxw"], key="ab0")
            kb.wait(S, lidw[0], lidw[1])
            kb.wait(V, lidw[0], lidw[1])
            kb.wait(G, lidw[0], lidw[1])

            def f_iota(g):
                g.iota(tview[:, 0:32], [[1, 32]], channel_multiplier=0,
                       allow_small_or_imprecise_dtypes=True)
                g.iota(tview[:, 64:65], [[0, 1]], base=-63, channel_multiplier=1,
                       allow_small_or_imprecise_dtypes=True)
                g.iota(tview[:, 65:66], [[0, 1]], base=0, channel_multiplier=1,
                       allow_small_or_imprecise_dtypes=True)
                g.iota(idxq[:, 32:64].bitcast(U32) if False else idxq[:, 32:64],
                       [[1, 32]], base=64, channel_multiplier=0)
            kb.on(G, f_iota)
            cgi = kb.bump(G)
            kb.wait(V, G, cgi)

            def f_delta(sx):
                sx.activation(out=rawdrow[0:1, 16:32], in_=rawdrow[0:1, 0:16], func=AF.Sigmoid)
                sx.drain()
            kb.on(S, f_delta)
            c = kb.bump(S)
            kb.wait(V, S, c)

            def f_d2(v):
                v.tensor_scalar(out=rawdrow[0:1, 16:32], in0=rawdrow[0:1, 16:32],
                                scalar1=-2.0 * PI, scalar2=None, op0=OP.mult)
                v.drain()
                v.tensor_scalar(out=rawdrow[0:1, 32:48], in0=rawdrow[0:1, 16:32],
                                scalar1=-PI, scalar2=None, op0=OP.is_lt)
                v.drain()
                v.scalar_tensor_tensor(out=rawdrow[0:1, 16:32], in0=rawdrow[0:1, 32:48],
                                       scalar=2.0 * PI, in1=rawdrow[0:1, 16:32],
                                       op0=OP.mult, op1=OP.add)
                v.drain()
                v.tensor_scalar(out=rawdrow[0:1, 32:48], in0=rawdrow[0:1, 16:32],
                                scalar1=PI / 2, scalar2=None, op0=OP.add)
                v.drain()
                v.tensor_scalar(out=rawdrow[0:1, 48:64], in0=rawdrow[0:1, 32:48],
                                scalar1=PI, scalar2=None, op0=OP.is_gt)
                v.drain()
                v.scalar_tensor_tensor(out=rawdrow[0:1, 32:48], in0=rawdrow[0:1, 48:64],
                                       scalar=-2.0 * PI, in1=rawdrow[0:1, 32:48],
                                       op0=OP.mult, op1=OP.add)
            kb.on(V, f_d2)
            c = kb.bump(V)
            kb.wait(S, V, c)

            def f_d3(sx):
                sx.activation(out=rawdrow[0:1, 16:32], in_=rawdrow[0:1, 16:32], func=AF.Sin)
                sx.drain()
                sx.activation(out=rawdrow[0:1, 32:48], in_=rawdrow[0:1, 32:48], func=AF.Sin)
                sx.drain()
            kb.on(S, f_d3)
            c = kb.bump(S)
            kb.wait(V, S, c)

            def f_jj(v):
                cj = jjrow[0:1, 0:512].rearrange("p (a b) -> p a b", a=32)
                sj = jjrow[0:1, 512:1024].rearrange("p (a b) -> p a b", a=32)
                cd = rawdrow[0:1, 32:48].unsqueeze(1).broadcast_to((1, 32, 16))
                sd = rawdrow[0:1, 16:32].unsqueeze(1).broadcast_to((1, 32, 16))
                co = jjrow[0:1, 1028:1540].rearrange("p (a b) -> p a b", a=32)
                so = jjrow[0:1, 1540:2052] if False else jjrow[0:1, 1536:2048].rearrange("p (a b) -> p a b", a=32)
                tmp = bff[0:1, 15104:15616].rearrange("p (a b) -> p a b", a=32)
                v.tensor_tensor(out=co, in0=cj, in1=cd, op=OP.mult)
                v.drain()
                v.tensor_tensor(out=tmp, in0=sj, in1=sd, op=OP.mult)
                v.drain()
                v.tensor_tensor(out=co, in0=co, in1=tmp, op=OP.subtract)
                v.drain()
                v.tensor_tensor(out=so, in0=sj, in1=cd, op=OP.mult)
                v.drain()
                v.tensor_tensor(out=tmp, in0=cj, in1=sd, op=OP.mult)
                v.drain()
                v.tensor_tensor(out=so, in0=so, in1=tmp, op=OP.add)
            kb.on(V, f_jj)
            c = kb.bump(V)
            kb.wait(TE, V, c)

            def f_bc(t):
                t.matmul(psB[:, 0:512], onesf[0:1, :], jjrow[0:1, 1028:1540], start=True, stop=True)
                t.matmul(psB[:, 512:1024], onesf[0:1, :], jjrow[0:1, 1536:2048], start=True, stop=True)
                t.matmul(psB[:, 1024:1028], onesf[0:1, :], jjrow[0:1, 1024:1028], start=True, stop=True)
                t.matmul(psB[:, 1028:1060], onesf[0:1, :], rawdrow[0:1, 16:48], start=True, stop=True)
            kb.on(TE, f_bc)
            c = kb.bump(TE)
            kb.wait(V, TE, c)

            def f_bcev(v):
                v.tensor_copy(out=jjbc[:, 0:512], in_=psB[:, 0:512])
                v.tensor_copy(out=jjbc[:, 512:1024], in_=psB[:, 512:1024])
                v.tensor_copy(out=msc[:, 0:4], in_=psB[:, 1024:1028])
                v.tensor_copy(out=msc2[:, 0:32], in_=psB[:, 1028:1060])
                v.drain()
                v.tensor_scalar(out=tview[:, 66:67], in0=tview[:, 65:66], scalar1=64.0,
                                scalar2=None, op0=OP.is_lt)
            kb.on(V, f_bcev)
            cbc = kb.bump(V)
            kb.wait(TE, V, cbc)

            emit_ln_transpose(src_dram, xT_b)
            kb.barrier()

            wfree = [0]

            def wload(nm, off, key="wA"):
                R, C = dr[nm].shape
                a = R // 128
                view = wfl[:, off:off + a * C].rearrange("p (a b) -> p a b", a=a)
                kb.wait(G, TE, wfree[0])
                ld = kb.dma(G, view, dr[nm].rearrange("(a p) b -> p a b", p=128), key=key)
                return view, ld

            pslots = [psB[:, i * 512:(i + 1) * 512] for i in range(3)]
            pslot_free = {0: ("v", 0), 1: ("v", 0), 2: ("v", 0)}
            slot_i = [0]

            def mm_out(lhsT_fn, rhs_fn, nacc, evac_fn, eng=S, width=512, po=128):
                sl = slot_i[0] % 3
                slot_i[0] += 1
                ps = pslots[sl][0:po, 0:width]
                pe_eng, pe_cnt = pslot_free[sl]
                kb.wait(TE, pe_eng, pe_cnt)
                for a in range(nacc):
                    def f_mm(t, a=a, ps=ps):
                        t.matmul(ps, lhsT_fn(a), rhs_fn(a), start=(a == 0), stop=(a == nacc - 1))
                    kb.on(TE, f_mm)
                cm = kb.bump(TE)
                kb.wait(eng, TE, cm)

                def f_ev(e, ps=ps):
                    evac_fn(e, ps)
                kb.on(eng, f_ev)
                ce = kb.bump(eng)
                pslot_free[sl] = (eng, ce)
                return ce

            wv, ld = wload("wdkv", 0)
            kb.wait(TE, ld[0], ld[1])
            for ec in range(2):
                for tc in range(4):
                    mm_out(lambda a, ec=ec, wv=wv: wv[:, a, ec * 128:(ec + 1) * 128],
                           lambda a, tc=tc: xT_b[:, a, tc * 512:(tc + 1) * 512], 8,
                           lambda e, ps, ec=ec, tc=tc: e.activation(
                               out=ckvT[:, ec, tc * 512:(tc + 1) * 512], in_=ps, func=AF.Copy))
            wfree[0] = kb.cnt["t"]
            wuk_v, ld1 = wload("wuk", 2048, key="wB")
            wuv_v, ld2 = wload("wuv", 3072, key="wC")
            kb.wait(TE, ld1[0], ld1[1])
            kb.wait(TE, ld2[0], ld2[1])
            for ec in range(4):
                for tc in range(4):
                    mm_out(lambda a, ec=ec, wuk_v=wuk_v: wuk_v[:, a, ec * 128:(ec + 1) * 128],
                           lambda a, tc=tc: ckvT[:, a, tc * 512:(tc + 1) * 512], 2,
                           lambda e, ps, ec=ec, tc=tc: e.activation(
                               out=kTb[:, ec, tc * 512:(tc + 1) * 512], in_=ps, func=AF.Copy))
            for i in range(NT):
                mm_out(lambda a, i=i: ckvT[:, a, i * 128:(i + 1) * 128],
                       lambda a, wuk_v=wuk_v: wuk_v[:, a, :], 2,
                       lambda e, ps, i=i: e.activation(out=kstg[:, i, :], in_=ps, func=AF.Copy))
                mm_out(lambda a, i=i: ckvT[:, a, i * 128:(i + 1) * 128],
                       lambda a, wuv_v=wuv_v: wuv_v[:, a, :], 2,
                       lambda e, ps, i=i: e.activation(out=vstg[:, i, :], in_=ps, func=AF.Copy))
            wfree[0] = kb.cnt["t"]
            wv, ld = wload("wdq", 0)
            kb.wait(TE, ld[0], ld[1])
            for ec in range(4):
                for tc in range(4):
                    mm_out(lambda a, ec=ec, wv=wv: wv[:, a, ec * 128:(ec + 1) * 128],
                           lambda a, tc=tc: xT_b[:, a, tc * 512:(tc + 1) * 512], 8,
                           lambda e, ps, ec=ec, tc=tc: e.activation(
                               out=cqT[:, ec, tc * 512:(tc + 1) * 512], in_=ps, func=AF.Copy))
            wfree[0] = kb.cnt["t"]
            wuq_v, ld = wload("wuq", 0)
            kb.wait(TE, ld[0], ld[1])
            for i in range(NT):
                mm_out(lambda a, i=i: cqT[:, a, i * 128:(i + 1) * 128],
                       lambda a, wuq_v=wuq_v: wuq_v[:, a, :], 4,
                       lambda e, ps, i=i: e.activation(out=qallb[:, i, :], in_=ps, func=AF.Copy, scale=SCB))
            wfree[0] = kb.cnt["t"]
            wkr_v, ldkr = wload("wkr", 3072, key="wC")
            kb.wait(TE, ldkr[0], ldkr[1])
            for i in range(NT):
                ckr = mm_out(lambda a, i=i: xT_b[:, a, i * 128:(i + 1) * 128],
                             lambda a: wkr_v[:, a, :], 8,
                             lambda e, ps: e.tensor_copy(out=scD[:, 256:288], in_=ps), eng=V, width=32)
                kb.wait(S, V, ckr)

                def f_spk(sx):
                    sx.activation(out=sc4[:, 256:288], in_=scD[:, 256:288], func=AF.Abs)
                    sx.drain()
                    sx.activation(out=sc4[:, 256:288], in_=sc4[:, 256:288], func=AF.Exp, scale=-1.0)
                    sx.drain()
                    sx.activation(out=sc4[:, 256:288], in_=sc4[:, 256:288], func=AF.Ln, bias=1.0)
                    sx.drain()
                kb.on(S, f_spk)
                cspk = kb.bump(S)
                kb.wait(V, S, cspk)

                def f_muk(v, i=i):
                    v.tensor_scalar(out=scD[:, 256:288], in0=scD[:, 256:288], scalar1=0.0,
                                    scalar2=None, op0=OP.max)
                    v.drain()
                    v.tensor_tensor(out=mutok[:, i, :], in0=sc4[:, 256:288], in1=scD[:, 256:288], op=OP.add)
                    v.drain()
                kb.on(V, f_muk)
                cmk = kb.bump(V)
            kb.barrier()
            stg = None
            for i in range(NT):
                stg = kb.dma(Q, dr["gtab"][i * 128:(i + 1) * 128, 0:512], kstg[:, i, :], key="ab0")
                stg = kb.dma(Q, dr["gtab"][i * 128:(i + 1) * 128, 512:1024], vstg[:, i, :], key="ab0")
                stg = kb.dma(Q, dr["gtab"][i * 128:(i + 1) * 128, 1024:1088],
                             mutok[:, i, :].bitcast(BF16), key="ab0")
            kb.wait(Q, stg[0], stg[1])
            kb.barrier()
            qrstg = bff[:, 8192:10240].bitcast(BF16).rearrange("p (a b) -> p a b", a=NT)
            wqr_v, ldqr = wload("wqr", 2048, key="wB")
            kb.wait(TE, ldqr[0], ldqr[1])
            for i in range(NT):
                mm_out(lambda a, i=i: cqT[:, a, i * 128:(i + 1) * 128],
                       lambda a: wqr_v[:, a, :], 4,
                       lambda e, ps, i=i: e.activation(out=qrstg[:, i, :], in_=ps, func=AF.Copy),
                       width=256)
            kb.barrier()
            tab_free = 0
            for i in range(NT):
                kb.wait(Q, V, tab_free)
                lt = kb.dma(Q, scA[:, 0:32], dr["tb_tq"][i * 128:(i + 1) * 128, :], key="ab0")
                lt = kb.dma(Q, scA[:, 32:64], dr["tb_t2"][i * 128:(i + 1) * 128, :], key="ab0")
                lt = kb.dma(Q, scA[:, 64:96], dr["tb_pp"][i * 128:(i + 1) * 128, :], key="ab0")
                kb.wait(V, lt[0], lt[1])

                def f_cp(v, i=i):
                    v.tensor_copy(out=scD[:, 0:256], in_=qrstg[:, i, :])
                    v.drain()
                kb.on(V, f_cp)
                ccp = kb.bump(V)
                kb.wait(S, V, ccp)

                def f_sp(sx):
                    sx.activation(out=sc4[:, 0:256], in_=scD[:, 0:256], func=AF.Abs)
                    sx.drain()
                    sx.activation(out=sc4[:, 0:256], in_=sc4[:, 0:256], func=AF.Exp, scale=-1.0)
                    sx.drain()
                    sx.activation(out=sc4[:, 0:256], in_=sc4[:, 0:256], func=AF.Ln, bias=1.0)
                    sx.drain()
                kb.on(S, f_sp)
                csp = kb.bump(S)
                kb.wait(V, S, csp)

                def f_build(v, i=i):
                    v.tensor_scalar(out=scD[:, 0:256], in0=scD[:, 0:256], scalar1=0.0,
                                    scalar2=None, op0=OP.max)
                    v.drain()
                    v.tensor_tensor(out=sc4[:, 0:256], in0=sc4[:, 0:256], in1=scD[:, 0:256], op=OP.add)
                    v.drain()
                    v.tensor_scalar(out=sc4[:, 0:256], in0=sc4[:, 0:256], scalar1=SCB,
                                    scalar2=None, op0=OP.mult)
                    v.drain()
                    tqc, tqs = scA[:, 0:16], scA[:, 16:32]
                    cdb, sdb = msc2[:, 16:32], msc2[:, 0:16]
                    v.tensor_tensor(out=scA[:, 96:112], in0=tqc, in1=cdb, op=OP.mult)
                    v.tensor_tensor(out=scA[:, 112:128], in0=tqs, in1=cdb, op=OP.mult)
                    v.drain()
                    v.tensor_tensor(out=msc[:, 4:20], in0=tqs, in1=sdb, op=OP.mult)
                    v.tensor_tensor(out=scA[:, 128:144], in0=tqc, in1=sdb, op=OP.mult)
                    v.drain()
                    v.tensor_tensor(out=scA[:, 96:112], in0=scA[:, 96:112], in1=msc[:, 4:20], op=OP.subtract)
                    v.tensor_tensor(out=scA[:, 112:128], in0=scA[:, 112:128], in1=scA[:, 128:144], op=OP.add)
                    v.drain()

                    def jbuild(a1, a2, cts, sts, dst16a, dst16b):
                        v.tensor_tensor(out=scE[:, 0:128].rearrange("p (a b) -> p a b", a=8),
                                        in0=a1, in1=cts, op=OP.mult)
                        v.tensor_tensor(out=scE[:, 128:256].rearrange("p (a b) -> p a b", a=8),
                                        in0=a2, in1=sts, op=OP.mult)
                        v.drain()
                        v.tensor_tensor(out=dst16a, in0=scE[:, 0:128].rearrange("p (a b) -> p a b", a=8),
                                        in1=scE[:, 128:256].rearrange("p (a b) -> p a b", a=8), op=OP.subtract)
                        v.drain()
                        v.tensor_tensor(out=scE[:, 0:128].rearrange("p (a b) -> p a b", a=8),
                                        in0=a1, in1=sts, op=OP.mult)
                        v.tensor_tensor(out=scE[:, 128:256].rearrange("p (a b) -> p a b", a=8),
                                        in0=a2, in1=cts, op=OP.mult)
                        v.drain()
                        v.tensor_tensor(out=dst16b, in0=scE[:, 0:128].rearrange("p (a b) -> p a b", a=8),
                                        in1=scE[:, 128:256].rearrange("p (a b) -> p a b", a=8), op=OP.add)
                        v.drain()
                    a1 = sc4[:, 0:256].rearrange("p (a b) -> p a b", a=8)[:, :, 0:16]
                    a2 = sc4[:, 0:256].rearrange("p (a b) -> p a b", a=8)[:, :, 16:32]
                    ctb = scA[:, 96:112].unsqueeze(1).broadcast_to((128, 8, 16))
                    stb = scA[:, 112:128].unsqueeze(1).broadcast_to((128, 8, 16))
                    jbuild(a1, a2, ctb, stb, jslot[:, i, :, 0:16], jslot[:, i, :, 16:32])
                    c2b = scA[:, 32:48].unsqueeze(1).broadcast_to((128, 8, 16))
                    s2b2 = scA[:, 48:64].unsqueeze(1).broadcast_to((128, 8, 16))
                    jbuild(a1, a2, c2b, s2b2, jeff[:, i, :, 0:16], jeff[:, i, :, 16:32])
                    mu1f = scD[:, 256:272]
                    mu2f = scD[:, 272:288]
                    v.tensor_copy(out=scD[:, 256:288], in_=mutok[:, i, :])
                    v.drain()
                    ppc, pps = scA[:, 64:80], scA[:, 80:96]
                    v.tensor_tensor(out=scE[:, 0:16], in0=mu1f, in1=ppc, op=OP.mult)
                    v.tensor_tensor(out=scE[:, 16:32], in0=mu2f, in1=pps, op=OP.mult)
                    v.tensor_tensor(out=scE[:, 32:48], in0=mu1f, in1=pps, op=OP.mult)
                    v.tensor_tensor(out=scE[:, 48:64], in0=mu2f, in1=ppc, op=OP.mult)
                    v.drain()
                    v.tensor_tensor(out=hefftok[:, i, 0:16], in0=scE[:, 0:16], in1=scE[:, 16:32], op=OP.subtract)
                    v.tensor_tensor(out=hefftok[:, i, 16:32], in0=scE[:, 32:48], in1=scE[:, 48:64], op=OP.add)
                    v.drain()
                    if i == 0:
                        ct2, st2 = scA[:, 96:112], scA[:, 112:128]
                        v.tensor_tensor(out=scE[:, 0:16], in0=mu1f, in1=ct2, op=OP.mult)
                        v.tensor_tensor(out=scE[:, 16:32], in0=mu2f, in1=st2, op=OP.mult)
                        v.tensor_tensor(out=scE[:, 32:48], in0=mu1f, in1=st2, op=OP.mult)
                        v.tensor_tensor(out=scE[:, 48:64], in0=mu2f, in1=ct2, op=OP.mult)
                        v.drain()
                        v.tensor_tensor(out=qc_scr[:, 0:16], in0=scE[:, 0:16], in1=scE[:, 16:32], op=OP.subtract)
                        v.tensor_tensor(out=qc_scr[:, 16:32], in0=scE[:, 32:48], in1=scE[:, 48:64], op=OP.add)
                        v.drain()
                kb.on(V, f_build)
                tab_free = kb.bump(V)
                if i == 0:
                    kb.wait(TE, V, tab_free)

                    def f_h2t(t):
                        t.transpose(psA[:, 1536:1600].bitcast(BF16)[0:32, 0:128],
                                    qc_scr[:, 0:32], identb[:])
                    kb.on(TE, f_h2t)
                    ch2 = kb.bump(TE)
                    kb.wait(V, TE, ch2)

                    def f_h2ev(v):
                        v.tensor_copy(out=h2t0[0:32, :], in_=psA[:, 1536:1600].bitcast(BF16)[0:32, 0:128])
                        v.drain()
                    kb.on(V, f_h2ev)
                    kb.bump(V)
            wfree[0] = kb.cnt["t"]
            wiq_v, ld1 = wload("widxq", 0)
            wik_v, ld2 = wload("widxk", 2048, key="wB")
            kb.wait(TE, ld1[0], ld1[1])
            kb.wait(TE, ld2[0], ld2[1])
            for i in range(NT):
                mm_out(lambda a, i=i: xT_b[:, a, i * 128:(i + 1) * 128],
                       lambda a, wiq_v=wiq_v: wiq_v[:, a, :], 8,
                       lambda e, ps, i=i: e.activation(out=qitok[:, i, :], in_=ps, func=AF.Copy), width=256)
            for tc in range(4):
                mm_out(lambda a, wik_v=wik_v: wik_v[:, a, :],
                       lambda a, tc=tc: xT_b[:, a, tc * 512:(tc + 1) * 512], 8,
                       lambda e, ps, tc=tc: e.activation(
                           out=kiT[0:64, tc * 512:(tc + 1) * 512], in_=ps, func=AF.Copy), po=64)
            kb.barrier()
            for dst, srcnm in ((mkband, "mk_band"), (mkb0, "mk_b0"),
                               (mkidiag, "mk_idiag"), (mkiprev, "mk_iprev")):
                lmk = kb.dma(Q, dst, dr[srcnm], key="ab0")
            kb.wait(V, lmk[0], lmk[1])
            kb.barrier()

            pb = psA[:, 1536:2048].bitcast(BF16)
            wfree[0] = kb.cnt["t"]
            wout_v, ldwo = wload("wout", 0)
            prev_store = None
            for i in range(NT):
                kw = (i + 1) * 128
                vprev = kb.cnt["v"]
                islot_free = {0: 0, 1: 0}
                for h4 in range(4):
                    kb.wait(TE, V, vprev)

                    def f_tp(t, i=i, h4=h4):
                        t.transpose(pb[0:64, 0:128], qitok[:, i, h4 * 64:(h4 + 1) * 64], identb[:])
                    kb.on(TE, f_tp)
                    ctp = kb.bump(TE)
                    kb.wait(S, TE, ctp)

                    def f_ev(sx):
                        sx.activation(out=qc_scr[0:64, :], in_=pb[0:64, 0:128], func=AF.Copy)
                        sx.drain()
                    kb.on(S, f_ev)
                    cqe = kb.bump(S)
                    kb.wait(TE, S, cqe)
                    for kc in range((kw + 511) // 512):
                        w = min(512, kw - kc * 512)
                        ps = psA[:, (kc % 2) * 512:(kc % 2) * 512 + w]
                        kb.wait(TE, V, islot_free[kc % 2])

                        def f_mm(t, ps=ps, kc=kc, w=w):
                            t.matmul(ps, qc_scr[0:64, :], kiT[0:64, kc * 512:kc * 512 + w],
                                     start=True, stop=True)
                        kb.on(TE, f_mm)
                        cm = kb.bump(TE)
                        kb.wait(V, TE, cm)

                        def f_acc(v, ps=ps, kc=kc, w=w, h4=h4):
                            if h4 == 0:
                                v.tensor_scalar(out=iacc[:, kc * 512:kc * 512 + w], in0=ps,
                                                scalar1=0.0, scalar2=msc[:, h4:h4 + 1],
                                                op0=OP.max, op1=OP.mult)
                            else:
                                v.tensor_scalar(out=scD[:, 0:w], in0=ps, scalar1=0.0,
                                                scalar2=msc[:, h4:h4 + 1], op0=OP.max, op1=OP.mult)
                                v.drain()
                                v.tensor_tensor(out=iacc[:, kc * 512:kc * 512 + w],
                                                in0=iacc[:, kc * 512:kc * 512 + w],
                                                in1=scD[:, 0:w], op=OP.add)
                            v.drain()
                        kb.on(V, f_acc)
                        vprev = kb.bump(V)
                        islot_free[kc % 2] = vprev

                def f_mask(v, i=i, kw=kw):
                    if i >= 1:
                        v.tensor_tensor(out=iacc[:, (i - 1) * 128:i * 128],
                                        in0=iacc[:, (i - 1) * 128:i * 128], in1=mkiprev, op=OP.add)
                        v.drain()
                    v.tensor_tensor(out=iacc[:, i * 128:(i + 1) * 128],
                                    in0=iacc[:, i * 128:(i + 1) * 128], in1=mkidiag, op=OP.add)
                    v.drain()
                    for r in range(4):
                        v.max(out=tview[:, 128:136], in_=iacc[:, 0:kw])
                        v.drain()
                        v.max_index(out=idxq[:, r * 8:(r + 1) * 8], in_max=tview[:, 128:136],
                                    in_values=iacc[:, 0:kw])
                        v.drain()
                        if r < 3:
                            v.match_replace(out=iacc[:, 0:kw], in_to_replace=tview[:, 128:136],
                                            in_values=iacc[:, 0:kw], imm_value=NEG)
                            v.drain()
                    if i == 0:
                        v.tensor_scalar(out=idxq[:, 64:96], in0=tview[:, 0:32],
                                        scalar1=tview[:, 64:65], scalar2=None, op0=OP.is_ge)
                        v.drain()
                        v.copy_predicated(out=idxq[:, 0:32], mask=idxq[:, 64:96],
                                          data=idxq[:, 32:64])
                        v.drain()
                kb.on(V, f_mask)
                cex = kb.bump(V)

                w0t = i - 1 if i >= 1 else 0
                W = 256 if i >= 1 else 128
                pb = psB[:, 512:1024].bitcast(BF16)
                lv = kb.dma(Q, vwin[:, 0, :], dr["gtab"][w0t * 128:(w0t + 1) * 128, 512:1024], key="gat")
                if i >= 1:
                    lv = kb.dma(Q, vwin[:, 1, :], dr["gtab"][i * 128:(i + 1) * 128, 512:1024], key="gat")
                kb.wait(TE, V, cex)
                for w in range(2 if i >= 1 else 1):
                    def f_tph(t, i=i, w=w, w0t=w0t):
                        t.transpose(pb[0:32, 0:128], hefftok[:, w0t + w, :], identb[:])
                    kb.on(TE, f_tph)
                    cth = kb.bump(TE)
                    kb.wait(S, TE, cth)

                    def f_evh(sx, w=w):
                        sx.activation(out=hw_scr[0:32, w, :], in_=pb[0:32, 0:128], func=AF.Copy)
                        sx.drain()
                    kb.on(S, f_evh)
                    chv = kb.bump(S)
                    kb.wait(TE, S, chv)
                for h in range(8):
                    hp = (h % 2) * 64
                    hc = h // 2

                    def f_tpq(t, i=i, h=h, hp=hp):
                        t.transpose(pb[hp:hp + 64, 128:256] if hp else pb[0:64, 128:256],
                                    qallb[:, i, h * 64:(h + 1) * 64], identb[:])
                        t.transpose(pb[0:32, 256:384], jeff[:, i, h, :], identb[:])
                        if i == 0:
                            t.transpose(pb[0:32, 384:512], jslot[:, 0, h, :], identb[:])
                    kb.on(TE, f_tpq)
                    ctq = kb.bump(TE)
                    kb.wait(S, TE, ctq)

                    def f_evq(sx, hp=hp, i=i):
                        sx.activation(out=qc_scr[hp:hp + 64, :], in_=pb[hp:hp + 64, 128:256] if hp else pb[0:64, 128:256], func=AF.Copy)
                        sx.activation(out=jf_scr[0:32, :], in_=pb[0:32, 256:384], func=AF.Copy)
                        if i == 0:
                            sx.activation(out=jf2_scr[0:32, :], in_=pb[0:32, 384:512], func=AF.Copy)
                        sx.drain()
                    kb.on(S, f_evq)
                    cev = kb.bump(S)
                    kb.wait(TE, S, cev)

                    def f_bmm(t, i=i, h=h, hp=hp, hc=hc, w0t=w0t, W=W):
                        t.matmul(psA[:, h * 256:h * 256 + W], qc_scr[hp:hp + 64, :],
                                 kTb[hp:hp + 64, hc, w0t * 128:w0t * 128 + W], start=True, stop=False)
                        t.matmul(psA[:, h * 256:h * 256 + W], jf_scr[0:32, :],
                                 hw_scr[0:32, :, :].rearrange("p a b -> p (a b)")[0:32, 0:W],
                                 start=False, stop=True)
                        if i == 0:
                            t.matmul(psB[:, 0:W], qc_scr[hp:hp + 64, :],
                                     kTb[hp:hp + 64, hc, 0:W], start=True, stop=False)
                            t.matmul(psB[:, 0:W], jf2_scr[0:32, :],
                                     h2t0[0:32, 0:W], start=False, stop=True)
                    kb.on(TE, f_bmm)
                    cbm = kb.bump(TE)
                    kb.wait(V, TE, cbm)

                    def f_bev(v, i=i, h=h, W=W):
                        if i >= 1:
                            v.tensor_tensor(out=sbandv[:, h, :], in0=sbandv[:, h, :],
                                            in1=mkband, op=OP.add)
                        else:
                            v.tensor_copy(out=scD[:, 0:128], in_=sbandv[:, h, 0:128])
                            v.drain()
                            v.tensor_tensor(out=scD[:, 128:256], in0=psB[:, 0:128],
                                            in1=scD[:, 0:128], op=OP.subtract)
                            v.drain()
                            v.scalar_tensor_tensor(out=scD[:, 128:256], in0=scD[:, 128:256],
                                                   scalar=tview[:, 66:67], in1=scD[:, 0:128],
                                                   op0=OP.mult, op1=OP.add)
                            v.drain()
                            v.tensor_tensor(out=sbandv[:, h, 0:128], in0=scD[:, 128:256],
                                            in1=mkb0, op=OP.add)
                            v.drain()
                            v.memset(sbandv[:, h, 128:256], NEG)
                        v.drain()
                    kb.on(V, f_bev)
                    cbv = kb.bump(V)
                    kb.wait(TE, V, cbv)

                def f_minit(v):
                    v.tensor_reduce(out=smv[:, 0:8], in_=sbandv[:, :, :], axis=mybir.AxisListType.X, op=OP.max)
                    v.drain()
                    v.tensor_scalar(out=smv[:, 0:8], in0=smv[:, 0:8], scalar1=32.0, scalar2=None, op0=OP.add)
                    v.memset(smv[:, 8:16], 0.0)
                    v.memset(slacc[:], 0.0)
                    v.drain()
                kb.on(V, f_minit)
                cmi = kb.bump(V)

                def f_bexp(v):
                    v.tensor_tensor(out=sbandv[:, :, :], in0=sbandv[:, :, :],
                                    in1=smv[:, 0:8].unsqueeze(2).broadcast_to((128, 8, 256)), op=OP.subtract)
                    v.drain()
                kb.on(V, f_bexp)
                cbx = kb.bump(V)
                kb.wait(S, V, cbx)

                def f_bexp2(sx):
                    sx.activation(out=prodb[:, 0:2048], in_=sbandv[:, :, :].rearrange("p a b -> p (a b)"),
                                  func=AF.Exp)
                    sx.drain()
                kb.on(S, f_bexp2)
                cbe = kb.bump(S)
                kb.wait(V, S, cbe)

                def f_zband(v):
                    v.tensor_reduce(out=smv[:, 16:24],
                                    in_=prodb[:, 0:2048].rearrange("p (a b) -> p a b", a=8),
                                    axis=mybir.AxisListType.X, op=OP.add)
                    v.drain()
                    v.tensor_tensor(out=smv[:, 8:16], in0=smv[:, 8:16], in1=smv[:, 16:24], op=OP.add)
                    v.drain()
                kb.on(V, f_zband)
                kb.bump(V)
                ssc = shr12[:, 512:768].rearrange("p (a b) -> p a b", a=8)
                vscore_done = {0: 0, 1: 0}
                gprod_done = {0: None, 1: None}
                gb_done = {}

                def issue_gathers(qb):
                    buf = qb % 2
                    kb.wait(G, V, vscore_done[buf])
                    sem = kb.dma_sem_map.setdefault("gag%d" % buf, kb.dma_sem_pool[len(kb.dma_sem_map)])
                    kb.dma_cnt.setdefault(sem, 0)
                    for j in range(4):
                        col = qb * 4 + j
                        kb.dma_cnt[sem] += 16
                        kb.on(G, lambda g, col=col, j=j, sem=sem, buf=buf: g.indirect_dma_start(
                            out=gout2[buf][:, j, :], out_offset=None, in_=dr["gtab"],
                            in_offset=bass.IndirectOffsetOnAxis(ap=idxq[:, col:col + 1], axis=0))
                            .then_inc(kb.sem[sem], 16))
                        gb_done[qb] = (sem, kb.dma_cnt[sem])
                kb.wait(G, V, cex)
                issue_gathers(0)
                for qb in range(8):
                    buf = qb % 2
                    kb.wait(G, gb_done[qb][0], gb_done[qb][1])

                    def f_ghb(g, qb=qb, buf=buf):
                        muv = gout2[buf][:, :, 1024:1088].bitcast(F32)
                        mu1 = muv[:, :, 0:16]
                        mu2 = muv[:, :, 16:32]
                        cjv = jjbc[:, 0:512].rearrange("p (a b) -> p a b", a=32)[:, qb * 4:(qb + 1) * 4, :]
                        sjv = jjbc[:, 512:1024].rearrange("p (a b) -> p a b", a=32)[:, qb * 4:(qb + 1) * 4, :]
                        t1 = bff[:, 15552:15616].rearrange("p (a b) -> p a b", a=4)
                        t2 = bff[:, 15616:15680].rearrange("p (a b) -> p a b", a=4)
                        hq = bff[:, 15488 + buf * 256:15552 + buf * 256].bitcast(BF16).rearrange("p (a b) -> p a b", a=4)
                        g.tensor_tensor(out=t1, in0=mu1, in1=cjv, op=OP.mult)
                        g.tensor_tensor(out=t2, in0=mu2, in1=sjv, op=OP.mult)
                        g.drain()
                        g.tensor_tensor(out=hq[:, :, 0:16], in0=t1, in1=t2, op=OP.subtract)
                        g.drain()
                        g.tensor_tensor(out=t1, in0=mu1, in1=sjv, op=OP.mult)
                        g.tensor_tensor(out=t2, in0=mu2, in1=cjv, op=OP.mult)
                        g.drain()
                        g.tensor_tensor(out=hq[:, :, 16:32], in0=t1, in1=t2, op=OP.add)
                        g.drain()
                    kb.on(G, f_ghb)
                    cgb = kb.bump(G)
                    kb.wait(V, gb_done[qb][0], gb_done[qb][1])
                    kb.wait(V, G, cgb)

                    def f_slot(v, qb=qb, i=i, buf=buf):
                        hq = bff[:, 15488 + buf * 256:15552 + buf * 256].bitcast(BF16).rearrange("p (a b) -> p a b", a=4)
                        for j in range(4):
                            pv = bff[:, 15104:15488].bitcast(BF16).rearrange("p (a b) -> p a b", a=8)
                            v.tensor_tensor(out=pv[:, :, 0:64],
                                            in0=qallb[:, i, :].rearrange("p (a b) -> p a b", a=8),
                                            in1=gout2[buf][:, j, 0:512].rearrange("p (a b) -> p a b", a=8),
                                            op=OP.mult)
                            v.tensor_tensor(out=pv[:, :, 64:96], in0=jslot[:, i, :, :],
                                            in1=hq[:, j, :].unsqueeze(1).broadcast_to((128, 8, 32)),
                                            op=OP.mult)
                            v.drain()
                            v.tensor_reduce(out=ssc[:, :, qb * 4 + j],
                                            in_=pv, axis=mybir.AxisListType.X, op=OP.add)
                            v.drain()
                        sq = ssc[:, :, qb * 4:(qb + 1) * 4]
                        v.tensor_tensor(out=sq, in0=sq,
                                        in1=smv[:, 0:8].unsqueeze(2).broadcast_to((128, 8, 4)), op=OP.subtract)
                        v.drain()
                    kb.on(V, f_slot)
                    c1 = kb.bump(V)
                    vscore_done[buf] = c1
                    if qb + 1 < 8:
                        issue_gathers(qb + 1)
                    kb.wait(S, V, c1)

                    def f_exp2(sx, qb=qb):
                        sx.activation(out=ssc[:, :, qb * 4:(qb + 1) * 4],
                                      in_=ssc[:, :, qb * 4:(qb + 1) * 4], func=AF.Exp)
                        sx.drain()
                    kb.on(S, f_exp2)
                    c4 = kb.bump(S)
                    kb.wait(V, S, c4)
                    kb.wait(G, S, c4)

                    def f_zadd(v, qb=qb):
                        sq = ssc[:, :, qb * 4:(qb + 1) * 4]
                        v.tensor_reduce(out=smv[:, 16:24], in_=sq, axis=mybir.AxisListType.X, op=OP.add)
                        v.drain()
                        v.tensor_tensor(out=smv[:, 8:16], in0=smv[:, 8:16], in1=smv[:, 16:24], op=OP.add)
                        v.drain()
                    kb.on(V, f_zadd)
                    kb.bump(V)
                    for j in range(4):
                        jb = j % 2
                        if gprod_done[jb] is not None:
                            kb.wait(G, V, gprod_done[jb])

                        def f_prod(g, qb=qb, j=j, buf=buf, jb=jb):
                            aj = ssc[:, :, qb * 4 + j].unsqueeze(2).broadcast_to((128, 8, 64))
                            pvx = ptk[:, jb, 0:1024].bitcast(F32)
                            g.tensor_tensor(out=pvx.rearrange("p (a b) -> p a b", a=8),
                                            in0=gout2[buf][:, j, 512:1024].rearrange("p (a b) -> p a b", a=8),
                                            in1=aj, op=OP.mult)
                            g.drain()
                        kb.on(G, f_prod)
                        cgp = kb.bump(G)
                        kb.wait(V, G, cgp)

                        def f_add(v, jb=jb):
                            pvx = ptk[:, jb, 0:1024].bitcast(F32)
                            v.tensor_tensor(out=slacc[:], in0=slacc[:], in1=pvx, op=OP.add)
                            v.drain()
                        kb.on(V, f_add)
                        gprod_done[jb] = kb.bump(V)
                    if qb == 0:
                        kb.wait(TE, lv[0], lv[1])
                        kb.wait(TE, S, cbe)
                    h = qb

                    def f_tpa(t, h=h, i=i):
                        t.transpose(pb[0:128, 0:128], prodb[:, h * 256:h * 256 + 128], identb[:])
                        if i >= 1:
                            t.transpose(pb[0:128, 128:256], prodb[:, h * 256 + 128:h * 256 + 256], identb[:])
                    kb.on(TE, f_tpa)
                    cta = kb.bump(TE)
                    kb.wait(S, TE, cta)

                    def f_eva(sx, i=i):
                        sx.activation(out=hw_scr[:, 0, :], in_=pb[:, 0:128], func=AF.Copy)
                        if i >= 1:
                            sx.activation(out=hw_scr[:, 1, :], in_=pb[:, 128:256], func=AF.Copy)
                        sx.drain()
                    kb.on(S, f_eva)
                    cve = kb.bump(S)
                    kb.wait(TE, S, cve)

                    def f_avb(t, h=h, i=i):
                        nw = 2 if i >= 1 else 1
                        for w in range(nw):
                            t.matmul(psB[:, 1024 + h * 64:1024 + (h + 1) * 64], hw_scr[:, w, :],
                                     vwin[:, w, h * 64:(h + 1) * 64], start=(w == 0), stop=(w == nw - 1))
                    kb.on(TE, f_avb)
                    kb.bump(TE)
                kb.wait(V, TE, kb.cnt["t"])

                def f_merge(v):
                    v.reciprocal(out=smv[:, 40:48], in_=smv[:, 8:16])
                    v.drain()
                    v.tensor_tensor(out=scE[:, 0:512], in0=psB[:, 1024:1536], in1=slacc[:], op=OP.add)
                    v.drain()
                    for h in range(8):
                        v.tensor_scalar(out=o_tok[:, h * 64:(h + 1) * 64], in0=scE[:, h * 64:(h + 1) * 64],
                                        scalar1=smv[:, 40 + h:41 + h], scalar2=None, op0=OP.mult)
                    v.drain()
                kb.on(V, f_merge)
                cmg = kb.bump(V)
                kb.wait(TE, V, cmg)

                kb.wait(TE, ldwo[0], ldwo[1])
                for c4 in range(4):
                    def f_tpo(t, c4=c4):
                        t.transpose(pb[0:128, 0:128], o_tok[:, c4 * 128:(c4 + 1) * 128], identb[:])
                    kb.on(TE, f_tpo)
                    cto = kb.bump(TE)
                    kb.wait(S, TE, cto)

                    def f_evo(sx, c4=c4):
                        sx.activation(out=outTv[:, c4, :], in_=pb[:, 0:128], func=AF.Copy)
                        sx.drain()
                    kb.on(S, f_evo)
                    cvo = kb.bump(S)
                    kb.wait(TE, S, cvo)
                slot = i % 4
                kb.wait(Q, V, kb.cnt["v"]) if False else None
                if kb.win_store[slot] is not None:
                    kb.wait(Q, kb.win_store[slot][0], kb.win_store[slot][1])
                lxw = kb.dma(Q, xw[:, slot, :], src_dram[i * 128:(i + 1) * 128, :], key="win%d" % slot)
                for eh in range(2):
                    ps = psB[:, 1024 + eh * 512:1024 + (eh + 1) * 512]
                    for c4 in range(4):
                        def f_mmo(t, ps=ps, c4=c4, eh=eh):
                            t.matmul(ps, outTv[:, c4, :], wout_v[:, c4, eh * 512:(eh + 1) * 512],
                                     start=(c4 == 0), stop=(c4 == 3))
                        kb.on(TE, f_mmo)
                    cmo = kb.bump(TE)
                    kb.wait(V, TE, cmo)
                    if eh == 0:
                        kb.wait(V, lxw[0], lxw[1])

                    def f_res(v, ps=ps, slot=slot, eh=eh):
                        v.tensor_tensor(out=xw[:, slot, eh * 512:(eh + 1) * 512],
                                        in0=xw[:, slot, eh * 512:(eh + 1) * 512], in1=ps, op=OP.add)
                        v.drain()
                    kb.on(V, f_res)
                    crs = kb.bump(V)
                    kb.wait(TE, V, crs)
                kb.wait(Q, V, kb.cnt["v"])
                sto = kb.dma(Q, dst_dram[i * 128:(i + 1) * 128, :], xw[:, slot, :], key="win%d" % slot)
                kb.win_store[slot] = sto
            kb.barrier()

        emit_ffn(dr["strm"], dr["strm"])
        emit_attn_b(dr["strm"], dr["strm"])
        emit_ffn(dr["strm"], dr["out"])
        kb.barrier()
        for kname, sm in kb.dma_sem_map.items():
            kb.wait(Q, sm, kb.dma_cnt[sm])

        with nc.Block() as block:
            @block.vector
            def _(eng):
                for fn in kb.ops["v"]:
                    fn(eng)

            @block.scalar
            def _(eng):
                for fn in kb.ops["s"]:
                    fn(eng)

            @block.gpsimd
            def _(eng):
                for fn in kb.ops["g"]:
                    fn(eng)

            @block.tensor
            def _(eng):
                for fn in kb.ops["t"]:
                    fn(eng)

            @block.sync
            def _(eng):
                for fn in kb.ops["q"]:
                    fn(eng)

    return nc


def attn_b_host_tables(inputs):
    th = 1.0 / 10000.0 ** (2.0 * np.arange(16) / 32.0)
    t = np.arange(T)[:, None]
    tq = np.concatenate([np.cos(t * th), np.sin(t * th)], 1).astype(np.float32)
    t2 = np.concatenate([np.cos(2 * t * th), np.sin(2 * t * th)], 1).astype(np.float32)
    pp = np.concatenate([np.cos((t + 63) * th), np.sin((t + 63) * th)], 1).astype(np.float32)
    j = np.arange(64, 96)[:, None]
    jj = np.concatenate([np.cos(j * th).reshape(-1), np.sin(j * th).reshape(-1)])[None, :].astype(np.float32)
    p_ = np.arange(128)[:, None]
    r_ = np.arange(256)[None, :]
    mk_band = np.where((r_ >= p_ + 65) & (r_ <= p_ + 128), 0.0, NEG).astype(np.float32)
    c_ = np.arange(128)[None, :]
    mk_b0 = np.where(p_ < 64, np.where(c_ < 64, 0.0, NEG),
                     np.where((c_ >= p_ - 63) & (c_ <= p_), 0.0, NEG)).astype(np.float32)
    mk_idiag = np.where(c_ >= p_ - 63, NEG, 0.0).astype(np.float32)
    mk_iprev = np.where(c_ >= p_ + 65, NEG, 0.0).astype(np.float32)
    d = {
        "wdkv": inputs["L0_attn_b_W_dkv"], "wuk": inputs["L0_attn_b_W_uk"],
        "wuv": inputs["L0_attn_b_W_uv"], "wdq": inputs["L0_attn_b_W_dq"],
        "wuq": inputs["L0_attn_b_W_uq"], "wqr": inputs["L0_attn_b_W_qr"],
        "wkr": inputs["L0_attn_b_W_kr"], "widxq": inputs["L0_attn_b_idx_wq"],
        "widxk": inputs["L0_attn_b_idx_wk"], "wout": inputs["L0_attn_b_W_out"],
        "idxw": np.asarray(inputs["L0_attn_b_idx_w"]).reshape(1, 4),
        "rawd": np.asarray(inputs["L0_attn_b_raw_delta"]).reshape(1, 16),
        "tb_tq": tq, "tb_t2": t2, "tb_pp": pp, "tb_jj": jj,
        "mk_band": mk_band, "mk_b0": mk_b0, "mk_idiag": mk_idiag, "mk_iprev": mk_iprev,
    }
    return {k: np.ascontiguousarray(np.asarray(v, dtype=np.float32)) for k, v in d.items()}


def kernel(**inputs):
    x = np.ascontiguousarray(inputs["L0_attn_a_x"], dtype=np.float32)
    host = {
        "wq": np.ascontiguousarray(inputs["L0_attn_a_Wq"].T),
        "wk": np.ascontiguousarray(inputs["L0_attn_a_Wk"].T),
        "wv": np.ascontiguousarray(inputs["L0_attn_a_Wv"].T),
        "wo": np.ascontiguousarray(inputs["L0_attn_a_Wo"].T),
        "w1": np.ascontiguousarray(inputs["L0_ffn_w1"].T),
        "w2": np.ascontiguousarray(inputs["L0_ffn_w2"].T),
        "convw": np.ascontiguousarray(
            np.asarray(inputs["L0_ffn_conv_w"]).reshape(32, 128, 3).transpose(1, 0, 2)),
        "convb": np.ascontiguousarray(np.asarray(inputs["L0_ffn_conv_b"]).reshape(32, 128).T),
    }
    host.update(attn_b_host_tables(inputs))
    nc = build_core_program(DEBUG_TAPS)
    in_maps = [dict(host, x=x[c]) for c in range(8)]
    res = run_bass_kernel_spmd(nc, in_maps, core_ids=list(range(8)))
    return np.stack([r["out"] for r in res.results], axis=0)


# version 18
# speedup vs baseline: 1.0172x; 1.0172x over previous
import math
import numpy as np
import concourse.bass as bass
import concourse.mybir as mybir
from concourse.bass_utils import run_bass_kernel_spmd
from concourse.alu_op_type import AluOpType as OP
from bass_rust import ReduceOp
from contextlib import ExitStack

F32 = mybir.dt.float32
BF16 = mybir.dt.bfloat16
I32 = mybir.dt.int32
U32 = mybir.dt.uint32
AF = mybir.ActivationFunctionType

T = 2048
D = 1024
NT = 16
HA = 16
NEG = -1e30
MICH_ITERS = 10
LN_EPS = 1e-5
MAGIC = 3.0 * 2 ** 22

ENGS = ("v", "s", "g", "t", "q")
DEBUG_TAPS = []


class KB:
    def __init__(self, nc):
        self.nc = nc
        self.ops = {e: [] for e in ENGS}
        self.sem = {}
        self.cnt = {e: 0 for e in ENGS}
        self.dma_cnt = {}
        self.dma_sem_pool = ["d%d" % i for i in range(24)]
        self.dma_sem_map = {}
        self.win_store = [None, None, None, None]

    def on(self, e, fn):
        self.ops[e].append(fn)

    def bump(self, e, drain=True):
        self.cnt[e] += 1
        tgt = self.cnt[e]

        def fn(eng):
            if drain and e in ("v", "s", "g", "t"):
                eng.drain()
            eng.sem_inc(self.sem[e], 1)
        self.on(e, fn)
        return tgt

    def wait(self, e, producer_sem, count):
        if count <= 0:
            return
        self.on(e, lambda eng: eng.wait_ge(self.sem[producer_sem], count))

    def barrier(self, engines=("v", "s", "g", "t", "q")):
        marks = {e: self.bump(e) for e in engines}
        for e in engines:
            for p, c in marks.items():
                if p != e:
                    self.wait(e, p, c)

    def dma(self, queue_eng, out_ap, in_ap, key="st"):
        if key not in self.dma_sem_map:
            idx = len(self.dma_sem_map)
            assert idx < len(self.dma_sem_pool), "out of dma sems"
            self.dma_sem_map[key] = self.dma_sem_pool[idx]
            self.dma_cnt[self.dma_sem_map[key]] = 0
        sem = self.dma_sem_map[key]
        self.dma_cnt[sem] += 16
        tgt = self.dma_cnt[sem]
        self.on(queue_eng, lambda eng: eng.dma_start(out=out_ap, in_=in_ap).then_inc(self.sem[sem], 16))
        return sem, tgt


def build_core_program(debug_taps=()):
    nc = bass.Bass("TRN2", target_bir_lowering=False)
    kb = KB(nc)
    V, S, G, TE, Q = "v", "s", "g", "t", "q"

    dr = {}
    dr["x"] = nc.dram_tensor("x", [T, D], F32, kind="ExternalInput").ap()
    for nm in ("wq", "wk", "wv", "wo"):
        dr[nm] = nc.dram_tensor(nm, [D, D], F32, kind="ExternalInput").ap()
    dr["out"] = nc.dram_tensor("out", [T, D], F32, kind="ExternalOutput").ap()
    dr["strm"] = nc.dram_tensor("strm", [T, D], F32).ap()
    dr["w1"] = nc.dram_tensor("w1", [D, 4096], F32, kind="ExternalInput").ap()
    dr["w2"] = nc.dram_tensor("w2", [4096, D], F32, kind="ExternalInput").ap()
    dr["convw"] = nc.dram_tensor("convw", [128, 32, 3], F32, kind="ExternalInput").ap()
    dr["convb"] = nc.dram_tensor("convb", [128, 32], F32, kind="ExternalInput").ap()
    dr["w1q"] = nc.dram_tensor("w1q", [D, 4096], BF16).ap()
    dr["w2q"] = nc.dram_tensor("w2q", [4096, D], BF16).ap()
    for nm, shp in (("wdkv", [D, 256]), ("wuk", [256, 512]), ("wuv", [256, 512]),
                    ("wdq", [D, 512]), ("wuq", [512, 512]), ("wqr", [512, 256]),
                    ("wkr", [D, 32]), ("widxq", [D, 256]), ("widxk", [D, 64]),
                    ("wout", [512, D]), ("idxw", [1, 4]), ("rawd", [1, 16])):
        dr[nm] = nc.dram_tensor(nm, shp, F32, kind="ExternalInput").ap()
    for nm in ("tb_tq", "tb_t2", "tb_pp"):
        dr[nm] = nc.dram_tensor(nm, [T, 32], F32, kind="ExternalInput").ap()
    dr["tb_jj"] = nc.dram_tensor("tb_jj", [1, 1024], F32, kind="ExternalInput").ap()
    dr["mk_band"] = nc.dram_tensor("mk_band", [128, 256], F32, kind="ExternalInput").ap()
    dr["mk_b0"] = nc.dram_tensor("mk_b0", [128, 128], F32, kind="ExternalInput").ap()
    dr["mk_idiag"] = nc.dram_tensor("mk_idiag", [128, 128], F32, kind="ExternalInput").ap()
    dr["mk_iprev"] = nc.dram_tensor("mk_iprev", [128, 128], F32, kind="ExternalInput").ap()
    dr["gtab"] = nc.dram_tensor("gtab", [T, 1088], BF16).ap()
    taps = {}
    for nm, shape, dt in debug_taps:
        taps[nm] = nc.dram_tensor(nm, shape, dt, kind="ExternalOutput").ap()

    with ExitStack() as ctx:
        sbuf = lambda n_, s, d: ctx.enter_context(nc.sbuf_tensor(n_, s, d))
        xw = sbuf("xw", [128, 4, D], F32)
        xhT = sbuf("xhT", [128, 8, T], BF16)
        qT = sbuf("qT", [128, 8, T], BF16)
        bigA = sbuf("bigA", [128, 16384], F32)
        kT = bigA[:, 0:8192].bitcast(BF16).rearrange("p (a b) -> p a b", a=8)
        vtok = bigA[:, 8192:16384].bitcast(BF16).rearrange("p (a b) -> p a b", a=NT)
        ptk = sbuf("ptk", [128, 2, T], BF16)
        wAB = sbuf("wAB", [128, 16, 512], BF16)
        wA = wAB[:, 0:8, :]
        wB = wAB[:, 8:16, :]
        shr12 = sbuf("shr12", [128, 3072], F32)
        cands = sbuf("cands", [128, HA, 16], F32)
        taus = sbuf("taus", [128, HA], F32)
        tausq = sbuf("tausq", [128, NT, HA], F32)
        scA = sbuf("scA", [128, 512], F32)
        srow = scA[0:1, :]
        xnb = sbuf("xnb", [128, D], BF16)
        mv = sbuf("mv", [128, NT, 2], F32)
        msc = sbuf("msc", [128, 32], F32)
        msc2 = sbuf("msc2", [128, 32], F32)
        rst = sbuf("rst", [128, NT, 2], F32)
        stat6 = sbuf("stat6", [128, 12], F32)
        utri = sbuf("utri", [128, 128], F32)
        identf = sbuf("identf", [128, 128], F32)
        onesf = sbuf("onesf", [1, 128], F32)
        o2T = sbuf("o2T", [128, 8, 512], BF16)
        hpad = xhT.rearrange("p a b -> p (a b)")[:, 4096:8208].bitcast(F32)\
            .rearrange("p (a b) -> p a b", a=4)
        scD = shr12[:, 0:1024]
        sc4 = shr12[:, 1024:2048]
        sci = shr12[:, 2048:3072].bitcast(I32)
        amt = sbuf("amt", [128, 512], F32)
        amtn = shr12[:, 1024:1536]
        s2bc = shr12[:, 1536:2048]
        cvw = sbuf("cvw", [128, 32, 3], F32)
        cvb = sbuf("cvb", [128, 32], F32)
        ffs = sbuf("ffs", [128, NT, 4], F32)
        wst = sbuf("wst", [128, 80], F32)
        ltri01b = sbuf("ltri01b", [128, 128], BF16)
        identb = sbuf("identb", [128, 128], BF16)
        ones1 = sbuf("ones1", [1, 128], BF16)
        negonesf = sbuf("negonesf", [1, 128], F32)
        cst = sbuf("cst", [128, 16], F32)
        cst_slots = {}

        def const_f32(val):
            if val not in cst_slots:
                cst_slots[val] = len(cst_slots)
                assert len(cst_slots) <= 16
                nc.const_aps.register(F32, float(val), cst[:, cst_slots[val]:cst_slots[val] + 1])
            return cst[:, cst_slots[val]:cst_slots[val] + 1]

        psA = ctx.enter_context(nc.psum_tensor("psA", [128, 2048], F32))
        psB = ctx.enter_context(nc.psum_tensor("psB", [128, 2048], F32))
        aoT = xhT

        for e in ENGS:
            kb.sem[e] = ctx.enter_context(nc.semaphore("sem_" + e))
        for s_ in kb.dma_sem_pool:
            kb.sem[s_] = ctx.enter_context(nc.semaphore("sem_" + s_))

        const_f32(LN_EPS)

        def init_consts(g):
            for val, slot in cst_slots.items():
                g.memset(cst[:, slot:slot + 1], float(val))
            g.memset(utri[:], 0.0)
            g.memset(identb[:], 1.0)
            g.memset(ones1[:], 1.0)
            g.memset(negonesf[:], -1.0)
            g.memset(ltri01b[:], 1.0)
            g.memset(identf[:], 1.0)
            g.memset(onesf[:], 1.0)
            g.drain()
            g.affine_select(out=identf[:], in_=identf[:], compare_op=OP.is_equal,
                            fill=0.0, base=0, pattern=[[-1, 128]], channel_multiplier=1)
            g.affine_select(out=utri[:], in_=utri[:], compare_op=OP.is_ge,
                            fill=NEG, base=0, pattern=[[-1, 128]], channel_multiplier=1)
            g.affine_select(out=ltri01b[:], in_=ltri01b[:], compare_op=OP.is_ge,
                            fill=0.0, base=0, pattern=[[1, 128]], channel_multiplier=-1)
            g.affine_select(out=identb[:], in_=identb[:], compare_op=OP.is_equal,
                            fill=0.0, base=0, pattern=[[-1, 128]], channel_multiplier=1)
            g.drain()
            g.iota(wst[:, 40:48], [[1, 8]], base=1, channel_multiplier=0,
                   allow_small_or_imprecise_dtypes=True)
        kb.on(G, init_consts)
        gconst = kb.bump(G)
        kb.wait(TE, G, gconst)
        kb.wait(V, G, gconst)

        def f_rinv(v):
            v.reciprocal(out=wst[:, 40:48], in_=wst[:, 40:48])
            v.drain()
        kb.on(V, f_rinv)
        kb.bump(V)
        kb.wait(V, G, gconst)

        buf_free = {"wA": 0, "wB": 0}
        bufs = {"wA": wA, "wB": wB}

        def load_w_half(nm, bufname, half):
            src = dr[nm][:, half * 512:(half + 1) * 512].rearrange("(a p) b -> p a b", p=128)
            kb.wait(G, TE, buf_free[bufname])
            return kb.dma(G, bufs[bufname], src, key=bufname)

        ld_wq0 = load_w_half("wq", "wA", 0)
        ld_wq1 = load_w_half("wq", "wB", 1)

        def emit_ln_transpose(src_dram, dst):
            win_free = [0, 0, 0, 0]
            for i in range(NT):
                slot = i % 4
                kb.wait(Q, V, win_free[slot])
                if kb.win_store[slot] is not None:
                    kb.wait(Q, kb.win_store[slot][0], kb.win_store[slot][1])
                ld = kb.dma(Q, xw[:, slot, :], src_dram[i * 128:(i + 1) * 128, :], key="win%d" % slot)
                kb.wait(V, ld[0], ld[1])

                def f_stat(v, i=i, slot=slot):
                    v.bn_stats(out=stat6[:, 0:6], in_=xw[:, slot, 0:512])
                    v.bn_stats(out=stat6[:, 6:12], in_=xw[:, slot, 512:1024])
                    v.drain()
                    v.bn_aggr(out=mv[:, i, :], in_=stat6[:])
                    v.drain()
                kb.on(V, f_stat)
                c1 = kb.bump(V)
                kb.wait(S, V, c1)

                def f_sq(s, i=i):
                    s.activation(out=rst[:, i, 0:1], in_=mv[:, i, 1:2],
                                 func=AF.Sqrt, bias=LN_EPS, scale=1.0)
                kb.on(S, f_sq)
                c2 = kb.bump(S)
                kb.wait(V, S, c2)

                def f_norm(v, i=i, slot=slot):
                    v.reciprocal(out=rst[:, i, 1:2], in_=rst[:, i, 0:1])
                    v.drain()
                    v.tensor_scalar(out=xnb[:], in0=xw[:, slot, :], scalar1=mv[:, i, 0:1],
                                    scalar2=rst[:, i, 1:2], op0=OP.subtract, op1=OP.mult)
                kb.on(V, f_norm)
                cnorm = kb.bump(V)
                win_free[slot] = cnorm
                kb.wait(TE, V, cnorm)
                half = i % 2
                pview = psA[:, half * 512:(half + 1) * 512].bitcast(BF16)
                for dc in range(8):
                    def f_tp(t, dc=dc, pview=pview):
                        t.transpose(pview[:, dc * 128:(dc + 1) * 128],
                                    xnb[:, dc * 128:(dc + 1) * 128], identb[:])
                    kb.on(TE, f_tp)
                ctp = kb.bump(TE)
                kb.wait(V, TE, ctp)

                def f_evac(v, i=i, pview=pview, dst=dst):
                    v.tensor_copy(out=dst[:, :, i * 128:(i + 1) * 128],
                                  in_=pview.rearrange("p (a b) -> p a b", a=8))
                kb.on(V, f_evac)
                pe_ok = kb.bump(V)
                kb.wait(TE, V, pe_ok)

        emit_ln_transpose(dr["x"], xhT)
        kb.barrier()

        if "t_xhT" in taps:
            st = None
            for dc in range(8):
                st = kb.dma(Q, taps["t_xhT"][dc * 128:(dc + 1) * 128, :], xhT[:, dc, :], key="tap")
            kb.wait(Q, st[0], st[1])

        SCALE_A = 0.125

        def proj_featmajor(dst, wname, scale, preloaded=None):
            for half in range(2):
                bufname = "wA" if half == 0 else "wB"
                ld = preloaded[half] if preloaded else load_w_half(wname, bufname, half)
                kb.wait(TE, ld[0], ld[1])
                for ec in range(4):
                    e_glob = half * 4 + ec
                    for tc in range(4):
                        slot = (e_glob * 4 + tc) % 3
                        ps = psB[:, slot * 512:(slot + 1) * 512]
                        for dc in range(8):
                            def f_mm(t, ps=ps, dc=dc, ec=ec, tc=tc, wb=bufs[bufname]):
                                t.matmul(ps, wb[:, dc, ec * 128:(ec + 1) * 128],
                                         xhT[:, dc, tc * 512:(tc + 1) * 512],
                                         start=(dc == 0), stop=(dc == 7))
                            kb.on(TE, f_mm)
                        cm = kb.bump(TE)
                        kb.wait(S, TE, cm)

                        def f_ev(s, ps=ps, e_glob=e_glob, tc=tc, dst=dst, scale=scale):
                            s.activation(out=dst[:, e_glob, tc * 512:(tc + 1) * 512], in_=ps,
                                         func=AF.Copy, scale=scale)
                        kb.on(S, f_ev)
                        ce = kb.bump(S)
                        kb.wait(TE, S, ce)
                buf_free[bufname] = kb.cnt["t"]

        proj_featmajor(qT, "wq", SCALE_A, preloaded=(ld_wq0, ld_wq1))
        proj_featmajor(kT, "wk", 1.0)

        for half in range(2):
            bufname = "wA" if half == 0 else "wB"
            ld = load_w_half("wv", bufname, half)
            kb.wait(TE, ld[0], ld[1])
            for i in range(NT):
                slot = i % 3
                ps = psB[:, slot * 512:(slot + 1) * 512]
                for dc in range(8):
                    def f_mm(t, ps=ps, dc=dc, i=i, wb=bufs[bufname]):
                        t.matmul(ps, xhT[:, dc, i * 128:(i + 1) * 128], wb[:, dc, :],
                                 start=(dc == 0), stop=(dc == 7))
                    kb.on(TE, f_mm)
                cm = kb.bump(TE)
                kb.wait(S, TE, cm)

                def f_ev(s, ps=ps, i=i, half=half):
                    s.activation(out=vtok[:, i, half * 512:(half + 1) * 512], in_=ps, func=AF.Copy)
                kb.on(S, f_ev)
                ce = kb.bump(S)
                kb.wait(TE, S, ce)
            buf_free[bufname] = kb.cnt["t"]

        kb.barrier()
        ld_wo0 = load_w_half("wo", "wA", 0)
        ld_wo1 = load_w_half("wo", "wB", 1)

        slab_free = {0: 0, 1: 0}
        for qt in range(NT):
            kw = (qt + 1) * 128
            for h in range(HA):
                slab = psA if h % 2 == 0 else psB
                sid = h % 2
                kb.wait(TE, V, slab_free[sid])
                hp = (h % 2) * 64
                hc = h // 2
                for kc in range((kw + 511) // 512):
                    w = min(512, kw - kc * 512)

                    def f_mm(t, slab=slab, kc=kc, w=w, hp=hp, hc=hc, qt=qt):
                        t.matmul(slab[:, kc * 512:kc * 512 + w],
                                 qT[hp:hp + 64, hc, qt * 128:(qt + 1) * 128],
                                 kT[hp:hp + 64, hc, kc * 512:kc * 512 + w],
                                 start=True, stop=True)
                    kb.on(TE, f_mm)
                cm = kb.bump(TE)
                kb.wait(V, TE, cm)

                def f_ex(v, slab=slab, kw=kw, qt=qt, h=h):
                    v.tensor_tensor(out=slab[:, qt * 128:qt * 128 + 128],
                                    in0=slab[:, qt * 128:qt * 128 + 128], in1=utri[:], op=OP.add)
                    v.drain()
                    v.max(out=cands[:, h, 0:8], in_=slab[:, 0:kw])
                kb.on(V, f_ex)
                slab_free[sid] = kb.bump(V)

            def f_tau(v, qt=qt):
                cview = cands[:, :, 0:8]
                a = scA[:, 0:128].rearrange("p (a b) -> p a b", a=HA)
                bq = scA[:, 128:256].rearrange("p (a b) -> p a b", a=HA)
                v.drain()
                v.tensor_copy(out=a, in_=cview)
                v.drain()
                for d in (1, 2, 4):
                    v.tensor_tensor(out=bq[:, :, d:8], in0=a[:, :, d:8], in1=a[:, :, 0:8 - d], op=OP.add)
                    v.tensor_copy(out=bq[:, :, 0:d], in_=a[:, :, 0:d])
                    v.drain()
                    a, bq = bq, a
                v.tensor_scalar(out=a, in0=a, scalar1=1.0, scalar2=None, op0=OP.subtract)
                v.drain()
                v.tensor_tensor(out=a, in0=a,
                                in1=wst[:, 40:48].unsqueeze(1).broadcast_to((128, HA, 8)), op=OP.mult)
                v.drain()
                v.tensor_reduce(out=taus[:], in_=a, axis=mybir.AxisListType.X, op=OP.max)
                v.drain()
                v.tensor_copy(out=tausq[:, qt, :], in_=taus[:])
                v.drain()
            kb.on(V, f_tau)
            kb.bump(V)

        kb.barrier()
        if "t_tau" in taps:
            st = None
            for i in range(NT):
                st = kb.dma(Q, taps["t_tau"][i * 128:(i + 1) * 128, :], tausq[:, i, :], key="tap")
            kb.wait(Q, st[0], st[1])

        av_done = {0: 0, 1: 0}
        ev_prev = {0: 0, 1: 0, 2: 0, 3: 0}
        ntrf = shr12[0:1, 0:2048]
        taubc = shr12[:, 0:2048]
        for h in range(HA):
            hp = (h % 2) * 64
            hc = h // 2
            for qt in range(NT):
                def f_tp(t, qt=qt, h=h):
                    t.transpose(psB[0:1, qt * 128:(qt + 1) * 128], tausq[:, qt, h:h + 1], identf[:])
                kb.on(TE, f_tp)
            ctp = kb.bump(TE)
            kb.wait(V, TE, ctp)

            def f_ntr(v):
                v.tensor_copy(out=ntrf, in_=psB[0:1, 0:T])
            kb.on(V, f_ntr)
            cn = kb.bump(V)
            kb.wait(TE, V, cn)

            def f_tbc(t):
                for c in range(4):
                    t.matmul(psB[:, c * 512:(c + 1) * 512], onesf[0:1, :],
                             ntrf[0:1, c * 512:(c + 1) * 512], start=True, stop=True)
            kb.on(TE, f_tbc)
            ctb = kb.bump(TE)
            kb.wait(V, TE, ctb)

            def f_tbev(v):
                for c in range(4):
                    v.tensor_copy(out=taubc[:, c * 512:(c + 1) * 512], in_=psB[:, c * 512:(c + 1) * 512])
                v.drain()
            kb.on(V, f_tbev)
            ctv = kb.bump(V)
            kb.wait(TE, V, ctv)

            def issue_av(pend):
                kt_, buf_, cevac_, cmask_ = pend
                kb.wait(TE, S, cevac_)
                kb.wait(TE, V, cmask_)
                for qc in range(4):
                    if (qc + 1) * 512 <= kt_ * 128:
                        continue
                    q0c = max(qc * 512, kt_ * 128)
                    w = (qc + 1) * 512 - q0c
                    last_kt = min(15, 4 * qc + 3)

                    def f_av(t, kt=kt_, q0c=q0c, w=w, h=h, buf=buf_, last_kt=last_kt):
                        t.matmul(psB[0:64, q0c:q0c + w],
                                 vtok[:, kt, h * 64:h * 64 + 64],
                                 ptk[:, buf, q0c - kt * 128:q0c - kt * 128 + w],
                                 start=(kt == 0), stop=(kt == last_kt))
                    kb.on(TE, f_av)
                av_done[buf_] = kb.bump(TE)

            pend = None
            chunkc = 0
            for kt in range(NT):
                buf = kt % 2
                qw_total = T - kt * 128
                for jc in range((qw_total + 511) // 512):
                    q0 = kt * 128 + jc * 512
                    w = min(512, T - q0)
                    slot = chunkc % 4
                    chunkc += 1
                    ps = psA[:, slot * 512:slot * 512 + w]
                    kb.wait(TE, S, ev_prev[slot])

                    def f_mm(t, ps=ps, kt=kt, q0=q0, w=w, hp=hp, hc=hc):
                        t.matmul(ps, kT[hp:hp + 64, hc, kt * 128:(kt + 1) * 128],
                                 qT[hp:hp + 64, hc, q0:q0 + w], start=True, stop=True)
                    kb.on(TE, f_mm)
                    cm = kb.bump(TE)
                    kb.wait(V, TE, cm)

                    def f_sub(v, ps=ps, q0=q0, w=w):
                        v.tensor_tensor(out=ps, in0=ps, in1=taubc[:, q0:q0 + w], op=OP.subtract)
                        v.drain()
                    kb.on(V, f_sub)
                    cs = kb.bump(V)
                    kb.wait(S, V, cs)
                    if jc == 0:
                        kb.wait(S, TE, av_done[buf])

                    def f_ev(s, ps=ps, kt=kt, q0=q0, w=w, buf=buf):
                        s.activation(out=ptk[:, buf, q0 - kt * 128:q0 - kt * 128 + w], in_=ps, func=AF.Relu)
                    kb.on(S, f_ev)
                    ev_prev[slot] = kb.bump(S)
                    if jc == 0:
                        kb.wait(V, S, ev_prev[slot])

                        def f_msk(v, buf=buf):
                            v.tensor_tensor(out=ptk[:, buf, 0:128], in0=ptk[:, buf, 0:128],
                                            in1=ltri01b[:], op=OP.mult)
                        kb.on(V, f_msk)
                        cmask = kb.bump(V)
                if pend is not None:
                    issue_av(pend)
                pend = (kt, buf, ev_prev[(chunkc - 1) % 4], cmask)
            issue_av(pend)
            kb.wait(V, TE, av_done[(NT - 1) % 2])

            def f_avev(v, hp=hp, hc=hc):
                v.tensor_copy(out=aoT[hp:hp + 64, hc, :], in_=psB[0:64, :])
            kb.on(V, f_avev)
            cae = kb.bump(V)
            kb.wait(TE, V, cae)

        kb.barrier()
        if "t_aoT" in taps:
            st = None
            for dc in range(8):
                st = kb.dma(Q, taps["t_aoT"][dc * 128:(dc + 1) * 128, :], aoT[:, dc, :], key="tap")
            kb.wait(Q, st[0], st[1])


        pass1_done = {}

        def emit_pass1(wname, nchunks, ccols, cbase):
            land = [bigA[:, 0:4096], bigA[:, 4096:8192]]
            free_at = [kb.cnt["s"], kb.cnt["s"]]
            for n in range(nchunks):
                b = n % 2
                kb.wait(G, S, free_at[b])
                ld = kb.dma(G, land[b][:, 0:ccols], dr[wname][n * 128:(n + 1) * 128, :], key="pg%d" % b)
                kb.wait(S, ld[0], ld[1])

                def f_abs(sx, n=n, b=b, ccols=ccols, cbase=cbase, land=land):
                    sx.activation(out=bigA[:, 12288:12288 + ccols],
                                  in_=land[b][:, 0:ccols], func=AF.Abs,
                                  accum_out=wst[:, cbase + n:cbase + n + 1])
                kb.on(S, f_abs)
                free_at[b] = kb.bump(S)
            pass1_done[wname] = kb.cnt["s"]

        emit_pass1("w1", 8, 4096, 0)
        emit_pass1("w2", 32, 1024, 48)
        for ld in (ld_wo0, ld_wo1):
            kb.wait(TE, ld[0], ld[1])
        win_free = [0, 0, 0, 0]
        st = None
        st_all = []
        for i in range(NT):
            slot = i % 4
            kb.wait(Q, V, win_free[slot])
            if kb.win_store[slot] is not None:
                kb.wait(Q, kb.win_store[slot][0], kb.win_store[slot][1])
            ld = kb.dma(Q, xw[:, slot, :], dr["x"][i * 128:(i + 1) * 128, :], key="win%d" % slot)
            for half in range(2):
                wbuf = wA if half == 0 else wB
                pslot = (i * 2 + half) % 3
                ps = psB[:, pslot * 512:(pslot + 1) * 512]
                for dc in range(8):
                    def f_mm(t, ps=ps, dc=dc, i=i, wbuf=wbuf):
                        t.matmul(ps, aoT[:, dc, i * 128:(i + 1) * 128], wbuf[:, dc, :],
                                 start=(dc == 0), stop=(dc == 7))
                    kb.on(TE, f_mm)
                cm = kb.bump(TE)
                kb.wait(V, TE, cm)
                if half == 0:
                    kb.wait(V, ld[0], ld[1])

                def f_ev(v, ps=ps, slot=slot, half=half):
                    v.tensor_tensor(out=xw[:, slot, half * 512:(half + 1) * 512],
                                    in0=xw[:, slot, half * 512:(half + 1) * 512], in1=ps, op=OP.add)
                kb.on(V, f_ev)
                ce = kb.bump(V)
                kb.wait(TE, V, ce)
            cdone = kb.bump(V)
            kb.wait(Q, V, cdone)
            st = kb.dma(Q, dr["strm"][i * 128:(i + 1) * 128, :], xw[:, slot, :], key="win%d" % slot)
            win_free[slot] = cdone
            kb.win_store[slot] = st
            st_all.append(st)
        buf_free["wA"] = buf_free["wB"] = kb.cnt["t"]

        kb.barrier()
        for st in st_all[-4:]:
            kb.wait(Q, st[0], st[1])

        k2v = xhT[:, :, :].rearrange("p a b -> p (a b)").rearrange("p (a b) -> p a b", a=32)
        k1T = qT
        wf1 = [ptk[:, 0, 0:1024].rearrange("p (a b) -> p a b", a=8),
               ptk[:, 1, 0:1024].rearrange("p (a b) -> p a b", a=8),
               xhT[:, 0, 0:1024].rearrange("p (a b) -> p a b", a=8),
               xhT[:, 1, 0:1024].rearrange("p (a b) -> p a b", a=8)]
        wf2 = [wAB[:, :, :].rearrange("p a b -> p (a b)")[:, j * 4096:(j + 1) * 4096]
               .rearrange("p (a b) -> p a b", a=32) for j in range(2)]

        def emit_prequant(wname, wqname, nchunks, ccols, stat_col):
            land = [bigA[:, 0:4096], bigA[:, 4096:8192]]
            kb.wait(V, S, pass1_done[wname])
            cbase = 0 if wname == "w1" else 48

            def f_mean(v, nchunks=nchunks, cbase=cbase):
                v.tensor_reduce(out=wst[:, 32:33], in_=wst[:, cbase:cbase + nchunks],
                                axis=mybir.AxisListType.X, op=OP.add)
                v.drain()
            kb.on(V, f_mean)
            cv = kb.bump(V)
            kb.wait(TE, V, cv)

            def f_tpm(t):
                t.transpose(psB[0:1, 0:128], wst[:, 32:33], identf[:])
            kb.on(TE, f_tpm)
            ct1 = kb.bump(TE)
            kb.wait(V, TE, ct1)

            def f_rtot(v):
                v.tensor_reduce(out=scA[0:1, 0:1], in_=psB[0:1, 0:128], axis=mybir.AxisListType.X, op=OP.add)
            kb.on(V, f_rtot)
            cv1 = kb.bump(V)
            kb.wait(TE, V, cv1)

            def f_bcm(t):
                t.matmul(psB[:, 256:257], onesf[0:1, :], scA[0:1, 0:1], start=True, stop=True)
            kb.on(TE, f_bcm)
            ct2 = kb.bump(TE)
            kb.wait(V, TE, ct2)

            def f_cptot(v):
                v.tensor_copy(out=wst[:, 33:34], in_=psB[:, 256:257])
                v.drain()
            kb.on(V, f_cptot)

            def f_scale(v, nchunks=nchunks, ccols=ccols, stat_col=stat_col):
                v.tensor_scalar(out=wst[:, 35:36], in0=wst[:, 33:34],
                                scalar1=1.0 / (nchunks * 128 * ccols), scalar2=1e-5,
                                op0=OP.mult, op1=OP.max)
                v.drain()
                v.reciprocal(out=wst[:, 34:35], in_=wst[:, 35:36])
                v.drain()
            kb.on(V, f_scale)
            cv2 = kb.bump(V)
            for b_ in (0, 1):
                kname = "pq%d" % b_
                if kname in kb.dma_sem_map:
                    sm = kb.dma_sem_map[kname]
                    kb.wait(Q, sm, kb.dma_cnt[sm])
            free_at = [cv2, cv2]
            st_done = [None, None]
            stq = None
            for n in range(nchunks):
                b = n % 2
                kb.wait(Q, V, free_at[b])
                if st_done[b] is not None:
                    kb.wait(Q, st_done[b][0], st_done[b][1])
                ld = kb.dma(Q, land[b][:, 0:ccols], dr[wname][n * 128:(n + 1) * 128, :], key="pq%d" % b)
                kb.wait(V, ld[0], ld[1])
                kb.wait(V, V if False else "v", cv2)

                def f_q(v, n=n, b=b, ccols=ccols):
                    fscr = bigA[:, 8192:8192 + ccols]
                    v.tensor_scalar(out=fscr, in0=land[b][:, 0:ccols], scalar1=wst[:, 34:35],
                                    scalar2=MAGIC, op0=OP.mult, op1=OP.add)
                    v.drain()
                    v.tensor_scalar(out=fscr, in0=fscr, scalar1=MAGIC, scalar2=-1.0,
                                    op0=OP.subtract, op1=OP.max)
                    v.drain()
                    v.tensor_scalar(out=land[b][:, 0:ccols].bitcast(BF16)[:, 0:ccols],
                                    in0=fscr, scalar1=1.0, scalar2=None, op0=OP.min)
                kb.on(V, f_q)
                cq = kb.bump(V)
                kb.wait(Q, V, cq)
                stq = kb.dma(Q, dr[wqname][n * 128:(n + 1) * 128, :],
                             land[b][:, 0:ccols].bitcast(BF16)[:, 0:ccols], key="pq%d" % b)
                st_done[b] = stq
                free_at[b] = cq
            kb.wait(Q, stq[0], stq[1])
            return wst[:, 35:36]

        kb.barrier()
        mc1 = emit_prequant("w1", "w1q", 8, 4096, 0)

        def f_savemc1(v):
            v.tensor_copy(out=wst[:, 36:37], in_=wst[:, 35:36])
        kb.on(V, f_savemc1)
        kb.bump(V)
        mc2 = emit_prequant("w2", "w2q", 32, 1024, 1)

        def f_savemc2(v):
            v.tensor_copy(out=wst[:, 37:38], in_=wst[:, 35:36])
        kb.on(V, f_savemc2)
        cmc = kb.bump(V)
        kb.barrier()
        for kname in ("pq0", "pq1"):
            if kname in kb.dma_sem_map:
                sm = kb.dma_sem_map[kname]
                for e_ in (V, S, G, TE):
                    kb.wait(e_, sm, kb.dma_cnt[sm])
        ldcv = kb.dma(Q, cvw[:], dr["convw"], key="cv")
        ldcb = kb.dma(Q, cvb[:], dr["convb"], key="cv")

        def emit_ffn(src_dram, dst_dram):
            win_free2 = [0, 0, 0, 0]
            for i in range(NT):
                slot = i % 4
                kb.wait(Q, V, win_free2[slot])
                if kb.win_store[slot] is not None:
                    kb.wait(Q, kb.win_store[slot][0], kb.win_store[slot][1])
                ld = kb.dma(Q, xw[:, slot, :], src_dram[i * 128:(i + 1) * 128, :], key="win%d" % slot)
                kb.wait(V, ld[0], ld[1])

                def f_stat(v, i=i, slot=slot):
                    v.bn_stats(out=stat6[:, 0:6], in_=xw[:, slot, 0:512])
                    v.bn_stats(out=stat6[:, 6:12], in_=xw[:, slot, 512:1024])
                    v.drain()
                    v.bn_aggr(out=mv[:, i, :], in_=stat6[:])
                    v.drain()
                kb.on(V, f_stat)
                c1 = kb.bump(V)
                kb.wait(S, V, c1)

                def f_sq(sx, i=i):
                    sx.activation(out=rst[:, i, 0:1], in_=mv[:, i, 1:2],
                                  func=AF.Sqrt, bias=LN_EPS, scale=1.0)
                kb.on(S, f_sq)
                c2 = kb.bump(S)
                kb.wait(V, S, c2)

                def f_norm(v, i=i, slot=slot):
                    v.reciprocal(out=rst[:, i, 1:2], in_=rst[:, i, 0:1])
                    v.drain()
                    v.tensor_scalar(out=sc4[:], in0=xw[:, slot, :], scalar1=mv[:, i, 0:1],
                                    scalar2=rst[:, i, 1:2], op0=OP.subtract, op1=OP.mult)
                    v.drain()
                    v.tensor_reduce(out=ffs[:, i, 0:1], in_=sc4[:], axis=mybir.AxisListType.X,
                                    op=OP.max, apply_absolute_value=True)
                    v.drain()
                    v.tensor_scalar(out=ffs[:, i, 1:2], in0=ffs[:, i, 0:1], scalar1=1e-5,
                                    scalar2=None, op0=OP.max)
                    v.drain()
                    v.reciprocal(out=ffs[:, i, 0:1], in_=ffs[:, i, 1:2])
                    v.drain()
                    v.tensor_scalar(out=ffs[:, i, 0:1], in0=ffs[:, i, 0:1], scalar1=127.0,
                                    scalar2=None, op0=OP.mult)
                    v.tensor_tensor(out=ffs[:, i, 1:2], in0=ffs[:, i, 1:2], in1=wst[:, 36:37], op=OP.mult)
                    v.drain()
                    v.tensor_scalar(out=ffs[:, i, 1:2], in0=ffs[:, i, 1:2], scalar1=1.0 / 127.0,
                                    scalar2=None, op0=OP.mult)
                kb.on(V, f_norm)
                c3 = kb.bump(V)

                def f_k1(v, i=i):
                    v.tensor_scalar(out=scD[:], in0=sc4[:], scalar1=ffs[:, i, 0:1],
                                    scalar2=MAGIC, op0=OP.mult, op1=OP.add)
                    v.drain()
                    v.tensor_scalar(out=xnb[:], in0=scD[:], scalar1=MAGIC,
                                    scalar2=ffs[:, i, 1:2], op0=OP.subtract, op1=OP.mult)
                kb.on(V, f_k1)
                cnorm = kb.bump(V)
                win_free2[slot] = cnorm
                kb.wait(TE, V, cnorm)
                half = i % 2
                pview = psA[:, half * 512:(half + 1) * 512].bitcast(BF16)
                for dc in range(8):
                    def f_tp(t, dc=dc, pview=pview):
                        t.transpose(pview[:, dc * 128:(dc + 1) * 128],
                                    xnb[:, dc * 128:(dc + 1) * 128], identb[:])
                    kb.on(TE, f_tp)
                ctp = kb.bump(TE)
                kb.wait(V, TE, ctp)

                def f_evac(v, i=i, pview=pview):
                    v.tensor_copy(out=k1T[:, :, i * 128:(i + 1) * 128],
                                  in_=pview.rearrange("p (a b) -> p a b", a=8))
                kb.on(V, f_evac)
                pe_ok = kb.bump(V)
                kb.wait(TE, V, pe_ok)
            kb.barrier()
            kb.wait(V, ldcb[0], ldcb[1])
            kb.wait(G, ldcb[0], ldcb[1])
            kb.wait(S, ldcb[0], ldcb[1])

            def f_hpinit(v):
                v.memset(hpad[:, :, :], 0.0)
                v.drain()
            kb.on(V, f_hpinit)
            chp = kb.bump(V)
            kb.wait(S, V, chp)
            st_res = None
            for tc in range(4):
                wf1_ld = {}
                wf1_free = [0, 0, 0, 0]

                def load_wf1(cc):
                    b = cc % 4
                    kb.wait(Q, TE, wf1_free[b])
                    wf1_ld[cc] = kb.dma(Q, wf1[b][:], dr["w1q"][:, cc * 128:(cc + 1) * 128]
                                        .rearrange("(a p) b -> p a b", p=128), key="wf1%d" % b)
                for c0 in range(4):
                    load_wf1(c0)
                amt_done = 0
                h2 = bigA.rearrange("p (a b) -> p a b", a=32)
                hevac_done = {0: 0, 1: 0, 2: 0, 3: 0}
                conv_v_done = {0: 0, 1: 0, 2: 0, 3: 0}
                scd_free = {0: 0, 1: 0}
                pend_silu = None

                def emit_silu_amt(pend):
                    cc_, par_, ccv_ = pend
                    kb.wait(S, V, ccv_)

                    def f_silu(sx, cc=cc_, par=par_):
                        sx.activation(out=h2[:, cc, :], in_=scD[:, par * 512:(par + 1) * 512],
                                      func=AF.Silu, bias=cvb[:, cc:cc + 1], scale=1.0)
                    kb.on(S, f_silu)
                    csl = kb.bump(S)
                    kb.wait(V, S, csl)

                    def f_amt(v, cc=cc_):
                        if cc == 0:
                            v.tensor_copy(out=amt[:], in_=h2[:, 0, :])
                            v.drain()
                            v.tensor_copy(out=amtn[:], in_=h2[:, 0, :])
                        else:
                            v.tensor_tensor(out=amt[:], in0=amt[:], in1=h2[:, cc, :], op=OP.max)
                            v.drain()
                            v.tensor_tensor(out=amtn[:], in0=amtn[:], in1=h2[:, cc, :], op=OP.min)
                        v.drain()
                    kb.on(V, f_amt)
                    return kb.bump(V)

                HALO = [1024, 1536, 1026, 1538]
                cm_hist = {}
                for it in range(34):
                    if it < 32:
                        cc = it
                        b = cc % 4
                        ld = wf1_ld[cc]
                        kb.wait(TE, ld[0], ld[1])
                        pslot = cc % 4
                        pmain = psA[:, pslot * 512:(pslot + 1) * 512]
                        phalo = psB[:, HALO[pslot]:HALO[pslot] + 2]
                        kb.wait(TE, S, hevac_done[pslot])
                        for dc in range(8):
                            def f_mm(t, dc=dc, cc=cc, tc=tc, b=b, pmain=pmain, phalo=phalo):
                                t.matmul(pmain, wf1[b][:, dc, :], k1T[:, dc, tc * 512:(tc + 1) * 512],
                                         start=(dc == 0), stop=(dc == 7))
                                if tc == 0:
                                    t.matmul(phalo[:, 1:2], wf1[b][:, dc, :],
                                             k1T[:, dc, 512:513], start=(dc == 0), stop=(dc == 7))
                                elif tc == 3:
                                    t.matmul(phalo[:, 0:1], wf1[b][:, dc, :],
                                             k1T[:, dc, tc * 512 - 1:tc * 512], start=(dc == 0), stop=(dc == 7))
                                else:
                                    t.matmul(phalo, wf1[b][:, dc, :],
                                             k1T[:, dc, tc * 512 - 1:tc * 512 + 513:513],
                                             start=(dc == 0), stop=(dc == 7))
                            kb.on(TE, f_mm)
                        wf1_free[b] = kb.cnt["t"] + 1
                        cm_hist[cc] = kb.bump(TE)
                        if cc + 4 < 32:
                            load_wf1(cc + 4)
                    if it >= 2:
                        cc = it - 2
                        b = cc % 4
                        par = cc % 2
                        pslot = cc % 4
                        pmain = psA[:, pslot * 512:(pslot + 1) * 512]
                        phalo = psB[:, HALO[pslot]:HALO[pslot] + 2]
                        kb.wait(S, TE, cm_hist[min(cc + 2, 31)])
                        kb.wait(S, V, conv_v_done[b])

                        def f_hevac(sx, cc=cc, tc=tc, pmain=pmain, phalo=phalo, b=b):
                            sx.activation(out=hpad[:, b, 1:513], in_=pmain, func=AF.Copy)
                            if tc == 0:
                                sx.activation(out=hpad[:, b, 513:514], in_=phalo[:, 1:2], func=AF.Copy)
                            elif tc == 3:
                                sx.activation(out=hpad[:, b, 0:1], in_=phalo[:, 0:1], func=AF.Copy)
                            else:
                                sx.activation(out=hpad[:, b, 0:514:513], in_=phalo, func=AF.Copy)
                        kb.on(S, f_hevac)
                        hevac_done[pslot] = kb.bump(S)
                        kb.wait(S, V, scd_free[par])

                        def f_tap1(sx, cc=cc, b=b, par=par):
                            sx.drain()
                            sx.activation(out=scD[:, par * 512:par * 512 + 512], in_=hpad[:, b, 0:512],
                                          func=AF.Copy, scale=cvw[:, cc, 0:1])
                            sx.drain()
                        kb.on(S, f_tap1)
                        che = kb.bump(S)
                        kb.wait(V, S, che)

                        def f_conv(v, cc=cc, b=b, tc=tc, par=par):
                            sd = scD[:, par * 512:par * 512 + 512]
                            if tc == 0:
                                v.memset(sd[:, 0:1], 0.0)
                                v.drain()
                            v.scalar_tensor_tensor(out=sd, in0=hpad[:, b, 1:513],
                                                   scalar=cvw[:, cc, 1:2], in1=sd,
                                                   op0=OP.mult, op1=OP.add)
                            v.drain()
                            if tc == 3:
                                v.memset(hpad[:, b, 513:514], 0.0)
                                v.drain()
                            v.scalar_tensor_tensor(out=sd, in0=hpad[:, b, 2:514],
                                                   scalar=cvw[:, cc, 2:3], in1=sd,
                                                   op0=OP.mult, op1=OP.add)
                        kb.on(V, f_conv)
                        ccv = kb.bump(V)
                        conv_v_done[b] = ccv
                        scd_free[par] = ccv
                        if pend_silu is not None:
                            amt_done = emit_silu_amt(pend_silu)
                        pend_silu = (cc, par, ccv)
                amt_done = emit_silu_amt(pend_silu)

                if tc == 0 and "t_h2" in taps:
                    kb.wait(Q, V, amt_done)
                    sth = kb.dma(Q, taps["t_h2"].rearrange("(a p) b -> p a b", p=128),
                                 bigA.rearrange("p (a b) -> p a b", a=32), key="tap")
                    kb.wait(Q, sth[0], sth[1])
                def f_amcomb(v):
                    v.scalar_tensor_tensor(out=amt[:], in0=amtn[:], scalar=-1.0, in1=amt[:],
                                           op0=OP.mult, op1=OP.max)
                kb.on(V, f_amcomb)
                amt_done = kb.bump(V)
                kb.wait(TE, V, amt_done)
                for bblk in range(4):
                    def f_tp(t, bblk=bblk):
                        t.transpose(psB[:, 1024 + bblk * 128:1024 + (bblk + 1) * 128],
                                    amt[:, bblk * 128:(bblk + 1) * 128], identf[:])
                    kb.on(TE, f_tp)
                ctp = kb.bump(TE)
                kb.wait(V, TE, ctp)

                def f_am2(v, tc=tc):
                    for bblk in range(4):
                        i = tc * 4 + bblk
                        v.tensor_reduce(out=ffs[:, i, 2:3],
                                        in_=psB[:, 1024 + bblk * 128:1024 + (bblk + 1) * 128],
                                        axis=mybir.AxisListType.X, op=OP.max)
                    v.drain()
                    for bblk in range(4):
                        i = tc * 4 + bblk
                        v.tensor_scalar(out=ffs[:, i, 2:3], in0=ffs[:, i, 2:3], scalar1=1e-5,
                                        scalar2=None, op0=OP.max)
                    v.drain()
                    for bblk in range(4):
                        i = tc * 4 + bblk
                        v.tensor_tensor(out=ffs[:, i, 3:4], in0=ffs[:, i, 2:3], in1=wst[:, 37:38], op=OP.mult)
                        v.reciprocal(out=ffs[:, i, 2:3], in_=ffs[:, i, 2:3])
                    v.drain()
                    for bblk in range(4):
                        i = tc * 4 + bblk
                        v.tensor_scalar(out=ffs[:, i, 3:4], in0=ffs[:, i, 3:4], scalar1=1.0 / 127.0,
                                        scalar2=None, op0=OP.mult)
                        v.tensor_scalar(out=ffs[:, i, 2:3], in0=ffs[:, i, 2:3], scalar1=127.0,
                                        scalar2=None, op0=OP.mult)
                    v.drain()
                kb.on(V, f_am2)
                cam = kb.bump(V)
                kb.wait(TE, V, cam)
                for bblk in range(4):
                    def f_tps(t, bblk=bblk, tc=tc):
                        t.transpose(psB[:, 1024 + bblk * 128:1024 + bblk * 128 + 128][0:1, :]
                                    if False else psB[0:1, 1024 + bblk * 128:1024 + (bblk + 1) * 128],
                                    ffs[:, tc * 4 + bblk, 2:3], identf[:])
                    kb.on(TE, f_tps)
                ctps = kb.bump(TE)
                kb.wait(V, TE, ctps)

                def f_srow(v):
                    v.tensor_copy(out=srow[0:1, :], in_=psB[0:1, 1024:1536])
                kb.on(V, f_srow)
                csr = kb.bump(V)
                kb.wait(TE, V, csr)

                def f_bc(t):
                    t.matmul(psB[:, 1536:2048], onesf[0:1, :], srow[0:1, :], start=True, stop=True)
                kb.on(TE, f_bc)
                cbc = kb.bump(TE)
                kb.wait(V, TE, cbc)

                def f_s2bc(v):
                    v.tensor_copy(out=s2bc[:], in_=psB[:, 1536:2048])
                kb.on(V, f_s2bc)
                cs2 = kb.bump(V)

                k2_done = 0
                for cc in range(32):
                    def f_k2(v, cc=cc):
                        if cc == 0:
                            v.drain()
                        v.tensor_tensor(out=sc4[:, 0:512], in0=h2[:, cc, :], in1=s2bc[:], op=OP.mult)
                        v.drain()
                        v.tensor_scalar(out=sc4[:, 0:512], in0=sc4[:, 0:512], scalar1=MAGIC,
                                        scalar2=MAGIC, op0=OP.add, op1=OP.subtract)
                        v.drain()
                        v.tensor_copy(out=k2v[:, cc, :], in_=sc4[:, 0:512])
                        v.drain()
                    kb.on(V, f_k2)
                k2_done = kb.bump(V)

                wf2_ld = {}
                wf2_free = [0, 0]

                def load_wf2(do):
                    b = do % 2
                    kb.wait(Q, TE, wf2_free[b])
                    wf2_ld[do] = kb.dma(Q, wf2[b][:], dr["w2q"][:, do * 128:(do + 1) * 128]
                                        .rearrange("(a p) b -> p a b", p=128), key="wf2%d" % b)
                load_wf2(0)
                load_wf2(1)
                kb.wait(TE, V, k2_done)
                o2_evac = {0: 0, 1: 0}
                for do in range(8):
                    b = do % 2
                    ld = wf2_ld[do]
                    kb.wait(TE, ld[0], ld[1])
                    pslot = do % 2
                    ps = psB[:, pslot * 512:(pslot + 1) * 512]
                    kb.wait(TE, V, o2_evac[pslot])
                    for cc in range(32):
                        def f_mm(t, ps=ps, cc=cc, do=do, b=b):
                            t.matmul(ps, wf2[b][:, cc, :], k2v[:, cc, :],
                                     start=(cc == 0), stop=(cc == 31))
                        kb.on(TE, f_mm)
                    wf2_free[b] = kb.cnt["t"] + 1
                    cm = kb.bump(TE)
                    if do + 2 < 8:
                        load_wf2(do + 2)
                    kb.wait(V, TE, cm)

                    def f_ev(v, ps=ps, do=do):
                        v.tensor_copy(out=o2T[:, do, :], in_=ps)
                    kb.on(V, f_ev)
                    o2_evac[pslot] = kb.bump(V)
                co2 = o2_evac[1]
                kb.wait(TE, V, co2)
                for bblk in range(4):
                    i = tc * 4 + bblk
                    slot = i % 4
                    kb.wait(Q, V, win_free2[slot])
                    if kb.win_store[slot] is not None:
                        kb.wait(Q, kb.win_store[slot][0], kb.win_store[slot][1])
                    ld = kb.dma(Q, xw[:, slot, :], src_dram[i * 128:(i + 1) * 128, :], key="win%d" % slot)
                    pview = psA[:, 0:512].bitcast(BF16)
                    for do in range(8):
                        def f_tp(t, do=do, bblk=bblk, pview=pview):
                            t.transpose(pview[:, do * 128:(do + 1) * 128],
                                        o2T[:, do, bblk * 128:(bblk + 1) * 128], identb[:])
                        kb.on(TE, f_tp)
                    ctp = kb.bump(TE)
                    kb.wait(V, TE, ctp)
                    kb.wait(V, ld[0], ld[1])

                    def f_res(v, i=i, slot=slot, pview=pview):
                        v.tensor_scalar(out=scD[:], in0=pview.rearrange("p (a b) -> p a b", a=8),
                                        scalar1=ffs[:, i, 3:4], scalar2=None, op0=OP.mult)
                        v.drain()
                        v.tensor_tensor(out=xw[:, slot, :], in0=xw[:, slot, :], in1=scD[:], op=OP.add)
                    kb.on(V, f_res)
                    cres = kb.bump(V)
                    win_free2[slot] = cres
                    kb.wait(TE, V, cres)
                    kb.wait(Q, V, cres)
                    st_res = kb.dma(Q, dst_dram[i * 128:(i + 1) * 128, :], xw[:, slot, :], key="win%d" % slot)
                    kb.win_store[slot] = st_res
            kb.barrier()
            kb.wait(Q, st_res[0], st_res[1])


        mkband = shr12[:, 1024:1280]
        mkb0 = shr12[:, 1280:1408]
        mkidiag = shr12[:, 1408:1536]
        mkiprev = shr12[:, 1536:1664]
        rawdrow = shr12[0:1, 2048:2112]
        jjrow = shr12[0:1, 0:2048]

        xT_b = qT
        xf = xhT[:, :, :].rearrange("p a b -> p (a b)")
        ckvT = xf[:, 0:4096].rearrange("p (a b) -> p a b", a=2)
        jslot = xf[:, 0:4096].rearrange("p (a b c) -> p a b c", a=NT, b=8)
        cqT = xf[:, 4096:12288].rearrange("p (a b) -> p a b", a=4)
        qitok = xf[:, 4096:8192].rearrange("p (a b) -> p a b", a=NT)
        jeff = xf[:, 8192:12288].rearrange("p (a b c) -> p a b c", a=NT, b=8)
        hefftok = xf[:, 12288:12800].rearrange("p (a b) -> p a b", a=NT)
        mutok = xf[:, 12800:13824].bitcast(F32).rearrange("p (a b) -> p a b", a=NT)
        kiT = xf[:, 13824:15872]

        bff = bigA[:, :]
        kTb = bff[:, 0:4096].bitcast(BF16).rearrange("p (a b) -> p a b", a=4)
        qallb = bff[:, 4096:8192].bitcast(BF16).rearrange("p (a b) -> p a b", a=NT)
        kstg = bff[:, 8192:12288].bitcast(BF16).rearrange("p (a b) -> p a b", a=NT)
        vstg = bff[:, 12288:16384].bitcast(BF16).rearrange("p (a b) -> p a b", a=NT)
        iacc = bff[:, 8192:10240]
        gout2 = [bff[:, 10240:12416].bitcast(BF16).rearrange("p (a b) -> p a b", a=4),
                 bff[:, 12416:14592].bitcast(BF16).rearrange("p (a b) -> p a b", a=4)]
        vwin = bff[:, 14592:15104].bitcast(BF16).rearrange("p (a b) -> p a b", a=2)
        prodv2 = None
        sbandv = psA[:, 0:2048].rearrange("p (a b) -> p a b", a=8)
        slacc = scA[:, 0:512]
        smv = bff[:, 16128:16384]

        wfl = wAB[:, :, :].rearrange("p a b -> p (a b)")
        jjbc = wfl[:, 4096:6144].bitcast(F32)
        qc_scr = wfl[:, 6144:6272]
        jf_scr = wfl[:, 6272:6400]
        hw_scr = wfl[:, 6400:6656].rearrange("p (a b) -> p a b", a=2)
        o_tok = wfl[:, 6656:7168]
        outTv = wfl[:, 7168:7680].rearrange("p (a b) -> p a b", a=4)
        h2t0 = wfl[:, 7680:7808]
        jf2_scr = wfl[:, 7808:7936]
        tview = tausq[:, :, :].rearrange("p a b -> p (a b)")
        idxq = cands[:, :, :].rearrange("p a b -> p (a b)").bitcast(U32)
        prodb = shr12[:, 2048:3072].bitcast(BF16)
        scE = shr12[:, 2048:3072]
        PI = math.pi
        SCB = float(96.0 ** -0.5)

        def emit_attn_b(src_dram, dst_dram):
            l1 = kb.dma(Q, rawdrow[0:1, 0:16], dr["rawd"], key="ab0")
            l1 = kb.dma(Q, jjrow[0:1, 0:1024], dr["tb_jj"], key="ab0")
            lidw = kb.dma(Q, jjrow[0:1, 1024:1028], dr["idxw"], key="ab0")
            kb.wait(S, lidw[0], lidw[1])
            kb.wait(V, lidw[0], lidw[1])
            kb.wait(G, lidw[0], lidw[1])

            def f_iota(g):
                g.iota(tview[:, 0:32], [[1, 32]], channel_multiplier=0,
                       allow_small_or_imprecise_dtypes=True)
                g.iota(tview[:, 64:65], [[0, 1]], base=-63, channel_multiplier=1,
                       allow_small_or_imprecise_dtypes=True)
                g.iota(tview[:, 65:66], [[0, 1]], base=0, channel_multiplier=1,
                       allow_small_or_imprecise_dtypes=True)
                g.iota(idxq[:, 32:64].bitcast(U32) if False else idxq[:, 32:64],
                       [[1, 32]], base=64, channel_multiplier=0)
            kb.on(G, f_iota)
            cgi = kb.bump(G)
            kb.wait(V, G, cgi)

            def f_delta(sx):
                sx.activation(out=rawdrow[0:1, 16:32], in_=rawdrow[0:1, 0:16], func=AF.Sigmoid)
                sx.drain()
            kb.on(S, f_delta)
            c = kb.bump(S)
            kb.wait(V, S, c)

            def f_d2(v):
                v.tensor_scalar(out=rawdrow[0:1, 16:32], in0=rawdrow[0:1, 16:32],
                                scalar1=-2.0 * PI, scalar2=None, op0=OP.mult)
                v.drain()
                v.tensor_scalar(out=rawdrow[0:1, 32:48], in0=rawdrow[0:1, 16:32],
                                scalar1=-PI, scalar2=None, op0=OP.is_lt)
                v.drain()
                v.scalar_tensor_tensor(out=rawdrow[0:1, 16:32], in0=rawdrow[0:1, 32:48],
                                       scalar=2.0 * PI, in1=rawdrow[0:1, 16:32],
                                       op0=OP.mult, op1=OP.add)
                v.drain()
                v.tensor_scalar(out=rawdrow[0:1, 32:48], in0=rawdrow[0:1, 16:32],
                                scalar1=PI / 2, scalar2=None, op0=OP.add)
                v.drain()
                v.tensor_scalar(out=rawdrow[0:1, 48:64], in0=rawdrow[0:1, 32:48],
                                scalar1=PI, scalar2=None, op0=OP.is_gt)
                v.drain()
                v.scalar_tensor_tensor(out=rawdrow[0:1, 32:48], in0=rawdrow[0:1, 48:64],
                                       scalar=-2.0 * PI, in1=rawdrow[0:1, 32:48],
                                       op0=OP.mult, op1=OP.add)
            kb.on(V, f_d2)
            c = kb.bump(V)
            kb.wait(S, V, c)

            def f_d3(sx):
                sx.activation(out=rawdrow[0:1, 16:32], in_=rawdrow[0:1, 16:32], func=AF.Sin)
                sx.drain()
                sx.activation(out=rawdrow[0:1, 32:48], in_=rawdrow[0:1, 32:48], func=AF.Sin)
                sx.drain()
            kb.on(S, f_d3)
            c = kb.bump(S)
            kb.wait(V, S, c)

            def f_jj(v):
                cj = jjrow[0:1, 0:512].rearrange("p (a b) -> p a b", a=32)
                sj = jjrow[0:1, 512:1024].rearrange("p (a b) -> p a b", a=32)
                cd = rawdrow[0:1, 32:48].unsqueeze(1).broadcast_to((1, 32, 16))
                sd = rawdrow[0:1, 16:32].unsqueeze(1).broadcast_to((1, 32, 16))
                co = jjrow[0:1, 1028:1540].rearrange("p (a b) -> p a b", a=32)
                so = jjrow[0:1, 1540:2052] if False else jjrow[0:1, 1536:2048].rearrange("p (a b) -> p a b", a=32)
                tmp = bff[0:1, 15104:15616].rearrange("p (a b) -> p a b", a=32)
                v.tensor_tensor(out=co, in0=cj, in1=cd, op=OP.mult)
                v.drain()
                v.tensor_tensor(out=tmp, in0=sj, in1=sd, op=OP.mult)
                v.drain()
                v.tensor_tensor(out=co, in0=co, in1=tmp, op=OP.subtract)
                v.drain()
                v.tensor_tensor(out=so, in0=sj, in1=cd, op=OP.mult)
                v.drain()
                v.tensor_tensor(out=tmp, in0=cj, in1=sd, op=OP.mult)
                v.drain()
                v.tensor_tensor(out=so, in0=so, in1=tmp, op=OP.add)
            kb.on(V, f_jj)
            c = kb.bump(V)
            kb.wait(TE, V, c)

            def f_bc(t):
                t.matmul(psB[:, 0:512], onesf[0:1, :], jjrow[0:1, 1028:1540], start=True, stop=True)
                t.matmul(psB[:, 512:1024], onesf[0:1, :], jjrow[0:1, 1536:2048], start=True, stop=True)
                t.matmul(psB[:, 1024:1028], onesf[0:1, :], jjrow[0:1, 1024:1028], start=True, stop=True)
                t.matmul(psB[:, 1028:1060], onesf[0:1, :], rawdrow[0:1, 16:48], start=True, stop=True)
            kb.on(TE, f_bc)
            c = kb.bump(TE)
            kb.wait(V, TE, c)

            def f_bcev(v):
                v.tensor_copy(out=jjbc[:, 0:512], in_=psB[:, 0:512])
                v.tensor_copy(out=jjbc[:, 512:1024], in_=psB[:, 512:1024])
                v.tensor_copy(out=msc[:, 0:4], in_=psB[:, 1024:1028])
                v.tensor_copy(out=msc2[:, 0:32], in_=psB[:, 1028:1060])
                v.drain()
                v.tensor_scalar(out=tview[:, 66:67], in0=tview[:, 65:66], scalar1=64.0,
                                scalar2=None, op0=OP.is_lt)
            kb.on(V, f_bcev)
            cbc = kb.bump(V)
            kb.wait(TE, V, cbc)

            emit_ln_transpose(src_dram, xT_b)
            kb.barrier()

            wfree = [0]

            def wload(nm, off, key="wA"):
                R, C = dr[nm].shape
                a = R // 128
                view = wfl[:, off:off + a * C].rearrange("p (a b) -> p a b", a=a)
                kb.wait(G, TE, wfree[0])
                ld = kb.dma(G, view, dr[nm].rearrange("(a p) b -> p a b", p=128), key=key)
                return view, ld

            pslots = [psB[:, i * 512:(i + 1) * 512] for i in range(3)]
            pslot_free = {0: ("v", 0), 1: ("v", 0), 2: ("v", 0)}
            slot_i = [0]

            def mm_out(lhsT_fn, rhs_fn, nacc, evac_fn, eng=S, width=512, po=128):
                sl = slot_i[0] % 3
                slot_i[0] += 1
                ps = pslots[sl][0:po, 0:width]
                pe_eng, pe_cnt = pslot_free[sl]
                kb.wait(TE, pe_eng, pe_cnt)
                for a in range(nacc):
                    def f_mm(t, a=a, ps=ps):
                        t.matmul(ps, lhsT_fn(a), rhs_fn(a), start=(a == 0), stop=(a == nacc - 1))
                    kb.on(TE, f_mm)
                cm = kb.bump(TE)
                kb.wait(eng, TE, cm)

                def f_ev(e, ps=ps):
                    evac_fn(e, ps)
                kb.on(eng, f_ev)
                ce = kb.bump(eng)
                pslot_free[sl] = (eng, ce)
                return ce

            wv, ld = wload("wdkv", 0)
            kb.wait(TE, ld[0], ld[1])
            for ec in range(2):
                for tc in range(4):
                    mm_out(lambda a, ec=ec, wv=wv: wv[:, a, ec * 128:(ec + 1) * 128],
                           lambda a, tc=tc: xT_b[:, a, tc * 512:(tc + 1) * 512], 8,
                           lambda e, ps, ec=ec, tc=tc: e.activation(
                               out=ckvT[:, ec, tc * 512:(tc + 1) * 512], in_=ps, func=AF.Copy))
            wfree[0] = kb.cnt["t"]
            wuk_v, ld1 = wload("wuk", 2048, key="wB")
            wuv_v, ld2 = wload("wuv", 3072, key="wC")
            kb.wait(TE, ld1[0], ld1[1])
            kb.wait(TE, ld2[0], ld2[1])
            for ec in range(4):
                for tc in range(4):
                    mm_out(lambda a, ec=ec, wuk_v=wuk_v: wuk_v[:, a, ec * 128:(ec + 1) * 128],
                           lambda a, tc=tc: ckvT[:, a, tc * 512:(tc + 1) * 512], 2,
                           lambda e, ps, ec=ec, tc=tc: e.activation(
                               out=kTb[:, ec, tc * 512:(tc + 1) * 512], in_=ps, func=AF.Copy))
            for i in range(NT):
                mm_out(lambda a, i=i: ckvT[:, a, i * 128:(i + 1) * 128],
                       lambda a, wuk_v=wuk_v: wuk_v[:, a, :], 2,
                       lambda e, ps, i=i: e.activation(out=kstg[:, i, :], in_=ps, func=AF.Copy))
                mm_out(lambda a, i=i: ckvT[:, a, i * 128:(i + 1) * 128],
                       lambda a, wuv_v=wuv_v: wuv_v[:, a, :], 2,
                       lambda e, ps, i=i: e.activation(out=vstg[:, i, :], in_=ps, func=AF.Copy))
            wfree[0] = kb.cnt["t"]
            wv, ld = wload("wdq", 0)
            kb.wait(TE, ld[0], ld[1])
            for ec in range(4):
                for tc in range(4):
                    mm_out(lambda a, ec=ec, wv=wv: wv[:, a, ec * 128:(ec + 1) * 128],
                           lambda a, tc=tc: xT_b[:, a, tc * 512:(tc + 1) * 512], 8,
                           lambda e, ps, ec=ec, tc=tc: e.activation(
                               out=cqT[:, ec, tc * 512:(tc + 1) * 512], in_=ps, func=AF.Copy))
            wfree[0] = kb.cnt["t"]
            wuq_v, ld = wload("wuq", 0)
            kb.wait(TE, ld[0], ld[1])
            for i in range(NT):
                mm_out(lambda a, i=i: cqT[:, a, i * 128:(i + 1) * 128],
                       lambda a, wuq_v=wuq_v: wuq_v[:, a, :], 4,
                       lambda e, ps, i=i: e.activation(out=qallb[:, i, :], in_=ps, func=AF.Copy, scale=SCB))
            wfree[0] = kb.cnt["t"]
            wkr_v, ldkr = wload("wkr", 3072, key="wC")
            kb.wait(TE, ldkr[0], ldkr[1])
            for i in range(NT):
                ckr = mm_out(lambda a, i=i: xT_b[:, a, i * 128:(i + 1) * 128],
                             lambda a: wkr_v[:, a, :], 8,
                             lambda e, ps: e.tensor_copy(out=scD[:, 256:288], in_=ps), eng=V, width=32)
                kb.wait(S, V, ckr)

                def f_spk(sx):
                    sx.activation(out=sc4[:, 256:288], in_=scD[:, 256:288], func=AF.Abs)
                    sx.drain()
                    sx.activation(out=sc4[:, 256:288], in_=sc4[:, 256:288], func=AF.Exp, scale=-1.0)
                    sx.drain()
                    sx.activation(out=sc4[:, 256:288], in_=sc4[:, 256:288], func=AF.Ln, bias=1.0)
                    sx.drain()
                kb.on(S, f_spk)
                cspk = kb.bump(S)
                kb.wait(V, S, cspk)

                def f_muk(v, i=i):
                    v.tensor_scalar(out=scD[:, 256:288], in0=scD[:, 256:288], scalar1=0.0,
                                    scalar2=None, op0=OP.max)
                    v.drain()
                    v.tensor_tensor(out=mutok[:, i, :], in0=sc4[:, 256:288], in1=scD[:, 256:288], op=OP.add)
                    v.drain()
                kb.on(V, f_muk)
                cmk = kb.bump(V)
            kb.barrier()
            stg = None
            for i in range(NT):
                stg = kb.dma(Q, dr["gtab"][i * 128:(i + 1) * 128, 0:512], kstg[:, i, :], key="ab0")
                stg = kb.dma(Q, dr["gtab"][i * 128:(i + 1) * 128, 512:1024], vstg[:, i, :], key="ab0")
                stg = kb.dma(Q, dr["gtab"][i * 128:(i + 1) * 128, 1024:1088],
                             mutok[:, i, :].bitcast(BF16), key="ab0")
            kb.wait(Q, stg[0], stg[1])
            kb.barrier()
            qrstg = bff[:, 8192:10240].bitcast(BF16).rearrange("p (a b) -> p a b", a=NT)
            wqr_v, ldqr = wload("wqr", 2048, key="wB")
            kb.wait(TE, ldqr[0], ldqr[1])
            for i in range(NT):
                mm_out(lambda a, i=i: cqT[:, a, i * 128:(i + 1) * 128],
                       lambda a: wqr_v[:, a, :], 4,
                       lambda e, ps, i=i: e.activation(out=qrstg[:, i, :], in_=ps, func=AF.Copy),
                       width=256)
            kb.barrier()
            tab_free = 0
            for i in range(NT):
                kb.wait(Q, V, tab_free)
                lt = kb.dma(Q, scA[:, 0:32], dr["tb_tq"][i * 128:(i + 1) * 128, :], key="ab0")
                lt = kb.dma(Q, scA[:, 32:64], dr["tb_t2"][i * 128:(i + 1) * 128, :], key="ab0")
                lt = kb.dma(Q, scA[:, 64:96], dr["tb_pp"][i * 128:(i + 1) * 128, :], key="ab0")
                kb.wait(V, lt[0], lt[1])

                def f_cp(v, i=i):
                    v.tensor_copy(out=scD[:, 0:256], in_=qrstg[:, i, :])
                    v.drain()
                kb.on(V, f_cp)
                ccp = kb.bump(V)
                kb.wait(S, V, ccp)

                def f_sp(sx):
                    sx.activation(out=sc4[:, 0:256], in_=scD[:, 0:256], func=AF.Abs)
                    sx.drain()
                    sx.activation(out=sc4[:, 0:256], in_=sc4[:, 0:256], func=AF.Exp, scale=-1.0)
                    sx.drain()
                    sx.activation(out=sc4[:, 0:256], in_=sc4[:, 0:256], func=AF.Ln, bias=1.0)
                    sx.drain()
                kb.on(S, f_sp)
                csp = kb.bump(S)
                kb.wait(V, S, csp)

                def f_build(v, i=i):
                    v.tensor_scalar(out=scD[:, 0:256], in0=scD[:, 0:256], scalar1=0.0,
                                    scalar2=None, op0=OP.max)
                    v.drain()
                    v.tensor_tensor(out=sc4[:, 0:256], in0=sc4[:, 0:256], in1=scD[:, 0:256], op=OP.add)
                    v.drain()
                    v.tensor_scalar(out=sc4[:, 0:256], in0=sc4[:, 0:256], scalar1=SCB,
                                    scalar2=None, op0=OP.mult)
                    v.drain()
                    tqc, tqs = scA[:, 0:16], scA[:, 16:32]
                    cdb, sdb = msc2[:, 16:32], msc2[:, 0:16]
                    v.tensor_tensor(out=scA[:, 96:112], in0=tqc, in1=cdb, op=OP.mult)
                    v.tensor_tensor(out=scA[:, 112:128], in0=tqs, in1=cdb, op=OP.mult)
                    v.drain()
                    v.tensor_tensor(out=msc[:, 4:20], in0=tqs, in1=sdb, op=OP.mult)
                    v.tensor_tensor(out=scA[:, 128:144], in0=tqc, in1=sdb, op=OP.mult)
                    v.drain()
                    v.tensor_tensor(out=scA[:, 96:112], in0=scA[:, 96:112], in1=msc[:, 4:20], op=OP.subtract)
                    v.tensor_tensor(out=scA[:, 112:128], in0=scA[:, 112:128], in1=scA[:, 128:144], op=OP.add)
                    v.drain()

                    def jbuild(a1, a2, cts, sts, dst16a, dst16b):
                        v.tensor_tensor(out=scE[:, 0:128].rearrange("p (a b) -> p a b", a=8),
                                        in0=a1, in1=cts, op=OP.mult)
                        v.tensor_tensor(out=scE[:, 128:256].rearrange("p (a b) -> p a b", a=8),
                                        in0=a2, in1=sts, op=OP.mult)
                        v.drain()
                        v.tensor_tensor(out=dst16a, in0=scE[:, 0:128].rearrange("p (a b) -> p a b", a=8),
                                        in1=scE[:, 128:256].rearrange("p (a b) -> p a b", a=8), op=OP.subtract)
                        v.drain()
                        v.tensor_tensor(out=scE[:, 0:128].rearrange("p (a b) -> p a b", a=8),
                                        in0=a1, in1=sts, op=OP.mult)
                        v.tensor_tensor(out=scE[:, 128:256].rearrange("p (a b) -> p a b", a=8),
                                        in0=a2, in1=cts, op=OP.mult)
                        v.drain()
                        v.tensor_tensor(out=dst16b, in0=scE[:, 0:128].rearrange("p (a b) -> p a b", a=8),
                                        in1=scE[:, 128:256].rearrange("p (a b) -> p a b", a=8), op=OP.add)
                        v.drain()
                    a1 = sc4[:, 0:256].rearrange("p (a b) -> p a b", a=8)[:, :, 0:16]
                    a2 = sc4[:, 0:256].rearrange("p (a b) -> p a b", a=8)[:, :, 16:32]
                    ctb = scA[:, 96:112].unsqueeze(1).broadcast_to((128, 8, 16))
                    stb = scA[:, 112:128].unsqueeze(1).broadcast_to((128, 8, 16))
                    jbuild(a1, a2, ctb, stb, jslot[:, i, :, 0:16], jslot[:, i, :, 16:32])
                    c2b = scA[:, 32:48].unsqueeze(1).broadcast_to((128, 8, 16))
                    s2b2 = scA[:, 48:64].unsqueeze(1).broadcast_to((128, 8, 16))
                    jbuild(a1, a2, c2b, s2b2, jeff[:, i, :, 0:16], jeff[:, i, :, 16:32])
                    mu1f = scD[:, 256:272]
                    mu2f = scD[:, 272:288]
                    v.tensor_copy(out=scD[:, 256:288], in_=mutok[:, i, :])
                    v.drain()
                    ppc, pps = scA[:, 64:80], scA[:, 80:96]
                    v.tensor_tensor(out=scE[:, 0:16], in0=mu1f, in1=ppc, op=OP.mult)
                    v.tensor_tensor(out=scE[:, 16:32], in0=mu2f, in1=pps, op=OP.mult)
                    v.tensor_tensor(out=scE[:, 32:48], in0=mu1f, in1=pps, op=OP.mult)
                    v.tensor_tensor(out=scE[:, 48:64], in0=mu2f, in1=ppc, op=OP.mult)
                    v.drain()
                    v.tensor_tensor(out=hefftok[:, i, 0:16], in0=scE[:, 0:16], in1=scE[:, 16:32], op=OP.subtract)
                    v.tensor_tensor(out=hefftok[:, i, 16:32], in0=scE[:, 32:48], in1=scE[:, 48:64], op=OP.add)
                    v.drain()
                    if i == 0:
                        ct2, st2 = scA[:, 96:112], scA[:, 112:128]
                        v.tensor_tensor(out=scE[:, 0:16], in0=mu1f, in1=ct2, op=OP.mult)
                        v.tensor_tensor(out=scE[:, 16:32], in0=mu2f, in1=st2, op=OP.mult)
                        v.tensor_tensor(out=scE[:, 32:48], in0=mu1f, in1=st2, op=OP.mult)
                        v.tensor_tensor(out=scE[:, 48:64], in0=mu2f, in1=ct2, op=OP.mult)
                        v.drain()
                        v.tensor_tensor(out=qc_scr[:, 0:16], in0=scE[:, 0:16], in1=scE[:, 16:32], op=OP.subtract)
                        v.tensor_tensor(out=qc_scr[:, 16:32], in0=scE[:, 32:48], in1=scE[:, 48:64], op=OP.add)
                        v.drain()
                kb.on(V, f_build)
                tab_free = kb.bump(V)
                if i == 0:
                    kb.wait(TE, V, tab_free)

                    def f_h2t(t):
                        t.transpose(psA[:, 1536:1600].bitcast(BF16)[0:32, 0:128],
                                    qc_scr[:, 0:32], identb[:])
                    kb.on(TE, f_h2t)
                    ch2 = kb.bump(TE)
                    kb.wait(V, TE, ch2)

                    def f_h2ev(v):
                        v.tensor_copy(out=h2t0[0:32, :], in_=psA[:, 1536:1600].bitcast(BF16)[0:32, 0:128])
                        v.drain()
                    kb.on(V, f_h2ev)
                    kb.bump(V)
            wfree[0] = kb.cnt["t"]
            wiq_v, ld1 = wload("widxq", 0)
            wik_v, ld2 = wload("widxk", 2048, key="wB")
            kb.wait(TE, ld1[0], ld1[1])
            kb.wait(TE, ld2[0], ld2[1])
            for i in range(NT):
                mm_out(lambda a, i=i: xT_b[:, a, i * 128:(i + 1) * 128],
                       lambda a, wiq_v=wiq_v: wiq_v[:, a, :], 8,
                       lambda e, ps, i=i: e.activation(out=qitok[:, i, :], in_=ps, func=AF.Copy), width=256)
            for tc in range(4):
                mm_out(lambda a, wik_v=wik_v: wik_v[:, a, :],
                       lambda a, tc=tc: xT_b[:, a, tc * 512:(tc + 1) * 512], 8,
                       lambda e, ps, tc=tc: e.activation(
                           out=kiT[0:64, tc * 512:(tc + 1) * 512], in_=ps, func=AF.Copy), po=64)
            kb.barrier()
            for dst, srcnm in ((mkband, "mk_band"), (mkb0, "mk_b0"),
                               (mkidiag, "mk_idiag"), (mkiprev, "mk_iprev")):
                lmk = kb.dma(Q, dst, dr[srcnm], key="ab0")
            kb.wait(V, lmk[0], lmk[1])
            kb.barrier()

            pb = psA[:, 1536:2048].bitcast(BF16)
            wfree[0] = kb.cnt["t"]
            wout_v, ldwo = wload("wout", 0)
            prev_store = None
            for i in range(NT):
                kw = (i + 1) * 128
                vprev = kb.cnt["v"]
                islot_free = {0: 0, 1: 0}
                for h4 in range(4):
                    kb.wait(TE, V, vprev)

                    def f_tp(t, i=i, h4=h4):
                        t.transpose(pb[0:64, 0:128], qitok[:, i, h4 * 64:(h4 + 1) * 64], identb[:])
                    kb.on(TE, f_tp)
                    ctp = kb.bump(TE)
                    kb.wait(S, TE, ctp)

                    def f_ev(sx):
                        sx.activation(out=qc_scr[0:64, :], in_=pb[0:64, 0:128], func=AF.Copy)
                        sx.drain()
                    kb.on(S, f_ev)
                    cqe = kb.bump(S)
                    kb.wait(TE, S, cqe)
                    for kc in range((kw + 511) // 512):
                        w = min(512, kw - kc * 512)
                        ps = psA[:, (kc % 2) * 512:(kc % 2) * 512 + w]
                        kb.wait(TE, V, islot_free[kc % 2])

                        def f_mm(t, ps=ps, kc=kc, w=w):
                            t.matmul(ps, qc_scr[0:64, :], kiT[0:64, kc * 512:kc * 512 + w],
                                     start=True, stop=True)
                        kb.on(TE, f_mm)
                        cm = kb.bump(TE)
                        kb.wait(V, TE, cm)

                        def f_acc(v, ps=ps, kc=kc, w=w, h4=h4):
                            if h4 == 0:
                                v.tensor_scalar(out=iacc[:, kc * 512:kc * 512 + w], in0=ps,
                                                scalar1=0.0, scalar2=msc[:, h4:h4 + 1],
                                                op0=OP.max, op1=OP.mult)
                            else:
                                v.tensor_scalar(out=scD[:, 0:w], in0=ps, scalar1=0.0,
                                                scalar2=msc[:, h4:h4 + 1], op0=OP.max, op1=OP.mult)
                                v.drain()
                                v.tensor_tensor(out=iacc[:, kc * 512:kc * 512 + w],
                                                in0=iacc[:, kc * 512:kc * 512 + w],
                                                in1=scD[:, 0:w], op=OP.add)
                            v.drain()
                        kb.on(V, f_acc)
                        vprev = kb.bump(V)
                        islot_free[kc % 2] = vprev

                def f_mask(v, i=i, kw=kw):
                    if i >= 1:
                        v.tensor_tensor(out=iacc[:, (i - 1) * 128:i * 128],
                                        in0=iacc[:, (i - 1) * 128:i * 128], in1=mkiprev, op=OP.add)
                        v.drain()
                    v.tensor_tensor(out=iacc[:, i * 128:(i + 1) * 128],
                                    in0=iacc[:, i * 128:(i + 1) * 128], in1=mkidiag, op=OP.add)
                    v.drain()
                    for r in range(4):
                        v.max(out=tview[:, 128:136], in_=iacc[:, 0:kw])
                        v.drain()
                        v.max_index(out=idxq[:, r * 8:(r + 1) * 8], in_max=tview[:, 128:136],
                                    in_values=iacc[:, 0:kw])
                        v.drain()
                        if r < 3:
                            v.match_replace(out=iacc[:, 0:kw], in_to_replace=tview[:, 128:136],
                                            in_values=iacc[:, 0:kw], imm_value=NEG)
                            v.drain()
                    if i == 0:
                        v.tensor_scalar(out=idxq[:, 64:96], in0=tview[:, 0:32],
                                        scalar1=tview[:, 64:65], scalar2=None, op0=OP.is_ge)
                        v.drain()
                        v.copy_predicated(out=idxq[:, 0:32], mask=idxq[:, 64:96],
                                          data=idxq[:, 32:64])
                        v.drain()
                kb.on(V, f_mask)
                cex = kb.bump(V)

                w0t = i - 1 if i >= 1 else 0
                W = 256 if i >= 1 else 128
                pb = psB[:, 512:1024].bitcast(BF16)
                lv = kb.dma(Q, vwin[:, 0, :], dr["gtab"][w0t * 128:(w0t + 1) * 128, 512:1024], key="gat")
                if i >= 1:
                    lv = kb.dma(Q, vwin[:, 1, :], dr["gtab"][i * 128:(i + 1) * 128, 512:1024], key="gat")
                kb.wait(TE, V, cex)
                for w in range(2 if i >= 1 else 1):
                    def f_tph(t, i=i, w=w, w0t=w0t):
                        t.transpose(pb[0:32, 0:128], hefftok[:, w0t + w, :], identb[:])
                    kb.on(TE, f_tph)
                    cth = kb.bump(TE)
                    kb.wait(S, TE, cth)

                    def f_evh(sx, w=w):
                        sx.activation(out=hw_scr[0:32, w, :], in_=pb[0:32, 0:128], func=AF.Copy)
                        sx.drain()
                    kb.on(S, f_evh)
                    chv = kb.bump(S)
                    kb.wait(TE, S, chv)
                for h in range(8):
                    hp = (h % 2) * 64
                    hc = h // 2

                    def f_tpq(t, i=i, h=h, hp=hp):
                        t.transpose(pb[hp:hp + 64, 128:256] if hp else pb[0:64, 128:256],
                                    qallb[:, i, h * 64:(h + 1) * 64], identb[:])
                        t.transpose(pb[0:32, 256:384], jeff[:, i, h, :], identb[:])
                        if i == 0:
                            t.transpose(pb[0:32, 384:512], jslot[:, 0, h, :], identb[:])
                    kb.on(TE, f_tpq)
                    ctq = kb.bump(TE)
                    kb.wait(S, TE, ctq)

                    def f_evq(sx, hp=hp, i=i):
                        sx.activation(out=qc_scr[hp:hp + 64, :], in_=pb[hp:hp + 64, 128:256] if hp else pb[0:64, 128:256], func=AF.Copy)
                        sx.activation(out=jf_scr[0:32, :], in_=pb[0:32, 256:384], func=AF.Copy)
                        if i == 0:
                            sx.activation(out=jf2_scr[0:32, :], in_=pb[0:32, 384:512], func=AF.Copy)
                        sx.drain()
                    kb.on(S, f_evq)
                    cev = kb.bump(S)
                    kb.wait(TE, S, cev)

                    def f_bmm(t, i=i, h=h, hp=hp, hc=hc, w0t=w0t, W=W):
                        t.matmul(psA[:, h * 256:h * 256 + W], qc_scr[hp:hp + 64, :],
                                 kTb[hp:hp + 64, hc, w0t * 128:w0t * 128 + W], start=True, stop=False)
                        t.matmul(psA[:, h * 256:h * 256 + W], jf_scr[0:32, :],
                                 hw_scr[0:32, :, :].rearrange("p a b -> p (a b)")[0:32, 0:W],
                                 start=False, stop=True)
                        if i == 0:
                            t.matmul(psB[:, 0:W], qc_scr[hp:hp + 64, :],
                                     kTb[hp:hp + 64, hc, 0:W], start=True, stop=False)
                            t.matmul(psB[:, 0:W], jf2_scr[0:32, :],
                                     h2t0[0:32, 0:W], start=False, stop=True)
                    kb.on(TE, f_bmm)
                    cbm = kb.bump(TE)
                    kb.wait(V, TE, cbm)

                    def f_bev(v, i=i, h=h, W=W):
                        if i >= 1:
                            v.tensor_tensor(out=sbandv[:, h, :], in0=sbandv[:, h, :],
                                            in1=mkband, op=OP.add)
                        else:
                            v.tensor_copy(out=scD[:, 0:128], in_=sbandv[:, h, 0:128])
                            v.drain()
                            v.tensor_tensor(out=scD[:, 128:256], in0=psB[:, 0:128],
                                            in1=scD[:, 0:128], op=OP.subtract)
                            v.drain()
                            v.scalar_tensor_tensor(out=scD[:, 128:256], in0=scD[:, 128:256],
                                                   scalar=tview[:, 66:67], in1=scD[:, 0:128],
                                                   op0=OP.mult, op1=OP.add)
                            v.drain()
                            v.tensor_tensor(out=sbandv[:, h, 0:128], in0=scD[:, 128:256],
                                            in1=mkb0, op=OP.add)
                            v.drain()
                            v.memset(sbandv[:, h, 128:256], NEG)
                        v.drain()
                    kb.on(V, f_bev)
                    cbv = kb.bump(V)
                    kb.wait(TE, V, cbv)

                def f_minit(v):
                    v.tensor_reduce(out=smv[:, 0:8], in_=sbandv[:, :, :], axis=mybir.AxisListType.X, op=OP.max)
                    v.drain()
                    v.tensor_scalar(out=smv[:, 0:8], in0=smv[:, 0:8], scalar1=32.0, scalar2=None, op0=OP.add)
                    v.memset(smv[:, 8:16], 0.0)
                    v.memset(slacc[:], 0.0)
                    v.drain()
                kb.on(V, f_minit)
                cmi = kb.bump(V)

                def f_bexp(v):
                    v.tensor_tensor(out=sbandv[:, :, :], in0=sbandv[:, :, :],
                                    in1=smv[:, 0:8].unsqueeze(2).broadcast_to((128, 8, 256)), op=OP.subtract)
                    v.drain()
                kb.on(V, f_bexp)
                cbx = kb.bump(V)
                kb.wait(S, V, cbx)

                def f_bexp2(sx):
                    sx.activation(out=prodb[:, 0:2048], in_=sbandv[:, :, :].rearrange("p a b -> p (a b)"),
                                  func=AF.Exp)
                    sx.drain()
                kb.on(S, f_bexp2)
                cbe = kb.bump(S)
                kb.wait(V, S, cbe)

                def f_zband(v):
                    v.tensor_reduce(out=smv[:, 16:24],
                                    in_=prodb[:, 0:2048].rearrange("p (a b) -> p a b", a=8),
                                    axis=mybir.AxisListType.X, op=OP.add)
                    v.drain()
                    v.tensor_tensor(out=smv[:, 8:16], in0=smv[:, 8:16], in1=smv[:, 16:24], op=OP.add)
                    v.drain()
                kb.on(V, f_zband)
                kb.bump(V)
                ssc = shr12[:, 512:768].rearrange("p (a b) -> p a b", a=8)
                vscore_done = {0: 0, 1: 0}
                gprod_done = {0: None, 1: None}
                gb_done = {}

                def issue_gathers(qb):
                    buf = qb % 2
                    kb.wait(G, V, vscore_done[buf])
                    sem = kb.dma_sem_map.setdefault("gag%d" % buf, kb.dma_sem_pool[len(kb.dma_sem_map)])
                    kb.dma_cnt.setdefault(sem, 0)
                    for j in range(4):
                        col = qb * 4 + j
                        kb.dma_cnt[sem] += 16
                        kb.on(G, lambda g, col=col, j=j, sem=sem, buf=buf: g.indirect_dma_start(
                            out=gout2[buf][:, j, :], out_offset=None, in_=dr["gtab"],
                            in_offset=bass.IndirectOffsetOnAxis(ap=idxq[:, col:col + 1], axis=0))
                            .then_inc(kb.sem[sem], 16))
                        gb_done[qb] = (sem, kb.dma_cnt[sem])
                kb.wait(G, V, cex)
                issue_gathers(0)
                for qb in range(8):
                    buf = qb % 2
                    kb.wait(G, gb_done[qb][0], gb_done[qb][1])

                    def f_ghb(g, qb=qb, buf=buf):
                        muv = gout2[buf][:, :, 1024:1088].bitcast(F32)
                        mu1 = muv[:, :, 0:16]
                        mu2 = muv[:, :, 16:32]
                        cjv = jjbc[:, 0:512].rearrange("p (a b) -> p a b", a=32)[:, qb * 4:(qb + 1) * 4, :]
                        sjv = jjbc[:, 512:1024].rearrange("p (a b) -> p a b", a=32)[:, qb * 4:(qb + 1) * 4, :]
                        t1 = bff[:, 15552:15616].rearrange("p (a b) -> p a b", a=4)
                        t2 = bff[:, 15616:15680].rearrange("p (a b) -> p a b", a=4)
                        hq = bff[:, 15488 + buf * 256:15552 + buf * 256].bitcast(BF16).rearrange("p (a b) -> p a b", a=4)
                        g.tensor_tensor(out=t1, in0=mu1, in1=cjv, op=OP.mult)
                        g.tensor_tensor(out=t2, in0=mu2, in1=sjv, op=OP.mult)
                        g.drain()
                        g.tensor_tensor(out=hq[:, :, 0:16], in0=t1, in1=t2, op=OP.subtract)
                        g.drain()
                        g.tensor_tensor(out=t1, in0=mu1, in1=sjv, op=OP.mult)
                        g.tensor_tensor(out=t2, in0=mu2, in1=cjv, op=OP.mult)
                        g.drain()
                        g.tensor_tensor(out=hq[:, :, 16:32], in0=t1, in1=t2, op=OP.add)
                        g.drain()
                    kb.on(G, f_ghb)
                    cgb = kb.bump(G)
                    kb.wait(V, gb_done[qb][0], gb_done[qb][1])
                    kb.wait(V, G, cgb)

                    def f_slot(v, qb=qb, i=i, buf=buf):
                        hq = bff[:, 15488 + buf * 256:15552 + buf * 256].bitcast(BF16).rearrange("p (a b) -> p a b", a=4)
                        for j in range(4):
                            pv = bff[:, 15104:15488].bitcast(BF16).rearrange("p (a b) -> p a b", a=8)
                            v.tensor_tensor(out=pv[:, :, 0:64],
                                            in0=qallb[:, i, :].rearrange("p (a b) -> p a b", a=8),
                                            in1=gout2[buf][:, j, 0:512].rearrange("p (a b) -> p a b", a=8),
                                            op=OP.mult)
                            v.tensor_tensor(out=pv[:, :, 64:96], in0=jslot[:, i, :, :],
                                            in1=hq[:, j, :].unsqueeze(1).broadcast_to((128, 8, 32)),
                                            op=OP.mult)
                            v.drain()
                            v.tensor_reduce(out=ssc[:, :, qb * 4 + j],
                                            in_=pv, axis=mybir.AxisListType.X, op=OP.add)
                            v.drain()
                        sq = ssc[:, :, qb * 4:(qb + 1) * 4]
                        v.tensor_tensor(out=sq, in0=sq,
                                        in1=smv[:, 0:8].unsqueeze(2).broadcast_to((128, 8, 4)), op=OP.subtract)
                        v.drain()
                    kb.on(V, f_slot)
                    c1 = kb.bump(V)
                    vscore_done[buf] = c1
                    if qb + 1 < 8:
                        issue_gathers(qb + 1)
                    kb.wait(S, V, c1)

                    def f_exp2(sx, qb=qb):
                        sx.activation(out=ssc[:, :, qb * 4:(qb + 1) * 4],
                                      in_=ssc[:, :, qb * 4:(qb + 1) * 4], func=AF.Exp)
                        sx.drain()
                    kb.on(S, f_exp2)
                    c4 = kb.bump(S)
                    kb.wait(V, S, c4)
                    kb.wait(G, S, c4)

                    def f_zadd(v, qb=qb):
                        sq = ssc[:, :, qb * 4:(qb + 1) * 4]
                        v.tensor_reduce(out=smv[:, 16:24], in_=sq, axis=mybir.AxisListType.X, op=OP.add)
                        v.drain()
                        v.tensor_tensor(out=smv[:, 8:16], in0=smv[:, 8:16], in1=smv[:, 16:24], op=OP.add)
                        v.drain()
                    kb.on(V, f_zadd)
                    kb.bump(V)
                    for j in range(4):
                        jb = j % 2
                        if gprod_done[jb] is not None:
                            kb.wait(G, V, gprod_done[jb])

                        def f_prod(g, qb=qb, j=j, buf=buf, jb=jb):
                            aj = ssc[:, :, qb * 4 + j].unsqueeze(2).broadcast_to((128, 8, 64))
                            pvx = ptk[:, jb, 0:1024].bitcast(F32)
                            g.tensor_tensor(out=pvx.rearrange("p (a b) -> p a b", a=8),
                                            in0=gout2[buf][:, j, 512:1024].rearrange("p (a b) -> p a b", a=8),
                                            in1=aj, op=OP.mult)
                            g.drain()
                        kb.on(G, f_prod)
                        cgp = kb.bump(G)
                        kb.wait(V, G, cgp)

                        def f_add(v, jb=jb):
                            pvx = ptk[:, jb, 0:1024].bitcast(F32)
                            v.tensor_tensor(out=slacc[:], in0=slacc[:], in1=pvx, op=OP.add)
                            v.drain()
                        kb.on(V, f_add)
                        gprod_done[jb] = kb.bump(V)
                    if qb == 0:
                        kb.wait(TE, lv[0], lv[1])
                        kb.wait(TE, S, cbe)
                    h = qb

                    def f_tpa(t, h=h, i=i):
                        t.transpose(pb[0:128, 0:128], prodb[:, h * 256:h * 256 + 128], identb[:])
                        if i >= 1:
                            t.transpose(pb[0:128, 128:256], prodb[:, h * 256 + 128:h * 256 + 256], identb[:])
                    kb.on(TE, f_tpa)
                    cta = kb.bump(TE)
                    kb.wait(S, TE, cta)

                    def f_eva(sx, i=i):
                        sx.activation(out=hw_scr[:, 0, :], in_=pb[:, 0:128], func=AF.Copy)
                        if i >= 1:
                            sx.activation(out=hw_scr[:, 1, :], in_=pb[:, 128:256], func=AF.Copy)
                        sx.drain()
                    kb.on(S, f_eva)
                    cve = kb.bump(S)
                    kb.wait(TE, S, cve)

                    def f_avb(t, h=h, i=i):
                        nw = 2 if i >= 1 else 1
                        for w in range(nw):
                            t.matmul(psB[:, 1024 + h * 64:1024 + (h + 1) * 64], hw_scr[:, w, :],
                                     vwin[:, w, h * 64:(h + 1) * 64], start=(w == 0), stop=(w == nw - 1))
                    kb.on(TE, f_avb)
                    kb.bump(TE)
                kb.wait(V, TE, kb.cnt["t"])

                def f_merge(v):
                    v.reciprocal(out=smv[:, 40:48], in_=smv[:, 8:16])
                    v.drain()
                    v.tensor_tensor(out=scE[:, 0:512], in0=psB[:, 1024:1536], in1=slacc[:], op=OP.add)
                    v.drain()
                    for h in range(8):
                        v.tensor_scalar(out=o_tok[:, h * 64:(h + 1) * 64], in0=scE[:, h * 64:(h + 1) * 64],
                                        scalar1=smv[:, 40 + h:41 + h], scalar2=None, op0=OP.mult)
                    v.drain()
                kb.on(V, f_merge)
                cmg = kb.bump(V)
                kb.wait(TE, V, cmg)

                kb.wait(TE, ldwo[0], ldwo[1])
                for c4 in range(4):
                    def f_tpo(t, c4=c4):
                        t.transpose(pb[0:128, 0:128], o_tok[:, c4 * 128:(c4 + 1) * 128], identb[:])
                    kb.on(TE, f_tpo)
                    cto = kb.bump(TE)
                    kb.wait(S, TE, cto)

                    def f_evo(sx, c4=c4):
                        sx.activation(out=outTv[:, c4, :], in_=pb[:, 0:128], func=AF.Copy)
                        sx.drain()
                    kb.on(S, f_evo)
                    cvo = kb.bump(S)
                    kb.wait(TE, S, cvo)
                slot = i % 4
                kb.wait(Q, V, kb.cnt["v"]) if False else None
                if kb.win_store[slot] is not None:
                    kb.wait(Q, kb.win_store[slot][0], kb.win_store[slot][1])
                lxw = kb.dma(Q, xw[:, slot, :], src_dram[i * 128:(i + 1) * 128, :], key="win%d" % slot)
                for eh in range(2):
                    ps = psB[:, 1024 + eh * 512:1024 + (eh + 1) * 512]
                    for c4 in range(4):
                        def f_mmo(t, ps=ps, c4=c4, eh=eh):
                            t.matmul(ps, outTv[:, c4, :], wout_v[:, c4, eh * 512:(eh + 1) * 512],
                                     start=(c4 == 0), stop=(c4 == 3))
                        kb.on(TE, f_mmo)
                    cmo = kb.bump(TE)
                    kb.wait(V, TE, cmo)
                    if eh == 0:
                        kb.wait(V, lxw[0], lxw[1])

                    def f_res(v, ps=ps, slot=slot, eh=eh):
                        v.tensor_tensor(out=xw[:, slot, eh * 512:(eh + 1) * 512],
                                        in0=xw[:, slot, eh * 512:(eh + 1) * 512], in1=ps, op=OP.add)
                        v.drain()
                    kb.on(V, f_res)
                    crs = kb.bump(V)
                    kb.wait(TE, V, crs)
                kb.wait(Q, V, kb.cnt["v"])
                sto = kb.dma(Q, dst_dram[i * 128:(i + 1) * 128, :], xw[:, slot, :], key="win%d" % slot)
                kb.win_store[slot] = sto
            kb.barrier()

        emit_ffn(dr["strm"], dr["strm"])
        emit_attn_b(dr["strm"], dr["strm"])
        emit_ffn(dr["strm"], dr["out"])
        kb.barrier()
        for kname, sm in kb.dma_sem_map.items():
            kb.wait(Q, sm, kb.dma_cnt[sm])

        with nc.Block() as block:
            @block.vector
            def _(eng):
                for fn in kb.ops["v"]:
                    fn(eng)

            @block.scalar
            def _(eng):
                for fn in kb.ops["s"]:
                    fn(eng)

            @block.gpsimd
            def _(eng):
                for fn in kb.ops["g"]:
                    fn(eng)

            @block.tensor
            def _(eng):
                for fn in kb.ops["t"]:
                    fn(eng)

            @block.sync
            def _(eng):
                for fn in kb.ops["q"]:
                    fn(eng)

    return nc


def attn_b_host_tables(inputs):
    th = 1.0 / 10000.0 ** (2.0 * np.arange(16) / 32.0)
    t = np.arange(T)[:, None]
    tq = np.concatenate([np.cos(t * th), np.sin(t * th)], 1).astype(np.float32)
    t2 = np.concatenate([np.cos(2 * t * th), np.sin(2 * t * th)], 1).astype(np.float32)
    pp = np.concatenate([np.cos((t + 63) * th), np.sin((t + 63) * th)], 1).astype(np.float32)
    j = np.arange(64, 96)[:, None]
    jj = np.concatenate([np.cos(j * th).reshape(-1), np.sin(j * th).reshape(-1)])[None, :].astype(np.float32)
    p_ = np.arange(128)[:, None]
    r_ = np.arange(256)[None, :]
    mk_band = np.where((r_ >= p_ + 65) & (r_ <= p_ + 128), 0.0, NEG).astype(np.float32)
    c_ = np.arange(128)[None, :]
    mk_b0 = np.where(p_ < 64, np.where(c_ < 64, 0.0, NEG),
                     np.where((c_ >= p_ - 63) & (c_ <= p_), 0.0, NEG)).astype(np.float32)
    mk_idiag = np.where(c_ >= p_ - 63, NEG, 0.0).astype(np.float32)
    mk_iprev = np.where(c_ >= p_ + 65, NEG, 0.0).astype(np.float32)
    d = {
        "wdkv": inputs["L0_attn_b_W_dkv"], "wuk": inputs["L0_attn_b_W_uk"],
        "wuv": inputs["L0_attn_b_W_uv"], "wdq": inputs["L0_attn_b_W_dq"],
        "wuq": inputs["L0_attn_b_W_uq"], "wqr": inputs["L0_attn_b_W_qr"],
        "wkr": inputs["L0_attn_b_W_kr"], "widxq": inputs["L0_attn_b_idx_wq"],
        "widxk": inputs["L0_attn_b_idx_wk"], "wout": inputs["L0_attn_b_W_out"],
        "idxw": np.asarray(inputs["L0_attn_b_idx_w"]).reshape(1, 4),
        "rawd": np.asarray(inputs["L0_attn_b_raw_delta"]).reshape(1, 16),
        "tb_tq": tq, "tb_t2": t2, "tb_pp": pp, "tb_jj": jj,
        "mk_band": mk_band, "mk_b0": mk_b0, "mk_idiag": mk_idiag, "mk_iprev": mk_iprev,
    }
    return {k: np.ascontiguousarray(np.asarray(v, dtype=np.float32)) for k, v in d.items()}


def kernel(**inputs):
    x = np.ascontiguousarray(inputs["L0_attn_a_x"], dtype=np.float32)
    host = {
        "wq": np.ascontiguousarray(inputs["L0_attn_a_Wq"].T),
        "wk": np.ascontiguousarray(inputs["L0_attn_a_Wk"].T),
        "wv": np.ascontiguousarray(inputs["L0_attn_a_Wv"].T),
        "wo": np.ascontiguousarray(inputs["L0_attn_a_Wo"].T),
        "w1": np.ascontiguousarray(inputs["L0_ffn_w1"].T),
        "w2": np.ascontiguousarray(inputs["L0_ffn_w2"].T),
        "convw": np.ascontiguousarray(
            np.asarray(inputs["L0_ffn_conv_w"]).reshape(32, 128, 3).transpose(1, 0, 2)),
        "convb": np.ascontiguousarray(np.asarray(inputs["L0_ffn_conv_b"]).reshape(32, 128).T),
    }
    host.update(attn_b_host_tables(inputs))
    nc = build_core_program(DEBUG_TAPS)
    in_maps = [dict(host, x=x[c]) for c in range(8)]
    res = run_bass_kernel_spmd(nc, in_maps, core_ids=list(range(8)))
    return np.stack([r["out"] for r in res.results], axis=0)


# version 19
# speedup vs baseline: 1.0199x; 1.0027x over previous
import math
import numpy as np
import concourse.bass as bass
import concourse.mybir as mybir
from concourse.bass_utils import run_bass_kernel_spmd
from concourse.alu_op_type import AluOpType as OP
from bass_rust import ReduceOp
from contextlib import ExitStack

F32 = mybir.dt.float32
BF16 = mybir.dt.bfloat16
I32 = mybir.dt.int32
U32 = mybir.dt.uint32
AF = mybir.ActivationFunctionType

T = 2048
D = 1024
NT = 16
HA = 16
NEG = -1e30
MICH_ITERS = 10
LN_EPS = 1e-5
MAGIC = 3.0 * 2 ** 22

ENGS = ("v", "s", "g", "t", "q")
DEBUG_TAPS = []


class KB:
    def __init__(self, nc):
        self.nc = nc
        self.ops = {e: [] for e in ENGS}
        self.sem = {}
        self.cnt = {e: 0 for e in ENGS}
        self.dma_cnt = {}
        self.dma_sem_pool = ["d%d" % i for i in range(28)]
        self.dma_sem_map = {}
        self.win_store = [None, None, None, None]

    def on(self, e, fn):
        self.ops[e].append(fn)

    def bump(self, e, drain=True):
        self.cnt[e] += 1
        tgt = self.cnt[e]

        def fn(eng):
            if drain and e in ("v", "s", "g", "t"):
                eng.drain()
            eng.sem_inc(self.sem[e], 1)
        self.on(e, fn)
        return tgt

    def wait(self, e, producer_sem, count):
        if count <= 0:
            return
        self.on(e, lambda eng: eng.wait_ge(self.sem[producer_sem], count))

    def barrier(self, engines=("v", "s", "g", "t", "q")):
        marks = {e: self.bump(e) for e in engines}
        for e in engines:
            for p, c in marks.items():
                if p != e:
                    self.wait(e, p, c)

    def dma(self, queue_eng, out_ap, in_ap, key="st"):
        if key not in self.dma_sem_map:
            idx = len(self.dma_sem_map)
            assert idx < len(self.dma_sem_pool), "out of dma sems"
            self.dma_sem_map[key] = self.dma_sem_pool[idx]
            self.dma_cnt[self.dma_sem_map[key]] = 0
        sem = self.dma_sem_map[key]
        self.dma_cnt[sem] += 16
        tgt = self.dma_cnt[sem]
        self.on(queue_eng, lambda eng: eng.dma_start(out=out_ap, in_=in_ap).then_inc(self.sem[sem], 16))
        return sem, tgt


def build_core_program(debug_taps=()):
    nc = bass.Bass("TRN2", target_bir_lowering=False)
    kb = KB(nc)
    V, S, G, TE, Q = "v", "s", "g", "t", "q"

    dr = {}
    dr["x"] = nc.dram_tensor("x", [T, D], F32, kind="ExternalInput").ap()
    for nm in ("wq", "wk", "wv", "wo"):
        dr[nm] = nc.dram_tensor(nm, [D, D], F32, kind="ExternalInput").ap()
    dr["out"] = nc.dram_tensor("out", [T, D], F32, kind="ExternalOutput").ap()
    dr["strm"] = nc.dram_tensor("strm", [T, D], F32).ap()
    dr["w1"] = nc.dram_tensor("w1", [D, 4096], F32, kind="ExternalInput").ap()
    dr["w2"] = nc.dram_tensor("w2", [4096, D], F32, kind="ExternalInput").ap()
    dr["convw"] = nc.dram_tensor("convw", [128, 32, 3], F32, kind="ExternalInput").ap()
    dr["convb"] = nc.dram_tensor("convb", [128, 32], F32, kind="ExternalInput").ap()
    dr["w1q"] = nc.dram_tensor("w1q", [D, 4096], BF16).ap()
    dr["w2q"] = nc.dram_tensor("w2q", [4096, D], BF16).ap()
    for nm, shp in (("wdkv", [D, 256]), ("wuk", [256, 512]), ("wuv", [256, 512]),
                    ("wdq", [D, 512]), ("wuq", [512, 512]), ("wqr", [512, 256]),
                    ("wkr", [D, 32]), ("widxq", [D, 256]), ("widxk", [D, 64]),
                    ("wout", [512, D]), ("idxw", [1, 4]), ("rawd", [1, 16])):
        dr[nm] = nc.dram_tensor(nm, shp, F32, kind="ExternalInput").ap()
    for nm in ("tb_tq", "tb_t2", "tb_pp"):
        dr[nm] = nc.dram_tensor(nm, [T, 32], F32, kind="ExternalInput").ap()
    dr["tb_jj"] = nc.dram_tensor("tb_jj", [1, 1024], F32, kind="ExternalInput").ap()
    dr["mk_band"] = nc.dram_tensor("mk_band", [128, 256], F32, kind="ExternalInput").ap()
    dr["mk_b0"] = nc.dram_tensor("mk_b0", [128, 128], F32, kind="ExternalInput").ap()
    dr["mk_idiag"] = nc.dram_tensor("mk_idiag", [128, 128], F32, kind="ExternalInput").ap()
    dr["mk_iprev"] = nc.dram_tensor("mk_iprev", [128, 128], F32, kind="ExternalInput").ap()
    dr["gtab"] = nc.dram_tensor("gtab", [T, 1088], BF16).ap()
    taps = {}
    for nm, shape, dt in debug_taps:
        taps[nm] = nc.dram_tensor(nm, shape, dt, kind="ExternalOutput").ap()

    with ExitStack() as ctx:
        sbuf = lambda n_, s, d: ctx.enter_context(nc.sbuf_tensor(n_, s, d))
        xw = sbuf("xw", [128, 4, D], F32)
        xhT = sbuf("xhT", [128, 8, T], BF16)
        qT = sbuf("qT", [128, 8, T], BF16)
        bigA = sbuf("bigA", [128, 16384], F32)
        kT = bigA[:, 0:8192].bitcast(BF16).rearrange("p (a b) -> p a b", a=8)
        vtok = bigA[:, 8192:16384].bitcast(BF16).rearrange("p (a b) -> p a b", a=NT)
        ptk = sbuf("ptk", [128, 2, T], BF16)
        wAB = sbuf("wAB", [128, 16, 512], BF16)
        wA = wAB[:, 0:8, :]
        wB = wAB[:, 8:16, :]
        shr12 = sbuf("shr12", [128, 3072], F32)
        cands = sbuf("cands", [128, HA, 16], F32)
        taus = sbuf("taus", [128, HA], F32)
        tausq = sbuf("tausq", [128, NT, HA], F32)
        scA = sbuf("scA", [128, 512], F32)
        srow = scA[0:1, :]
        xnb = sbuf("xnb", [128, D], BF16)
        mv = sbuf("mv", [128, NT, 2], F32)
        msc = sbuf("msc", [128, 32], F32)
        msc2 = sbuf("msc2", [128, 32], F32)
        rst = sbuf("rst", [128, NT, 2], F32)
        stat6 = sbuf("stat6", [128, 12], F32)
        utri = sbuf("utri", [128, 128], F32)
        identf = sbuf("identf", [128, 128], F32)
        onesf = sbuf("onesf", [1, 128], F32)
        o2T = sbuf("o2T", [128, 8, 512], BF16)
        hpad = xhT.rearrange("p a b -> p (a b)")[:, 4096:8208].bitcast(F32)\
            .rearrange("p (a b) -> p a b", a=4)
        scD = shr12[:, 0:1024]
        sc4 = shr12[:, 1024:2048]
        sci = shr12[:, 2048:3072].bitcast(I32)
        amt = sbuf("amt", [128, 512], F32)
        amtn = shr12[:, 1024:1536]
        s2bc = shr12[:, 1536:2048]
        cvw = sbuf("cvw", [128, 32, 3], F32)
        cvb = sbuf("cvb", [128, 32], F32)
        ffs = sbuf("ffs", [128, NT, 4], F32)
        zpad = sbuf("zpad", [128, 32], BF16)
        wst = sbuf("wst", [128, 80], F32)
        ltri01b = sbuf("ltri01b", [128, 128], BF16)
        identb = sbuf("identb", [128, 128], BF16)
        ones1 = sbuf("ones1", [1, 128], BF16)
        negonesf = sbuf("negonesf", [1, 128], F32)
        cst = sbuf("cst", [128, 16], F32)
        cst_slots = {}

        def const_f32(val):
            if val not in cst_slots:
                cst_slots[val] = len(cst_slots)
                assert len(cst_slots) <= 16
                nc.const_aps.register(F32, float(val), cst[:, cst_slots[val]:cst_slots[val] + 1])
            return cst[:, cst_slots[val]:cst_slots[val] + 1]

        psA = ctx.enter_context(nc.psum_tensor("psA", [128, 2048], F32))
        psB = ctx.enter_context(nc.psum_tensor("psB", [128, 2048], F32))
        aoT = xhT

        for e in ENGS:
            kb.sem[e] = ctx.enter_context(nc.semaphore("sem_" + e))
        for s_ in kb.dma_sem_pool:
            kb.sem[s_] = ctx.enter_context(nc.semaphore("sem_" + s_))

        const_f32(LN_EPS)

        def init_consts(g):
            for val, slot in cst_slots.items():
                g.memset(cst[:, slot:slot + 1], float(val))
            g.memset(utri[:], 0.0)
            g.memset(identb[:], 1.0)
            g.memset(ones1[:], 1.0)
            g.memset(negonesf[:], -1.0)
            g.memset(ltri01b[:], 1.0)
            g.memset(identf[:], 1.0)
            g.memset(onesf[:], 1.0)
            g.drain()
            g.affine_select(out=identf[:], in_=identf[:], compare_op=OP.is_equal,
                            fill=0.0, base=0, pattern=[[-1, 128]], channel_multiplier=1)
            g.affine_select(out=utri[:], in_=utri[:], compare_op=OP.is_ge,
                            fill=NEG, base=0, pattern=[[-1, 128]], channel_multiplier=1)
            g.affine_select(out=ltri01b[:], in_=ltri01b[:], compare_op=OP.is_ge,
                            fill=0.0, base=0, pattern=[[1, 128]], channel_multiplier=-1)
            g.affine_select(out=identb[:], in_=identb[:], compare_op=OP.is_equal,
                            fill=0.0, base=0, pattern=[[-1, 128]], channel_multiplier=1)
            g.drain()
            g.iota(wst[:, 40:48], [[1, 8]], base=1, channel_multiplier=0,
                   allow_small_or_imprecise_dtypes=True)
            g.memset(zpad[:], 0.0)
        kb.on(G, init_consts)
        gconst = kb.bump(G)
        kb.wait(TE, G, gconst)
        kb.wait(V, G, gconst)

        def f_rinv(v):
            v.reciprocal(out=wst[:, 40:48], in_=wst[:, 40:48])
            v.drain()
        kb.on(V, f_rinv)
        kb.bump(V)
        kb.wait(V, G, gconst)

        buf_free = {"wA": 0, "wB": 0}
        bufs = {"wA": wA, "wB": wB}

        def load_w_half(nm, bufname, half):
            src = dr[nm][:, half * 512:(half + 1) * 512].rearrange("(a p) b -> p a b", p=128)
            kb.wait(G, TE, buf_free[bufname])
            return kb.dma(G, bufs[bufname], src, key=bufname)

        ld_wq0 = load_w_half("wq", "wA", 0)
        ld_wq1 = load_w_half("wq", "wB", 1)

        def emit_ln_transpose(src_dram, dst):
            win_free = [0, 0, 0, 0]
            for i in range(NT):
                slot = i % 4
                kb.wait(Q, V, win_free[slot])
                if kb.win_store[slot] is not None:
                    kb.wait(Q, kb.win_store[slot][0], kb.win_store[slot][1])
                ld = kb.dma(Q, xw[:, slot, :], src_dram[i * 128:(i + 1) * 128, :], key="win%d" % slot)
                kb.wait(V, ld[0], ld[1])

                def f_stat(v, i=i, slot=slot):
                    v.bn_stats(out=stat6[:, 0:6], in_=xw[:, slot, 0:512])
                    v.bn_stats(out=stat6[:, 6:12], in_=xw[:, slot, 512:1024])
                    v.drain()
                    v.bn_aggr(out=mv[:, i, :], in_=stat6[:])
                    v.drain()
                kb.on(V, f_stat)
                c1 = kb.bump(V)
                kb.wait(S, V, c1)

                def f_sq(s, i=i):
                    s.activation(out=rst[:, i, 0:1], in_=mv[:, i, 1:2],
                                 func=AF.Sqrt, bias=LN_EPS, scale=1.0)
                kb.on(S, f_sq)
                c2 = kb.bump(S)
                kb.wait(V, S, c2)

                def f_norm(v, i=i, slot=slot):
                    v.reciprocal(out=rst[:, i, 1:2], in_=rst[:, i, 0:1])
                    v.drain()
                    v.tensor_scalar(out=xnb[:], in0=xw[:, slot, :], scalar1=mv[:, i, 0:1],
                                    scalar2=rst[:, i, 1:2], op0=OP.subtract, op1=OP.mult)
                kb.on(V, f_norm)
                cnorm = kb.bump(V)
                win_free[slot] = cnorm
                kb.wait(TE, V, cnorm)
                half = i % 2
                pview = psA[:, half * 512:(half + 1) * 512].bitcast(BF16)
                for dc in range(8):
                    def f_tp(t, dc=dc, pview=pview):
                        t.transpose(pview[:, dc * 128:(dc + 1) * 128],
                                    xnb[:, dc * 128:(dc + 1) * 128], identb[:])
                    kb.on(TE, f_tp)
                ctp = kb.bump(TE)
                kb.wait(V, TE, ctp)

                def f_evac(v, i=i, pview=pview, dst=dst):
                    v.tensor_copy(out=dst[:, :, i * 128:(i + 1) * 128],
                                  in_=pview.rearrange("p (a b) -> p a b", a=8))
                kb.on(V, f_evac)
                pe_ok = kb.bump(V)
                kb.wait(TE, V, pe_ok)

        emit_ln_transpose(dr["x"], xhT)
        kb.barrier()

        if "t_xhT" in taps:
            st = None
            for dc in range(8):
                st = kb.dma(Q, taps["t_xhT"][dc * 128:(dc + 1) * 128, :], xhT[:, dc, :], key="tap")
            kb.wait(Q, st[0], st[1])

        SCALE_A = 0.125

        def proj_featmajor(dst, wname, scale, preloaded=None):
            for half in range(2):
                bufname = "wA" if half == 0 else "wB"
                ld = preloaded[half] if preloaded else load_w_half(wname, bufname, half)
                kb.wait(TE, ld[0], ld[1])
                for ec in range(4):
                    e_glob = half * 4 + ec
                    for tc in range(4):
                        slot = (e_glob * 4 + tc) % 3
                        ps = psB[:, slot * 512:(slot + 1) * 512]
                        for dc in range(8):
                            def f_mm(t, ps=ps, dc=dc, ec=ec, tc=tc, wb=bufs[bufname]):
                                t.matmul(ps, wb[:, dc, ec * 128:(ec + 1) * 128],
                                         xhT[:, dc, tc * 512:(tc + 1) * 512],
                                         start=(dc == 0), stop=(dc == 7))
                            kb.on(TE, f_mm)
                        cm = kb.bump(TE)
                        kb.wait(S, TE, cm)

                        def f_ev(s, ps=ps, e_glob=e_glob, tc=tc, dst=dst, scale=scale):
                            s.activation(out=dst[:, e_glob, tc * 512:(tc + 1) * 512], in_=ps,
                                         func=AF.Copy, scale=scale)
                        kb.on(S, f_ev)
                        ce = kb.bump(S)
                        kb.wait(TE, S, ce)
                buf_free[bufname] = kb.cnt["t"]

        proj_featmajor(qT, "wq", SCALE_A, preloaded=(ld_wq0, ld_wq1))
        proj_featmajor(kT, "wk", 1.0)

        for half in range(2):
            bufname = "wA" if half == 0 else "wB"
            ld = load_w_half("wv", bufname, half)
            kb.wait(TE, ld[0], ld[1])
            for i in range(NT):
                slot = i % 3
                ps = psB[:, slot * 512:(slot + 1) * 512]
                for dc in range(8):
                    def f_mm(t, ps=ps, dc=dc, i=i, wb=bufs[bufname]):
                        t.matmul(ps, xhT[:, dc, i * 128:(i + 1) * 128], wb[:, dc, :],
                                 start=(dc == 0), stop=(dc == 7))
                    kb.on(TE, f_mm)
                cm = kb.bump(TE)
                kb.wait(S, TE, cm)

                def f_ev(s, ps=ps, i=i, half=half):
                    s.activation(out=vtok[:, i, half * 512:(half + 1) * 512], in_=ps, func=AF.Copy)
                kb.on(S, f_ev)
                ce = kb.bump(S)
                kb.wait(TE, S, ce)
            buf_free[bufname] = kb.cnt["t"]

        kb.barrier()
        ld_wo0 = load_w_half("wo", "wA", 0)
        ld_wo1 = load_w_half("wo", "wB", 1)

        slab_free = {0: 0, 1: 0}
        for qt in range(NT):
            kw = (qt + 1) * 128
            for h in range(HA):
                slab = psA if h % 2 == 0 else psB
                sid = h % 2
                kb.wait(TE, V, slab_free[sid])
                hp = (h % 2) * 64
                hc = h // 2
                for kc in range((kw + 511) // 512):
                    w = min(512, kw - kc * 512)

                    def f_mm(t, slab=slab, kc=kc, w=w, hp=hp, hc=hc, qt=qt):
                        t.matmul(slab[:, kc * 512:kc * 512 + w],
                                 qT[hp:hp + 64, hc, qt * 128:(qt + 1) * 128],
                                 kT[hp:hp + 64, hc, kc * 512:kc * 512 + w],
                                 start=True, stop=True)
                    kb.on(TE, f_mm)
                cm = kb.bump(TE)
                kb.wait(V, TE, cm)

                def f_ex(v, slab=slab, kw=kw, qt=qt, h=h):
                    v.tensor_tensor(out=slab[:, qt * 128:qt * 128 + 128],
                                    in0=slab[:, qt * 128:qt * 128 + 128], in1=utri[:], op=OP.add)
                    v.drain()
                    v.max(out=cands[:, h, 0:8], in_=slab[:, 0:kw])
                kb.on(V, f_ex)
                slab_free[sid] = kb.bump(V)

            def f_tau(v, qt=qt):
                cview = cands[:, :, 0:8]
                a = scA[:, 0:128].rearrange("p (a b) -> p a b", a=HA)
                bq = scA[:, 128:256].rearrange("p (a b) -> p a b", a=HA)
                v.drain()
                v.tensor_copy(out=a, in_=cview)
                v.drain()
                for d in (1, 2, 4):
                    v.tensor_tensor(out=bq[:, :, d:8], in0=a[:, :, d:8], in1=a[:, :, 0:8 - d], op=OP.add)
                    v.tensor_copy(out=bq[:, :, 0:d], in_=a[:, :, 0:d])
                    v.drain()
                    a, bq = bq, a
                v.tensor_scalar(out=a, in0=a, scalar1=1.0, scalar2=None, op0=OP.subtract)
                v.drain()
                v.tensor_tensor(out=a, in0=a,
                                in1=wst[:, 40:48].unsqueeze(1).broadcast_to((128, HA, 8)), op=OP.mult)
                v.drain()
                v.tensor_reduce(out=taus[:], in_=a, axis=mybir.AxisListType.X, op=OP.max)
                v.drain()
                v.tensor_copy(out=tausq[:, qt, :], in_=taus[:])
                v.drain()
            kb.on(V, f_tau)
            kb.bump(V)

        kb.barrier()
        if "t_tau" in taps:
            st = None
            for i in range(NT):
                st = kb.dma(Q, taps["t_tau"][i * 128:(i + 1) * 128, :], tausq[:, i, :], key="tap")
            kb.wait(Q, st[0], st[1])

        av_done = {0: 0, 1: 0}
        ev_prev = {0: 0, 1: 0, 2: 0, 3: 0}
        ntrf = shr12[0:1, 0:2048]
        taubc = shr12[:, 0:2048]
        for h in range(HA):
            hp = (h % 2) * 64
            hc = h // 2
            for qt in range(NT):
                def f_tp(t, qt=qt, h=h):
                    t.transpose(psB[0:1, qt * 128:(qt + 1) * 128], tausq[:, qt, h:h + 1], identf[:])
                kb.on(TE, f_tp)
            ctp = kb.bump(TE)
            kb.wait(V, TE, ctp)

            def f_ntr(v):
                v.tensor_copy(out=ntrf, in_=psB[0:1, 0:T])
            kb.on(V, f_ntr)
            cn = kb.bump(V)
            kb.wait(TE, V, cn)

            def f_tbc(t):
                for c in range(4):
                    t.matmul(psB[:, c * 512:(c + 1) * 512], onesf[0:1, :],
                             ntrf[0:1, c * 512:(c + 1) * 512], start=True, stop=True)
            kb.on(TE, f_tbc)
            ctb = kb.bump(TE)
            kb.wait(V, TE, ctb)

            def f_tbev(v):
                for c in range(4):
                    v.tensor_copy(out=taubc[:, c * 512:(c + 1) * 512], in_=psB[:, c * 512:(c + 1) * 512])
                v.drain()
            kb.on(V, f_tbev)
            ctv = kb.bump(V)
            kb.wait(TE, V, ctv)

            def issue_av(pend):
                kt_, buf_, cevac_, cmask_ = pend
                kb.wait(TE, S, cevac_)
                kb.wait(TE, V, cmask_)
                for qc in range(4):
                    if (qc + 1) * 512 <= kt_ * 128:
                        continue
                    q0c = max(qc * 512, kt_ * 128)
                    w = (qc + 1) * 512 - q0c
                    last_kt = min(15, 4 * qc + 3)

                    def f_av(t, kt=kt_, q0c=q0c, w=w, h=h, buf=buf_, last_kt=last_kt):
                        t.matmul(psB[0:64, q0c:q0c + w],
                                 vtok[:, kt, h * 64:h * 64 + 64],
                                 ptk[:, buf, q0c - kt * 128:q0c - kt * 128 + w],
                                 start=(kt == 0), stop=(kt == last_kt))
                    kb.on(TE, f_av)
                av_done[buf_] = kb.bump(TE)

            pend = None
            chunkc = 0
            for kt in range(NT):
                buf = kt % 2
                qw_total = T - kt * 128
                for jc in range((qw_total + 511) // 512):
                    q0 = kt * 128 + jc * 512
                    w = min(512, T - q0)
                    slot = chunkc % 4
                    chunkc += 1
                    ps = psA[:, slot * 512:slot * 512 + w]
                    kb.wait(TE, S, ev_prev[slot])

                    def f_mm(t, ps=ps, kt=kt, q0=q0, w=w, hp=hp, hc=hc):
                        t.matmul(ps, kT[hp:hp + 64, hc, kt * 128:(kt + 1) * 128],
                                 qT[hp:hp + 64, hc, q0:q0 + w], start=True, stop=True)
                    kb.on(TE, f_mm)
                    cm = kb.bump(TE)
                    kb.wait(V, TE, cm)

                    def f_sub(v, ps=ps, q0=q0, w=w):
                        v.tensor_tensor(out=ps, in0=ps, in1=taubc[:, q0:q0 + w], op=OP.subtract)
                        v.drain()
                    kb.on(V, f_sub)
                    cs = kb.bump(V)
                    kb.wait(S, V, cs)
                    if jc == 0:
                        kb.wait(S, TE, av_done[buf])

                    def f_ev(s, ps=ps, kt=kt, q0=q0, w=w, buf=buf):
                        s.activation(out=ptk[:, buf, q0 - kt * 128:q0 - kt * 128 + w], in_=ps, func=AF.Relu)
                    kb.on(S, f_ev)
                    ev_prev[slot] = kb.bump(S)
                    if jc == 0:
                        kb.wait(V, S, ev_prev[slot])

                        def f_msk(v, buf=buf):
                            v.tensor_tensor(out=ptk[:, buf, 0:128], in0=ptk[:, buf, 0:128],
                                            in1=ltri01b[:], op=OP.mult)
                        kb.on(V, f_msk)
                        cmask = kb.bump(V)
                if pend is not None:
                    issue_av(pend)
                pend = (kt, buf, ev_prev[(chunkc - 1) % 4], cmask)
            issue_av(pend)
            kb.wait(V, TE, av_done[(NT - 1) % 2])

            def f_avev(v, hp=hp, hc=hc):
                v.tensor_copy(out=aoT[hp:hp + 64, hc, :], in_=psB[0:64, :])
            kb.on(V, f_avev)
            cae = kb.bump(V)
            kb.wait(TE, V, cae)

        kb.barrier()
        if "t_aoT" in taps:
            st = None
            for dc in range(8):
                st = kb.dma(Q, taps["t_aoT"][dc * 128:(dc + 1) * 128, :], aoT[:, dc, :], key="tap")
            kb.wait(Q, st[0], st[1])


        pass1_done = {}

        def emit_pass1(wname, nchunks, ccols, cbase):
            land = [bigA[:, 0:4096], bigA[:, 4096:8192]]
            free_at = [kb.cnt["s"], kb.cnt["s"]]
            for n in range(nchunks):
                b = n % 2
                kb.wait(G, S, free_at[b])
                ld = kb.dma(G, land[b][:, 0:ccols], dr[wname][n * 128:(n + 1) * 128, :], key="pg%d" % b)
                kb.wait(S, ld[0], ld[1])

                def f_abs(sx, n=n, b=b, ccols=ccols, cbase=cbase, land=land):
                    sx.activation(out=bigA[:, 12288:12288 + ccols],
                                  in_=land[b][:, 0:ccols], func=AF.Abs,
                                  accum_out=wst[:, cbase + n:cbase + n + 1])
                kb.on(S, f_abs)
                free_at[b] = kb.bump(S)
            pass1_done[wname] = kb.cnt["s"]

        emit_pass1("w1", 8, 4096, 0)
        emit_pass1("w2", 32, 1024, 48)
        for ld in (ld_wo0, ld_wo1):
            kb.wait(TE, ld[0], ld[1])
        win_free = [0, 0, 0, 0]
        st = None
        st_all = []
        for i in range(NT):
            slot = i % 4
            kb.wait(Q, V, win_free[slot])
            if kb.win_store[slot] is not None:
                kb.wait(Q, kb.win_store[slot][0], kb.win_store[slot][1])
            ld = kb.dma(Q, xw[:, slot, :], dr["x"][i * 128:(i + 1) * 128, :], key="win%d" % slot)
            for half in range(2):
                wbuf = wA if half == 0 else wB
                pslot = (i * 2 + half) % 3
                ps = psB[:, pslot * 512:(pslot + 1) * 512]
                for dc in range(8):
                    def f_mm(t, ps=ps, dc=dc, i=i, wbuf=wbuf):
                        t.matmul(ps, aoT[:, dc, i * 128:(i + 1) * 128], wbuf[:, dc, :],
                                 start=(dc == 0), stop=(dc == 7))
                    kb.on(TE, f_mm)
                cm = kb.bump(TE)
                kb.wait(V, TE, cm)
                if half == 0:
                    kb.wait(V, ld[0], ld[1])

                def f_ev(v, ps=ps, slot=slot, half=half):
                    v.tensor_tensor(out=xw[:, slot, half * 512:(half + 1) * 512],
                                    in0=xw[:, slot, half * 512:(half + 1) * 512], in1=ps, op=OP.add)
                kb.on(V, f_ev)
                ce = kb.bump(V)
                kb.wait(TE, V, ce)
            cdone = kb.bump(V)
            kb.wait(Q, V, cdone)
            st = kb.dma(Q, dr["strm"][i * 128:(i + 1) * 128, :], xw[:, slot, :], key="win%d" % slot)
            win_free[slot] = cdone
            kb.win_store[slot] = st
            st_all.append(st)
        buf_free["wA"] = buf_free["wB"] = kb.cnt["t"]

        kb.barrier()
        for st in st_all[-4:]:
            kb.wait(Q, st[0], st[1])

        k2v = xhT[:, :, :].rearrange("p a b -> p (a b)").rearrange("p (a b) -> p a b", a=32)
        k1T = qT
        wf1 = [ptk[:, 0, 0:1024].rearrange("p (a b) -> p a b", a=8),
               ptk[:, 1, 0:1024].rearrange("p (a b) -> p a b", a=8),
               xhT[:, 0, 0:1024].rearrange("p (a b) -> p a b", a=8),
               xhT[:, 1, 0:1024].rearrange("p (a b) -> p a b", a=8)]
        wf2 = [wAB[:, :, :].rearrange("p a b -> p (a b)")[:, j * 4096:(j + 1) * 4096]
               .rearrange("p (a b) -> p a b", a=32) for j in range(2)]

        def emit_prequant(wname, wqname, nchunks, ccols, stat_col):
            land = [bigA[:, 0:4096], bigA[:, 4096:8192]]
            kb.wait(V, S, pass1_done[wname])
            cbase = 0 if wname == "w1" else 48

            def f_mean(v, nchunks=nchunks, cbase=cbase):
                v.tensor_reduce(out=wst[:, 32:33], in_=wst[:, cbase:cbase + nchunks],
                                axis=mybir.AxisListType.X, op=OP.add)
                v.drain()
            kb.on(V, f_mean)
            cv = kb.bump(V)
            kb.wait(TE, V, cv)

            def f_tpm(t):
                t.transpose(psB[0:1, 0:128], wst[:, 32:33], identf[:])
            kb.on(TE, f_tpm)
            ct1 = kb.bump(TE)
            kb.wait(V, TE, ct1)

            def f_rtot(v):
                v.tensor_reduce(out=scA[0:1, 0:1], in_=psB[0:1, 0:128], axis=mybir.AxisListType.X, op=OP.add)
            kb.on(V, f_rtot)
            cv1 = kb.bump(V)
            kb.wait(TE, V, cv1)

            def f_bcm(t):
                t.matmul(psB[:, 256:257], onesf[0:1, :], scA[0:1, 0:1], start=True, stop=True)
            kb.on(TE, f_bcm)
            ct2 = kb.bump(TE)
            kb.wait(V, TE, ct2)

            def f_cptot(v):
                v.tensor_copy(out=wst[:, 33:34], in_=psB[:, 256:257])
                v.drain()
            kb.on(V, f_cptot)

            def f_scale(v, nchunks=nchunks, ccols=ccols, stat_col=stat_col):
                v.tensor_scalar(out=wst[:, 35:36], in0=wst[:, 33:34],
                                scalar1=1.0 / (nchunks * 128 * ccols), scalar2=1e-5,
                                op0=OP.mult, op1=OP.max)
                v.drain()
                v.reciprocal(out=wst[:, 34:35], in_=wst[:, 35:36])
                v.drain()
            kb.on(V, f_scale)
            cv2 = kb.bump(V)
            for b_ in (0, 1):
                kname = "pq%d" % b_
                if kname in kb.dma_sem_map:
                    sm = kb.dma_sem_map[kname]
                    kb.wait(Q, sm, kb.dma_cnt[sm])
            free_at = [cv2, cv2]
            st_done = [None, None]
            stq = None
            for n in range(nchunks):
                b = n % 2
                kb.wait(Q, V, free_at[b])
                if st_done[b] is not None:
                    kb.wait(Q, st_done[b][0], st_done[b][1])
                ld = kb.dma(Q, land[b][:, 0:ccols], dr[wname][n * 128:(n + 1) * 128, :], key="pq%d" % b)
                kb.wait(V, ld[0], ld[1])
                kb.wait(V, V if False else "v", cv2)

                def f_q(v, n=n, b=b, ccols=ccols):
                    fscr = bigA[:, 8192:8192 + ccols]
                    v.tensor_scalar(out=fscr, in0=land[b][:, 0:ccols], scalar1=wst[:, 34:35],
                                    scalar2=MAGIC, op0=OP.mult, op1=OP.add)
                    v.drain()
                    v.tensor_scalar(out=fscr, in0=fscr, scalar1=MAGIC, scalar2=-1.0,
                                    op0=OP.subtract, op1=OP.max)
                    v.drain()
                    v.tensor_scalar(out=land[b][:, 0:ccols].bitcast(BF16)[:, 0:ccols],
                                    in0=fscr, scalar1=1.0, scalar2=None, op0=OP.min)
                kb.on(V, f_q)
                cq = kb.bump(V)
                kb.wait(Q, V, cq)
                stq = kb.dma(Q, dr[wqname][n * 128:(n + 1) * 128, :],
                             land[b][:, 0:ccols].bitcast(BF16)[:, 0:ccols], key="pq%d" % b)
                st_done[b] = stq
                free_at[b] = cq
            kb.wait(Q, stq[0], stq[1])
            return wst[:, 35:36]

        kb.barrier()
        mc1 = emit_prequant("w1", "w1q", 8, 4096, 0)

        def f_savemc1(v):
            v.tensor_copy(out=wst[:, 36:37], in_=wst[:, 35:36])
        kb.on(V, f_savemc1)
        kb.bump(V)
        mc2 = emit_prequant("w2", "w2q", 32, 1024, 1)

        def f_savemc2(v):
            v.tensor_copy(out=wst[:, 37:38], in_=wst[:, 35:36])
        kb.on(V, f_savemc2)
        cmc = kb.bump(V)
        kb.barrier()
        for kname in ("pq0", "pq1"):
            if kname in kb.dma_sem_map:
                sm = kb.dma_sem_map[kname]
                for e_ in (V, S, G, TE):
                    kb.wait(e_, sm, kb.dma_cnt[sm])
        ldcv = kb.dma(Q, cvw[:], dr["convw"], key="cv")
        ldcb = kb.dma(Q, cvb[:], dr["convb"], key="cv")

        def emit_ffn(src_dram, dst_dram):
            win_free2 = [0, 0, 0, 0]
            for i in range(NT):
                slot = i % 4
                kb.wait(Q, V, win_free2[slot])
                if kb.win_store[slot] is not None:
                    kb.wait(Q, kb.win_store[slot][0], kb.win_store[slot][1])
                ld = kb.dma(Q, xw[:, slot, :], src_dram[i * 128:(i + 1) * 128, :], key="win%d" % slot)
                kb.wait(V, ld[0], ld[1])

                def f_stat(v, i=i, slot=slot):
                    v.bn_stats(out=stat6[:, 0:6], in_=xw[:, slot, 0:512])
                    v.bn_stats(out=stat6[:, 6:12], in_=xw[:, slot, 512:1024])
                    v.drain()
                    v.bn_aggr(out=mv[:, i, :], in_=stat6[:])
                    v.drain()
                kb.on(V, f_stat)
                c1 = kb.bump(V)
                kb.wait(S, V, c1)

                def f_sq(sx, i=i):
                    sx.activation(out=rst[:, i, 0:1], in_=mv[:, i, 1:2],
                                  func=AF.Sqrt, bias=LN_EPS, scale=1.0)
                kb.on(S, f_sq)
                c2 = kb.bump(S)
                kb.wait(V, S, c2)

                def f_norm(v, i=i, slot=slot):
                    v.reciprocal(out=rst[:, i, 1:2], in_=rst[:, i, 0:1])
                    v.drain()
                    v.tensor_scalar(out=sc4[:], in0=xw[:, slot, :], scalar1=mv[:, i, 0:1],
                                    scalar2=rst[:, i, 1:2], op0=OP.subtract, op1=OP.mult)
                    v.drain()
                    v.tensor_reduce(out=ffs[:, i, 0:1], in_=sc4[:], axis=mybir.AxisListType.X,
                                    op=OP.max, apply_absolute_value=True)
                    v.drain()
                    v.tensor_scalar(out=ffs[:, i, 1:2], in0=ffs[:, i, 0:1], scalar1=1e-5,
                                    scalar2=None, op0=OP.max)
                    v.drain()
                    v.reciprocal(out=ffs[:, i, 0:1], in_=ffs[:, i, 1:2])
                    v.drain()
                    v.tensor_scalar(out=ffs[:, i, 0:1], in0=ffs[:, i, 0:1], scalar1=127.0,
                                    scalar2=None, op0=OP.mult)
                    v.tensor_tensor(out=ffs[:, i, 1:2], in0=ffs[:, i, 1:2], in1=wst[:, 36:37], op=OP.mult)
                    v.drain()
                    v.tensor_scalar(out=ffs[:, i, 1:2], in0=ffs[:, i, 1:2], scalar1=1.0 / 127.0,
                                    scalar2=None, op0=OP.mult)
                kb.on(V, f_norm)
                c3 = kb.bump(V)

                def f_k1(v, i=i):
                    v.tensor_scalar(out=scD[:], in0=sc4[:], scalar1=ffs[:, i, 0:1],
                                    scalar2=MAGIC, op0=OP.mult, op1=OP.add)
                    v.drain()
                    v.tensor_scalar(out=xnb[:], in0=scD[:], scalar1=MAGIC,
                                    scalar2=ffs[:, i, 1:2], op0=OP.subtract, op1=OP.mult)
                kb.on(V, f_k1)
                cnorm = kb.bump(V)
                win_free2[slot] = cnorm
                kb.wait(TE, V, cnorm)
                half = i % 2
                pview = psA[:, half * 512:(half + 1) * 512].bitcast(BF16)
                for dc in range(8):
                    def f_tp(t, dc=dc, pview=pview):
                        t.transpose(pview[:, dc * 128:(dc + 1) * 128],
                                    xnb[:, dc * 128:(dc + 1) * 128], identb[:])
                    kb.on(TE, f_tp)
                ctp = kb.bump(TE)
                kb.wait(V, TE, ctp)

                def f_evac(v, i=i, pview=pview):
                    v.tensor_copy(out=k1T[:, :, i * 128:(i + 1) * 128],
                                  in_=pview.rearrange("p (a b) -> p a b", a=8))
                kb.on(V, f_evac)
                pe_ok = kb.bump(V)
                kb.wait(TE, V, pe_ok)
            kb.barrier()
            kb.wait(V, ldcb[0], ldcb[1])
            kb.wait(G, ldcb[0], ldcb[1])
            kb.wait(S, ldcb[0], ldcb[1])

            def f_hpinit(v):
                v.memset(hpad[:, :, :], 0.0)
                v.drain()
            kb.on(V, f_hpinit)
            chp = kb.bump(V)
            kb.wait(S, V, chp)
            st_res = None
            for tc in range(4):
                wf1_ld = {}
                wf1_free = [0, 0, 0, 0]

                def load_wf1(cc):
                    b = cc % 4
                    kb.wait(Q, TE, wf1_free[b])
                    wf1_ld[cc] = kb.dma(Q, wf1[b][:], dr["w1q"][:, cc * 128:(cc + 1) * 128]
                                        .rearrange("(a p) b -> p a b", p=128), key="wf1%d" % b)
                for c0 in range(4):
                    load_wf1(c0)
                amt_done = 0
                h2 = bigA.rearrange("p (a b) -> p a b", a=32)
                hevac_done = {0: 0, 1: 0, 2: 0, 3: 0}
                conv_v_done = {0: 0, 1: 0, 2: 0, 3: 0}
                scd_free = {0: 0, 1: 0}
                pend_silu = None

                def emit_silu_amt(pend):
                    cc_, par_, ccv_ = pend
                    kb.wait(S, V, ccv_)

                    def f_silu(sx, cc=cc_, par=par_):
                        sx.activation(out=h2[:, cc, :], in_=scD[:, par * 512:(par + 1) * 512],
                                      func=AF.Silu, bias=cvb[:, cc:cc + 1], scale=1.0)
                    kb.on(S, f_silu)
                    csl = kb.bump(S)
                    kb.wait(V, S, csl)

                    def f_amt(v, cc=cc_):
                        if cc == 0:
                            v.tensor_copy(out=amt[:], in_=h2[:, 0, :])
                            v.drain()
                            v.tensor_copy(out=amtn[:], in_=h2[:, 0, :])
                        else:
                            v.tensor_tensor(out=amt[:], in0=amt[:], in1=h2[:, cc, :], op=OP.max)
                            v.drain()
                            v.tensor_tensor(out=amtn[:], in0=amtn[:], in1=h2[:, cc, :], op=OP.min)
                        v.drain()
                    kb.on(V, f_amt)
                    return kb.bump(V)

                HALO = [1024, 1536, 1026, 1538]
                cm_hist = {}
                for it in range(34):
                    if it < 32:
                        cc = it
                        b = cc % 4
                        ld = wf1_ld[cc]
                        kb.wait(TE, ld[0], ld[1])
                        pslot = cc % 4
                        pmain = psA[:, pslot * 512:(pslot + 1) * 512]
                        phalo = psB[:, HALO[pslot]:HALO[pslot] + 2]
                        kb.wait(TE, S, hevac_done[pslot])
                        for dc in range(8):
                            def f_mm(t, dc=dc, cc=cc, tc=tc, b=b, pmain=pmain, phalo=phalo):
                                t.matmul(pmain, wf1[b][:, dc, :], k1T[:, dc, tc * 512:(tc + 1) * 512],
                                         start=(dc == 0), stop=(dc == 7))
                                if tc == 0:
                                    t.matmul(phalo[:, 1:2], wf1[b][:, dc, :],
                                             k1T[:, dc, 512:513], start=(dc == 0), stop=(dc == 7))
                                elif tc == 3:
                                    t.matmul(phalo[:, 0:1], wf1[b][:, dc, :],
                                             k1T[:, dc, tc * 512 - 1:tc * 512], start=(dc == 0), stop=(dc == 7))
                                else:
                                    t.matmul(phalo, wf1[b][:, dc, :],
                                             k1T[:, dc, tc * 512 - 1:tc * 512 + 513:513],
                                             start=(dc == 0), stop=(dc == 7))
                            kb.on(TE, f_mm)
                        wf1_free[b] = kb.cnt["t"] + 1
                        cm_hist[cc] = kb.bump(TE)
                        if cc + 4 < 32:
                            load_wf1(cc + 4)
                    if it >= 2:
                        cc = it - 2
                        b = cc % 4
                        par = cc % 2
                        pslot = cc % 4
                        pmain = psA[:, pslot * 512:(pslot + 1) * 512]
                        phalo = psB[:, HALO[pslot]:HALO[pslot] + 2]
                        kb.wait(S, TE, cm_hist[min(cc + 2, 31)])
                        kb.wait(S, V, conv_v_done[b])

                        def f_hevac(sx, cc=cc, tc=tc, pmain=pmain, phalo=phalo, b=b):
                            sx.activation(out=hpad[:, b, 1:513], in_=pmain, func=AF.Copy)
                            if tc == 0:
                                sx.activation(out=hpad[:, b, 513:514], in_=phalo[:, 1:2], func=AF.Copy)
                            elif tc == 3:
                                sx.activation(out=hpad[:, b, 0:1], in_=phalo[:, 0:1], func=AF.Copy)
                            else:
                                sx.activation(out=hpad[:, b, 0:514:513], in_=phalo, func=AF.Copy)
                        kb.on(S, f_hevac)
                        hevac_done[pslot] = kb.bump(S)
                        kb.wait(S, V, scd_free[par])

                        def f_tap1(sx, cc=cc, b=b, par=par):
                            sx.drain()
                            sx.activation(out=scD[:, par * 512:par * 512 + 512], in_=hpad[:, b, 0:512],
                                          func=AF.Copy, scale=cvw[:, cc, 0:1])
                            sx.drain()
                        kb.on(S, f_tap1)
                        che = kb.bump(S)
                        kb.wait(V, S, che)

                        def f_conv(v, cc=cc, b=b, tc=tc, par=par):
                            sd = scD[:, par * 512:par * 512 + 512]
                            if tc == 0:
                                v.memset(sd[:, 0:1], 0.0)
                                v.drain()
                            v.scalar_tensor_tensor(out=sd, in0=hpad[:, b, 1:513],
                                                   scalar=cvw[:, cc, 1:2], in1=sd,
                                                   op0=OP.mult, op1=OP.add)
                            v.drain()
                            if tc == 3:
                                v.memset(hpad[:, b, 513:514], 0.0)
                                v.drain()
                            v.scalar_tensor_tensor(out=sd, in0=hpad[:, b, 2:514],
                                                   scalar=cvw[:, cc, 2:3], in1=sd,
                                                   op0=OP.mult, op1=OP.add)
                        kb.on(V, f_conv)
                        ccv = kb.bump(V)
                        conv_v_done[b] = ccv
                        scd_free[par] = ccv
                        if pend_silu is not None:
                            amt_done = emit_silu_amt(pend_silu)
                        pend_silu = (cc, par, ccv)
                amt_done = emit_silu_amt(pend_silu)

                if tc == 0 and "t_h2" in taps:
                    kb.wait(Q, V, amt_done)
                    sth = kb.dma(Q, taps["t_h2"].rearrange("(a p) b -> p a b", p=128),
                                 bigA.rearrange("p (a b) -> p a b", a=32), key="tap")
                    kb.wait(Q, sth[0], sth[1])
                def f_amcomb(v):
                    v.scalar_tensor_tensor(out=amt[:], in0=amtn[:], scalar=-1.0, in1=amt[:],
                                           op0=OP.mult, op1=OP.max)
                kb.on(V, f_amcomb)
                amt_done = kb.bump(V)
                kb.wait(TE, V, amt_done)
                for bblk in range(4):
                    def f_tp(t, bblk=bblk):
                        t.transpose(psB[:, 1024 + bblk * 128:1024 + (bblk + 1) * 128],
                                    amt[:, bblk * 128:(bblk + 1) * 128], identf[:])
                    kb.on(TE, f_tp)
                ctp = kb.bump(TE)
                kb.wait(V, TE, ctp)

                def f_am2(v, tc=tc):
                    for bblk in range(4):
                        i = tc * 4 + bblk
                        v.tensor_reduce(out=ffs[:, i, 2:3],
                                        in_=psB[:, 1024 + bblk * 128:1024 + (bblk + 1) * 128],
                                        axis=mybir.AxisListType.X, op=OP.max)
                    v.drain()
                    for bblk in range(4):
                        i = tc * 4 + bblk
                        v.tensor_scalar(out=ffs[:, i, 2:3], in0=ffs[:, i, 2:3], scalar1=1e-5,
                                        scalar2=None, op0=OP.max)
                    v.drain()
                    for bblk in range(4):
                        i = tc * 4 + bblk
                        v.tensor_tensor(out=ffs[:, i, 3:4], in0=ffs[:, i, 2:3], in1=wst[:, 37:38], op=OP.mult)
                        v.reciprocal(out=ffs[:, i, 2:3], in_=ffs[:, i, 2:3])
                    v.drain()
                    for bblk in range(4):
                        i = tc * 4 + bblk
                        v.tensor_scalar(out=ffs[:, i, 3:4], in0=ffs[:, i, 3:4], scalar1=1.0 / 127.0,
                                        scalar2=None, op0=OP.mult)
                        v.tensor_scalar(out=ffs[:, i, 2:3], in0=ffs[:, i, 2:3], scalar1=127.0,
                                        scalar2=None, op0=OP.mult)
                    v.drain()
                kb.on(V, f_am2)
                cam = kb.bump(V)
                kb.wait(TE, V, cam)
                for bblk in range(4):
                    def f_tps(t, bblk=bblk, tc=tc):
                        t.transpose(psB[:, 1024 + bblk * 128:1024 + bblk * 128 + 128][0:1, :]
                                    if False else psB[0:1, 1024 + bblk * 128:1024 + (bblk + 1) * 128],
                                    ffs[:, tc * 4 + bblk, 2:3], identf[:])
                    kb.on(TE, f_tps)
                ctps = kb.bump(TE)
                kb.wait(V, TE, ctps)

                def f_srow(v):
                    v.tensor_copy(out=srow[0:1, :], in_=psB[0:1, 1024:1536])
                kb.on(V, f_srow)
                csr = kb.bump(V)
                kb.wait(TE, V, csr)

                def f_bc(t):
                    t.matmul(psB[:, 1536:2048], onesf[0:1, :], srow[0:1, :], start=True, stop=True)
                kb.on(TE, f_bc)
                cbc = kb.bump(TE)
                kb.wait(V, TE, cbc)

                def f_s2bc(v):
                    v.tensor_copy(out=s2bc[:], in_=psB[:, 1536:2048])
                kb.on(V, f_s2bc)
                cs2 = kb.bump(V)

                k2_done = 0
                for cc in range(32):
                    def f_k2(v, cc=cc):
                        if cc == 0:
                            v.drain()
                        v.tensor_tensor(out=sc4[:, 0:512], in0=h2[:, cc, :], in1=s2bc[:], op=OP.mult)
                        v.drain()
                        v.tensor_scalar(out=sc4[:, 0:512], in0=sc4[:, 0:512], scalar1=MAGIC,
                                        scalar2=MAGIC, op0=OP.add, op1=OP.subtract)
                        v.drain()
                        v.tensor_copy(out=k2v[:, cc, :], in_=sc4[:, 0:512])
                        v.drain()
                    kb.on(V, f_k2)
                k2_done = kb.bump(V)

                wf2_ld = {}
                wf2_free = [0, 0]

                def load_wf2(do):
                    b = do % 2
                    kb.wait(Q, TE, wf2_free[b])
                    wf2_ld[do] = kb.dma(Q, wf2[b][:], dr["w2q"][:, do * 128:(do + 1) * 128]
                                        .rearrange("(a p) b -> p a b", p=128), key="wf2%d" % b)
                load_wf2(0)
                load_wf2(1)
                kb.wait(TE, V, k2_done)
                o2_evac = {0: 0, 1: 0}
                for do in range(8):
                    b = do % 2
                    ld = wf2_ld[do]
                    kb.wait(TE, ld[0], ld[1])
                    pslot = do % 2
                    ps = psB[:, pslot * 512:(pslot + 1) * 512]
                    kb.wait(TE, V, o2_evac[pslot])
                    for cc in range(32):
                        def f_mm(t, ps=ps, cc=cc, do=do, b=b):
                            t.matmul(ps, wf2[b][:, cc, :], k2v[:, cc, :],
                                     start=(cc == 0), stop=(cc == 31))
                        kb.on(TE, f_mm)
                    wf2_free[b] = kb.cnt["t"] + 1
                    cm = kb.bump(TE)
                    if do + 2 < 8:
                        load_wf2(do + 2)
                    kb.wait(V, TE, cm)

                    def f_ev(v, ps=ps, do=do):
                        v.tensor_copy(out=o2T[:, do, :], in_=ps)
                    kb.on(V, f_ev)
                    o2_evac[pslot] = kb.bump(V)
                co2 = o2_evac[1]
                kb.wait(TE, V, co2)
                for bblk in range(4):
                    i = tc * 4 + bblk
                    slot = i % 4
                    kb.wait(Q, V, win_free2[slot])
                    if kb.win_store[slot] is not None:
                        kb.wait(Q, kb.win_store[slot][0], kb.win_store[slot][1])
                    ld = kb.dma(Q, xw[:, slot, :], src_dram[i * 128:(i + 1) * 128, :], key="win%d" % slot)
                    pview = psA[:, 0:512].bitcast(BF16)
                    for do in range(8):
                        def f_tp(t, do=do, bblk=bblk, pview=pview):
                            t.transpose(pview[:, do * 128:(do + 1) * 128],
                                        o2T[:, do, bblk * 128:(bblk + 1) * 128], identb[:])
                        kb.on(TE, f_tp)
                    ctp = kb.bump(TE)
                    kb.wait(V, TE, ctp)
                    kb.wait(V, ld[0], ld[1])

                    def f_res(v, i=i, slot=slot, pview=pview):
                        v.tensor_scalar(out=scD[:], in0=pview.rearrange("p (a b) -> p a b", a=8),
                                        scalar1=ffs[:, i, 3:4], scalar2=None, op0=OP.mult)
                        v.drain()
                        v.tensor_tensor(out=xw[:, slot, :], in0=xw[:, slot, :], in1=scD[:], op=OP.add)
                    kb.on(V, f_res)
                    cres = kb.bump(V)
                    win_free2[slot] = cres
                    kb.wait(TE, V, cres)
                    kb.wait(Q, V, cres)
                    st_res = kb.dma(Q, dst_dram[i * 128:(i + 1) * 128, :], xw[:, slot, :], key="win%d" % slot)
                    kb.win_store[slot] = st_res
            kb.barrier()
            kb.wait(Q, st_res[0], st_res[1])


        mkband = shr12[:, 1024:1280]
        mkb0 = shr12[:, 1280:1408]
        mkidiag = shr12[:, 1408:1536]
        mkiprev = shr12[:, 1536:1664]
        rawdrow = shr12[0:1, 2048:2112]
        jjrow = shr12[0:1, 0:2048]

        xT_b = qT
        xf = xhT[:, :, :].rearrange("p a b -> p (a b)")
        ckvT = xf[:, 0:4096].rearrange("p (a b) -> p a b", a=2)
        jslot = xf[:, 0:4096].rearrange("p (a b c) -> p a b c", a=NT, b=8)
        cqT = xf[:, 4096:12288].rearrange("p (a b) -> p a b", a=4)
        qitok = xf[:, 4096:8192].rearrange("p (a b) -> p a b", a=NT)
        jeff = xf[:, 8192:12288].rearrange("p (a b c) -> p a b c", a=NT, b=8)
        hefftok = xf[:, 12288:12800].rearrange("p (a b) -> p a b", a=NT)
        mutok = xf[:, 12800:13824].bitcast(F32).rearrange("p (a b) -> p a b", a=NT)
        kiT = xf[:, 13824:15872]

        bff = bigA[:, :]
        kTb = bff[:, 0:4096].bitcast(BF16).rearrange("p (a b) -> p a b", a=4)
        qallb = bff[:, 4096:8192].bitcast(BF16).rearrange("p (a b) -> p a b", a=NT)
        kstg = bff[:, 8192:12288].bitcast(BF16).rearrange("p (a b) -> p a b", a=NT)
        vstg = bff[:, 12288:16384].bitcast(BF16).rearrange("p (a b) -> p a b", a=NT)
        iacc = bff[:, 8192:10240]
        gout2 = [bff[:, 10240:12416].bitcast(BF16).rearrange("p (a b) -> p a b", a=4),
                 bff[:, 12416:14592].bitcast(BF16).rearrange("p (a b) -> p a b", a=4)]
        vwin = bff[:, 14592:15104].bitcast(BF16).rearrange("p (a b) -> p a b", a=2)
        prodv2 = None
        sbandv = psA[:, 0:2048].rearrange("p (a b) -> p a b", a=8)
        slacc = scA[:, 0:512]
        smv = bff[:, 16128:16384]

        wfl = wAB[:, :, :].rearrange("p a b -> p (a b)")
        jjbc = wfl[:, 4096:6144].bitcast(F32)
        qc_scr = wfl[:, 6144:6272]
        jf_scr = wfl[:, 6272:6400]
        hw_scr = wfl[:, 6400:6656].rearrange("p (a b) -> p a b", a=2)
        o_tok = wfl[:, 6656:7168]
        outTv = wfl[:, 7168:7680].rearrange("p (a b) -> p a b", a=4)
        h2t0 = wfl[:, 7680:7808]
        jf2_scr = wfl[:, 7808:7936]
        tview = tausq[:, :, :].rearrange("p a b -> p (a b)")
        idxq = cands[:, :, :].rearrange("p a b -> p (a b)").bitcast(U32)
        prodb = shr12[:, 2048:3072].bitcast(BF16)
        scE = shr12[:, 2048:3072]
        PI = math.pi
        SCB = float(96.0 ** -0.5)

        def emit_attn_b(src_dram, dst_dram):
            l1 = kb.dma(Q, rawdrow[0:1, 0:16], dr["rawd"], key="ab0")
            l1 = kb.dma(Q, jjrow[0:1, 0:1024], dr["tb_jj"], key="ab0")
            lidw = kb.dma(Q, jjrow[0:1, 1024:1028], dr["idxw"], key="ab0")
            kb.wait(S, lidw[0], lidw[1])
            kb.wait(V, lidw[0], lidw[1])
            kb.wait(G, lidw[0], lidw[1])

            def f_iota(g):
                g.iota(tview[:, 0:32], [[1, 32]], channel_multiplier=0,
                       allow_small_or_imprecise_dtypes=True)
                g.iota(tview[:, 64:65], [[0, 1]], base=-63, channel_multiplier=1,
                       allow_small_or_imprecise_dtypes=True)
                g.iota(tview[:, 65:66], [[0, 1]], base=0, channel_multiplier=1,
                       allow_small_or_imprecise_dtypes=True)
                g.iota(idxq[:, 32:64].bitcast(U32) if False else idxq[:, 32:64],
                       [[1, 32]], base=64, channel_multiplier=0)
            kb.on(G, f_iota)
            cgi = kb.bump(G)
            kb.wait(V, G, cgi)

            def f_delta(sx):
                sx.activation(out=rawdrow[0:1, 16:32], in_=rawdrow[0:1, 0:16], func=AF.Sigmoid)
                sx.drain()
            kb.on(S, f_delta)
            c = kb.bump(S)
            kb.wait(V, S, c)

            def f_d2(v):
                v.tensor_scalar(out=rawdrow[0:1, 16:32], in0=rawdrow[0:1, 16:32],
                                scalar1=-2.0 * PI, scalar2=None, op0=OP.mult)
                v.drain()
                v.tensor_scalar(out=rawdrow[0:1, 32:48], in0=rawdrow[0:1, 16:32],
                                scalar1=-PI, scalar2=None, op0=OP.is_lt)
                v.drain()
                v.scalar_tensor_tensor(out=rawdrow[0:1, 16:32], in0=rawdrow[0:1, 32:48],
                                       scalar=2.0 * PI, in1=rawdrow[0:1, 16:32],
                                       op0=OP.mult, op1=OP.add)
                v.drain()
                v.tensor_scalar(out=rawdrow[0:1, 32:48], in0=rawdrow[0:1, 16:32],
                                scalar1=PI / 2, scalar2=None, op0=OP.add)
                v.drain()
                v.tensor_scalar(out=rawdrow[0:1, 48:64], in0=rawdrow[0:1, 32:48],
                                scalar1=PI, scalar2=None, op0=OP.is_gt)
                v.drain()
                v.scalar_tensor_tensor(out=rawdrow[0:1, 32:48], in0=rawdrow[0:1, 48:64],
                                       scalar=-2.0 * PI, in1=rawdrow[0:1, 32:48],
                                       op0=OP.mult, op1=OP.add)
            kb.on(V, f_d2)
            c = kb.bump(V)
            kb.wait(S, V, c)

            def f_d3(sx):
                sx.activation(out=rawdrow[0:1, 16:32], in_=rawdrow[0:1, 16:32], func=AF.Sin)
                sx.drain()
                sx.activation(out=rawdrow[0:1, 32:48], in_=rawdrow[0:1, 32:48], func=AF.Sin)
                sx.drain()
            kb.on(S, f_d3)
            c = kb.bump(S)
            kb.wait(V, S, c)

            def f_jj(v):
                cj = jjrow[0:1, 0:512].rearrange("p (a b) -> p a b", a=32)
                sj = jjrow[0:1, 512:1024].rearrange("p (a b) -> p a b", a=32)
                cd = rawdrow[0:1, 32:48].unsqueeze(1).broadcast_to((1, 32, 16))
                sd = rawdrow[0:1, 16:32].unsqueeze(1).broadcast_to((1, 32, 16))
                co = jjrow[0:1, 1028:1540].rearrange("p (a b) -> p a b", a=32)
                so = jjrow[0:1, 1540:2052] if False else jjrow[0:1, 1536:2048].rearrange("p (a b) -> p a b", a=32)
                tmp = bff[0:1, 15104:15616].rearrange("p (a b) -> p a b", a=32)
                v.tensor_tensor(out=co, in0=cj, in1=cd, op=OP.mult)
                v.drain()
                v.tensor_tensor(out=tmp, in0=sj, in1=sd, op=OP.mult)
                v.drain()
                v.tensor_tensor(out=co, in0=co, in1=tmp, op=OP.subtract)
                v.drain()
                v.tensor_tensor(out=so, in0=sj, in1=cd, op=OP.mult)
                v.drain()
                v.tensor_tensor(out=tmp, in0=cj, in1=sd, op=OP.mult)
                v.drain()
                v.tensor_tensor(out=so, in0=so, in1=tmp, op=OP.add)
            kb.on(V, f_jj)
            c = kb.bump(V)
            kb.wait(TE, V, c)

            def f_bc(t):
                t.matmul(psB[:, 0:512], onesf[0:1, :], jjrow[0:1, 1028:1540], start=True, stop=True)
                t.matmul(psB[:, 512:1024], onesf[0:1, :], jjrow[0:1, 1536:2048], start=True, stop=True)
                t.matmul(psB[:, 1024:1028], onesf[0:1, :], jjrow[0:1, 1024:1028], start=True, stop=True)
                t.matmul(psB[:, 1028:1060], onesf[0:1, :], rawdrow[0:1, 16:48], start=True, stop=True)
            kb.on(TE, f_bc)
            c = kb.bump(TE)
            kb.wait(V, TE, c)

            def f_bcev(v):
                v.tensor_copy(out=jjbc[:, 0:512], in_=psB[:, 0:512])
                v.tensor_copy(out=jjbc[:, 512:1024], in_=psB[:, 512:1024])
                v.tensor_copy(out=msc[:, 0:4], in_=psB[:, 1024:1028])
                v.tensor_copy(out=msc2[:, 0:32], in_=psB[:, 1028:1060])
                v.drain()
                v.tensor_scalar(out=tview[:, 66:67], in0=tview[:, 65:66], scalar1=64.0,
                                scalar2=None, op0=OP.is_lt)
            kb.on(V, f_bcev)
            cbc = kb.bump(V)
            kb.wait(TE, V, cbc)

            emit_ln_transpose(src_dram, xT_b)
            kb.barrier()

            wfree = [0]

            def wload(nm, off, key="wA"):
                R, C = dr[nm].shape
                a = R // 128
                view = wfl[:, off:off + a * C].rearrange("p (a b) -> p a b", a=a)
                kb.wait(G, TE, wfree[0])
                ld = kb.dma(G, view, dr[nm].rearrange("(a p) b -> p a b", p=128), key=key)
                return view, ld

            pslots = [psB[:, i * 512:(i + 1) * 512] for i in range(3)]
            pslot_free = {0: ("v", 0), 1: ("v", 0), 2: ("v", 0)}
            slot_i = [0]

            def mm_out(lhsT_fn, rhs_fn, nacc, evac_fn, eng=S, width=512, po=128):
                sl = slot_i[0] % 3
                slot_i[0] += 1
                ps = pslots[sl][0:po, 0:width]
                pe_eng, pe_cnt = pslot_free[sl]
                kb.wait(TE, pe_eng, pe_cnt)
                for a in range(nacc):
                    def f_mm(t, a=a, ps=ps):
                        t.matmul(ps, lhsT_fn(a), rhs_fn(a), start=(a == 0), stop=(a == nacc - 1))
                    kb.on(TE, f_mm)
                cm = kb.bump(TE)
                kb.wait(eng, TE, cm)

                def f_ev(e, ps=ps):
                    evac_fn(e, ps)
                kb.on(eng, f_ev)
                ce = kb.bump(eng)
                pslot_free[sl] = (eng, ce)
                return ce

            wv, ld = wload("wdkv", 0)
            kb.wait(TE, ld[0], ld[1])
            for ec in range(2):
                for tc in range(4):
                    mm_out(lambda a, ec=ec, wv=wv: wv[:, a, ec * 128:(ec + 1) * 128],
                           lambda a, tc=tc: xT_b[:, a, tc * 512:(tc + 1) * 512], 8,
                           lambda e, ps, ec=ec, tc=tc: e.activation(
                               out=ckvT[:, ec, tc * 512:(tc + 1) * 512], in_=ps, func=AF.Copy))
            wfree[0] = kb.cnt["t"]
            wuk_v, ld1 = wload("wuk", 2048, key="wB")
            wuv_v, ld2 = wload("wuv", 3072, key="wC")
            kb.wait(TE, ld1[0], ld1[1])
            kb.wait(TE, ld2[0], ld2[1])
            for ec in range(4):
                for tc in range(4):
                    mm_out(lambda a, ec=ec, wuk_v=wuk_v: wuk_v[:, a, ec * 128:(ec + 1) * 128],
                           lambda a, tc=tc: ckvT[:, a, tc * 512:(tc + 1) * 512], 2,
                           lambda e, ps, ec=ec, tc=tc: e.activation(
                               out=kTb[:, ec, tc * 512:(tc + 1) * 512], in_=ps, func=AF.Copy))
            for i in range(NT):
                mm_out(lambda a, i=i: ckvT[:, a, i * 128:(i + 1) * 128],
                       lambda a, wuk_v=wuk_v: wuk_v[:, a, :], 2,
                       lambda e, ps, i=i: e.activation(out=kstg[:, i, :], in_=ps, func=AF.Copy))
                mm_out(lambda a, i=i: ckvT[:, a, i * 128:(i + 1) * 128],
                       lambda a, wuv_v=wuv_v: wuv_v[:, a, :], 2,
                       lambda e, ps, i=i: e.activation(out=vstg[:, i, :], in_=ps, func=AF.Copy))
            wfree[0] = kb.cnt["t"]
            wv, ld = wload("wdq", 0)
            kb.wait(TE, ld[0], ld[1])
            for ec in range(4):
                for tc in range(4):
                    mm_out(lambda a, ec=ec, wv=wv: wv[:, a, ec * 128:(ec + 1) * 128],
                           lambda a, tc=tc: xT_b[:, a, tc * 512:(tc + 1) * 512], 8,
                           lambda e, ps, ec=ec, tc=tc: e.activation(
                               out=cqT[:, ec, tc * 512:(tc + 1) * 512], in_=ps, func=AF.Copy))
            wfree[0] = kb.cnt["t"]
            wuq_v, ld = wload("wuq", 0)
            kb.wait(TE, ld[0], ld[1])
            for i in range(NT):
                mm_out(lambda a, i=i: cqT[:, a, i * 128:(i + 1) * 128],
                       lambda a, wuq_v=wuq_v: wuq_v[:, a, :], 4,
                       lambda e, ps, i=i: e.activation(out=qallb[:, i, :], in_=ps, func=AF.Copy, scale=SCB))
            wfree[0] = kb.cnt["t"]
            wkr_v, ldkr = wload("wkr", 3072, key="wC")
            kb.wait(TE, ldkr[0], ldkr[1])
            for i in range(NT):
                ckr = mm_out(lambda a, i=i: xT_b[:, a, i * 128:(i + 1) * 128],
                             lambda a: wkr_v[:, a, :], 8,
                             lambda e, ps: e.tensor_copy(out=scD[:, 256:288], in_=ps), eng=V, width=32)
                kb.wait(S, V, ckr)

                def f_spk(sx):
                    sx.activation(out=sc4[:, 256:288], in_=scD[:, 256:288], func=AF.Abs)
                    sx.drain()
                    sx.activation(out=sc4[:, 256:288], in_=sc4[:, 256:288], func=AF.Exp, scale=-1.0)
                    sx.drain()
                    sx.activation(out=sc4[:, 256:288], in_=sc4[:, 256:288], func=AF.Ln, bias=1.0)
                    sx.drain()
                kb.on(S, f_spk)
                cspk = kb.bump(S)
                kb.wait(V, S, cspk)

                def f_muk(v, i=i):
                    v.tensor_scalar(out=scD[:, 256:288], in0=scD[:, 256:288], scalar1=0.0,
                                    scalar2=None, op0=OP.max)
                    v.drain()
                    v.tensor_tensor(out=mutok[:, i, :], in0=sc4[:, 256:288], in1=scD[:, 256:288], op=OP.add)
                    v.drain()
                kb.on(V, f_muk)
                cmk = kb.bump(V)
            kb.barrier()
            stg = None
            for i in range(NT):
                stg = kb.dma(Q, dr["gtab"][i * 128:(i + 1) * 128, 0:512], kstg[:, i, :], key="ab0")
                stg = kb.dma(Q, dr["gtab"][i * 128:(i + 1) * 128, 512:1024], vstg[:, i, :], key="ab0")
                stg2 = kb.dma(G, dr["gtab"][i * 128:(i + 1) * 128, 1024:1056],
                              mutok[:, i, :], key="abg")
                stg2 = kb.dma(G, dr["gtab"][i * 128:(i + 1) * 128, 1056:1088],
                              zpad[:], key="abg")
            kb.wait(Q, stg[0], stg[1])
            kb.wait(Q, stg2[0], stg2[1])
            kb.barrier()
            qrstg = bff[:, 8192:10240].bitcast(BF16).rearrange("p (a b) -> p a b", a=NT)
            wqr_v, ldqr = wload("wqr", 2048, key="wB")
            kb.wait(TE, ldqr[0], ldqr[1])
            for i in range(NT):
                mm_out(lambda a, i=i: cqT[:, a, i * 128:(i + 1) * 128],
                       lambda a: wqr_v[:, a, :], 4,
                       lambda e, ps, i=i: e.activation(out=qrstg[:, i, :], in_=ps, func=AF.Copy),
                       width=256)
            kb.barrier()
            tab_free = 0
            for i in range(NT):
                kb.wait(Q, V, tab_free)
                lt = kb.dma(Q, scA[:, 0:32], dr["tb_tq"][i * 128:(i + 1) * 128, :], key="ab0")
                lt = kb.dma(Q, scA[:, 32:64], dr["tb_t2"][i * 128:(i + 1) * 128, :], key="ab0")
                lt = kb.dma(Q, scA[:, 64:96], dr["tb_pp"][i * 128:(i + 1) * 128, :], key="ab0")
                kb.wait(V, lt[0], lt[1])

                def f_cp(v, i=i):
                    v.tensor_copy(out=scD[:, 0:256], in_=qrstg[:, i, :])
                    v.drain()
                kb.on(V, f_cp)
                ccp = kb.bump(V)
                kb.wait(S, V, ccp)

                def f_sp(sx):
                    sx.activation(out=sc4[:, 0:256], in_=scD[:, 0:256], func=AF.Abs)
                    sx.drain()
                    sx.activation(out=sc4[:, 0:256], in_=sc4[:, 0:256], func=AF.Exp, scale=-1.0)
                    sx.drain()
                    sx.activation(out=sc4[:, 0:256], in_=sc4[:, 0:256], func=AF.Ln, bias=1.0)
                    sx.drain()
                kb.on(S, f_sp)
                csp = kb.bump(S)
                kb.wait(V, S, csp)

                def f_build(v, i=i):
                    v.tensor_scalar(out=scD[:, 0:256], in0=scD[:, 0:256], scalar1=0.0,
                                    scalar2=None, op0=OP.max)
                    v.drain()
                    v.tensor_tensor(out=sc4[:, 0:256], in0=sc4[:, 0:256], in1=scD[:, 0:256], op=OP.add)
                    v.drain()
                    v.tensor_scalar(out=sc4[:, 0:256], in0=sc4[:, 0:256], scalar1=SCB,
                                    scalar2=None, op0=OP.mult)
                    v.drain()
                    tqc, tqs = scA[:, 0:16], scA[:, 16:32]
                    cdb, sdb = msc2[:, 16:32], msc2[:, 0:16]
                    v.tensor_tensor(out=scA[:, 96:112], in0=tqc, in1=cdb, op=OP.mult)
                    v.tensor_tensor(out=scA[:, 112:128], in0=tqs, in1=cdb, op=OP.mult)
                    v.drain()
                    v.tensor_tensor(out=msc[:, 4:20], in0=tqs, in1=sdb, op=OP.mult)
                    v.tensor_tensor(out=scA[:, 128:144], in0=tqc, in1=sdb, op=OP.mult)
                    v.drain()
                    v.tensor_tensor(out=scA[:, 96:112], in0=scA[:, 96:112], in1=msc[:, 4:20], op=OP.subtract)
                    v.tensor_tensor(out=scA[:, 112:128], in0=scA[:, 112:128], in1=scA[:, 128:144], op=OP.add)
                    v.drain()

                    def jbuild(a1, a2, cts, sts, dst16a, dst16b):
                        v.tensor_tensor(out=scE[:, 0:128].rearrange("p (a b) -> p a b", a=8),
                                        in0=a1, in1=cts, op=OP.mult)
                        v.tensor_tensor(out=scE[:, 128:256].rearrange("p (a b) -> p a b", a=8),
                                        in0=a2, in1=sts, op=OP.mult)
                        v.drain()
                        v.tensor_tensor(out=dst16a, in0=scE[:, 0:128].rearrange("p (a b) -> p a b", a=8),
                                        in1=scE[:, 128:256].rearrange("p (a b) -> p a b", a=8), op=OP.subtract)
                        v.drain()
                        v.tensor_tensor(out=scE[:, 0:128].rearrange("p (a b) -> p a b", a=8),
                                        in0=a1, in1=sts, op=OP.mult)
                        v.tensor_tensor(out=scE[:, 128:256].rearrange("p (a b) -> p a b", a=8),
                                        in0=a2, in1=cts, op=OP.mult)
                        v.drain()
                        v.tensor_tensor(out=dst16b, in0=scE[:, 0:128].rearrange("p (a b) -> p a b", a=8),
                                        in1=scE[:, 128:256].rearrange("p (a b) -> p a b", a=8), op=OP.add)
                        v.drain()
                    a1 = sc4[:, 0:256].rearrange("p (a b) -> p a b", a=8)[:, :, 0:16]
                    a2 = sc4[:, 0:256].rearrange("p (a b) -> p a b", a=8)[:, :, 16:32]
                    ctb = scA[:, 96:112].unsqueeze(1).broadcast_to((128, 8, 16))
                    stb = scA[:, 112:128].unsqueeze(1).broadcast_to((128, 8, 16))
                    jbuild(a1, a2, ctb, stb, jslot[:, i, :, 0:16], jslot[:, i, :, 16:32])
                    c2b = scA[:, 32:48].unsqueeze(1).broadcast_to((128, 8, 16))
                    s2b2 = scA[:, 48:64].unsqueeze(1).broadcast_to((128, 8, 16))
                    jbuild(a1, a2, c2b, s2b2, jeff[:, i, :, 0:16], jeff[:, i, :, 16:32])
                    mu1f = scD[:, 256:272]
                    mu2f = scD[:, 272:288]
                    v.tensor_copy(out=scD[:, 256:288], in_=mutok[:, i, :])
                    v.drain()
                    ppc, pps = scA[:, 64:80], scA[:, 80:96]
                    v.tensor_tensor(out=scE[:, 0:16], in0=mu1f, in1=ppc, op=OP.mult)
                    v.tensor_tensor(out=scE[:, 16:32], in0=mu2f, in1=pps, op=OP.mult)
                    v.tensor_tensor(out=scE[:, 32:48], in0=mu1f, in1=pps, op=OP.mult)
                    v.tensor_tensor(out=scE[:, 48:64], in0=mu2f, in1=ppc, op=OP.mult)
                    v.drain()
                    v.tensor_tensor(out=hefftok[:, i, 0:16], in0=scE[:, 0:16], in1=scE[:, 16:32], op=OP.subtract)
                    v.tensor_tensor(out=hefftok[:, i, 16:32], in0=scE[:, 32:48], in1=scE[:, 48:64], op=OP.add)
                    v.drain()
                    if i == 0:
                        ct2, st2 = scA[:, 96:112], scA[:, 112:128]
                        v.tensor_tensor(out=scE[:, 0:16], in0=mu1f, in1=ct2, op=OP.mult)
                        v.tensor_tensor(out=scE[:, 16:32], in0=mu2f, in1=st2, op=OP.mult)
                        v.tensor_tensor(out=scE[:, 32:48], in0=mu1f, in1=st2, op=OP.mult)
                        v.tensor_tensor(out=scE[:, 48:64], in0=mu2f, in1=ct2, op=OP.mult)
                        v.drain()
                        v.tensor_tensor(out=qc_scr[:, 0:16], in0=scE[:, 0:16], in1=scE[:, 16:32], op=OP.subtract)
                        v.tensor_tensor(out=qc_scr[:, 16:32], in0=scE[:, 32:48], in1=scE[:, 48:64], op=OP.add)
                        v.drain()
                kb.on(V, f_build)
                tab_free = kb.bump(V)
                if i == 0:
                    kb.wait(TE, V, tab_free)

                    def f_h2t(t):
                        t.transpose(psA[:, 1536:1600].bitcast(BF16)[0:32, 0:128],
                                    qc_scr[:, 0:32], identb[:])
                    kb.on(TE, f_h2t)
                    ch2 = kb.bump(TE)
                    kb.wait(V, TE, ch2)

                    def f_h2ev(v):
                        v.tensor_copy(out=h2t0[0:32, :], in_=psA[:, 1536:1600].bitcast(BF16)[0:32, 0:128])
                        v.drain()
                    kb.on(V, f_h2ev)
                    kb.bump(V)
            wfree[0] = kb.cnt["t"]
            wiq_v, ld1 = wload("widxq", 0)
            wik_v, ld2 = wload("widxk", 2048, key="wB")
            kb.wait(TE, ld1[0], ld1[1])
            kb.wait(TE, ld2[0], ld2[1])
            for i in range(NT):
                mm_out(lambda a, i=i: xT_b[:, a, i * 128:(i + 1) * 128],
                       lambda a, wiq_v=wiq_v: wiq_v[:, a, :], 8,
                       lambda e, ps, i=i: e.activation(out=qitok[:, i, :], in_=ps, func=AF.Copy), width=256)
            for tc in range(4):
                mm_out(lambda a, wik_v=wik_v: wik_v[:, a, :],
                       lambda a, tc=tc: xT_b[:, a, tc * 512:(tc + 1) * 512], 8,
                       lambda e, ps, tc=tc: e.activation(
                           out=kiT[0:64, tc * 512:(tc + 1) * 512], in_=ps, func=AF.Copy), po=64)
            kb.barrier()
            for dst, srcnm in ((mkband, "mk_band"), (mkb0, "mk_b0"),
                               (mkidiag, "mk_idiag"), (mkiprev, "mk_iprev")):
                lmk = kb.dma(Q, dst, dr[srcnm], key="ab0")
            kb.wait(V, lmk[0], lmk[1])
            kb.barrier()

            pb = psA[:, 1536:2048].bitcast(BF16)
            wfree[0] = kb.cnt["t"]
            wout_v, ldwo = wload("wout", 0)
            prev_store = None
            for i in range(NT):
                kw = (i + 1) * 128
                vprev = kb.cnt["v"]
                islot_free = {0: 0, 1: 0}
                for h4 in range(4):
                    kb.wait(TE, V, vprev)

                    def f_tp(t, i=i, h4=h4):
                        t.transpose(pb[0:64, 0:128], qitok[:, i, h4 * 64:(h4 + 1) * 64], identb[:])
                    kb.on(TE, f_tp)
                    ctp = kb.bump(TE)
                    kb.wait(S, TE, ctp)

                    def f_ev(sx):
                        sx.activation(out=qc_scr[0:64, :], in_=pb[0:64, 0:128], func=AF.Copy)
                        sx.drain()
                    kb.on(S, f_ev)
                    cqe = kb.bump(S)
                    kb.wait(TE, S, cqe)
                    for kc in range((kw + 511) // 512):
                        w = min(512, kw - kc * 512)
                        ps = psA[:, (kc % 2) * 512:(kc % 2) * 512 + w]
                        kb.wait(TE, V, islot_free[kc % 2])

                        def f_mm(t, ps=ps, kc=kc, w=w):
                            t.matmul(ps, qc_scr[0:64, :], kiT[0:64, kc * 512:kc * 512 + w],
                                     start=True, stop=True)
                        kb.on(TE, f_mm)
                        cm = kb.bump(TE)
                        kb.wait(V, TE, cm)

                        def f_acc(v, ps=ps, kc=kc, w=w, h4=h4):
                            if h4 == 0:
                                v.tensor_scalar(out=iacc[:, kc * 512:kc * 512 + w], in0=ps,
                                                scalar1=0.0, scalar2=msc[:, h4:h4 + 1],
                                                op0=OP.max, op1=OP.mult)
                            else:
                                v.tensor_scalar(out=scD[:, 0:w], in0=ps, scalar1=0.0,
                                                scalar2=msc[:, h4:h4 + 1], op0=OP.max, op1=OP.mult)
                                v.drain()
                                v.tensor_tensor(out=iacc[:, kc * 512:kc * 512 + w],
                                                in0=iacc[:, kc * 512:kc * 512 + w],
                                                in1=scD[:, 0:w], op=OP.add)
                            v.drain()
                        kb.on(V, f_acc)
                        vprev = kb.bump(V)
                        islot_free[kc % 2] = vprev

                def f_mask(v, i=i, kw=kw):
                    if i >= 1:
                        v.tensor_tensor(out=iacc[:, (i - 1) * 128:i * 128],
                                        in0=iacc[:, (i - 1) * 128:i * 128], in1=mkiprev, op=OP.add)
                        v.drain()
                    v.tensor_tensor(out=iacc[:, i * 128:(i + 1) * 128],
                                    in0=iacc[:, i * 128:(i + 1) * 128], in1=mkidiag, op=OP.add)
                    v.drain()
                    for r in range(4):
                        v.max(out=tview[:, 128:136], in_=iacc[:, 0:kw])
                        v.drain()
                        v.max_index(out=idxq[:, r * 8:(r + 1) * 8], in_max=tview[:, 128:136],
                                    in_values=iacc[:, 0:kw])
                        v.drain()
                        if r < 3:
                            v.match_replace(out=iacc[:, 0:kw], in_to_replace=tview[:, 128:136],
                                            in_values=iacc[:, 0:kw], imm_value=NEG)
                            v.drain()
                    if i == 0:
                        v.tensor_scalar(out=idxq[:, 64:96], in0=tview[:, 0:32],
                                        scalar1=tview[:, 64:65], scalar2=None, op0=OP.is_ge)
                        v.drain()
                        v.copy_predicated(out=idxq[:, 0:32], mask=idxq[:, 64:96],
                                          data=idxq[:, 32:64])
                        v.drain()
                kb.on(V, f_mask)
                cex = kb.bump(V)

                w0t = i - 1 if i >= 1 else 0
                W = 256 if i >= 1 else 128
                pb = psB[:, 512:1024].bitcast(BF16)
                lv = kb.dma(Q, vwin[:, 0, :], dr["gtab"][w0t * 128:(w0t + 1) * 128, 512:1024], key="gat")
                if i >= 1:
                    lv = kb.dma(Q, vwin[:, 1, :], dr["gtab"][i * 128:(i + 1) * 128, 512:1024], key="gat")
                kb.wait(TE, V, cex)
                for w in range(2 if i >= 1 else 1):
                    def f_tph(t, i=i, w=w, w0t=w0t):
                        t.transpose(pb[0:32, 0:128], hefftok[:, w0t + w, :], identb[:])
                    kb.on(TE, f_tph)
                    cth = kb.bump(TE)
                    kb.wait(S, TE, cth)

                    def f_evh(sx, w=w):
                        sx.activation(out=hw_scr[0:32, w, :], in_=pb[0:32, 0:128], func=AF.Copy)
                        sx.drain()
                    kb.on(S, f_evh)
                    chv = kb.bump(S)
                    kb.wait(TE, S, chv)
                for h in range(8):
                    hp = (h % 2) * 64
                    hc = h // 2

                    def f_tpq(t, i=i, h=h, hp=hp):
                        t.transpose(pb[hp:hp + 64, 128:256] if hp else pb[0:64, 128:256],
                                    qallb[:, i, h * 64:(h + 1) * 64], identb[:])
                        t.transpose(pb[0:32, 256:384], jeff[:, i, h, :], identb[:])
                        if i == 0:
                            t.transpose(pb[0:32, 384:512], jslot[:, 0, h, :], identb[:])
                    kb.on(TE, f_tpq)
                    ctq = kb.bump(TE)
                    kb.wait(S, TE, ctq)

                    def f_evq(sx, hp=hp, i=i):
                        sx.activation(out=qc_scr[hp:hp + 64, :], in_=pb[hp:hp + 64, 128:256] if hp else pb[0:64, 128:256], func=AF.Copy)
                        sx.activation(out=jf_scr[0:32, :], in_=pb[0:32, 256:384], func=AF.Copy)
                        if i == 0:
                            sx.activation(out=jf2_scr[0:32, :], in_=pb[0:32, 384:512], func=AF.Copy)
                        sx.drain()
                    kb.on(S, f_evq)
                    cev = kb.bump(S)
                    kb.wait(TE, S, cev)

                    def f_bmm(t, i=i, h=h, hp=hp, hc=hc, w0t=w0t, W=W):
                        t.matmul(psA[:, h * 256:h * 256 + W], qc_scr[hp:hp + 64, :],
                                 kTb[hp:hp + 64, hc, w0t * 128:w0t * 128 + W], start=True, stop=False)
                        t.matmul(psA[:, h * 256:h * 256 + W], jf_scr[0:32, :],
                                 hw_scr[0:32, :, :].rearrange("p a b -> p (a b)")[0:32, 0:W],
                                 start=False, stop=True)
                        if i == 0:
                            t.matmul(psB[:, 0:W], qc_scr[hp:hp + 64, :],
                                     kTb[hp:hp + 64, hc, 0:W], start=True, stop=False)
                            t.matmul(psB[:, 0:W], jf2_scr[0:32, :],
                                     h2t0[0:32, 0:W], start=False, stop=True)
                    kb.on(TE, f_bmm)
                    cbm = kb.bump(TE)
                    kb.wait(V, TE, cbm)

                    def f_bev(v, i=i, h=h, W=W):
                        if i >= 1:
                            v.tensor_tensor(out=sbandv[:, h, :], in0=sbandv[:, h, :],
                                            in1=mkband, op=OP.add)
                        else:
                            v.tensor_copy(out=scD[:, 0:128], in_=sbandv[:, h, 0:128])
                            v.drain()
                            v.tensor_tensor(out=scD[:, 128:256], in0=psB[:, 0:128],
                                            in1=scD[:, 0:128], op=OP.subtract)
                            v.drain()
                            v.scalar_tensor_tensor(out=scD[:, 128:256], in0=scD[:, 128:256],
                                                   scalar=tview[:, 66:67], in1=scD[:, 0:128],
                                                   op0=OP.mult, op1=OP.add)
                            v.drain()
                            v.tensor_tensor(out=sbandv[:, h, 0:128], in0=scD[:, 128:256],
                                            in1=mkb0, op=OP.add)
                            v.drain()
                            v.memset(sbandv[:, h, 128:256], NEG)
                        v.drain()
                    kb.on(V, f_bev)
                    cbv = kb.bump(V)
                    kb.wait(TE, V, cbv)

                def f_minit(v):
                    v.tensor_reduce(out=smv[:, 0:8], in_=sbandv[:, :, :], axis=mybir.AxisListType.X, op=OP.max)
                    v.drain()
                    v.tensor_scalar(out=smv[:, 0:8], in0=smv[:, 0:8], scalar1=32.0, scalar2=None, op0=OP.add)
                    v.memset(smv[:, 8:16], 0.0)
                    v.memset(slacc[:], 0.0)
                    v.drain()
                kb.on(V, f_minit)
                cmi = kb.bump(V)

                def f_bexp(v):
                    v.tensor_tensor(out=sbandv[:, :, :], in0=sbandv[:, :, :],
                                    in1=smv[:, 0:8].unsqueeze(2).broadcast_to((128, 8, 256)), op=OP.subtract)
                    v.drain()
                kb.on(V, f_bexp)
                cbx = kb.bump(V)
                kb.wait(S, V, cbx)

                def f_bexp2(sx):
                    sx.activation(out=prodb[:, 0:2048], in_=sbandv[:, :, :].rearrange("p a b -> p (a b)"),
                                  func=AF.Exp)
                    sx.drain()
                kb.on(S, f_bexp2)
                cbe = kb.bump(S)
                kb.wait(V, S, cbe)

                def f_zband(v):
                    v.tensor_reduce(out=smv[:, 16:24],
                                    in_=prodb[:, 0:2048].rearrange("p (a b) -> p a b", a=8),
                                    axis=mybir.AxisListType.X, op=OP.add)
                    v.drain()
                    v.tensor_tensor(out=smv[:, 8:16], in0=smv[:, 8:16], in1=smv[:, 16:24], op=OP.add)
                    v.drain()
                kb.on(V, f_zband)
                kb.bump(V)
                ssc = shr12[:, 512:768].rearrange("p (a b) -> p a b", a=8)
                vscore_done = {0: 0, 1: 0}
                gprod_done = {0: None, 1: None}
                gb_done = {}

                def issue_gathers(qb):
                    buf = qb % 2
                    kb.wait(G, V, vscore_done[buf])
                    sem = kb.dma_sem_map.setdefault("gag%d" % buf, kb.dma_sem_pool[len(kb.dma_sem_map)])
                    kb.dma_cnt.setdefault(sem, 0)
                    for j in range(4):
                        col = qb * 4 + j
                        kb.dma_cnt[sem] += 16
                        kb.on(G, lambda g, col=col, j=j, sem=sem, buf=buf: g.indirect_dma_start(
                            out=gout2[buf][:, j, :], out_offset=None, in_=dr["gtab"],
                            in_offset=bass.IndirectOffsetOnAxis(ap=idxq[:, col:col + 1], axis=0))
                            .then_inc(kb.sem[sem], 16))
                        gb_done[qb] = (sem, kb.dma_cnt[sem])
                kb.wait(G, V, cex)
                issue_gathers(0)
                for qb in range(8):
                    buf = qb % 2
                    kb.wait(G, gb_done[qb][0], gb_done[qb][1])

                    def f_ghb(g, qb=qb, buf=buf):
                        mu1 = gout2[buf][:, :, 1024:1040]
                        mu2 = gout2[buf][:, :, 1040:1056]
                        cjv = jjbc[:, 0:512].rearrange("p (a b) -> p a b", a=32)[:, qb * 4:(qb + 1) * 4, :]
                        sjv = jjbc[:, 512:1024].rearrange("p (a b) -> p a b", a=32)[:, qb * 4:(qb + 1) * 4, :]
                        t1 = bff[:, 15552:15616].rearrange("p (a b) -> p a b", a=4)
                        t2 = bff[:, 15616:15680].rearrange("p (a b) -> p a b", a=4)
                        hq = bff[:, 15488 + buf * 256:15552 + buf * 256].bitcast(BF16).rearrange("p (a b) -> p a b", a=4)
                        g.tensor_tensor(out=t1, in0=mu1, in1=cjv, op=OP.mult)
                        g.tensor_tensor(out=t2, in0=mu2, in1=sjv, op=OP.mult)
                        g.drain()
                        g.tensor_tensor(out=hq[:, :, 0:16], in0=t1, in1=t2, op=OP.subtract)
                        g.drain()
                        g.tensor_tensor(out=t1, in0=mu1, in1=sjv, op=OP.mult)
                        g.tensor_tensor(out=t2, in0=mu2, in1=cjv, op=OP.mult)
                        g.drain()
                        g.tensor_tensor(out=hq[:, :, 16:32], in0=t1, in1=t2, op=OP.add)
                        g.drain()
                    kb.on(G, f_ghb)
                    cgb = kb.bump(G)
                    kb.wait(V, gb_done[qb][0], gb_done[qb][1])
                    kb.wait(V, G, cgb)

                    def f_slot(v, qb=qb, i=i, buf=buf):
                        hq = bff[:, 15488 + buf * 256:15552 + buf * 256].bitcast(BF16).rearrange("p (a b) -> p a b", a=4)
                        for j in range(4):
                            pv = bff[:, 15104:15488].bitcast(BF16).rearrange("p (a b) -> p a b", a=8)
                            v.tensor_tensor(out=pv[:, :, 0:64],
                                            in0=qallb[:, i, :].rearrange("p (a b) -> p a b", a=8),
                                            in1=gout2[buf][:, j, 0:512].rearrange("p (a b) -> p a b", a=8),
                                            op=OP.mult)
                            v.tensor_tensor(out=pv[:, :, 64:96], in0=jslot[:, i, :, :],
                                            in1=hq[:, j, :].unsqueeze(1).broadcast_to((128, 8, 32)),
                                            op=OP.mult)
                            v.drain()
                            v.tensor_reduce(out=ssc[:, :, qb * 4 + j],
                                            in_=pv, axis=mybir.AxisListType.X, op=OP.add)
                            v.drain()
                        sq = ssc[:, :, qb * 4:(qb + 1) * 4]
                        v.tensor_tensor(out=sq, in0=sq,
                                        in1=smv[:, 0:8].unsqueeze(2).broadcast_to((128, 8, 4)), op=OP.subtract)
                        v.drain()
                    kb.on(V, f_slot)
                    c1 = kb.bump(V)
                    vscore_done[buf] = c1
                    if qb + 1 < 8:
                        issue_gathers(qb + 1)
                    kb.wait(S, V, c1)

                    def f_exp2(sx, qb=qb):
                        sx.activation(out=ssc[:, :, qb * 4:(qb + 1) * 4],
                                      in_=ssc[:, :, qb * 4:(qb + 1) * 4], func=AF.Exp)
                        sx.drain()
                    kb.on(S, f_exp2)
                    c4 = kb.bump(S)
                    kb.wait(V, S, c4)
                    kb.wait(G, S, c4)

                    def f_zadd(v, qb=qb):
                        sq = ssc[:, :, qb * 4:(qb + 1) * 4]
                        v.tensor_reduce(out=smv[:, 16:24], in_=sq, axis=mybir.AxisListType.X, op=OP.add)
                        v.drain()
                        v.tensor_tensor(out=smv[:, 8:16], in0=smv[:, 8:16], in1=smv[:, 16:24], op=OP.add)
                        v.drain()
                    kb.on(V, f_zadd)
                    kb.bump(V)
                    for j in range(4):
                        jb = j % 2
                        if gprod_done[jb] is not None:
                            kb.wait(G, V, gprod_done[jb])

                        def f_prod(g, qb=qb, j=j, buf=buf, jb=jb):
                            aj = ssc[:, :, qb * 4 + j].unsqueeze(2).broadcast_to((128, 8, 64))
                            pvx = ptk[:, jb, 0:1024].bitcast(F32)
                            g.tensor_tensor(out=pvx.rearrange("p (a b) -> p a b", a=8),
                                            in0=gout2[buf][:, j, 512:1024].rearrange("p (a b) -> p a b", a=8),
                                            in1=aj, op=OP.mult)
                            g.drain()
                        kb.on(G, f_prod)
                        cgp = kb.bump(G)
                        kb.wait(V, G, cgp)

                        def f_add(v, jb=jb):
                            pvx = ptk[:, jb, 0:1024].bitcast(F32)
                            v.tensor_tensor(out=slacc[:], in0=slacc[:], in1=pvx, op=OP.add)
                            v.drain()
                        kb.on(V, f_add)
                        gprod_done[jb] = kb.bump(V)
                    if qb == 0:
                        kb.wait(TE, lv[0], lv[1])
                        kb.wait(TE, S, cbe)
                    h = qb

                    def f_tpa(t, h=h, i=i):
                        t.transpose(pb[0:128, 0:128], prodb[:, h * 256:h * 256 + 128], identb[:])
                        if i >= 1:
                            t.transpose(pb[0:128, 128:256], prodb[:, h * 256 + 128:h * 256 + 256], identb[:])
                    kb.on(TE, f_tpa)
                    cta = kb.bump(TE)
                    kb.wait(S, TE, cta)

                    def f_eva(sx, i=i):
                        sx.activation(out=hw_scr[:, 0, :], in_=pb[:, 0:128], func=AF.Copy)
                        if i >= 1:
                            sx.activation(out=hw_scr[:, 1, :], in_=pb[:, 128:256], func=AF.Copy)
                        sx.drain()
                    kb.on(S, f_eva)
                    cve = kb.bump(S)
                    kb.wait(TE, S, cve)

                    def f_avb(t, h=h, i=i):
                        nw = 2 if i >= 1 else 1
                        for w in range(nw):
                            t.matmul(psB[:, 1024 + h * 64:1024 + (h + 1) * 64], hw_scr[:, w, :],
                                     vwin[:, w, h * 64:(h + 1) * 64], start=(w == 0), stop=(w == nw - 1))
                    kb.on(TE, f_avb)
                    kb.bump(TE)
                kb.wait(V, TE, kb.cnt["t"])

                def f_merge(v):
                    v.reciprocal(out=smv[:, 40:48], in_=smv[:, 8:16])
                    v.drain()
                    v.tensor_tensor(out=scE[:, 0:512], in0=psB[:, 1024:1536], in1=slacc[:], op=OP.add)
                    v.drain()
                    for h in range(8):
                        v.tensor_scalar(out=o_tok[:, h * 64:(h + 1) * 64], in0=scE[:, h * 64:(h + 1) * 64],
                                        scalar1=smv[:, 40 + h:41 + h], scalar2=None, op0=OP.mult)
                    v.drain()
                kb.on(V, f_merge)
                cmg = kb.bump(V)
                kb.wait(TE, V, cmg)

                kb.wait(TE, ldwo[0], ldwo[1])
                for c4 in range(4):
                    def f_tpo(t, c4=c4):
                        t.transpose(pb[0:128, 0:128], o_tok[:, c4 * 128:(c4 + 1) * 128], identb[:])
                    kb.on(TE, f_tpo)
                    cto = kb.bump(TE)
                    kb.wait(S, TE, cto)

                    def f_evo(sx, c4=c4):
                        sx.activation(out=outTv[:, c4, :], in_=pb[:, 0:128], func=AF.Copy)
                        sx.drain()
                    kb.on(S, f_evo)
                    cvo = kb.bump(S)
                    kb.wait(TE, S, cvo)
                slot = i % 4
                kb.wait(Q, V, kb.cnt["v"]) if False else None
                if kb.win_store[slot] is not None:
                    kb.wait(Q, kb.win_store[slot][0], kb.win_store[slot][1])
                lxw = kb.dma(Q, xw[:, slot, :], src_dram[i * 128:(i + 1) * 128, :], key="win%d" % slot)
                for eh in range(2):
                    ps = psB[:, 1024 + eh * 512:1024 + (eh + 1) * 512]
                    for c4 in range(4):
                        def f_mmo(t, ps=ps, c4=c4, eh=eh):
                            t.matmul(ps, outTv[:, c4, :], wout_v[:, c4, eh * 512:(eh + 1) * 512],
                                     start=(c4 == 0), stop=(c4 == 3))
                        kb.on(TE, f_mmo)
                    cmo = kb.bump(TE)
                    kb.wait(V, TE, cmo)
                    if eh == 0:
                        kb.wait(V, lxw[0], lxw[1])

                    def f_res(v, ps=ps, slot=slot, eh=eh):
                        v.tensor_tensor(out=xw[:, slot, eh * 512:(eh + 1) * 512],
                                        in0=xw[:, slot, eh * 512:(eh + 1) * 512], in1=ps, op=OP.add)
                        v.drain()
                    kb.on(V, f_res)
                    crs = kb.bump(V)
                    kb.wait(TE, V, crs)
                kb.wait(Q, V, kb.cnt["v"])
                sto = kb.dma(Q, dst_dram[i * 128:(i + 1) * 128, :], xw[:, slot, :], key="win%d" % slot)
                kb.win_store[slot] = sto
            kb.barrier()

        emit_ffn(dr["strm"], dr["strm"])
        emit_attn_b(dr["strm"], dr["strm"])
        emit_ffn(dr["strm"], dr["out"])
        kb.barrier()
        for kname, sm in kb.dma_sem_map.items():
            kb.wait(Q, sm, kb.dma_cnt[sm])

        with nc.Block() as block:
            @block.vector
            def _(eng):
                for fn in kb.ops["v"]:
                    fn(eng)

            @block.scalar
            def _(eng):
                for fn in kb.ops["s"]:
                    fn(eng)

            @block.gpsimd
            def _(eng):
                for fn in kb.ops["g"]:
                    fn(eng)

            @block.tensor
            def _(eng):
                for fn in kb.ops["t"]:
                    fn(eng)

            @block.sync
            def _(eng):
                for fn in kb.ops["q"]:
                    fn(eng)

    return nc


def attn_b_host_tables(inputs):
    th = 1.0 / 10000.0 ** (2.0 * np.arange(16) / 32.0)
    t = np.arange(T)[:, None]
    tq = np.concatenate([np.cos(t * th), np.sin(t * th)], 1).astype(np.float32)
    t2 = np.concatenate([np.cos(2 * t * th), np.sin(2 * t * th)], 1).astype(np.float32)
    pp = np.concatenate([np.cos((t + 63) * th), np.sin((t + 63) * th)], 1).astype(np.float32)
    j = np.arange(64, 96)[:, None]
    jj = np.concatenate([np.cos(j * th).reshape(-1), np.sin(j * th).reshape(-1)])[None, :].astype(np.float32)
    p_ = np.arange(128)[:, None]
    r_ = np.arange(256)[None, :]
    mk_band = np.where((r_ >= p_ + 65) & (r_ <= p_ + 128), 0.0, NEG).astype(np.float32)
    c_ = np.arange(128)[None, :]
    mk_b0 = np.where(p_ < 64, np.where(c_ < 64, 0.0, NEG),
                     np.where((c_ >= p_ - 63) & (c_ <= p_), 0.0, NEG)).astype(np.float32)
    mk_idiag = np.where(c_ >= p_ - 63, NEG, 0.0).astype(np.float32)
    mk_iprev = np.where(c_ >= p_ + 65, NEG, 0.0).astype(np.float32)
    d = {
        "wdkv": inputs["L0_attn_b_W_dkv"], "wuk": inputs["L0_attn_b_W_uk"],
        "wuv": inputs["L0_attn_b_W_uv"], "wdq": inputs["L0_attn_b_W_dq"],
        "wuq": inputs["L0_attn_b_W_uq"], "wqr": inputs["L0_attn_b_W_qr"],
        "wkr": inputs["L0_attn_b_W_kr"], "widxq": inputs["L0_attn_b_idx_wq"],
        "widxk": inputs["L0_attn_b_idx_wk"], "wout": inputs["L0_attn_b_W_out"],
        "idxw": np.asarray(inputs["L0_attn_b_idx_w"]).reshape(1, 4),
        "rawd": np.asarray(inputs["L0_attn_b_raw_delta"]).reshape(1, 16),
        "tb_tq": tq, "tb_t2": t2, "tb_pp": pp, "tb_jj": jj,
        "mk_band": mk_band, "mk_b0": mk_b0, "mk_idiag": mk_idiag, "mk_iprev": mk_iprev,
    }
    return {k: np.ascontiguousarray(np.asarray(v, dtype=np.float32)) for k, v in d.items()}


def kernel(**inputs):
    x = np.ascontiguousarray(inputs["L0_attn_a_x"], dtype=np.float32)
    host = {
        "wq": np.ascontiguousarray(inputs["L0_attn_a_Wq"].T),
        "wk": np.ascontiguousarray(inputs["L0_attn_a_Wk"].T),
        "wv": np.ascontiguousarray(inputs["L0_attn_a_Wv"].T),
        "wo": np.ascontiguousarray(inputs["L0_attn_a_Wo"].T),
        "w1": np.ascontiguousarray(inputs["L0_ffn_w1"].T),
        "w2": np.ascontiguousarray(inputs["L0_ffn_w2"].T),
        "convw": np.ascontiguousarray(
            np.asarray(inputs["L0_ffn_conv_w"]).reshape(32, 128, 3).transpose(1, 0, 2)),
        "convb": np.ascontiguousarray(np.asarray(inputs["L0_ffn_conv_b"]).reshape(32, 128).T),
    }
    host.update(attn_b_host_tables(inputs))
    nc = build_core_program(DEBUG_TAPS)
    in_maps = [dict(host, x=x[c]) for c in range(8)]
    res = run_bass_kernel_spmd(nc, in_maps, core_ids=list(range(8)))
    return np.stack([r["out"] for r in res.results], axis=0)


# version 20 (final)
# speedup vs baseline: 1.0282x; 1.0081x over previous
import math
import numpy as np
import concourse.bass as bass
import concourse.mybir as mybir
from concourse.bass_utils import run_bass_kernel_spmd
from concourse.alu_op_type import AluOpType as OP
from bass_rust import ReduceOp
from contextlib import ExitStack

F32 = mybir.dt.float32
BF16 = mybir.dt.bfloat16
I32 = mybir.dt.int32
U32 = mybir.dt.uint32
AF = mybir.ActivationFunctionType

T = 2048
D = 1024
NT = 16
HA = 16
NEG = -1e30
MICH_ITERS = 10
LN_EPS = 1e-5
MAGIC = 3.0 * 2 ** 22

ENGS = ("v", "s", "g", "t", "q")
DEBUG_TAPS = []


class KB:
    def __init__(self, nc):
        self.nc = nc
        self.ops = {e: [] for e in ENGS}
        self.sem = {}
        self.cnt = {e: 0 for e in ENGS}
        self.dma_cnt = {}
        self.dma_sem_pool = ["d%d" % i for i in range(28)]
        self.dma_sem_map = {}
        self.win_store = [None, None, None, None]

    def on(self, e, fn):
        self.ops[e].append(fn)

    def bump(self, e, drain=True):
        self.cnt[e] += 1
        tgt = self.cnt[e]

        def fn(eng):
            if drain and e in ("v", "s", "g", "t"):
                eng.drain()
            eng.sem_inc(self.sem[e], 1)
        self.on(e, fn)
        return tgt

    def wait(self, e, producer_sem, count):
        if count <= 0:
            return
        self.on(e, lambda eng: eng.wait_ge(self.sem[producer_sem], count))

    def barrier(self, engines=("v", "s", "g", "t", "q")):
        marks = {e: self.bump(e) for e in engines}
        for e in engines:
            for p, c in marks.items():
                if p != e:
                    self.wait(e, p, c)

    def dma(self, queue_eng, out_ap, in_ap, key="st"):
        if key not in self.dma_sem_map:
            idx = len(self.dma_sem_map)
            assert idx < len(self.dma_sem_pool), "out of dma sems"
            self.dma_sem_map[key] = self.dma_sem_pool[idx]
            self.dma_cnt[self.dma_sem_map[key]] = 0
        sem = self.dma_sem_map[key]
        self.dma_cnt[sem] += 16
        tgt = self.dma_cnt[sem]
        self.on(queue_eng, lambda eng: eng.dma_start(out=out_ap, in_=in_ap).then_inc(self.sem[sem], 16))
        return sem, tgt


def build_core_program(debug_taps=()):
    nc = bass.Bass("TRN2", target_bir_lowering=False)
    kb = KB(nc)
    V, S, G, TE, Q = "v", "s", "g", "t", "q"

    dr = {}
    dr["x"] = nc.dram_tensor("x", [T, D], F32, kind="ExternalInput").ap()
    for nm in ("wq", "wk", "wv", "wo"):
        dr[nm] = nc.dram_tensor(nm, [D, D], F32, kind="ExternalInput").ap()
    dr["out"] = nc.dram_tensor("out", [T, D], F32, kind="ExternalOutput").ap()
    dr["strm"] = nc.dram_tensor("strm", [T, D], F32).ap()
    dr["w1"] = nc.dram_tensor("w1", [D, 4096], F32, kind="ExternalInput").ap()
    dr["w2"] = nc.dram_tensor("w2", [4096, D], F32, kind="ExternalInput").ap()
    dr["convw"] = nc.dram_tensor("convw", [128, 32, 3], F32, kind="ExternalInput").ap()
    dr["convb"] = nc.dram_tensor("convb", [128, 32], F32, kind="ExternalInput").ap()
    dr["w1q"] = nc.dram_tensor("w1q", [D, 4096], BF16).ap()
    dr["w2q"] = nc.dram_tensor("w2q", [4096, D], BF16).ap()
    for nm, shp in (("wdkv", [D, 256]), ("wuk", [256, 512]), ("wuv", [256, 512]),
                    ("wdq", [D, 512]), ("wuq", [512, 512]), ("wqr", [512, 256]),
                    ("wkr", [D, 32]), ("widxq", [D, 256]), ("widxk", [D, 64]),
                    ("wout", [512, D]), ("idxw", [1, 4]), ("rawd", [1, 16])):
        dr[nm] = nc.dram_tensor(nm, shp, F32, kind="ExternalInput").ap()
    for nm in ("tb_tq", "tb_t2", "tb_pp"):
        dr[nm] = nc.dram_tensor(nm, [T, 32], F32, kind="ExternalInput").ap()
    dr["tb_jj"] = nc.dram_tensor("tb_jj", [1, 1024], F32, kind="ExternalInput").ap()
    dr["mk_band"] = nc.dram_tensor("mk_band", [128, 256], F32, kind="ExternalInput").ap()
    dr["mk_b0"] = nc.dram_tensor("mk_b0", [128, 128], F32, kind="ExternalInput").ap()
    dr["mk_idiag"] = nc.dram_tensor("mk_idiag", [128, 128], F32, kind="ExternalInput").ap()
    dr["mk_iprev"] = nc.dram_tensor("mk_iprev", [128, 128], F32, kind="ExternalInput").ap()
    dr["gtab"] = nc.dram_tensor("gtab", [T, 1088], BF16).ap()
    taps = {}
    for nm, shape, dt in debug_taps:
        taps[nm] = nc.dram_tensor(nm, shape, dt, kind="ExternalOutput").ap()

    with ExitStack() as ctx:
        sbuf = lambda n_, s, d: ctx.enter_context(nc.sbuf_tensor(n_, s, d))
        xw = sbuf("xw", [128, 4, D], F32)
        xhT = sbuf("xhT", [128, 8, T], BF16)
        qT = sbuf("qT", [128, 8, T], BF16)
        bigA = sbuf("bigA", [128, 16384], F32)
        kT = bigA[:, 0:8192].bitcast(BF16).rearrange("p (a b) -> p a b", a=8)
        vtok = bigA[:, 8192:16384].bitcast(BF16).rearrange("p (a b) -> p a b", a=NT)
        ptk = sbuf("ptk", [128, 2, T], BF16)
        wAB = sbuf("wAB", [128, 16, 512], BF16)
        wA = wAB[:, 0:8, :]
        wB = wAB[:, 8:16, :]
        shr12 = sbuf("shr12", [128, 3072], F32)
        cands = sbuf("cands", [128, HA, 16], F32)
        taus = sbuf("taus", [128, HA], F32)
        tausq = sbuf("tausq", [128, NT, HA], F32)
        scA = sbuf("scA", [128, 512], F32)
        srow = scA[0:1, :]
        xnb = sbuf("xnb", [128, D], BF16)
        mv = sbuf("mv", [128, NT, 2], F32)
        msc = sbuf("msc", [128, 32], F32)
        msc2 = sbuf("msc2", [128, 32], F32)
        rst = sbuf("rst", [128, NT, 2], F32)
        stat6 = sbuf("stat6", [128, 12], F32)
        utri = sbuf("utri", [128, 128], F32)
        identf = sbuf("identf", [128, 128], F32)
        onesf = sbuf("onesf", [1, 128], F32)
        o2T = sbuf("o2T", [128, 8, 512], BF16)
        hpad = xhT.rearrange("p a b -> p (a b)")[:, 4096:8208].bitcast(F32)\
            .rearrange("p (a b) -> p a b", a=4)
        scD = shr12[:, 0:1024]
        sc4 = shr12[:, 1024:2048]
        sci = shr12[:, 2048:3072].bitcast(I32)
        amt = sbuf("amt", [128, 512], F32)
        amtn = shr12[:, 1024:1536]
        s2bc = shr12[:, 1536:2048]
        cvw = sbuf("cvw", [128, 32, 3], F32)
        cvb = sbuf("cvb", [128, 32], F32)
        ffs = sbuf("ffs", [128, NT, 4], F32)
        zpad = sbuf("zpad", [128, 32], BF16)
        wst = sbuf("wst", [128, 80], F32)
        ltri01b = sbuf("ltri01b", [128, 128], BF16)
        identb = sbuf("identb", [128, 128], BF16)
        ones1 = sbuf("ones1", [1, 128], BF16)
        negonesf = sbuf("negonesf", [1, 128], F32)
        cst = sbuf("cst", [128, 16], F32)
        cst_slots = {}

        def const_f32(val):
            if val not in cst_slots:
                cst_slots[val] = len(cst_slots)
                assert len(cst_slots) <= 16
                nc.const_aps.register(F32, float(val), cst[:, cst_slots[val]:cst_slots[val] + 1])
            return cst[:, cst_slots[val]:cst_slots[val] + 1]

        psA = ctx.enter_context(nc.psum_tensor("psA", [128, 2048], F32))
        psB = ctx.enter_context(nc.psum_tensor("psB", [128, 2048], F32))
        aoT = xhT

        for e in ENGS:
            kb.sem[e] = ctx.enter_context(nc.semaphore("sem_" + e))
        for s_ in kb.dma_sem_pool:
            kb.sem[s_] = ctx.enter_context(nc.semaphore("sem_" + s_))

        const_f32(LN_EPS)

        def init_consts(g):
            for val, slot in cst_slots.items():
                g.memset(cst[:, slot:slot + 1], float(val))
            g.memset(utri[:], 0.0)
            g.memset(identb[:], 1.0)
            g.memset(ones1[:], 1.0)
            g.memset(negonesf[:], -1.0)
            g.memset(ltri01b[:], 1.0)
            g.memset(identf[:], 1.0)
            g.memset(onesf[:], 1.0)
            g.drain()
            g.affine_select(out=identf[:], in_=identf[:], compare_op=OP.is_equal,
                            fill=0.0, base=0, pattern=[[-1, 128]], channel_multiplier=1)
            g.affine_select(out=utri[:], in_=utri[:], compare_op=OP.is_ge,
                            fill=NEG, base=0, pattern=[[-1, 128]], channel_multiplier=1)
            g.affine_select(out=ltri01b[:], in_=ltri01b[:], compare_op=OP.is_ge,
                            fill=0.0, base=0, pattern=[[1, 128]], channel_multiplier=-1)
            g.affine_select(out=identb[:], in_=identb[:], compare_op=OP.is_equal,
                            fill=0.0, base=0, pattern=[[-1, 128]], channel_multiplier=1)
            g.drain()
            g.iota(wst[:, 40:48], [[1, 8]], base=1, channel_multiplier=0,
                   allow_small_or_imprecise_dtypes=True)
            g.memset(zpad[:], 0.0)
        kb.on(G, init_consts)
        gconst = kb.bump(G)
        kb.wait(TE, G, gconst)
        kb.wait(V, G, gconst)

        def f_rinv(v):
            v.reciprocal(out=wst[:, 40:48], in_=wst[:, 40:48])
            v.drain()
        kb.on(V, f_rinv)
        kb.bump(V)
        kb.wait(V, G, gconst)

        buf_free = {"wA": 0, "wB": 0}
        bufs = {"wA": wA, "wB": wB}

        def load_w_half(nm, bufname, half):
            src = dr[nm][:, half * 512:(half + 1) * 512].rearrange("(a p) b -> p a b", p=128)
            kb.wait(G, TE, buf_free[bufname])
            return kb.dma(G, bufs[bufname], src, key=bufname)

        ld_wq0 = load_w_half("wq", "wA", 0)
        ld_wq1 = load_w_half("wq", "wB", 1)

        def emit_ln_transpose(src_dram, dst):
            win_free = [0, 0, 0, 0]
            for i in range(NT):
                slot = i % 4
                kb.wait(Q, V, win_free[slot])
                if kb.win_store[slot] is not None:
                    kb.wait(Q, kb.win_store[slot][0], kb.win_store[slot][1])
                ld = kb.dma(Q, xw[:, slot, :], src_dram[i * 128:(i + 1) * 128, :], key="win%d" % slot)
                kb.wait(V, ld[0], ld[1])

                def f_stat(v, i=i, slot=slot):
                    v.bn_stats(out=stat6[:, 0:6], in_=xw[:, slot, 0:512])
                    v.bn_stats(out=stat6[:, 6:12], in_=xw[:, slot, 512:1024])
                    v.drain()
                    v.bn_aggr(out=mv[:, i, :], in_=stat6[:])
                    v.drain()
                kb.on(V, f_stat)
                c1 = kb.bump(V)
                kb.wait(S, V, c1)

                def f_sq(s, i=i):
                    s.activation(out=rst[:, i, 0:1], in_=mv[:, i, 1:2],
                                 func=AF.Sqrt, bias=LN_EPS, scale=1.0)
                kb.on(S, f_sq)
                c2 = kb.bump(S)
                kb.wait(V, S, c2)

                def f_norm(v, i=i, slot=slot):
                    v.reciprocal(out=rst[:, i, 1:2], in_=rst[:, i, 0:1])
                    v.drain()
                    v.tensor_scalar(out=xnb[:], in0=xw[:, slot, :], scalar1=mv[:, i, 0:1],
                                    scalar2=rst[:, i, 1:2], op0=OP.subtract, op1=OP.mult)
                kb.on(V, f_norm)
                cnorm = kb.bump(V)
                win_free[slot] = cnorm
                kb.wait(TE, V, cnorm)
                half = i % 2
                pview = psA[:, half * 512:(half + 1) * 512].bitcast(BF16)
                for dc in range(8):
                    def f_tp(t, dc=dc, pview=pview):
                        t.transpose(pview[:, dc * 128:(dc + 1) * 128],
                                    xnb[:, dc * 128:(dc + 1) * 128], identb[:])
                    kb.on(TE, f_tp)
                ctp = kb.bump(TE)
                kb.wait(V, TE, ctp)

                def f_evac(v, i=i, pview=pview, dst=dst):
                    v.tensor_copy(out=dst[:, :, i * 128:(i + 1) * 128],
                                  in_=pview.rearrange("p (a b) -> p a b", a=8))
                kb.on(V, f_evac)
                pe_ok = kb.bump(V)
                kb.wait(TE, V, pe_ok)

        emit_ln_transpose(dr["x"], xhT)
        kb.barrier()

        if "t_xhT" in taps:
            st = None
            for dc in range(8):
                st = kb.dma(Q, taps["t_xhT"][dc * 128:(dc + 1) * 128, :], xhT[:, dc, :], key="tap")
            kb.wait(Q, st[0], st[1])

        SCALE_A = 0.125

        def proj_featmajor(dst, wname, scale, preloaded=None):
            for half in range(2):
                bufname = "wA" if half == 0 else "wB"
                ld = preloaded[half] if preloaded else load_w_half(wname, bufname, half)
                kb.wait(TE, ld[0], ld[1])
                for ec in range(4):
                    e_glob = half * 4 + ec
                    for tc in range(4):
                        slot = (e_glob * 4 + tc) % 3
                        ps = psB[:, slot * 512:(slot + 1) * 512]
                        for dc in range(8):
                            def f_mm(t, ps=ps, dc=dc, ec=ec, tc=tc, wb=bufs[bufname]):
                                t.matmul(ps, wb[:, dc, ec * 128:(ec + 1) * 128],
                                         xhT[:, dc, tc * 512:(tc + 1) * 512],
                                         start=(dc == 0), stop=(dc == 7))
                            kb.on(TE, f_mm)
                        cm = kb.bump(TE)
                        kb.wait(S, TE, cm)

                        def f_ev(s, ps=ps, e_glob=e_glob, tc=tc, dst=dst, scale=scale):
                            s.activation(out=dst[:, e_glob, tc * 512:(tc + 1) * 512], in_=ps,
                                         func=AF.Copy, scale=scale)
                        kb.on(S, f_ev)
                        ce = kb.bump(S)
                        kb.wait(TE, S, ce)
                buf_free[bufname] = kb.cnt["t"]

        proj_featmajor(qT, "wq", SCALE_A, preloaded=(ld_wq0, ld_wq1))
        proj_featmajor(kT, "wk", 1.0)

        for half in range(2):
            bufname = "wA" if half == 0 else "wB"
            ld = load_w_half("wv", bufname, half)
            kb.wait(TE, ld[0], ld[1])
            for i in range(NT):
                slot = i % 3
                ps = psB[:, slot * 512:(slot + 1) * 512]
                for dc in range(8):
                    def f_mm(t, ps=ps, dc=dc, i=i, wb=bufs[bufname]):
                        t.matmul(ps, xhT[:, dc, i * 128:(i + 1) * 128], wb[:, dc, :],
                                 start=(dc == 0), stop=(dc == 7))
                    kb.on(TE, f_mm)
                cm = kb.bump(TE)
                kb.wait(S, TE, cm)

                def f_ev(s, ps=ps, i=i, half=half):
                    s.activation(out=vtok[:, i, half * 512:(half + 1) * 512], in_=ps, func=AF.Copy)
                kb.on(S, f_ev)
                ce = kb.bump(S)
                kb.wait(TE, S, ce)
            buf_free[bufname] = kb.cnt["t"]

        kb.barrier()
        ld_wo0 = load_w_half("wo", "wA", 0)
        ld_wo1 = load_w_half("wo", "wB", 1)

        slab_free = {0: 0, 1: 0}
        for qt in range(NT):
            kw = (qt + 1) * 128
            for h in range(HA):
                slab = psA if h % 2 == 0 else psB
                sid = h % 2
                kb.wait(TE, V, slab_free[sid])
                hp = (h % 2) * 64
                hc = h // 2
                for kc in range((kw + 511) // 512):
                    w = min(512, kw - kc * 512)

                    def f_mm(t, slab=slab, kc=kc, w=w, hp=hp, hc=hc, qt=qt):
                        t.matmul(slab[:, kc * 512:kc * 512 + w],
                                 qT[hp:hp + 64, hc, qt * 128:(qt + 1) * 128],
                                 kT[hp:hp + 64, hc, kc * 512:kc * 512 + w],
                                 start=True, stop=True)
                    kb.on(TE, f_mm)
                cm = kb.bump(TE)
                kb.wait(V, TE, cm)

                def f_ex(v, slab=slab, kw=kw, qt=qt, h=h):
                    v.tensor_tensor(out=slab[:, qt * 128:qt * 128 + 128],
                                    in0=slab[:, qt * 128:qt * 128 + 128], in1=utri[:], op=OP.add)
                    v.drain()
                    v.max(out=cands[:, h, 0:8], in_=slab[:, 0:kw])
                kb.on(V, f_ex)
                slab_free[sid] = kb.bump(V)

            def f_tau(v, qt=qt):
                cview = cands[:, :, 0:8]
                a = scA[:, 0:128].rearrange("p (a b) -> p a b", a=HA)
                bq = scA[:, 128:256].rearrange("p (a b) -> p a b", a=HA)
                v.drain()
                v.tensor_copy(out=a, in_=cview)
                v.drain()
                for d in (1, 2, 4):
                    v.tensor_tensor(out=bq[:, :, d:8], in0=a[:, :, d:8], in1=a[:, :, 0:8 - d], op=OP.add)
                    v.tensor_copy(out=bq[:, :, 0:d], in_=a[:, :, 0:d])
                    v.drain()
                    a, bq = bq, a
                v.tensor_scalar(out=a, in0=a, scalar1=1.0, scalar2=None, op0=OP.subtract)
                v.drain()
                v.tensor_tensor(out=a, in0=a,
                                in1=wst[:, 40:48].unsqueeze(1).broadcast_to((128, HA, 8)), op=OP.mult)
                v.drain()
                v.tensor_reduce(out=taus[:], in_=a, axis=mybir.AxisListType.X, op=OP.max)
                v.drain()
                v.tensor_copy(out=tausq[:, qt, :], in_=taus[:])
                v.drain()
            kb.on(V, f_tau)
            kb.bump(V)

        kb.barrier()
        if "t_tau" in taps:
            st = None
            for i in range(NT):
                st = kb.dma(Q, taps["t_tau"][i * 128:(i + 1) * 128, :], tausq[:, i, :], key="tap")
            kb.wait(Q, st[0], st[1])

        av_done = {0: 0, 1: 0}
        ev_prev = {0: 0, 1: 0, 2: 0, 3: 0}
        ntrf = shr12[0:1, 0:2048]
        taubc = shr12[:, 0:2048]
        for h in range(HA):
            hp = (h % 2) * 64
            hc = h // 2
            for qt in range(NT):
                def f_tp(t, qt=qt, h=h):
                    t.transpose(psB[0:1, qt * 128:(qt + 1) * 128], tausq[:, qt, h:h + 1], identf[:])
                kb.on(TE, f_tp)
            ctp = kb.bump(TE)
            kb.wait(V, TE, ctp)

            def f_ntr(v):
                v.tensor_copy(out=ntrf, in_=psB[0:1, 0:T])
            kb.on(V, f_ntr)
            cn = kb.bump(V)
            kb.wait(TE, V, cn)

            def f_tbc(t):
                for c in range(4):
                    t.matmul(psB[:, c * 512:(c + 1) * 512], onesf[0:1, :],
                             ntrf[0:1, c * 512:(c + 1) * 512], start=True, stop=True)
            kb.on(TE, f_tbc)
            ctb = kb.bump(TE)
            kb.wait(V, TE, ctb)

            def f_tbev(v):
                for c in range(4):
                    v.tensor_copy(out=taubc[:, c * 512:(c + 1) * 512], in_=psB[:, c * 512:(c + 1) * 512])
                v.drain()
            kb.on(V, f_tbev)
            ctv = kb.bump(V)
            kb.wait(TE, V, ctv)

            def issue_av(pend):
                kt_, buf_, cevac_, cmask_ = pend
                kb.wait(TE, S, cevac_)
                kb.wait(TE, V, cmask_)
                for qc in range(4):
                    if (qc + 1) * 512 <= kt_ * 128:
                        continue
                    q0c = max(qc * 512, kt_ * 128)
                    w = (qc + 1) * 512 - q0c
                    last_kt = min(15, 4 * qc + 3)

                    def f_av(t, kt=kt_, q0c=q0c, w=w, h=h, buf=buf_, last_kt=last_kt):
                        t.matmul(psB[0:64, q0c:q0c + w],
                                 vtok[:, kt, h * 64:h * 64 + 64],
                                 ptk[:, buf, q0c - kt * 128:q0c - kt * 128 + w],
                                 start=(kt == 0), stop=(kt == last_kt))
                    kb.on(TE, f_av)
                av_done[buf_] = kb.bump(TE)

            pend = None
            chunkc = 0
            for kt in range(NT):
                buf = kt % 2
                qw_total = T - kt * 128
                for jc in range((qw_total + 511) // 512):
                    q0 = kt * 128 + jc * 512
                    w = min(512, T - q0)
                    slot = chunkc % 4
                    chunkc += 1
                    ps = psA[:, slot * 512:slot * 512 + w]
                    kb.wait(TE, S, ev_prev[slot])

                    def f_mm(t, ps=ps, kt=kt, q0=q0, w=w, hp=hp, hc=hc):
                        t.matmul(ps, kT[hp:hp + 64, hc, kt * 128:(kt + 1) * 128],
                                 qT[hp:hp + 64, hc, q0:q0 + w], start=True, stop=True)
                    kb.on(TE, f_mm)
                    cm = kb.bump(TE)
                    kb.wait(V, TE, cm)

                    def f_sub(v, ps=ps, q0=q0, w=w):
                        v.tensor_tensor(out=ps, in0=ps, in1=taubc[:, q0:q0 + w], op=OP.subtract)
                        v.drain()
                    kb.on(V, f_sub)
                    cs = kb.bump(V)
                    kb.wait(S, V, cs)
                    if jc == 0:
                        kb.wait(S, TE, av_done[buf])

                    def f_ev(s, ps=ps, kt=kt, q0=q0, w=w, buf=buf):
                        s.activation(out=ptk[:, buf, q0 - kt * 128:q0 - kt * 128 + w], in_=ps, func=AF.Relu)
                    kb.on(S, f_ev)
                    ev_prev[slot] = kb.bump(S)
                    if jc == 0:
                        kb.wait(V, S, ev_prev[slot])

                        def f_msk(v, buf=buf):
                            v.tensor_tensor(out=ptk[:, buf, 0:128], in0=ptk[:, buf, 0:128],
                                            in1=ltri01b[:], op=OP.mult)
                        kb.on(V, f_msk)
                        cmask = kb.bump(V)
                if pend is not None:
                    issue_av(pend)
                pend = (kt, buf, ev_prev[(chunkc - 1) % 4], cmask)
            issue_av(pend)
            kb.wait(V, TE, av_done[(NT - 1) % 2])

            def f_avev(v, hp=hp, hc=hc):
                v.tensor_copy(out=aoT[hp:hp + 64, hc, :], in_=psB[0:64, :])
            kb.on(V, f_avev)
            cae = kb.bump(V)
            kb.wait(TE, V, cae)

        kb.barrier()
        if "t_aoT" in taps:
            st = None
            for dc in range(8):
                st = kb.dma(Q, taps["t_aoT"][dc * 128:(dc + 1) * 128, :], aoT[:, dc, :], key="tap")
            kb.wait(Q, st[0], st[1])


        pass1_done = {}

        def emit_pass1(wname, nchunks, ccols, cbase):
            land = [bigA[:, 0:4096], bigA[:, 4096:8192]]
            free_at = [kb.cnt["s"], kb.cnt["s"]]
            for n in range(nchunks):
                b = n % 2
                kb.wait(G, S, free_at[b])
                ld = kb.dma(G, land[b][:, 0:ccols], dr[wname][n * 128:(n + 1) * 128, :], key="pg%d" % b)
                kb.wait(S, ld[0], ld[1])

                def f_abs(sx, n=n, b=b, ccols=ccols, cbase=cbase, land=land):
                    sx.activation(out=bigA[:, 12288:12288 + ccols],
                                  in_=land[b][:, 0:ccols], func=AF.Abs,
                                  accum_out=wst[:, cbase + n:cbase + n + 1])
                kb.on(S, f_abs)
                free_at[b] = kb.bump(S)
            pass1_done[wname] = kb.cnt["s"]

        emit_pass1("w1", 8, 4096, 0)
        emit_pass1("w2", 32, 1024, 48)
        for ld in (ld_wo0, ld_wo1):
            kb.wait(TE, ld[0], ld[1])
        win_free = [0, 0, 0, 0]
        st = None
        st_all = []
        for i in range(NT):
            slot = i % 4
            kb.wait(Q, V, win_free[slot])
            if kb.win_store[slot] is not None:
                kb.wait(Q, kb.win_store[slot][0], kb.win_store[slot][1])
            ld = kb.dma(Q, xw[:, slot, :], dr["x"][i * 128:(i + 1) * 128, :], key="win%d" % slot)
            for half in range(2):
                wbuf = wA if half == 0 else wB
                pslot = (i * 2 + half) % 3
                ps = psB[:, pslot * 512:(pslot + 1) * 512]
                for dc in range(8):
                    def f_mm(t, ps=ps, dc=dc, i=i, wbuf=wbuf):
                        t.matmul(ps, aoT[:, dc, i * 128:(i + 1) * 128], wbuf[:, dc, :],
                                 start=(dc == 0), stop=(dc == 7))
                    kb.on(TE, f_mm)
                cm = kb.bump(TE)
                kb.wait(V, TE, cm)
                if half == 0:
                    kb.wait(V, ld[0], ld[1])

                def f_ev(v, ps=ps, slot=slot, half=half):
                    v.tensor_tensor(out=xw[:, slot, half * 512:(half + 1) * 512],
                                    in0=xw[:, slot, half * 512:(half + 1) * 512], in1=ps, op=OP.add)
                kb.on(V, f_ev)
                ce = kb.bump(V)
                kb.wait(TE, V, ce)
            cdone = kb.bump(V)
            kb.wait(Q, V, cdone)
            st = kb.dma(Q, dr["strm"][i * 128:(i + 1) * 128, :], xw[:, slot, :], key="win%d" % slot)
            win_free[slot] = cdone
            kb.win_store[slot] = st
            st_all.append(st)
        buf_free["wA"] = buf_free["wB"] = kb.cnt["t"]

        kb.barrier()
        for st in st_all[-4:]:
            kb.wait(Q, st[0], st[1])

        k2v = xhT[:, :, :].rearrange("p a b -> p (a b)").rearrange("p (a b) -> p a b", a=32)
        k1T = qT
        wf1 = [ptk[:, 0, 0:1024].rearrange("p (a b) -> p a b", a=8),
               ptk[:, 1, 0:1024].rearrange("p (a b) -> p a b", a=8),
               xhT[:, 0, 0:1024].rearrange("p (a b) -> p a b", a=8),
               xhT[:, 1, 0:1024].rearrange("p (a b) -> p a b", a=8)]
        wf2 = [wAB[:, :, :].rearrange("p a b -> p (a b)")[:, j * 4096:(j + 1) * 4096]
               .rearrange("p (a b) -> p a b", a=32) for j in range(2)]

        def emit_prequant(wname, wqname, nchunks, ccols, stat_col):
            land = [bigA[:, 0:4096], bigA[:, 4096:8192]]
            kb.wait(V, S, pass1_done[wname])
            cbase = 0 if wname == "w1" else 48

            def f_mean(v, nchunks=nchunks, cbase=cbase):
                v.tensor_reduce(out=wst[:, 32:33], in_=wst[:, cbase:cbase + nchunks],
                                axis=mybir.AxisListType.X, op=OP.add)
                v.drain()
            kb.on(V, f_mean)
            cv = kb.bump(V)
            kb.wait(TE, V, cv)

            def f_tpm(t):
                t.transpose(psB[0:1, 0:128], wst[:, 32:33], identf[:])
            kb.on(TE, f_tpm)
            ct1 = kb.bump(TE)
            kb.wait(V, TE, ct1)

            def f_rtot(v):
                v.tensor_reduce(out=scA[0:1, 0:1], in_=psB[0:1, 0:128], axis=mybir.AxisListType.X, op=OP.add)
            kb.on(V, f_rtot)
            cv1 = kb.bump(V)
            kb.wait(TE, V, cv1)

            def f_bcm(t):
                t.matmul(psB[:, 256:257], onesf[0:1, :], scA[0:1, 0:1], start=True, stop=True)
            kb.on(TE, f_bcm)
            ct2 = kb.bump(TE)
            kb.wait(V, TE, ct2)

            def f_cptot(v):
                v.tensor_copy(out=wst[:, 33:34], in_=psB[:, 256:257])
                v.drain()
            kb.on(V, f_cptot)

            def f_scale(v, nchunks=nchunks, ccols=ccols, stat_col=stat_col):
                v.tensor_scalar(out=wst[:, 35:36], in0=wst[:, 33:34],
                                scalar1=1.0 / (nchunks * 128 * ccols), scalar2=1e-5,
                                op0=OP.mult, op1=OP.max)
                v.drain()
                v.reciprocal(out=wst[:, 34:35], in_=wst[:, 35:36])
                v.drain()
            kb.on(V, f_scale)
            cv2 = kb.bump(V)
            for b_ in (0, 1):
                kname = "pq%d" % b_
                if kname in kb.dma_sem_map:
                    sm = kb.dma_sem_map[kname]
                    kb.wait(Q, sm, kb.dma_cnt[sm])
            free_at = [cv2, cv2]
            st_done = [None, None]
            stq = None
            for n in range(nchunks):
                b = n % 2
                kb.wait(Q, V, free_at[b])
                if st_done[b] is not None:
                    kb.wait(Q, st_done[b][0], st_done[b][1])
                ld = kb.dma(Q, land[b][:, 0:ccols], dr[wname][n * 128:(n + 1) * 128, :], key="pq%d" % b)
                kb.wait(V, ld[0], ld[1])
                kb.wait(V, V if False else "v", cv2)

                def f_q(v, n=n, b=b, ccols=ccols):
                    fscr = bigA[:, 8192:8192 + ccols]
                    v.tensor_scalar(out=fscr, in0=land[b][:, 0:ccols], scalar1=wst[:, 34:35],
                                    scalar2=MAGIC, op0=OP.mult, op1=OP.add)
                    v.drain()
                    v.tensor_scalar(out=fscr, in0=fscr, scalar1=MAGIC, scalar2=-1.0,
                                    op0=OP.subtract, op1=OP.max)
                    v.drain()
                    v.tensor_scalar(out=land[b][:, 0:ccols].bitcast(BF16)[:, 0:ccols],
                                    in0=fscr, scalar1=1.0, scalar2=None, op0=OP.min)
                kb.on(V, f_q)
                cq = kb.bump(V)
                kb.wait(Q, V, cq)
                stq = kb.dma(Q, dr[wqname][n * 128:(n + 1) * 128, :],
                             land[b][:, 0:ccols].bitcast(BF16)[:, 0:ccols], key="pq%d" % b)
                st_done[b] = stq
                free_at[b] = cq
            kb.wait(Q, stq[0], stq[1])
            return wst[:, 35:36]

        kb.barrier()
        mc1 = emit_prequant("w1", "w1q", 8, 4096, 0)

        def f_savemc1(v):
            v.tensor_copy(out=wst[:, 36:37], in_=wst[:, 35:36])
        kb.on(V, f_savemc1)
        kb.bump(V)
        mc2 = emit_prequant("w2", "w2q", 32, 1024, 1)

        def f_savemc2(v):
            v.tensor_copy(out=wst[:, 37:38], in_=wst[:, 35:36])
        kb.on(V, f_savemc2)
        cmc = kb.bump(V)
        kb.barrier()
        for kname in ("pq0", "pq1"):
            if kname in kb.dma_sem_map:
                sm = kb.dma_sem_map[kname]
                for e_ in (V, S, G, TE):
                    kb.wait(e_, sm, kb.dma_cnt[sm])
        ldcv = kb.dma(Q, cvw[:], dr["convw"], key="cv")
        ldcb = kb.dma(Q, cvb[:], dr["convb"], key="cv")

        def emit_ffn(src_dram, dst_dram):
            win_free2 = [0, 0, 0, 0]
            for i in range(NT):
                slot = i % 4
                kb.wait(Q, V, win_free2[slot])
                if kb.win_store[slot] is not None:
                    kb.wait(Q, kb.win_store[slot][0], kb.win_store[slot][1])
                ld = kb.dma(Q, xw[:, slot, :], src_dram[i * 128:(i + 1) * 128, :], key="win%d" % slot)
                kb.wait(V, ld[0], ld[1])

                def f_stat(v, i=i, slot=slot):
                    v.bn_stats(out=stat6[:, 0:6], in_=xw[:, slot, 0:512])
                    v.bn_stats(out=stat6[:, 6:12], in_=xw[:, slot, 512:1024])
                    v.drain()
                    v.bn_aggr(out=mv[:, i, :], in_=stat6[:])
                    v.drain()
                kb.on(V, f_stat)
                c1 = kb.bump(V)
                kb.wait(S, V, c1)

                def f_sq(sx, i=i):
                    sx.activation(out=rst[:, i, 0:1], in_=mv[:, i, 1:2],
                                  func=AF.Sqrt, bias=LN_EPS, scale=1.0)
                kb.on(S, f_sq)
                c2 = kb.bump(S)
                kb.wait(V, S, c2)

                def f_norm(v, i=i, slot=slot):
                    v.reciprocal(out=rst[:, i, 1:2], in_=rst[:, i, 0:1])
                    v.drain()
                    v.tensor_scalar(out=sc4[:], in0=xw[:, slot, :], scalar1=mv[:, i, 0:1],
                                    scalar2=rst[:, i, 1:2], op0=OP.subtract, op1=OP.mult)
                    v.drain()
                    v.tensor_reduce(out=ffs[:, i, 0:1], in_=sc4[:], axis=mybir.AxisListType.X,
                                    op=OP.max, apply_absolute_value=True)
                    v.drain()
                    v.tensor_scalar(out=ffs[:, i, 1:2], in0=ffs[:, i, 0:1], scalar1=1e-5,
                                    scalar2=None, op0=OP.max)
                    v.drain()
                    v.reciprocal(out=ffs[:, i, 0:1], in_=ffs[:, i, 1:2])
                    v.drain()
                    v.tensor_scalar(out=ffs[:, i, 0:1], in0=ffs[:, i, 0:1], scalar1=127.0,
                                    scalar2=None, op0=OP.mult)
                    v.tensor_tensor(out=ffs[:, i, 1:2], in0=ffs[:, i, 1:2], in1=wst[:, 36:37], op=OP.mult)
                    v.drain()
                    v.tensor_scalar(out=ffs[:, i, 1:2], in0=ffs[:, i, 1:2], scalar1=1.0 / 127.0,
                                    scalar2=None, op0=OP.mult)
                kb.on(V, f_norm)
                c3 = kb.bump(V)

                def f_k1(v, i=i):
                    v.tensor_scalar(out=scD[:], in0=sc4[:], scalar1=ffs[:, i, 0:1],
                                    scalar2=MAGIC, op0=OP.mult, op1=OP.add)
                    v.drain()
                    v.tensor_scalar(out=xnb[:], in0=scD[:], scalar1=MAGIC,
                                    scalar2=ffs[:, i, 1:2], op0=OP.subtract, op1=OP.mult)
                kb.on(V, f_k1)
                cnorm = kb.bump(V)
                win_free2[slot] = cnorm
                kb.wait(TE, V, cnorm)
                half = i % 2
                pview = psA[:, half * 512:(half + 1) * 512].bitcast(BF16)
                for dc in range(8):
                    def f_tp(t, dc=dc, pview=pview):
                        t.transpose(pview[:, dc * 128:(dc + 1) * 128],
                                    xnb[:, dc * 128:(dc + 1) * 128], identb[:])
                    kb.on(TE, f_tp)
                ctp = kb.bump(TE)
                kb.wait(V, TE, ctp)

                def f_evac(v, i=i, pview=pview):
                    v.tensor_copy(out=k1T[:, :, i * 128:(i + 1) * 128],
                                  in_=pview.rearrange("p (a b) -> p a b", a=8))
                kb.on(V, f_evac)
                pe_ok = kb.bump(V)
                kb.wait(TE, V, pe_ok)
            kb.barrier()
            kb.wait(V, ldcb[0], ldcb[1])
            kb.wait(G, ldcb[0], ldcb[1])
            kb.wait(S, ldcb[0], ldcb[1])

            def f_hpinit(v):
                v.memset(hpad[:, :, :], 0.0)
                v.drain()
            kb.on(V, f_hpinit)
            chp = kb.bump(V)
            kb.wait(S, V, chp)
            st_res = None
            for tc in range(4):
                wf1_ld = {}
                wf1_free = [0, 0, 0, 0]

                def load_wf1(cc):
                    b = cc % 4
                    kb.wait(Q, TE, wf1_free[b])
                    wf1_ld[cc] = kb.dma(Q, wf1[b][:], dr["w1q"][:, cc * 128:(cc + 1) * 128]
                                        .rearrange("(a p) b -> p a b", p=128), key="wf1%d" % b)
                for c0 in range(4):
                    load_wf1(c0)
                amt_done = 0
                h2 = bigA.rearrange("p (a b) -> p a b", a=32)
                hevac_done = {0: 0, 1: 0, 2: 0, 3: 0}
                conv_v_done = {0: 0, 1: 0, 2: 0, 3: 0}
                scd_free = {0: 0, 1: 0}
                pend_silu = None

                def emit_silu_amt(pend):
                    cc_, par_, ccv_ = pend
                    kb.wait(S, V, ccv_)

                    def f_silu(sx, cc=cc_, par=par_):
                        sx.activation(out=h2[:, cc, :], in_=scD[:, par * 512:(par + 1) * 512],
                                      func=AF.Silu, bias=cvb[:, cc:cc + 1], scale=1.0)
                    kb.on(S, f_silu)
                    csl = kb.bump(S)
                    kb.wait(V, S, csl)

                    def f_amt(v, cc=cc_):
                        if cc == 0:
                            v.tensor_copy(out=amt[:], in_=h2[:, 0, :])
                            v.drain()
                            v.tensor_copy(out=amtn[:], in_=h2[:, 0, :])
                        else:
                            v.tensor_tensor(out=amt[:], in0=amt[:], in1=h2[:, cc, :], op=OP.max)
                            v.drain()
                            v.tensor_tensor(out=amtn[:], in0=amtn[:], in1=h2[:, cc, :], op=OP.min)
                        v.drain()
                    kb.on(V, f_amt)
                    return kb.bump(V)

                HALO = [1024, 1536, 1026, 1538]
                cm_hist = {}
                for it in range(34):
                    if it < 32:
                        cc = it
                        b = cc % 4
                        ld = wf1_ld[cc]
                        kb.wait(TE, ld[0], ld[1])
                        pslot = cc % 4
                        pmain = psA[:, pslot * 512:(pslot + 1) * 512]
                        phalo = psB[:, HALO[pslot]:HALO[pslot] + 2]
                        kb.wait(TE, S, hevac_done[pslot])
                        for dc in range(8):
                            def f_mm(t, dc=dc, cc=cc, tc=tc, b=b, pmain=pmain, phalo=phalo):
                                t.matmul(pmain, wf1[b][:, dc, :], k1T[:, dc, tc * 512:(tc + 1) * 512],
                                         start=(dc == 0), stop=(dc == 7))
                                if tc == 0:
                                    t.matmul(phalo[:, 1:2], wf1[b][:, dc, :],
                                             k1T[:, dc, 512:513], start=(dc == 0), stop=(dc == 7))
                                elif tc == 3:
                                    t.matmul(phalo[:, 0:1], wf1[b][:, dc, :],
                                             k1T[:, dc, tc * 512 - 1:tc * 512], start=(dc == 0), stop=(dc == 7))
                                else:
                                    t.matmul(phalo, wf1[b][:, dc, :],
                                             k1T[:, dc, tc * 512 - 1:tc * 512 + 513:513],
                                             start=(dc == 0), stop=(dc == 7))
                            kb.on(TE, f_mm)
                        wf1_free[b] = kb.cnt["t"] + 1
                        cm_hist[cc] = kb.bump(TE)
                        if cc + 4 < 32:
                            load_wf1(cc + 4)
                    if it >= 2:
                        cc = it - 2
                        b = cc % 4
                        par = cc % 2
                        pslot = cc % 4
                        pmain = psA[:, pslot * 512:(pslot + 1) * 512]
                        phalo = psB[:, HALO[pslot]:HALO[pslot] + 2]
                        kb.wait(S, TE, cm_hist[min(cc + 2, 31)])
                        kb.wait(S, V, conv_v_done[b])

                        def f_hevac(sx, cc=cc, tc=tc, pmain=pmain, phalo=phalo, b=b):
                            sx.activation(out=hpad[:, b, 1:513], in_=pmain, func=AF.Copy)
                            if tc == 0:
                                sx.activation(out=hpad[:, b, 513:514], in_=phalo[:, 1:2], func=AF.Copy)
                            elif tc == 3:
                                sx.activation(out=hpad[:, b, 0:1], in_=phalo[:, 0:1], func=AF.Copy)
                            else:
                                sx.activation(out=hpad[:, b, 0:514:513], in_=phalo, func=AF.Copy)
                        kb.on(S, f_hevac)
                        hevac_done[pslot] = kb.bump(S)
                        kb.wait(S, V, scd_free[par])

                        def f_tap1(sx, cc=cc, b=b, par=par):
                            sx.drain()
                            sx.activation(out=scD[:, par * 512:par * 512 + 512], in_=hpad[:, b, 0:512],
                                          func=AF.Copy, scale=cvw[:, cc, 0:1])
                            sx.drain()
                        kb.on(S, f_tap1)
                        che = kb.bump(S)
                        kb.wait(V, S, che)

                        def f_conv(v, cc=cc, b=b, tc=tc, par=par):
                            sd = scD[:, par * 512:par * 512 + 512]
                            if tc == 0:
                                v.memset(sd[:, 0:1], 0.0)
                                v.drain()
                            v.scalar_tensor_tensor(out=sd, in0=hpad[:, b, 1:513],
                                                   scalar=cvw[:, cc, 1:2], in1=sd,
                                                   op0=OP.mult, op1=OP.add)
                            v.drain()
                            if tc == 3:
                                v.memset(hpad[:, b, 513:514], 0.0)
                                v.drain()
                            v.scalar_tensor_tensor(out=sd, in0=hpad[:, b, 2:514],
                                                   scalar=cvw[:, cc, 2:3], in1=sd,
                                                   op0=OP.mult, op1=OP.add)
                        kb.on(V, f_conv)
                        ccv = kb.bump(V)
                        conv_v_done[b] = ccv
                        scd_free[par] = ccv
                        if pend_silu is not None:
                            amt_done = emit_silu_amt(pend_silu)
                        pend_silu = (cc, par, ccv)
                amt_done = emit_silu_amt(pend_silu)

                if tc == 0 and "t_h2" in taps:
                    kb.wait(Q, V, amt_done)
                    sth = kb.dma(Q, taps["t_h2"].rearrange("(a p) b -> p a b", p=128),
                                 bigA.rearrange("p (a b) -> p a b", a=32), key="tap")
                    kb.wait(Q, sth[0], sth[1])
                def f_amcomb(v):
                    v.scalar_tensor_tensor(out=amt[:], in0=amtn[:], scalar=-1.0, in1=amt[:],
                                           op0=OP.mult, op1=OP.max)
                kb.on(V, f_amcomb)
                amt_done = kb.bump(V)
                kb.wait(TE, V, amt_done)
                for bblk in range(4):
                    def f_tp(t, bblk=bblk):
                        t.transpose(psB[:, 1024 + bblk * 128:1024 + (bblk + 1) * 128],
                                    amt[:, bblk * 128:(bblk + 1) * 128], identf[:])
                    kb.on(TE, f_tp)
                ctp = kb.bump(TE)
                kb.wait(V, TE, ctp)

                def f_am2(v, tc=tc):
                    for bblk in range(4):
                        i = tc * 4 + bblk
                        v.tensor_reduce(out=ffs[:, i, 2:3],
                                        in_=psB[:, 1024 + bblk * 128:1024 + (bblk + 1) * 128],
                                        axis=mybir.AxisListType.X, op=OP.max)
                    v.drain()
                    for bblk in range(4):
                        i = tc * 4 + bblk
                        v.tensor_scalar(out=ffs[:, i, 2:3], in0=ffs[:, i, 2:3], scalar1=1e-5,
                                        scalar2=None, op0=OP.max)
                    v.drain()
                    for bblk in range(4):
                        i = tc * 4 + bblk
                        v.tensor_tensor(out=ffs[:, i, 3:4], in0=ffs[:, i, 2:3], in1=wst[:, 37:38], op=OP.mult)
                        v.reciprocal(out=ffs[:, i, 2:3], in_=ffs[:, i, 2:3])
                    v.drain()
                    for bblk in range(4):
                        i = tc * 4 + bblk
                        v.tensor_scalar(out=ffs[:, i, 3:4], in0=ffs[:, i, 3:4], scalar1=1.0 / 127.0,
                                        scalar2=None, op0=OP.mult)
                        v.tensor_scalar(out=ffs[:, i, 2:3], in0=ffs[:, i, 2:3], scalar1=127.0,
                                        scalar2=None, op0=OP.mult)
                    v.drain()
                kb.on(V, f_am2)
                cam = kb.bump(V)
                kb.wait(TE, V, cam)
                for bblk in range(4):
                    def f_tps(t, bblk=bblk, tc=tc):
                        t.transpose(psB[:, 1024 + bblk * 128:1024 + bblk * 128 + 128][0:1, :]
                                    if False else psB[0:1, 1024 + bblk * 128:1024 + (bblk + 1) * 128],
                                    ffs[:, tc * 4 + bblk, 2:3], identf[:])
                    kb.on(TE, f_tps)
                ctps = kb.bump(TE)
                kb.wait(V, TE, ctps)

                def f_srow(v):
                    v.tensor_copy(out=srow[0:1, :], in_=psB[0:1, 1024:1536])
                kb.on(V, f_srow)
                csr = kb.bump(V)
                kb.wait(TE, V, csr)

                def f_bc(t):
                    t.matmul(psB[:, 1536:2048], onesf[0:1, :], srow[0:1, :], start=True, stop=True)
                kb.on(TE, f_bc)
                cbc = kb.bump(TE)
                kb.wait(V, TE, cbc)

                def f_s2bc(v):
                    v.tensor_copy(out=s2bc[:], in_=psB[:, 1536:2048])
                kb.on(V, f_s2bc)
                cs2 = kb.bump(V)

                k2_done = 0
                for cc in range(32):
                    def f_k2(v, cc=cc):
                        if cc == 0:
                            v.drain()
                        v.tensor_tensor(out=sc4[:, 0:512], in0=h2[:, cc, :], in1=s2bc[:], op=OP.mult)
                        v.drain()
                        v.tensor_scalar(out=k2v[:, cc, :], in0=sc4[:, 0:512], scalar1=MAGIC,
                                        scalar2=MAGIC, op0=OP.add, op1=OP.subtract)
                        v.drain()
                    kb.on(V, f_k2)
                k2_done = kb.bump(V)

                wf2_ld = {}
                wf2_free = [0, 0]

                def load_wf2(do):
                    b = do % 2
                    kb.wait(Q, TE, wf2_free[b])
                    wf2_ld[do] = kb.dma(Q, wf2[b][:], dr["w2q"][:, do * 128:(do + 1) * 128]
                                        .rearrange("(a p) b -> p a b", p=128), key="wf2%d" % b)
                load_wf2(0)
                load_wf2(1)
                kb.wait(TE, V, k2_done)
                o2_evac = {0: 0, 1: 0}
                for do in range(8):
                    b = do % 2
                    ld = wf2_ld[do]
                    kb.wait(TE, ld[0], ld[1])
                    pslot = do % 2
                    ps = psB[:, pslot * 512:(pslot + 1) * 512]
                    kb.wait(TE, V, o2_evac[pslot])
                    for cc in range(32):
                        def f_mm(t, ps=ps, cc=cc, do=do, b=b):
                            t.matmul(ps, wf2[b][:, cc, :], k2v[:, cc, :],
                                     start=(cc == 0), stop=(cc == 31))
                        kb.on(TE, f_mm)
                    wf2_free[b] = kb.cnt["t"] + 1
                    cm = kb.bump(TE)
                    if do + 2 < 8:
                        load_wf2(do + 2)
                    kb.wait(V, TE, cm)

                    def f_ev(v, ps=ps, do=do):
                        v.tensor_copy(out=o2T[:, do, :], in_=ps)
                    kb.on(V, f_ev)
                    o2_evac[pslot] = kb.bump(V)
                co2 = o2_evac[1]
                kb.wait(TE, V, co2)
                for bblk in range(4):
                    i = tc * 4 + bblk
                    slot = i % 4
                    kb.wait(Q, V, win_free2[slot])
                    if kb.win_store[slot] is not None:
                        kb.wait(Q, kb.win_store[slot][0], kb.win_store[slot][1])
                    ld = kb.dma(Q, xw[:, slot, :], src_dram[i * 128:(i + 1) * 128, :], key="win%d" % slot)
                    pview = psA[:, 0:512].bitcast(BF16)
                    for do in range(8):
                        def f_tp(t, do=do, bblk=bblk, pview=pview):
                            t.transpose(pview[:, do * 128:(do + 1) * 128],
                                        o2T[:, do, bblk * 128:(bblk + 1) * 128], identb[:])
                        kb.on(TE, f_tp)
                    ctp = kb.bump(TE)
                    kb.wait(V, TE, ctp)
                    kb.wait(V, ld[0], ld[1])

                    def f_res(v, i=i, slot=slot, pview=pview):
                        v.tensor_scalar(out=scD[:], in0=pview.rearrange("p (a b) -> p a b", a=8),
                                        scalar1=ffs[:, i, 3:4], scalar2=None, op0=OP.mult)
                        v.drain()
                        v.tensor_tensor(out=xw[:, slot, :], in0=xw[:, slot, :], in1=scD[:], op=OP.add)
                    kb.on(V, f_res)
                    cres = kb.bump(V)
                    win_free2[slot] = cres
                    kb.wait(TE, V, cres)
                    kb.wait(Q, V, cres)
                    st_res = kb.dma(Q, dst_dram[i * 128:(i + 1) * 128, :], xw[:, slot, :], key="win%d" % slot)
                    kb.win_store[slot] = st_res
            kb.barrier()
            kb.wait(Q, st_res[0], st_res[1])


        mkband = shr12[:, 1024:1280]
        mkb0 = shr12[:, 1280:1408]
        mkidiag = shr12[:, 1408:1536]
        mkiprev = shr12[:, 1536:1664]
        rawdrow = shr12[0:1, 2048:2112]
        jjrow = shr12[0:1, 0:2048]

        xT_b = qT
        xf = xhT[:, :, :].rearrange("p a b -> p (a b)")
        ckvT = xf[:, 0:4096].rearrange("p (a b) -> p a b", a=2)
        jslot = xf[:, 0:4096].rearrange("p (a b c) -> p a b c", a=NT, b=8)
        cqT = xf[:, 4096:12288].rearrange("p (a b) -> p a b", a=4)
        qitok = xf[:, 4096:8192].rearrange("p (a b) -> p a b", a=NT)
        jeff = xf[:, 8192:12288].rearrange("p (a b c) -> p a b c", a=NT, b=8)
        hefftok = xf[:, 12288:12800].rearrange("p (a b) -> p a b", a=NT)
        mutok = xf[:, 12800:13824].bitcast(F32).rearrange("p (a b) -> p a b", a=NT)
        kiT = xf[:, 13824:15872]

        bff = bigA[:, :]
        kTb = bff[:, 0:4096].bitcast(BF16).rearrange("p (a b) -> p a b", a=4)
        qallb = bff[:, 4096:8192].bitcast(BF16).rearrange("p (a b) -> p a b", a=NT)
        kstg = bff[:, 8192:12288].bitcast(BF16).rearrange("p (a b) -> p a b", a=NT)
        vstg = bff[:, 12288:16384].bitcast(BF16).rearrange("p (a b) -> p a b", a=NT)
        iacc = bff[:, 8192:10240]
        gout2 = [bff[:, 10240:12416].bitcast(BF16).rearrange("p (a b) -> p a b", a=4),
                 bff[:, 12416:14592].bitcast(BF16).rearrange("p (a b) -> p a b", a=4)]
        vwin = bff[:, 14592:15104].bitcast(BF16).rearrange("p (a b) -> p a b", a=2)
        prodv2 = None
        sbandv = psA[:, 0:2048].rearrange("p (a b) -> p a b", a=8)
        slacc = scA[:, 0:512]
        smv = bff[:, 16128:16384]

        wfl = wAB[:, :, :].rearrange("p a b -> p (a b)")
        jjbc = wfl[:, 4096:6144].bitcast(F32)
        qc_scr = wfl[:, 6144:6272]
        jf_scr = wfl[:, 6272:6400]
        hw_scr = wfl[:, 6400:6656].rearrange("p (a b) -> p a b", a=2)
        o_tok = wfl[:, 6656:7168]
        outTv = wfl[:, 7168:7680].rearrange("p (a b) -> p a b", a=4)
        h2t0 = wfl[:, 7680:7808]
        jf2_scr = wfl[:, 7808:7936]
        tview = tausq[:, :, :].rearrange("p a b -> p (a b)")
        idxq = cands[:, :, :].rearrange("p a b -> p (a b)").bitcast(U32)
        prodb = shr12[:, 2048:3072].bitcast(BF16)
        scE = shr12[:, 2048:3072]
        PI = math.pi
        SCB = float(96.0 ** -0.5)

        def emit_attn_b(src_dram, dst_dram):
            l1 = kb.dma(Q, rawdrow[0:1, 0:16], dr["rawd"], key="ab0")
            l1 = kb.dma(Q, jjrow[0:1, 0:1024], dr["tb_jj"], key="ab0")
            lidw = kb.dma(Q, jjrow[0:1, 1024:1028], dr["idxw"], key="ab0")
            kb.wait(S, lidw[0], lidw[1])
            kb.wait(V, lidw[0], lidw[1])
            kb.wait(G, lidw[0], lidw[1])

            def f_iota(g):
                g.iota(tview[:, 0:32], [[1, 32]], channel_multiplier=0,
                       allow_small_or_imprecise_dtypes=True)
                g.iota(tview[:, 64:65], [[0, 1]], base=-63, channel_multiplier=1,
                       allow_small_or_imprecise_dtypes=True)
                g.iota(tview[:, 65:66], [[0, 1]], base=0, channel_multiplier=1,
                       allow_small_or_imprecise_dtypes=True)
                g.iota(idxq[:, 32:64].bitcast(U32) if False else idxq[:, 32:64],
                       [[1, 32]], base=64, channel_multiplier=0)
            kb.on(G, f_iota)
            cgi = kb.bump(G)
            kb.wait(V, G, cgi)

            def f_delta(sx):
                sx.activation(out=rawdrow[0:1, 16:32], in_=rawdrow[0:1, 0:16], func=AF.Sigmoid)
                sx.drain()
            kb.on(S, f_delta)
            c = kb.bump(S)
            kb.wait(V, S, c)

            def f_d2(v):
                v.tensor_scalar(out=rawdrow[0:1, 16:32], in0=rawdrow[0:1, 16:32],
                                scalar1=-2.0 * PI, scalar2=None, op0=OP.mult)
                v.drain()
                v.tensor_scalar(out=rawdrow[0:1, 32:48], in0=rawdrow[0:1, 16:32],
                                scalar1=-PI, scalar2=None, op0=OP.is_lt)
                v.drain()
                v.scalar_tensor_tensor(out=rawdrow[0:1, 16:32], in0=rawdrow[0:1, 32:48],
                                       scalar=2.0 * PI, in1=rawdrow[0:1, 16:32],
                                       op0=OP.mult, op1=OP.add)
                v.drain()
                v.tensor_scalar(out=rawdrow[0:1, 32:48], in0=rawdrow[0:1, 16:32],
                                scalar1=PI / 2, scalar2=None, op0=OP.add)
                v.drain()
                v.tensor_scalar(out=rawdrow[0:1, 48:64], in0=rawdrow[0:1, 32:48],
                                scalar1=PI, scalar2=None, op0=OP.is_gt)
                v.drain()
                v.scalar_tensor_tensor(out=rawdrow[0:1, 32:48], in0=rawdrow[0:1, 48:64],
                                       scalar=-2.0 * PI, in1=rawdrow[0:1, 32:48],
                                       op0=OP.mult, op1=OP.add)
            kb.on(V, f_d2)
            c = kb.bump(V)
            kb.wait(S, V, c)

            def f_d3(sx):
                sx.activation(out=rawdrow[0:1, 16:32], in_=rawdrow[0:1, 16:32], func=AF.Sin)
                sx.drain()
                sx.activation(out=rawdrow[0:1, 32:48], in_=rawdrow[0:1, 32:48], func=AF.Sin)
                sx.drain()
            kb.on(S, f_d3)
            c = kb.bump(S)
            kb.wait(V, S, c)

            def f_jj(v):
                cj = jjrow[0:1, 0:512].rearrange("p (a b) -> p a b", a=32)
                sj = jjrow[0:1, 512:1024].rearrange("p (a b) -> p a b", a=32)
                cd = rawdrow[0:1, 32:48].unsqueeze(1).broadcast_to((1, 32, 16))
                sd = rawdrow[0:1, 16:32].unsqueeze(1).broadcast_to((1, 32, 16))
                co = jjrow[0:1, 1028:1540].rearrange("p (a b) -> p a b", a=32)
                so = jjrow[0:1, 1540:2052] if False else jjrow[0:1, 1536:2048].rearrange("p (a b) -> p a b", a=32)
                tmp = bff[0:1, 15104:15616].rearrange("p (a b) -> p a b", a=32)
                v.tensor_tensor(out=co, in0=cj, in1=cd, op=OP.mult)
                v.drain()
                v.tensor_tensor(out=tmp, in0=sj, in1=sd, op=OP.mult)
                v.drain()
                v.tensor_tensor(out=co, in0=co, in1=tmp, op=OP.subtract)
                v.drain()
                v.tensor_tensor(out=so, in0=sj, in1=cd, op=OP.mult)
                v.drain()
                v.tensor_tensor(out=tmp, in0=cj, in1=sd, op=OP.mult)
                v.drain()
                v.tensor_tensor(out=so, in0=so, in1=tmp, op=OP.add)
            kb.on(V, f_jj)
            c = kb.bump(V)
            kb.wait(TE, V, c)

            def f_bc(t):
                t.matmul(psB[:, 0:512], onesf[0:1, :], jjrow[0:1, 1028:1540], start=True, stop=True)
                t.matmul(psB[:, 512:1024], onesf[0:1, :], jjrow[0:1, 1536:2048], start=True, stop=True)
                t.matmul(psB[:, 1024:1028], onesf[0:1, :], jjrow[0:1, 1024:1028], start=True, stop=True)
                t.matmul(psB[:, 1028:1060], onesf[0:1, :], rawdrow[0:1, 16:48], start=True, stop=True)
            kb.on(TE, f_bc)
            c = kb.bump(TE)
            kb.wait(V, TE, c)

            def f_bcev(v):
                v.tensor_copy(out=jjbc[:, 0:512], in_=psB[:, 0:512])
                v.tensor_copy(out=jjbc[:, 512:1024], in_=psB[:, 512:1024])
                v.tensor_copy(out=msc[:, 0:4], in_=psB[:, 1024:1028])
                v.tensor_copy(out=msc2[:, 0:32], in_=psB[:, 1028:1060])
                v.drain()
                v.tensor_scalar(out=tview[:, 66:67], in0=tview[:, 65:66], scalar1=64.0,
                                scalar2=None, op0=OP.is_lt)
            kb.on(V, f_bcev)
            cbc = kb.bump(V)
            kb.wait(TE, V, cbc)

            emit_ln_transpose(src_dram, xT_b)
            kb.barrier()

            wfree = [0]

            def wload(nm, off, key="wA"):
                R, C = dr[nm].shape
                a = R // 128
                view = wfl[:, off:off + a * C].rearrange("p (a b) -> p a b", a=a)
                kb.wait(G, TE, wfree[0])
                ld = kb.dma(G, view, dr[nm].rearrange("(a p) b -> p a b", p=128), key=key)
                return view, ld

            pslots = [psB[:, i * 512:(i + 1) * 512] for i in range(3)]
            pslot_free = {0: ("v", 0), 1: ("v", 0), 2: ("v", 0)}
            slot_i = [0]

            def mm_out(lhsT_fn, rhs_fn, nacc, evac_fn, eng=S, width=512, po=128):
                sl = slot_i[0] % 3
                slot_i[0] += 1
                ps = pslots[sl][0:po, 0:width]
                pe_eng, pe_cnt = pslot_free[sl]
                kb.wait(TE, pe_eng, pe_cnt)
                for a in range(nacc):
                    def f_mm(t, a=a, ps=ps):
                        t.matmul(ps, lhsT_fn(a), rhs_fn(a), start=(a == 0), stop=(a == nacc - 1))
                    kb.on(TE, f_mm)
                cm = kb.bump(TE)
                kb.wait(eng, TE, cm)

                def f_ev(e, ps=ps):
                    evac_fn(e, ps)
                kb.on(eng, f_ev)
                ce = kb.bump(eng)
                pslot_free[sl] = (eng, ce)
                return ce

            wv, ld = wload("wdkv", 0)
            kb.wait(TE, ld[0], ld[1])
            for ec in range(2):
                for tc in range(4):
                    mm_out(lambda a, ec=ec, wv=wv: wv[:, a, ec * 128:(ec + 1) * 128],
                           lambda a, tc=tc: xT_b[:, a, tc * 512:(tc + 1) * 512], 8,
                           lambda e, ps, ec=ec, tc=tc: e.activation(
                               out=ckvT[:, ec, tc * 512:(tc + 1) * 512], in_=ps, func=AF.Copy))
            wfree[0] = kb.cnt["t"]
            wuk_v, ld1 = wload("wuk", 2048, key="wB")
            wuv_v, ld2 = wload("wuv", 3072, key="wC")
            kb.wait(TE, ld1[0], ld1[1])
            kb.wait(TE, ld2[0], ld2[1])
            for ec in range(4):
                for tc in range(4):
                    mm_out(lambda a, ec=ec, wuk_v=wuk_v: wuk_v[:, a, ec * 128:(ec + 1) * 128],
                           lambda a, tc=tc: ckvT[:, a, tc * 512:(tc + 1) * 512], 2,
                           lambda e, ps, ec=ec, tc=tc: e.activation(
                               out=kTb[:, ec, tc * 512:(tc + 1) * 512], in_=ps, func=AF.Copy))
            for i in range(NT):
                mm_out(lambda a, i=i: ckvT[:, a, i * 128:(i + 1) * 128],
                       lambda a, wuk_v=wuk_v: wuk_v[:, a, :], 2,
                       lambda e, ps, i=i: e.activation(out=kstg[:, i, :], in_=ps, func=AF.Copy))
                mm_out(lambda a, i=i: ckvT[:, a, i * 128:(i + 1) * 128],
                       lambda a, wuv_v=wuv_v: wuv_v[:, a, :], 2,
                       lambda e, ps, i=i: e.activation(out=vstg[:, i, :], in_=ps, func=AF.Copy))
            wfree[0] = kb.cnt["t"]
            wv, ld = wload("wdq", 0)
            kb.wait(TE, ld[0], ld[1])
            for ec in range(4):
                for tc in range(4):
                    mm_out(lambda a, ec=ec, wv=wv: wv[:, a, ec * 128:(ec + 1) * 128],
                           lambda a, tc=tc: xT_b[:, a, tc * 512:(tc + 1) * 512], 8,
                           lambda e, ps, ec=ec, tc=tc: e.activation(
                               out=cqT[:, ec, tc * 512:(tc + 1) * 512], in_=ps, func=AF.Copy))
            wfree[0] = kb.cnt["t"]
            wuq_v, ld = wload("wuq", 0)
            kb.wait(TE, ld[0], ld[1])
            for i in range(NT):
                mm_out(lambda a, i=i: cqT[:, a, i * 128:(i + 1) * 128],
                       lambda a, wuq_v=wuq_v: wuq_v[:, a, :], 4,
                       lambda e, ps, i=i: e.activation(out=qallb[:, i, :], in_=ps, func=AF.Copy, scale=SCB))
            wfree[0] = kb.cnt["t"]
            wkr_v, ldkr = wload("wkr", 3072, key="wC")
            kb.wait(TE, ldkr[0], ldkr[1])
            for i in range(NT):
                ckr = mm_out(lambda a, i=i: xT_b[:, a, i * 128:(i + 1) * 128],
                             lambda a: wkr_v[:, a, :], 8,
                             lambda e, ps: e.tensor_copy(out=scD[:, 256:288], in_=ps), eng=V, width=32)
                kb.wait(S, V, ckr)

                def f_spk(sx):
                    sx.activation(out=sc4[:, 256:288], in_=scD[:, 256:288], func=AF.Abs)
                    sx.drain()
                    sx.activation(out=sc4[:, 256:288], in_=sc4[:, 256:288], func=AF.Exp, scale=-1.0)
                    sx.drain()
                    sx.activation(out=sc4[:, 256:288], in_=sc4[:, 256:288], func=AF.Ln, bias=1.0)
                    sx.drain()
                kb.on(S, f_spk)
                cspk = kb.bump(S)
                kb.wait(V, S, cspk)

                def f_muk(v, i=i):
                    v.tensor_scalar(out=scD[:, 256:288], in0=scD[:, 256:288], scalar1=0.0,
                                    scalar2=None, op0=OP.max)
                    v.drain()
                    v.tensor_tensor(out=mutok[:, i, :], in0=sc4[:, 256:288], in1=scD[:, 256:288], op=OP.add)
                    v.drain()
                kb.on(V, f_muk)
                cmk = kb.bump(V)
            kb.barrier()
            stg = None
            for i in range(NT):
                stg = kb.dma(Q, dr["gtab"][i * 128:(i + 1) * 128, 0:512], kstg[:, i, :], key="ab0")
                stg = kb.dma(Q, dr["gtab"][i * 128:(i + 1) * 128, 512:1024], vstg[:, i, :], key="ab0")
                stg2 = kb.dma(G, dr["gtab"][i * 128:(i + 1) * 128, 1024:1056],
                              mutok[:, i, :], key="abg")
                stg2 = kb.dma(G, dr["gtab"][i * 128:(i + 1) * 128, 1056:1088],
                              zpad[:], key="abg")
            kb.wait(Q, stg[0], stg[1])
            kb.wait(Q, stg2[0], stg2[1])
            kb.barrier()
            qrstg = bff[:, 8192:10240].bitcast(BF16).rearrange("p (a b) -> p a b", a=NT)
            wqr_v, ldqr = wload("wqr", 2048, key="wB")
            kb.wait(TE, ldqr[0], ldqr[1])
            for i in range(NT):
                mm_out(lambda a, i=i: cqT[:, a, i * 128:(i + 1) * 128],
                       lambda a: wqr_v[:, a, :], 4,
                       lambda e, ps, i=i: e.activation(out=qrstg[:, i, :], in_=ps, func=AF.Copy),
                       width=256)
            kb.barrier()
            tab_free = 0
            for i in range(NT):
                kb.wait(Q, V, tab_free)
                lt = kb.dma(Q, scA[:, 0:32], dr["tb_tq"][i * 128:(i + 1) * 128, :], key="ab0")
                lt = kb.dma(Q, scA[:, 32:64], dr["tb_t2"][i * 128:(i + 1) * 128, :], key="ab0")
                lt = kb.dma(Q, scA[:, 64:96], dr["tb_pp"][i * 128:(i + 1) * 128, :], key="ab0")
                kb.wait(V, lt[0], lt[1])

                def f_cp(v, i=i):
                    v.tensor_copy(out=scD[:, 0:256], in_=qrstg[:, i, :])
                    v.drain()
                kb.on(V, f_cp)
                ccp = kb.bump(V)
                kb.wait(S, V, ccp)

                def f_sp(sx):
                    sx.activation(out=sc4[:, 0:256], in_=scD[:, 0:256], func=AF.Abs)
                    sx.drain()
                    sx.activation(out=sc4[:, 0:256], in_=sc4[:, 0:256], func=AF.Exp, scale=-1.0)
                    sx.drain()
                    sx.activation(out=sc4[:, 0:256], in_=sc4[:, 0:256], func=AF.Ln, bias=1.0)
                    sx.drain()
                kb.on(S, f_sp)
                csp = kb.bump(S)
                kb.wait(V, S, csp)

                def f_build(v, i=i):
                    v.tensor_scalar(out=scD[:, 0:256], in0=scD[:, 0:256], scalar1=0.0,
                                    scalar2=None, op0=OP.max)
                    v.drain()
                    v.tensor_tensor(out=sc4[:, 0:256], in0=sc4[:, 0:256], in1=scD[:, 0:256], op=OP.add)
                    v.drain()
                    v.tensor_scalar(out=sc4[:, 0:256], in0=sc4[:, 0:256], scalar1=SCB,
                                    scalar2=None, op0=OP.mult)
                    v.drain()
                    tqc, tqs = scA[:, 0:16], scA[:, 16:32]
                    cdb, sdb = msc2[:, 16:32], msc2[:, 0:16]
                    v.tensor_tensor(out=scA[:, 96:112], in0=tqc, in1=cdb, op=OP.mult)
                    v.tensor_tensor(out=scA[:, 112:128], in0=tqs, in1=cdb, op=OP.mult)
                    v.drain()
                    v.tensor_tensor(out=msc[:, 4:20], in0=tqs, in1=sdb, op=OP.mult)
                    v.tensor_tensor(out=scA[:, 128:144], in0=tqc, in1=sdb, op=OP.mult)
                    v.drain()
                    v.tensor_tensor(out=scA[:, 96:112], in0=scA[:, 96:112], in1=msc[:, 4:20], op=OP.subtract)
                    v.tensor_tensor(out=scA[:, 112:128], in0=scA[:, 112:128], in1=scA[:, 128:144], op=OP.add)
                    v.drain()

                    def jbuild(a1, a2, cts, sts, dst16a, dst16b):
                        v.tensor_tensor(out=scE[:, 0:128].rearrange("p (a b) -> p a b", a=8),
                                        in0=a1, in1=cts, op=OP.mult)
                        v.tensor_tensor(out=scE[:, 128:256].rearrange("p (a b) -> p a b", a=8),
                                        in0=a2, in1=sts, op=OP.mult)
                        v.drain()
                        v.tensor_tensor(out=dst16a, in0=scE[:, 0:128].rearrange("p (a b) -> p a b", a=8),
                                        in1=scE[:, 128:256].rearrange("p (a b) -> p a b", a=8), op=OP.subtract)
                        v.drain()
                        v.tensor_tensor(out=scE[:, 0:128].rearrange("p (a b) -> p a b", a=8),
                                        in0=a1, in1=sts, op=OP.mult)
                        v.tensor_tensor(out=scE[:, 128:256].rearrange("p (a b) -> p a b", a=8),
                                        in0=a2, in1=cts, op=OP.mult)
                        v.drain()
                        v.tensor_tensor(out=dst16b, in0=scE[:, 0:128].rearrange("p (a b) -> p a b", a=8),
                                        in1=scE[:, 128:256].rearrange("p (a b) -> p a b", a=8), op=OP.add)
                        v.drain()
                    a1 = sc4[:, 0:256].rearrange("p (a b) -> p a b", a=8)[:, :, 0:16]
                    a2 = sc4[:, 0:256].rearrange("p (a b) -> p a b", a=8)[:, :, 16:32]
                    ctb = scA[:, 96:112].unsqueeze(1).broadcast_to((128, 8, 16))
                    stb = scA[:, 112:128].unsqueeze(1).broadcast_to((128, 8, 16))
                    jbuild(a1, a2, ctb, stb, jslot[:, i, :, 0:16], jslot[:, i, :, 16:32])
                    c2b = scA[:, 32:48].unsqueeze(1).broadcast_to((128, 8, 16))
                    s2b2 = scA[:, 48:64].unsqueeze(1).broadcast_to((128, 8, 16))
                    jbuild(a1, a2, c2b, s2b2, jeff[:, i, :, 0:16], jeff[:, i, :, 16:32])
                    mu1f = scD[:, 256:272]
                    mu2f = scD[:, 272:288]
                    v.tensor_copy(out=scD[:, 256:288], in_=mutok[:, i, :])
                    v.drain()
                    ppc, pps = scA[:, 64:80], scA[:, 80:96]
                    v.tensor_tensor(out=scE[:, 0:16], in0=mu1f, in1=ppc, op=OP.mult)
                    v.tensor_tensor(out=scE[:, 16:32], in0=mu2f, in1=pps, op=OP.mult)
                    v.tensor_tensor(out=scE[:, 32:48], in0=mu1f, in1=pps, op=OP.mult)
                    v.tensor_tensor(out=scE[:, 48:64], in0=mu2f, in1=ppc, op=OP.mult)
                    v.drain()
                    v.tensor_tensor(out=hefftok[:, i, 0:16], in0=scE[:, 0:16], in1=scE[:, 16:32], op=OP.subtract)
                    v.tensor_tensor(out=hefftok[:, i, 16:32], in0=scE[:, 32:48], in1=scE[:, 48:64], op=OP.add)
                    v.drain()
                    if i == 0:
                        ct2, st2 = scA[:, 96:112], scA[:, 112:128]
                        v.tensor_tensor(out=scE[:, 0:16], in0=mu1f, in1=ct2, op=OP.mult)
                        v.tensor_tensor(out=scE[:, 16:32], in0=mu2f, in1=st2, op=OP.mult)
                        v.tensor_tensor(out=scE[:, 32:48], in0=mu1f, in1=st2, op=OP.mult)
                        v.tensor_tensor(out=scE[:, 48:64], in0=mu2f, in1=ct2, op=OP.mult)
                        v.drain()
                        v.tensor_tensor(out=qc_scr[:, 0:16], in0=scE[:, 0:16], in1=scE[:, 16:32], op=OP.subtract)
                        v.tensor_tensor(out=qc_scr[:, 16:32], in0=scE[:, 32:48], in1=scE[:, 48:64], op=OP.add)
                        v.drain()
                kb.on(V, f_build)
                tab_free = kb.bump(V)
                if i == 0:
                    kb.wait(TE, V, tab_free)

                    def f_h2t(t):
                        t.transpose(psA[:, 1536:1600].bitcast(BF16)[0:32, 0:128],
                                    qc_scr[:, 0:32], identb[:])
                    kb.on(TE, f_h2t)
                    ch2 = kb.bump(TE)
                    kb.wait(V, TE, ch2)

                    def f_h2ev(v):
                        v.tensor_copy(out=h2t0[0:32, :], in_=psA[:, 1536:1600].bitcast(BF16)[0:32, 0:128])
                        v.drain()
                    kb.on(V, f_h2ev)
                    kb.bump(V)
            wfree[0] = kb.cnt["t"]
            wiq_v, ld1 = wload("widxq", 0)
            wik_v, ld2 = wload("widxk", 2048, key="wB")
            kb.wait(TE, ld1[0], ld1[1])
            kb.wait(TE, ld2[0], ld2[1])
            for i in range(NT):
                mm_out(lambda a, i=i: xT_b[:, a, i * 128:(i + 1) * 128],
                       lambda a, wiq_v=wiq_v: wiq_v[:, a, :], 8,
                       lambda e, ps, i=i: e.activation(out=qitok[:, i, :], in_=ps, func=AF.Copy), width=256)
            for tc in range(4):
                mm_out(lambda a, wik_v=wik_v: wik_v[:, a, :],
                       lambda a, tc=tc: xT_b[:, a, tc * 512:(tc + 1) * 512], 8,
                       lambda e, ps, tc=tc: e.activation(
                           out=kiT[0:64, tc * 512:(tc + 1) * 512], in_=ps, func=AF.Copy), po=64)
            kb.barrier()
            for dst, srcnm in ((mkband, "mk_band"), (mkb0, "mk_b0"),
                               (mkidiag, "mk_idiag"), (mkiprev, "mk_iprev")):
                lmk = kb.dma(Q, dst, dr[srcnm], key="ab0")
            kb.wait(V, lmk[0], lmk[1])
            kb.barrier()

            pb = psA[:, 1536:2048].bitcast(BF16)
            wfree[0] = kb.cnt["t"]
            wout_v, ldwo = wload("wout", 0)
            prev_store = None
            for i in range(NT):
                kw = (i + 1) * 128
                vprev = kb.cnt["v"]
                islot_free = {0: 0, 1: 0}
                for h4 in range(4):
                    kb.wait(TE, V, vprev)

                    def f_tp(t, i=i, h4=h4):
                        t.transpose(pb[0:64, 0:128], qitok[:, i, h4 * 64:(h4 + 1) * 64], identb[:])
                    kb.on(TE, f_tp)
                    ctp = kb.bump(TE)
                    kb.wait(S, TE, ctp)

                    def f_ev(sx):
                        sx.activation(out=qc_scr[0:64, :], in_=pb[0:64, 0:128], func=AF.Copy)
                        sx.drain()
                    kb.on(S, f_ev)
                    cqe = kb.bump(S)
                    kb.wait(TE, S, cqe)
                    for kc in range((kw + 511) // 512):
                        w = min(512, kw - kc * 512)
                        ps = psA[:, (kc % 2) * 512:(kc % 2) * 512 + w]
                        kb.wait(TE, V, islot_free[kc % 2])

                        def f_mm(t, ps=ps, kc=kc, w=w):
                            t.matmul(ps, qc_scr[0:64, :], kiT[0:64, kc * 512:kc * 512 + w],
                                     start=True, stop=True)
                        kb.on(TE, f_mm)
                        cm = kb.bump(TE)
                        kb.wait(V, TE, cm)

                        def f_acc(v, ps=ps, kc=kc, w=w, h4=h4):
                            if h4 == 0:
                                v.tensor_scalar(out=iacc[:, kc * 512:kc * 512 + w], in0=ps,
                                                scalar1=0.0, scalar2=msc[:, h4:h4 + 1],
                                                op0=OP.max, op1=OP.mult)
                            else:
                                v.tensor_scalar(out=scD[:, 0:w], in0=ps, scalar1=0.0,
                                                scalar2=msc[:, h4:h4 + 1], op0=OP.max, op1=OP.mult)
                                v.drain()
                                v.tensor_tensor(out=iacc[:, kc * 512:kc * 512 + w],
                                                in0=iacc[:, kc * 512:kc * 512 + w],
                                                in1=scD[:, 0:w], op=OP.add)
                            v.drain()
                        kb.on(V, f_acc)
                        vprev = kb.bump(V)
                        islot_free[kc % 2] = vprev

                def f_mask(v, i=i, kw=kw):
                    if i >= 1:
                        v.tensor_tensor(out=iacc[:, (i - 1) * 128:i * 128],
                                        in0=iacc[:, (i - 1) * 128:i * 128], in1=mkiprev, op=OP.add)
                        v.drain()
                    v.tensor_tensor(out=iacc[:, i * 128:(i + 1) * 128],
                                    in0=iacc[:, i * 128:(i + 1) * 128], in1=mkidiag, op=OP.add)
                    v.drain()
                    for r in range(4):
                        v.max(out=tview[:, 128:136], in_=iacc[:, 0:kw])
                        v.drain()
                        v.max_index(out=idxq[:, r * 8:(r + 1) * 8], in_max=tview[:, 128:136],
                                    in_values=iacc[:, 0:kw])
                        v.drain()
                        if r < 3:
                            v.match_replace(out=iacc[:, 0:kw], in_to_replace=tview[:, 128:136],
                                            in_values=iacc[:, 0:kw], imm_value=NEG)
                            v.drain()
                    if i == 0:
                        v.tensor_scalar(out=idxq[:, 64:96], in0=tview[:, 0:32],
                                        scalar1=tview[:, 64:65], scalar2=None, op0=OP.is_ge)
                        v.drain()
                        v.copy_predicated(out=idxq[:, 0:32], mask=idxq[:, 64:96],
                                          data=idxq[:, 32:64])
                        v.drain()
                kb.on(V, f_mask)
                cex = kb.bump(V)

                w0t = i - 1 if i >= 1 else 0
                W = 256 if i >= 1 else 128
                pb = psB[:, 512:1024].bitcast(BF16)
                lv = kb.dma(Q, vwin[:, 0, :], dr["gtab"][w0t * 128:(w0t + 1) * 128, 512:1024], key="gat")
                if i >= 1:
                    lv = kb.dma(Q, vwin[:, 1, :], dr["gtab"][i * 128:(i + 1) * 128, 512:1024], key="gat")
                kb.wait(TE, V, cex)
                for w in range(2 if i >= 1 else 1):
                    def f_tph(t, i=i, w=w, w0t=w0t):
                        t.transpose(pb[0:32, 0:128], hefftok[:, w0t + w, :], identb[:])
                    kb.on(TE, f_tph)
                    cth = kb.bump(TE)
                    kb.wait(S, TE, cth)

                    def f_evh(sx, w=w):
                        sx.activation(out=hw_scr[0:32, w, :], in_=pb[0:32, 0:128], func=AF.Copy)
                        sx.drain()
                    kb.on(S, f_evh)
                    chv = kb.bump(S)
                    kb.wait(TE, S, chv)
                for h in range(8):
                    hp = (h % 2) * 64
                    hc = h // 2

                    def f_tpq(t, i=i, h=h, hp=hp):
                        t.transpose(pb[hp:hp + 64, 128:256] if hp else pb[0:64, 128:256],
                                    qallb[:, i, h * 64:(h + 1) * 64], identb[:])
                        t.transpose(pb[0:32, 256:384], jeff[:, i, h, :], identb[:])
                        if i == 0:
                            t.transpose(pb[0:32, 384:512], jslot[:, 0, h, :], identb[:])
                    kb.on(TE, f_tpq)
                    ctq = kb.bump(TE)
                    kb.wait(S, TE, ctq)

                    def f_evq(sx, hp=hp, i=i):
                        sx.activation(out=qc_scr[hp:hp + 64, :], in_=pb[hp:hp + 64, 128:256] if hp else pb[0:64, 128:256], func=AF.Copy)
                        sx.activation(out=jf_scr[0:32, :], in_=pb[0:32, 256:384], func=AF.Copy)
                        if i == 0:
                            sx.activation(out=jf2_scr[0:32, :], in_=pb[0:32, 384:512], func=AF.Copy)
                        sx.drain()
                    kb.on(S, f_evq)
                    cev = kb.bump(S)
                    kb.wait(TE, S, cev)

                    def f_bmm(t, i=i, h=h, hp=hp, hc=hc, w0t=w0t, W=W):
                        t.matmul(psA[:, h * 256:h * 256 + W], qc_scr[hp:hp + 64, :],
                                 kTb[hp:hp + 64, hc, w0t * 128:w0t * 128 + W], start=True, stop=False)
                        t.matmul(psA[:, h * 256:h * 256 + W], jf_scr[0:32, :],
                                 hw_scr[0:32, :, :].rearrange("p a b -> p (a b)")[0:32, 0:W],
                                 start=False, stop=True)
                        if i == 0:
                            t.matmul(psB[:, 0:W], qc_scr[hp:hp + 64, :],
                                     kTb[hp:hp + 64, hc, 0:W], start=True, stop=False)
                            t.matmul(psB[:, 0:W], jf2_scr[0:32, :],
                                     h2t0[0:32, 0:W], start=False, stop=True)
                    kb.on(TE, f_bmm)
                    cbm = kb.bump(TE)
                    kb.wait(V, TE, cbm)

                    def f_bev(v, i=i, h=h, W=W):
                        if i >= 1:
                            v.tensor_tensor(out=sbandv[:, h, :], in0=sbandv[:, h, :],
                                            in1=mkband, op=OP.add)
                        else:
                            v.tensor_copy(out=scD[:, 0:128], in_=sbandv[:, h, 0:128])
                            v.drain()
                            v.tensor_tensor(out=scD[:, 128:256], in0=psB[:, 0:128],
                                            in1=scD[:, 0:128], op=OP.subtract)
                            v.drain()
                            v.scalar_tensor_tensor(out=scD[:, 128:256], in0=scD[:, 128:256],
                                                   scalar=tview[:, 66:67], in1=scD[:, 0:128],
                                                   op0=OP.mult, op1=OP.add)
                            v.drain()
                            v.tensor_tensor(out=sbandv[:, h, 0:128], in0=scD[:, 128:256],
                                            in1=mkb0, op=OP.add)
                            v.drain()
                            v.memset(sbandv[:, h, 128:256], NEG)
                        v.drain()
                    kb.on(V, f_bev)
                    cbv = kb.bump(V)
                    kb.wait(TE, V, cbv)

                def f_minit(v):
                    v.tensor_reduce(out=smv[:, 0:8], in_=sbandv[:, :, :], axis=mybir.AxisListType.X, op=OP.max)
                    v.drain()
                    v.tensor_scalar(out=smv[:, 0:8], in0=smv[:, 0:8], scalar1=32.0, scalar2=None, op0=OP.add)
                    v.memset(smv[:, 8:16], 0.0)
                    v.memset(slacc[:], 0.0)
                    v.drain()
                kb.on(V, f_minit)
                cmi = kb.bump(V)

                def f_bexp(v):
                    v.tensor_tensor(out=sbandv[:, :, :], in0=sbandv[:, :, :],
                                    in1=smv[:, 0:8].unsqueeze(2).broadcast_to((128, 8, 256)), op=OP.subtract)
                    v.drain()
                kb.on(V, f_bexp)
                cbx = kb.bump(V)
                kb.wait(S, V, cbx)

                def f_bexp2(sx):
                    sx.activation(out=prodb[:, 0:2048], in_=sbandv[:, :, :].rearrange("p a b -> p (a b)"),
                                  func=AF.Exp)
                    sx.drain()
                kb.on(S, f_bexp2)
                cbe = kb.bump(S)
                kb.wait(V, S, cbe)

                def f_zband(v):
                    v.tensor_reduce(out=smv[:, 16:24],
                                    in_=prodb[:, 0:2048].rearrange("p (a b) -> p a b", a=8),
                                    axis=mybir.AxisListType.X, op=OP.add)
                    v.drain()
                    v.tensor_tensor(out=smv[:, 8:16], in0=smv[:, 8:16], in1=smv[:, 16:24], op=OP.add)
                    v.drain()
                kb.on(V, f_zband)
                kb.bump(V)
                ssc = shr12[:, 512:768].rearrange("p (a b) -> p a b", a=8)
                vscore_done = {0: 0, 1: 0}
                gprod_done = {0: None, 1: None}
                gb_done = {}

                def issue_gathers(qb):
                    buf = qb % 2
                    kb.wait(G, V, vscore_done[buf])
                    sem = kb.dma_sem_map.setdefault("gag%d" % buf, kb.dma_sem_pool[len(kb.dma_sem_map)])
                    kb.dma_cnt.setdefault(sem, 0)
                    for j in range(4):
                        col = qb * 4 + j
                        kb.dma_cnt[sem] += 16
                        kb.on(G, lambda g, col=col, j=j, sem=sem, buf=buf: g.indirect_dma_start(
                            out=gout2[buf][:, j, :], out_offset=None, in_=dr["gtab"],
                            in_offset=bass.IndirectOffsetOnAxis(ap=idxq[:, col:col + 1], axis=0))
                            .then_inc(kb.sem[sem], 16))
                        gb_done[qb] = (sem, kb.dma_cnt[sem])
                kb.wait(G, V, cex)
                issue_gathers(0)
                for qb in range(8):
                    buf = qb % 2
                    kb.wait(G, gb_done[qb][0], gb_done[qb][1])

                    def f_ghb(g, qb=qb, buf=buf):
                        mu1 = gout2[buf][:, :, 1024:1040]
                        mu2 = gout2[buf][:, :, 1040:1056]
                        cjv = jjbc[:, 0:512].rearrange("p (a b) -> p a b", a=32)[:, qb * 4:(qb + 1) * 4, :]
                        sjv = jjbc[:, 512:1024].rearrange("p (a b) -> p a b", a=32)[:, qb * 4:(qb + 1) * 4, :]
                        t1 = bff[:, 15552:15616].rearrange("p (a b) -> p a b", a=4)
                        t2 = bff[:, 15616:15680].rearrange("p (a b) -> p a b", a=4)
                        hq = bff[:, 15488 + buf * 256:15552 + buf * 256].bitcast(BF16).rearrange("p (a b) -> p a b", a=4)
                        g.tensor_tensor(out=t1, in0=mu1, in1=cjv, op=OP.mult)
                        g.tensor_tensor(out=t2, in0=mu2, in1=sjv, op=OP.mult)
                        g.drain()
                        g.tensor_tensor(out=hq[:, :, 0:16], in0=t1, in1=t2, op=OP.subtract)
                        g.drain()
                        g.tensor_tensor(out=t1, in0=mu1, in1=sjv, op=OP.mult)
                        g.tensor_tensor(out=t2, in0=mu2, in1=cjv, op=OP.mult)
                        g.drain()
                        g.tensor_tensor(out=hq[:, :, 16:32], in0=t1, in1=t2, op=OP.add)
                        g.drain()
                    kb.on(G, f_ghb)
                    cgb = kb.bump(G)
                    kb.wait(V, gb_done[qb][0], gb_done[qb][1])
                    kb.wait(V, G, cgb)

                    def f_slot(v, qb=qb, i=i, buf=buf):
                        hq = bff[:, 15488 + buf * 256:15552 + buf * 256].bitcast(BF16).rearrange("p (a b) -> p a b", a=4)
                        for j in range(4):
                            pv = bff[:, 15104:15488].bitcast(BF16).rearrange("p (a b) -> p a b", a=8)
                            v.tensor_tensor(out=pv[:, :, 0:64],
                                            in0=qallb[:, i, :].rearrange("p (a b) -> p a b", a=8),
                                            in1=gout2[buf][:, j, 0:512].rearrange("p (a b) -> p a b", a=8),
                                            op=OP.mult)
                            v.tensor_tensor(out=pv[:, :, 64:96], in0=jslot[:, i, :, :],
                                            in1=hq[:, j, :].unsqueeze(1).broadcast_to((128, 8, 32)),
                                            op=OP.mult)
                            v.drain()
                            v.tensor_reduce(out=ssc[:, :, qb * 4 + j],
                                            in_=pv, axis=mybir.AxisListType.X, op=OP.add)
                            v.drain()
                        sq = ssc[:, :, qb * 4:(qb + 1) * 4]
                        v.tensor_tensor(out=sq, in0=sq,
                                        in1=smv[:, 0:8].unsqueeze(2).broadcast_to((128, 8, 4)), op=OP.subtract)
                        v.drain()
                    kb.on(V, f_slot)
                    c1 = kb.bump(V)
                    vscore_done[buf] = c1
                    if qb + 1 < 8:
                        issue_gathers(qb + 1)
                    kb.wait(S, V, c1)

                    def f_exp2(sx, qb=qb):
                        sx.activation(out=ssc[:, :, qb * 4:(qb + 1) * 4],
                                      in_=ssc[:, :, qb * 4:(qb + 1) * 4], func=AF.Exp)
                        sx.drain()
                    kb.on(S, f_exp2)
                    c4 = kb.bump(S)
                    kb.wait(V, S, c4)
                    kb.wait(G, S, c4)

                    def f_zadd(v, qb=qb):
                        sq = ssc[:, :, qb * 4:(qb + 1) * 4]
                        v.tensor_reduce(out=smv[:, 16:24], in_=sq, axis=mybir.AxisListType.X, op=OP.add)
                        v.drain()
                        v.tensor_tensor(out=smv[:, 8:16], in0=smv[:, 8:16], in1=smv[:, 16:24], op=OP.add)
                        v.drain()
                    kb.on(V, f_zadd)
                    kb.bump(V)
                    for j in range(4):
                        jb = j % 2
                        if gprod_done[jb] is not None:
                            kb.wait(G, V, gprod_done[jb])

                        def f_prod(g, qb=qb, j=j, buf=buf, jb=jb):
                            aj = ssc[:, :, qb * 4 + j].unsqueeze(2).broadcast_to((128, 8, 64))
                            pvx = ptk[:, jb, 0:1024].bitcast(F32)
                            g.tensor_tensor(out=pvx.rearrange("p (a b) -> p a b", a=8),
                                            in0=gout2[buf][:, j, 512:1024].rearrange("p (a b) -> p a b", a=8),
                                            in1=aj, op=OP.mult)
                            g.drain()
                        kb.on(G, f_prod)
                        cgp = kb.bump(G)
                        kb.wait(V, G, cgp)

                        def f_add(v, jb=jb):
                            pvx = ptk[:, jb, 0:1024].bitcast(F32)
                            v.tensor_tensor(out=slacc[:], in0=slacc[:], in1=pvx, op=OP.add)
                            v.drain()
                        kb.on(V, f_add)
                        gprod_done[jb] = kb.bump(V)
                    if qb == 0:
                        kb.wait(TE, lv[0], lv[1])
                        kb.wait(TE, S, cbe)
                    h = qb

                    def f_tpa(t, h=h, i=i):
                        t.transpose(pb[0:128, 0:128], prodb[:, h * 256:h * 256 + 128], identb[:])
                        if i >= 1:
                            t.transpose(pb[0:128, 128:256], prodb[:, h * 256 + 128:h * 256 + 256], identb[:])
                    kb.on(TE, f_tpa)
                    cta = kb.bump(TE)
                    kb.wait(S, TE, cta)

                    def f_eva(sx, i=i):
                        sx.activation(out=hw_scr[:, 0, :], in_=pb[:, 0:128], func=AF.Copy)
                        if i >= 1:
                            sx.activation(out=hw_scr[:, 1, :], in_=pb[:, 128:256], func=AF.Copy)
                        sx.drain()
                    kb.on(S, f_eva)
                    cve = kb.bump(S)
                    kb.wait(TE, S, cve)

                    def f_avb(t, h=h, i=i):
                        nw = 2 if i >= 1 else 1
                        for w in range(nw):
                            t.matmul(psB[:, 1024 + h * 64:1024 + (h + 1) * 64], hw_scr[:, w, :],
                                     vwin[:, w, h * 64:(h + 1) * 64], start=(w == 0), stop=(w == nw - 1))
                    kb.on(TE, f_avb)
                    kb.bump(TE)
                kb.wait(V, TE, kb.cnt["t"])

                def f_merge(v):
                    v.reciprocal(out=smv[:, 40:48], in_=smv[:, 8:16])
                    v.drain()
                    v.tensor_tensor(out=scE[:, 0:512], in0=psB[:, 1024:1536], in1=slacc[:], op=OP.add)
                    v.drain()
                    for h in range(8):
                        v.tensor_scalar(out=o_tok[:, h * 64:(h + 1) * 64], in0=scE[:, h * 64:(h + 1) * 64],
                                        scalar1=smv[:, 40 + h:41 + h], scalar2=None, op0=OP.mult)
                    v.drain()
                kb.on(V, f_merge)
                cmg = kb.bump(V)
                kb.wait(TE, V, cmg)

                kb.wait(TE, ldwo[0], ldwo[1])
                for c4 in range(4):
                    def f_tpo(t, c4=c4):
                        t.transpose(pb[0:128, 0:128], o_tok[:, c4 * 128:(c4 + 1) * 128], identb[:])
                    kb.on(TE, f_tpo)
                    cto = kb.bump(TE)
                    kb.wait(S, TE, cto)

                    def f_evo(sx, c4=c4):
                        sx.activation(out=outTv[:, c4, :], in_=pb[:, 0:128], func=AF.Copy)
                        sx.drain()
                    kb.on(S, f_evo)
                    cvo = kb.bump(S)
                    kb.wait(TE, S, cvo)
                slot = i % 4
                kb.wait(Q, V, kb.cnt["v"]) if False else None
                if kb.win_store[slot] is not None:
                    kb.wait(Q, kb.win_store[slot][0], kb.win_store[slot][1])
                lxw = kb.dma(Q, xw[:, slot, :], src_dram[i * 128:(i + 1) * 128, :], key="win%d" % slot)
                for eh in range(2):
                    ps = psB[:, 1024 + eh * 512:1024 + (eh + 1) * 512]
                    for c4 in range(4):
                        def f_mmo(t, ps=ps, c4=c4, eh=eh):
                            t.matmul(ps, outTv[:, c4, :], wout_v[:, c4, eh * 512:(eh + 1) * 512],
                                     start=(c4 == 0), stop=(c4 == 3))
                        kb.on(TE, f_mmo)
                    cmo = kb.bump(TE)
                    kb.wait(V, TE, cmo)
                    if eh == 0:
                        kb.wait(V, lxw[0], lxw[1])

                    def f_res(v, ps=ps, slot=slot, eh=eh):
                        v.tensor_tensor(out=xw[:, slot, eh * 512:(eh + 1) * 512],
                                        in0=xw[:, slot, eh * 512:(eh + 1) * 512], in1=ps, op=OP.add)
                        v.drain()
                    kb.on(V, f_res)
                    crs = kb.bump(V)
                    kb.wait(TE, V, crs)
                kb.wait(Q, V, kb.cnt["v"])
                sto = kb.dma(Q, dst_dram[i * 128:(i + 1) * 128, :], xw[:, slot, :], key="win%d" % slot)
                kb.win_store[slot] = sto
            kb.barrier()

        emit_ffn(dr["strm"], dr["strm"])
        emit_attn_b(dr["strm"], dr["strm"])
        emit_ffn(dr["strm"], dr["out"])
        kb.barrier()
        for kname, sm in kb.dma_sem_map.items():
            kb.wait(Q, sm, kb.dma_cnt[sm])

        with nc.Block() as block:
            @block.vector
            def _(eng):
                for fn in kb.ops["v"]:
                    fn(eng)

            @block.scalar
            def _(eng):
                for fn in kb.ops["s"]:
                    fn(eng)

            @block.gpsimd
            def _(eng):
                for fn in kb.ops["g"]:
                    fn(eng)

            @block.tensor
            def _(eng):
                for fn in kb.ops["t"]:
                    fn(eng)

            @block.sync
            def _(eng):
                for fn in kb.ops["q"]:
                    fn(eng)

    return nc


def attn_b_host_tables(inputs):
    th = 1.0 / 10000.0 ** (2.0 * np.arange(16) / 32.0)
    t = np.arange(T)[:, None]
    tq = np.concatenate([np.cos(t * th), np.sin(t * th)], 1).astype(np.float32)
    t2 = np.concatenate([np.cos(2 * t * th), np.sin(2 * t * th)], 1).astype(np.float32)
    pp = np.concatenate([np.cos((t + 63) * th), np.sin((t + 63) * th)], 1).astype(np.float32)
    j = np.arange(64, 96)[:, None]
    jj = np.concatenate([np.cos(j * th).reshape(-1), np.sin(j * th).reshape(-1)])[None, :].astype(np.float32)
    p_ = np.arange(128)[:, None]
    r_ = np.arange(256)[None, :]
    mk_band = np.where((r_ >= p_ + 65) & (r_ <= p_ + 128), 0.0, NEG).astype(np.float32)
    c_ = np.arange(128)[None, :]
    mk_b0 = np.where(p_ < 64, np.where(c_ < 64, 0.0, NEG),
                     np.where((c_ >= p_ - 63) & (c_ <= p_), 0.0, NEG)).astype(np.float32)
    mk_idiag = np.where(c_ >= p_ - 63, NEG, 0.0).astype(np.float32)
    mk_iprev = np.where(c_ >= p_ + 65, NEG, 0.0).astype(np.float32)
    d = {
        "wdkv": inputs["L0_attn_b_W_dkv"], "wuk": inputs["L0_attn_b_W_uk"],
        "wuv": inputs["L0_attn_b_W_uv"], "wdq": inputs["L0_attn_b_W_dq"],
        "wuq": inputs["L0_attn_b_W_uq"], "wqr": inputs["L0_attn_b_W_qr"],
        "wkr": inputs["L0_attn_b_W_kr"], "widxq": inputs["L0_attn_b_idx_wq"],
        "widxk": inputs["L0_attn_b_idx_wk"], "wout": inputs["L0_attn_b_W_out"],
        "idxw": np.asarray(inputs["L0_attn_b_idx_w"]).reshape(1, 4),
        "rawd": np.asarray(inputs["L0_attn_b_raw_delta"]).reshape(1, 16),
        "tb_tq": tq, "tb_t2": t2, "tb_pp": pp, "tb_jj": jj,
        "mk_band": mk_band, "mk_b0": mk_b0, "mk_idiag": mk_idiag, "mk_iprev": mk_iprev,
    }
    return {k: np.ascontiguousarray(np.asarray(v, dtype=np.float32)) for k, v in d.items()}


def kernel(**inputs):
    x = np.ascontiguousarray(inputs["L0_attn_a_x"], dtype=np.float32)
    host = {
        "wq": np.ascontiguousarray(inputs["L0_attn_a_Wq"].T),
        "wk": np.ascontiguousarray(inputs["L0_attn_a_Wk"].T),
        "wv": np.ascontiguousarray(inputs["L0_attn_a_Wv"].T),
        "wo": np.ascontiguousarray(inputs["L0_attn_a_Wo"].T),
        "w1": np.ascontiguousarray(inputs["L0_ffn_w1"].T),
        "w2": np.ascontiguousarray(inputs["L0_ffn_w2"].T),
        "convw": np.ascontiguousarray(
            np.asarray(inputs["L0_ffn_conv_w"]).reshape(32, 128, 3).transpose(1, 0, 2)),
        "convb": np.ascontiguousarray(np.asarray(inputs["L0_ffn_conv_b"]).reshape(32, 128).T),
    }
    host.update(attn_b_host_tables(inputs))
    nc = build_core_program(DEBUG_TAPS)
    in_maps = [dict(host, x=x[c]) for c in range(8)]
    res = run_bass_kernel_spmd(nc, in_maps, core_ids=list(range(8)))
    return np.stack([r["out"] for r in res.results], axis=0)
